# version 10
# baseline (speedup 1.0000x reference)
import numpy as np
import ml_dtypes
import concourse.bass as bass
import concourse.mybir as mybir
from concourse import library_config
from concourse.bass_utils import run_bass_kernel_spmd

F32 = mybir.dt.float32
BF16 = mybir.dt.bfloat16
AF = mybir.ActivationFunctionType
ALU = mybir.AluOpType
AX = mybir.AxisListType
bfloat16 = ml_dtypes.bfloat16

B, N, D = 4, 4096, 1024
HEADS, DH, HALF = 16, 64, 32
NQ = 2048
NPAIR = 8
SCALE = float(DH) ** -0.5
MAGIC = 12582912.0
EPS = 1e-5
INT8_MAX = 127.0


def _host_tables():
    j = np.arange(HALF, dtype=np.float64)
    theta = 10000.0 ** (-j / HALF)
    pos = np.arange(N, dtype=np.float64)
    ang = np.outer(theta, pos)
    cos = np.cos(ang)
    sin = np.sin(ang)
    CCt = np.concatenate([cos, cos, cos, cos], axis=0)
    SSt = np.concatenate([-sin, sin, -sin, sin], axis=0)
    return np.concatenate([CCt, SSt], axis=1)


class _Emit:
    ENGS = ("sync", "tensor", "vector", "scalar", "gpsimd")

    def __init__(self, sems):
        self.ops = {e: [] for e in self.ENGS}
        self.cnt = {k: 0 for k in sems}
        self.sems = sems

    def wt(self, eng, tok):
        if tok is not None:
            self.w(eng, tok[0], tok[1])

    def w(self, eng, sem, val):
        if val <= 0:
            return
        assert val <= self.cnt[sem], f"wait {sem}>={val} emitted={self.cnt[sem]}"
        s = self.sems[sem]
        self.ops[eng].append(lambda e, s=s, v=val: e.wait_ge(s, v))

    def op(self, eng, fn, inc=None, n=1):
        if inc is None:
            self.ops[eng].append(fn)
            return None
        s = self.sems[inc]
        self.cnt[inc] += n
        self.ops[eng].append(lambda e, fn=fn, s=s, n=n: fn(e).then_inc(s, n))
        return self.cnt[inc]

    def opd(self, eng, fn, inc=None, n=1):
        r = self.op(eng, fn, inc=inc, n=n)
        self.ops[eng].append(lambda e: e.drain())
        return r


class _DPool:

    def __init__(self, E, names):
        self.E = E
        self.names = names
        self.i = 0

    def dma(self, eng, fn):
        s = self.names[self.i % len(self.names)]
        self.i += 1
        self.E.w(eng, s, self.E.cnt[s])
        v = self.E.op(eng, fn, inc=s, n=16)
        return (s, v)


def build(debug=False):
    nc = bass.Bass("TRN2", target_bir_lowering=False)

    xin = nc.dram_tensor("xin", [N, D], F32, kind="ExternalInput").ap()
    wpair = nc.dram_tensor("wpair", [NPAIR, 128, 3072], F32, kind="ExternalInput").ap()
    wout_d = nc.dram_tensor("wout", [128, 8192], F32, kind="ExternalInput").ap()
    rhsT_d = nc.dram_tensor("rhsT", [128, 8192], F32, kind="ExternalInput").ap()
    bout_d = nc.dram_tensor("bout", [128, 1024], F32, kind="ExternalInput").ap()
    tabs_d = nc.dram_tensor("tabs", [128, 8192], BF16, kind="ExternalInput").ap()
    ident_d = nc.dram_tensor("ident", [128, 128], BF16, kind="ExternalInput").ap()
    identf_d = nc.dram_tensor("identf", [128, 128], F32, kind="ExternalInput").ap()
    ones2_d = nc.dram_tensor("ones2", [128, 2], BF16, kind="ExternalInput").ap()
    ident2_d = nc.dram_tensor("ident2", [2, 2], BF16, kind="ExternalInput").ap()
    onescol_d = nc.dram_tensor("onescol", [1, 128], F32, kind="ExternalInput").ap()
    onescolb_d = nc.dram_tensor("onescolb", [1, 128], BF16, kind="ExternalInput").ap()
    out_d = nc.dram_tensor("out", [NQ, D], F32, kind="ExternalOutput").ap()
    aoT_d = nc.dram_tensor("aoT", [NPAIR, 128, NQ], BF16, kind="Internal").ap()

    from contextlib import ExitStack
    with ExitStack() as _stk:
        en = _stk.enter_context
        ident_sb = en(nc.sbuf_tensor([128, 128], BF16))
        identf_sb = en(nc.sbuf_tensor([128, 128], F32))
        ones2_sb = en(nc.sbuf_tensor([128, 2], BF16))
        ident2_sb = en(nc.sbuf_tensor([2, 2], BF16))
        onescol_sb = en(nc.sbuf_tensor([1, 128], F32))
        zc2_sb = en(nc.sbuf_tensor([2, 128], BF16))
        xt0 = en(nc.sbuf_tensor([128, 1024], F32))
        xt1 = en(nc.sbuf_tensor([128, 1024], F32))
        st_sb = en(nc.sbuf_tensor([128, 16], F32))
        nm_sb = en(nc.sbuf_tensor([2, 8], F32))
        biasAB = en(nc.sbuf_tensor([128, 4], F32))
        rbc = en(nc.sbuf_tensor([64, 512], F32))
        rbch = en(nc.sbuf_tensor([1, 512], BF16))
        rbcl = en(nc.sbuf_tensor([1, 512], BF16))
        onescolb_sb = en(nc.sbuf_tensor([1, 128], BF16))
        rsT_sb = en(nc.sbuf_tensor([8, 128], F32))
        ls16 = en(nc.sbuf_tensor([128, 16], F32))
        abf = en(nc.sbuf_tensor([128, 77952], BF16))
        af32 = en(nc.sbuf_tensor([128, 10272], F32))
        PSC = en(nc.psum_tensor([128, 3072], F32))
        OA = en(nc.psum_tensor([128, 512], F32))
        PJ = en(nc.psum_tensor([128, 512], F32))
        sems = {}
        for name in (["ld%d" % i for i in range(8)] + ["st0", "st1", "dg0", "dg1",
                                                       "tT", "vV", "sS", "gG"]):
            sems[name] = en(nc.semaphore(name="sem_" + name))
        block = en(nc.Block())
        E = _Emit(sems)
        PL = _DPool(E, ["ld%d" % i for i in range(8)])
        PS = _DPool(E, ["st0", "st1"])
        PG = _DPool(E, ["dg0", "dg1"])

        xnT = abf[:, 0:32768]
        tabs = abf[:, 32768:40960]
        CC = tabs[:, 0:4096]
        SS = tabs[:, 4096:8192]
        wbf = abf[:, 40960:44032]
        QTb = (abf[:, 44032:46080], abf[:, 46080:48128])
        KTb = (abf[:, 48128:52224], abf[:, 52224:56320])
        Vpb = (abf[:, 56320:60480], abf[:, 60480:64640])
        Tstg = abf[:, 64640:68736]
        Tsw = abf[:, 68736:72832]
        stag = (abf[:, 72832:73856], abf[:, 73856:74880], abf[:, 74880:75904])
        aopair = abf[:, 75904:77952]

        aoTs = abf[:, 0:16384]
        woutb = abf[:, 16384:24576]
        qrhs = abf[:, 24576:32768]
        qlhsT16 = abf[:, 44032:60416]
        tr_bf = tabs[:, 2048:3072]

        wstg = af32[:, 0:3072]
        oaev = af32[:, 3072:3584]
        vstg = af32[:, 3584:4096]
        xnb32 = af32[:, 5120:6144]
        rstg = af32[:, 0:4096]
        trash = af32[:, 4096:5120]
        rsb = af32[:, 5120:6144]
        bouts = af32[:, 6144:7168]
        gbuf = af32[:, 7168:8192]
        sbf = af32[:, 8192:9216]
        p3t = af32[:, 9216:10240]
        rsv = af32[:, 10240:10248]
        rcv = af32[:, 10248:10256]
        rsrow = p3t[0:1, 0:1024]
        outb = (rstg[:, 0:1024], rstg[:, 1024:2048])

        st_sum = st_sb[:, 0:1]; st_ssq = st_sb[:, 1:2]; st_mean = st_sb[:, 2:3]
        st_ex2 = st_sb[:, 3:4]; st_m2 = st_sb[:, 4:5]; st_var = st_sb[:, 5:6]
        st_vep = st_sb[:, 6:7]; st_s0 = st_sb[:, 7:8]; st_r0 = st_sb[:, 8:9]
        st_a = st_sb[:, 9:10]; st_b = st_sb[:, 10:11]; st_c = st_sb[:, 11:12]
        st_rstd = st_sb[:, 12:13]; st_amax = st_sb[:, 13:14]
        st_lrcp = st_sb[:, 15:16]

        toks = {}
        for name, dst, src in (("ident", ident_sb, ident_d), ("identf", identf_sb, identf_d),
                               ("ones2", ones2_sb, ones2_d), ("ident2", ident2_sb, ident2_d),
                               ("onescol", onescol_sb, onescol_d), ("onescolb", onescolb_sb, onescolb_d), ("tabs", tabs, tabs_d)):
            toks[name] = PL.dma("sync", lambda e, d=dst, s=src: e.dma_start(out=d[:], in_=s[:]))

        E.op("vector", lambda e: e.memset(zc2_sb[:], 0.0))
        for vb in Vpb:
            vo = vb.rearrange("p (t h c) -> p t h c", t=32, h=2)[:, :, :, 64:65]
            E.op("vector", lambda e, vo=vo: e.memset(vo, 1.0))
        E.op("vector", lambda e: e.nop(), inc="vV")

        xt = (xt0, xt1)
        v_tile_done = [0] * 33
        t_tr_done = [0] * 33
        first_tr = True

        def emit_ln(buf, out_ap):
            E.op("vector", lambda e, b=buf: e.reduce_sum(out=st_sum, in_=b[:], axis=AX.X))
            E.op("vector", lambda e: e.drain())
            E.opd("vector", lambda e: e.tensor_scalar_mul(out=st_mean, in0=st_sum, scalar1=1.0 / D))
            E.opd("vector", lambda e: e.tensor_scalar_mul(out=st_ex2, in0=st_ssq, scalar1=1.0 / D))
            E.opd("vector", lambda e: e.tensor_mul(out=st_m2, in0=st_mean, in1=st_mean))
            E.opd("vector", lambda e: e.tensor_sub(out=st_var, in0=st_ex2, in1=st_m2))
            E.opd("vector", lambda e: e.tensor_scalar_add(out=st_vep, in0=st_var, scalar1=EPS), inc="vV")
            v_vep = E.cnt["vV"]
            E.w("scalar", "vV", v_vep)
            E.opd("scalar", lambda e: e.activation(out=st_s0, in_=st_vep, func=AF.Ln))
            E.op("scalar", lambda e: e.activation(out=st_r0, in_=st_s0, func=AF.Exp, scale=-0.5), inc="sS")
            s_r0 = E.cnt["sS"]
            E.w("vector", "sS", s_r0)
            E.opd("vector", lambda e: e.tensor_mul(out=st_a, in0=st_r0, in1=st_r0))
            E.opd("vector", lambda e: e.tensor_mul(out=st_b, in0=st_a, in1=st_vep))
            E.opd("vector", lambda e: e.tensor_scalar(out=st_c, in0=st_b, scalar1=-0.5, scalar2=1.5, op0=ALU.mult, op1=ALU.add))
            E.opd("vector", lambda e: e.tensor_mul(out=st_rstd, in0=st_r0, in1=st_c))
            E.opd("vector", lambda e, o=out_ap, b=buf: e.tensor_scalar(out=o, in0=b[:], scalar1=st_mean, scalar2=st_rstd, op0=ALU.subtract, op1=ALU.mult), inc="vV")
            return E.cnt["vV"]

        for i in range(32):
            if i >= 2:
                E.w("sync", "vV", v_tile_done[i - 2])
            tok_x = PL.dma("sync", lambda e, i=i: e.dma_start(out=xt[i % 2][:], in_=xin[i * 128:(i + 1) * 128, :]))
            E.wt("scalar", tok_x)
            E.op("scalar", lambda e, i=i: e.activation(out=trash[:], in_=xt[i % 2][:], func=AF.Square, accum_out=st_ssq), inc="sS")
            s_sq = E.cnt["sS"]
            E.wt("vector", tok_x)
            E.w("vector", "sS", s_sq)
            if i >= 1:
                E.w("vector", "tT", t_tr_done[i - 1])
            v_xn = emit_ln(xt[i % 2], xnb32[:])
            E.w("tensor", "vV", v_xn)
            if first_tr:
                E.wt("tensor", toks["identf"])
                first_tr = False
            for rnd in range(2):
                if rnd == 1:
                    E.w("tensor", "vV", E.cnt["vV"])
                for c in range(4):
                    cc = rnd * 4 + c
                    E.op("tensor", lambda e, cc=cc, c=c: e.transpose(out=PJ[:, c * 128:(c + 1) * 128], in_=xnb32[:, cc * 128:(cc + 1) * 128], identity=identf_sb[:]), inc="tT" if c == 3 else None)
                t_r = E.cnt["tT"]
                E.w("vector", "tT", t_r)
                dst = xnT.rearrange("p (c t) -> p c t", c=8)[:, rnd * 4:rnd * 4 + 4, i * 128:(i + 1) * 128]
                E.op("vector", lambda e, dst=dst: e.tensor_copy(out=dst, in_=PJ[:, 0:512]), inc="vV")
            t_tr_done[i] = E.cnt["tT"]
            v_tile_done[i] = E.cnt["vV"]
        v_xnT_all = E.cnt["vV"]

        state = {"wconv": 0, "last_proj_t": 0, "pj_free": [0], "rope_ready": [0, 0],
                 "vp_ready": [0, 0], "bias_ready": [0, 0],
                 "prev_exp": [0, 0, 0], "prev_av": [0, 0, 0]}
        tok_spill = [None] * NPAIR

        def gen_proj(p):
            par = p % 2
            QT, KT, Vp = QTb[par], KTb[par], Vpb[par]
            E.w("sync", "vV", state["wconv"])
            tok_w = PL.dma("sync", lambda e, p=p: e.dma_start(out=wstg[:], in_=wpair[p]))
            E.wt("vector", tok_w)
            E.w("vector", "tT", state["last_proj_t"])
            if p == 0:
                E.wt("vector", toks["tabs"])
            E.op("vector", lambda e: e.tensor_copy(out=wbf[:], in_=wstg[:]), inc="vV")
            state["wconv"] = E.cnt["vV"]
            wconv = state["wconv"]
            yield

            evac_marks = {}
            pj_free = state["pj_free"]

            rope_g = [0]

            def proj_slices(sel_off, nslices):
                for s in range(nslices):
                    E.w("tensor", "vV", max(v_xnT_all, wconv))
                    E.w("tensor", "vV", pj_free[0])
                    if s == 0 and sel_off != 0:
                        E.w("vector", "gG", rope_g[0])
                    for c in range(8):
                        lhsT = wbf[:, c * 384 + sel_off: c * 384 + sel_off + 128]
                        rhs = xnT[:, c * 4096 + s * 512: c * 4096 + (s + 1) * 512]
                        E.op("tensor", lambda e, l=lhsT, r=rhs, c=c:
                             e.matmul(out=PJ[:], lhsT=l, rhs=r, start=(c == 0), stop=(c == 7)),
                             inc="tT" if c == 7 else None)
                    t_mm = E.cnt["tT"]
                    E.w("vector", "tT", t_mm)
                    if sel_off == 256:
                        E.opd("vector", lambda e: e.tensor_copy(out=vstg[:], in_=PJ[:]), inc="vV")
                    else:
                        E.opd("vector", lambda e, s=s:
                              e.tensor_copy(out=Tstg[:, s * 512:(s + 1) * 512], in_=PJ[:]), inc="vV")
                    evac_marks[(sel_off, s)] = E.cnt["vV"]
                    pj_free[0] = E.cnt["vV"]
                    yield

            def rope(n, dst):
                E.w("gpsimd", "vV", evac_marks[(0 if n == 2048 else 128, (n // 512) - 1)])
                if p == 0:
                    E.wt("gpsimd", toks["tabs"])
                for sb, db in ((32, 0), (0, 32), (96, 64), (64, 96)):
                    E.op("gpsimd", lambda e, n=n, sb=sb, db=db: e.tensor_copy(out=Tsw[db:db + 32, 0:n], in_=Tstg[sb:sb + 32, 0:n]))
                E.op("gpsimd", lambda e: e.drain())
                E.op("gpsimd", lambda e, n=n: e.tensor_mul(out=Tsw[:, 0:n], in0=Tsw[:, 0:n], in1=SS[:, 0:n]))
                E.opd("gpsimd", lambda e, n=n: e.tensor_mul(out=Tstg[:, 0:n], in0=Tstg[:, 0:n], in1=CC[:, 0:n]))
                E.opd("gpsimd", lambda e, n=n, dst=dst: e.tensor_add(out=dst, in0=Tstg[:, 0:n], in1=Tsw[:, 0:n]), inc="gG")

            def norms(src, ncols, out_col):
                E.op("gpsimd", lambda e, src=src, n=ncols: e.tensor_mul(out=Tsw[:, 0:n], in0=src, in1=src), inc="gG")
                g_sq = E.cnt["gG"]
                E.w("tensor", "gG", g_sq)
                E.w("tensor", "vV", pj_free[0])
                nsl = ncols // 512
                for s in range(nsl):
                    E.op("tensor", lambda e, s=s, nsl=nsl: e.matmul(
                        out=PJ[0:2, :], lhsT=ones2_sb[:],
                        rhs=Tsw[:, s * 512:(s + 1) * 512], start=(s == 0), stop=(s == nsl - 1)),
                        inc="tT" if s == nsl - 1 else None)
                t_nm = E.cnt["tT"]
                E.w("gpsimd", "tT", t_nm)
                E.op("gpsimd", lambda e: e.engine_nop(), inc="gG")
                E.w("vector", "tT", t_nm)
                E.op("vector", lambda e: e.reduce_max(out=st_sb[0:2, 0:1], in_=PJ[0:2, :], axis=AX.X))
                E.op("vector", lambda e: e.drain())
                E.opd("vector", lambda e, oc=out_col: e.tensor_copy(out=nm_sb[:, oc:oc + 1], in_=st_sb[0:2, 0:1]), inc="vV")
                pj_free[0] = E.cnt["vV"]

            yield from proj_slices(0, 4)
            rope(2048, QT[:, :])
            rope_g[0] = E.cnt["gG"]
            yield
            norms(QT[:, :], 2048, 0)
            yield
            yield from proj_slices(128, 8)
            rope(4096, KT[:, :])
            rope_g[0] = E.cnt["gG"]
            yield
            norms(KT[:, :], 4096, 1)
            yield
            E.opd("vector", lambda e: e.tensor_add(out=nm_sb[:, 2:3], in0=nm_sb[:, 0:1], in1=nm_sb[:, 1:2]), inc="vV")
            E.opd("vector", lambda e: e.tensor_scalar_mul(out=nm_sb[:, 4:5], in0=nm_sb[:, 2:3], scalar1=SCALE * 0.5))
            E.op("vector", lambda e: e.tensor_scalar(out=zc2_sb[:], in0=zc2_sb[:], scalar1=nm_sb[:, 4:5], scalar2=None, op0=ALU.add), inc="vV")
            v_mb = E.cnt["vV"]
            E.w("tensor", "vV", v_mb)
            E.w("tensor", "vV", pj_free[0])
            if p == 0:
                E.wt("tensor", toks["ident2"])
                E.wt("tensor", toks["ones2"])
            E.op("tensor", lambda e: e.matmul(out=PJ[:, 0:2], lhsT=zc2_sb[:], rhs=ident2_sb[:], start=True, stop=True), inc="tT")
            t_mb = E.cnt["tT"]
            E.w("vector", "tT", t_mb)
            E.op("vector", lambda e, par=par: e.tensor_scalar_mul(out=biasAB[:, par * 2:par * 2 + 2], in0=PJ[:, 0:2], scalar1=-1.0))
            E.opd("vector", lambda e: e.memset(zc2_sb[:], 0.0), inc="vV")
            state["bias_ready"][par] = E.cnt["vV"]
            pj_free[0] = E.cnt["vV"]
            yield
            gen_v = proj_slices(256, 8)
            for s in range(8):
                next(gen_v)
                E.w("tensor", "vV", evac_marks[(256, s)])
                E.w("tensor", "vV", pj_free[0])
                for j in range(4):
                    E.op("tensor", lambda e, j=j: e.transpose(
                        out=PJ[:, j * 128:(j + 1) * 128],
                        in_=vstg[:, j * 128:(j + 1) * 128],
                        identity=identf_sb[:]), inc="tT" if j == 3 else None)
                t_vt = E.cnt["tT"]
                E.w("vector", "tT", t_vt)
                dst = Vp.rearrange("p (t h c) -> p t h c", t=32, h=2)[:, 4 * s:4 * s + 4, :, 0:64]
                src = PJ.rearrange("p (j h c) -> p j h c", j=4, h=2, c=64)[:, :, :, 0:64]
                E.opd("vector", lambda e, dst=dst, src=src: e.tensor_copy(out=dst, in_=src), inc="vV")
                pj_free[0] = E.cnt["vV"]
                evac_marks[(256, s)] = E.cnt["vV"]
                yield
            state["last_proj_t"] = E.cnt["tT"]
            state["vp_ready"][par] = E.cnt["vV"]
            state["rope_ready"][par] = E.cnt["gG"]

        def run_gen(g):
            if g is not None:
                for _ in g:
                    pass

        run_gen(gen_proj(0))

        for p in range(NPAIR):
            par = p % 2
            QT, KT, Vp = QTb[par], KTb[par], Vpb[par]
            bias = biasAB[:, par * 2:par * 2 + 2]
            gen = gen_proj(p + 1) if p + 1 < NPAIR else None

            import os as _os
            if _os.environ.get("KDBG"):
                print(f"pair {p}: vp_ready={state['vp_ready']}, bias={state['bias_ready']}, vV={E.cnt['vV']}")
            E.w("tensor", "vV", state["vp_ready"][par])
            E.w("tensor", "gG", state["rope_ready"][par])
            GRP = [(2 * i, 2) for i in range(16)]
            exp_done = dict()
            av_done = dict()
            G = 0
            v_oa_free = state.get("prev_oa", 0)
            v_norm_prev = state.get("prev_norm", 0)
            pending_norm = []
            pending_tail = []
            gen_budget = 0.0
            first_qhip = True
            for q in range(4):
                for hip in range(2):
                    g0 = G
                    gl = g0 + len(GRP) - 1
                    for gi, (base, ln) in enumerate(GRP):
                        g = g0 + gi
                        bank = PSC[:, (g % 3) * 1024:(g % 3) * 1024 + 1024]
                        if g >= 3:
                            E.w("tensor", "sS", exp_done[g - 3])
                        elif state["prev_exp"][g % 3]:
                            E.w("tensor", "sS", state["prev_exp"][g % 3])
                        for k in range(ln):
                            ci = base + k
                            E.op("tensor", lambda e, ci=ci, k=k, q=q, hip=hip, bank=bank, KT=KT, QT=QT: e.matmul(
                                out=bank[:, k * 512:(k + 1) * 512],
                                lhsT=KT[hip * 64:(hip + 1) * 64, ci * 128:(ci + 1) * 128],
                                rhs=QT[hip * 64:(hip + 1) * 64, q * 512:(q + 1) * 512],
                                start=True, stop=True), inc="tT" if k == ln - 1 else None)
                        t_sc = E.cnt["tT"]
                        E.w("scalar", "tT", t_sc)
                        if g >= 3:
                            E.w("scalar", "tT", av_done[g - 3])
                        elif state["prev_av"][g % 3]:
                            E.w("scalar", "tT", state["prev_av"][g % 3])
                        if first_qhip and gi == 0:
                            E.w("scalar", "vV", state["bias_ready"][par])
                        E.op("scalar", lambda e, ln=ln, g=g, bank=bank, bias=bias, hip=hip: e.activation(
                            out=stag[g % 3][:, 0:ln * 512], in_=bank[:, 0:ln * 512], func=AF.Exp,
                            bias=bias[:, hip:hip + 1], scale=SCALE), inc="sS")
                        exp_done[g] = E.cnt["sS"]
                        if gi == 0 and pending_tail:
                            v_oa_free = pending_tail.pop(0)()
                        if gi == 2 and pending_norm:
                            v_norm_prev = pending_norm.pop(0)()
                        gen_budget += 34.0 / 128.0
                        while gen is not None and gen_budget >= 1.0:
                            gen_budget -= 1.0
                            try:
                                next(gen)
                            except StopIteration:
                                gen = None
                        if gi >= 1:
                            gp = g - 1
                            E.w("tensor", "sS", exp_done[gp])
                            if gi == 1:
                                E.w("tensor", "vV", v_oa_free)
                            bb, bl = GRP[gi - 1]
                            for k in range(bl):
                                ci = bb + k
                                E.op("tensor", lambda e, ci=ci, k=k, gp=gp, hip=hip, Vp=Vp,
                                     st_=(gp == g0 and k == 0): e.matmul(
                                    out=OA[0:65, :], lhsT=Vp[:, ci * 130 + hip * 65: ci * 130 + hip * 65 + 65],
                                    rhs=stag[gp % 3][:, k * 512:(k + 1) * 512],
                                    start=st_, stop=False), inc="tT" if k == bl - 1 else None)
                            av_done[gp] = E.cnt["tT"]
                    G = gl + 1
                    first_qhip = False

                    def _deferred_tail(gl=gl, hip=hip, Vp=Vp, q=q, spill_wait=(p >= 1 and q == 0 and hip == 0)):
                        E.w("tensor", "sS", exp_done[gl])
                        bb, bl = GRP[-1]
                        for k in range(bl):
                            ci = bb + k
                            E.op("tensor", lambda e, ci=ci, k=k, gl=gl, hip=hip, Vp=Vp,
                                 sp_=(k == bl - 1): e.matmul(
                                out=OA[0:65, :], lhsT=Vp[:, ci * 130 + hip * 65: ci * 130 + hip * 65 + 65],
                                rhs=stag[gl % 3][:, k * 512:(k + 1) * 512],
                                start=False, stop=sp_), inc="tT" if k == bl - 1 else None)
                        av_done[gl] = E.cnt["tT"]
                        E.w("vector", "tT", av_done[gl])
                        if spill_wait:
                            E.wt("vector", tok_spill[p - 1])
                        E.opd("vector", lambda e: e.reciprocal(out=rbc[0:1, :], in_=OA[64:65, :]))
                        E.opd("vector", lambda e: e.tensor_copy(out=rbch[0:1, :], in_=rbc[0:1, :]))
                        E.opd("vector", lambda e: e.scalar_tensor_tensor(out=rbcl[0:1, :], in0=rbch[0:1, :], scalar=-1.0, in1=rbc[0:1, :], op0=ALU.mult, op1=ALU.add))
                        E.opd("vector", lambda e: e.tensor_copy(out=oaev[0:64, :], in_=OA[0:64, :]), inc="vV")
                        return E.cnt["vV"]
                    pending_tail.append(_deferred_tail)

                    def _deferred_norm(q=q, hip=hip, first=(p == 0 and q == 0 and hip == 0)):
                        E.w("tensor", "vV", v_oa_free)
                        E.w("tensor", "vV", state["pj_free"][0])
                        if first:
                            E.wt("tensor", toks["onescol"])
                        E.op("tensor", lambda e: e.matmul(out=PJ[0:64, :], lhsT=onescolb_sb[0:1, 0:64], rhs=rbch[0:1, :], start=True, stop=False))
                        E.op("tensor", lambda e: e.matmul(out=PJ[0:64, :], lhsT=onescolb_sb[0:1, 0:64], rhs=rbcl[0:1, :], start=False, stop=True), inc="tT")
                        t_bc = E.cnt["tT"]
                        E.w("vector", "tT", t_bc)
                        E.opd("vector", lambda e, q=q, hip=hip: e.tensor_mul(
                            out=aopair[hip * 64:(hip + 1) * 64, q * 512:(q + 1) * 512],
                            in0=oaev[0:64, :], in1=PJ[0:64, :]), inc="vV")
                        state["pj_free"][0] = E.cnt["vV"]
                        return E.cnt["vV"]
                    pending_norm.append(_deferred_norm)
            while pending_tail:
                v_oa_free = pending_tail.pop(0)()
            while pending_norm:
                v_norm_prev = pending_norm.pop(0)()
            run_gen(gen)
            state["prev_oa"] = v_oa_free
            state["prev_norm"] = v_norm_prev
            pe = [0, 0, 0]; pa = [0, 0, 0]
            for g in (125, 126, 127):
                pe[g % 3] = exp_done[g]
                pa[g % 3] = av_done[g]
            state["prev_exp"] = pe
            state["prev_av"] = pa
            v_ao = E.cnt["vV"]
            E.w("gpsimd", "vV", v_ao)
            tok_spill[p] = PG.dma("gpsimd", lambda e, p=p: e.dma_start(out=aoT_d[p], in_=aopair[:]))

        p2_v = E.cnt["vV"]; p2_t = E.cnt["tT"]; p2_s = E.cnt["sS"]

        E.w("sync", "vV", p2_v)
        E.wt("sync", tok_spill[NPAIR - 2])
        E.wt("sync", tok_spill[NPAIR - 1])
        v_prev = 0
        for h in range(2):
            if h == 1:
                E.w("sync", "vV", v_prev)
            tok_wo = PL.dma("sync", lambda e, h=h: e.dma_start(out=rstg[:], in_=wout_d[:, h * 4096:(h + 1) * 4096]))
            E.wt("vector", tok_wo)
            if h == 0:
                E.w("vector", "tT", p2_t)
            E.op("vector", lambda e, h=h: e.tensor_copy(out=woutb[:, h * 4096:(h + 1) * 4096], in_=rstg[:]), inc="vV")
            v_prev = E.cnt["vV"]
        v_qq = 0
        for h in range(2):
            E.w("sync", "vV", v_prev if h == 0 else v_qq)
            tok_rh = PL.dma("sync", lambda e, h=h: e.dma_start(out=rstg[:], in_=rhsT_d[:, h * 4096:(h + 1) * 4096]))
            for c4 in range(4):
                cc = h * 4 + c4
                E.wt("vector", tok_rh)
                E.op("vector", lambda e, c4=c4: e.reduce_max(out=st_amax, in_=rstg[:, c4 * 1024:(c4 + 1) * 1024], axis=AX.X, apply_absolute_value=True))
                E.op("vector", lambda e: e.drain())
                E.opd("vector", lambda e: e.tensor_scalar_max(out=st_amax, in0=st_amax, scalar1=1e-12))
                E.opd("vector", lambda e, cc=cc: e.tensor_scalar_mul(out=ls16[:, cc:cc + 1], in0=st_amax, scalar1=1.0 / INT8_MAX))
                E.opd("vector", lambda e, cc=cc: e.reciprocal(out=st_lrcp, in_=ls16[:, cc:cc + 1]))
                E.opd("vector", lambda e, c4=c4: e.tensor_scalar(out=p3t[:], in0=rstg[:, c4 * 1024:(c4 + 1) * 1024], scalar1=st_lrcp, scalar2=MAGIC, op0=ALU.mult, op1=ALU.add))
                E.opd("vector", lambda e: e.tensor_scalar(out=p3t[:], in0=p3t[:], scalar1=MAGIC, scalar2=None, op0=ALU.subtract), inc="vV")
                v_q8 = E.cnt["vV"]
                for rnd in range(2):
                    E.w("tensor", "vV", E.cnt["vV"])
                    for fc in range(4):
                        fcc = rnd * 4 + fc
                        E.op("tensor", lambda e, fcc=fcc, fc=fc: e.transpose(
                            out=PJ[:, fc * 128:(fc + 1) * 128], in_=p3t[:, fcc * 128:(fcc + 1) * 128],
                            identity=identf_sb[:]), inc="tT" if fc == 3 else None)
                    t_qb = E.cnt["tT"]
                    E.w("vector", "tT", t_qb)
                    dst = qrhs.rearrange("p (f c) -> p f c", f=8)[:, rnd * 4:rnd * 4 + 4, cc * 128:(cc + 1) * 128]
                    src = PJ.rearrange("p (f c) -> p f c", f=4)
                    E.op("vector", lambda e, dst=dst, src=src: e.tensor_copy(out=dst, in_=src), inc="vV")
                v_qq = E.cnt["vV"]
        v_qrhs = E.cnt["vV"]
        E.opd("vector", lambda e: e.tensor_copy(out=rsv[:, 0:8], in_=ls16[:, 0:8]), inc="vV")
        v_rsv = E.cnt["vV"]
        E.w("tensor", "vV", max(v_qrhs, v_rsv))
        E.op("tensor", lambda e: e.transpose(out=PJ[0:8, 0:128], in_=rsv[:], identity=identf_sb[:]), inc="tT")
        t_rs = E.cnt["tT"]
        E.w("vector", "tT", t_rs)
        E.op("vector", lambda e: e.tensor_copy(out=rsT_sb[:], in_=PJ[0:8, 0:128]), inc="vV")
        v_rsT = E.cnt["vV"]
        E.w("gpsimd", "vV", v_rsT)
        tok_rr = PG.dma("gpsimd", lambda e: e.dma_start(out=rsrow, in_=rsT_sb[:]))
        E.wt("tensor", tok_rr)
        E.op("tensor", lambda e: e.matmul(out=PSC[:, 0:512], lhsT=onescol_sb[:], rhs=rsrow[0:1, 0:512], start=True, stop=True))
        E.op("tensor", lambda e: e.matmul(out=PSC[:, 512:1024], lhsT=onescol_sb[:], rhs=rsrow[0:1, 512:1024], start=True, stop=True), inc="tT")
        t_rsb = E.cnt["tT"]
        E.w("vector", "tT", t_rsb)
        E.op("vector", lambda e: e.tensor_copy(out=rsb[:, 0:512], in_=PSC[:, 0:512]))
        E.opd("vector", lambda e: e.tensor_copy(out=rsb[:, 512:1024], in_=PSC[:, 512:1024]), inc="vV")
        v_rsb = E.cnt["vV"]
        tok_bo = PL.dma("sync", lambda e: e.dma_start(out=bouts[:], in_=bout_d[:]))
        aoT_src = aoT_d.rearrange("p q c -> q p c")
        aoT_dst = aoTs.rearrange("q (p c) -> q p c", p=NPAIR)
        tok_ao = PL.dma("sync", lambda e, s=aoT_src, d=aoT_dst: e.dma_start(out=d, in_=s))

        v_a = [0] * 16
        for t in range(16):
            if t >= 2:
                E.w("sync", "vV", v_a[t - 2])
            tok_x = PL.dma("sync", lambda e, t=t: e.dma_start(out=xt[t % 2][:], in_=xin[t * 128:(t + 1) * 128, :]))
            E.wt("scalar", tok_x)
            E.op("scalar", lambda e, t=t: e.activation(out=tr_bf[:], in_=xt[t % 2][:], func=AF.Square, accum_out=st_ssq), inc="sS")
            s_sq = E.cnt["sS"]
            E.wt("vector", tok_x)
            E.w("vector", "sS", s_sq)
            if t == 0:
                E.w("vector", "vV", v_rsb)
            emit_ln(xt[t % 2], sbf[:])
            E.op("vector", lambda e: e.reduce_max(out=st_amax, in_=sbf[:], axis=AX.X, apply_absolute_value=True))
            E.op("vector", lambda e: e.drain())
            E.opd("vector", lambda e: e.tensor_scalar_max(out=st_amax, in0=st_amax, scalar1=1e-12))
            E.opd("vector", lambda e, t=t: e.tensor_scalar_mul(out=ls16[:, t:t + 1], in0=st_amax, scalar1=1.0 / INT8_MAX))
            E.opd("vector", lambda e, t=t: e.reciprocal(out=st_lrcp, in_=ls16[:, t:t + 1]))
            E.opd("vector", lambda e: e.tensor_scalar(out=p3t[:], in0=sbf[:], scalar1=st_lrcp, scalar2=MAGIC, op0=ALU.mult, op1=ALU.add))
            E.opd("vector", lambda e: e.tensor_scalar(out=p3t[:], in0=p3t[:], scalar1=MAGIC, scalar2=None, op0=ALU.subtract), inc="vV")
            v_q = E.cnt["vV"]
            for rnd in range(2):
                E.w("tensor", "vV", E.cnt["vV"])
                if t == 0 and rnd == 0:
                    pass
                for c in range(4):
                    cc = rnd * 4 + c
                    E.op("tensor", lambda e, cc=cc, c=c: e.transpose(
                        out=PJ[:, c * 128:(c + 1) * 128], in_=p3t[:, cc * 128:(cc + 1) * 128],
                        identity=identf_sb[:]), inc="tT" if c == 3 else None)
                t_r = E.cnt["tT"]
                E.w("vector", "tT", t_r)
                dst = qlhsT16[:, t * 1024 + rnd * 512: t * 1024 + rnd * 512 + 512]
                E.op("vector", lambda e, dst=dst: e.tensor_copy(out=dst, in_=PJ[:, 0:512]), inc="vV")
            v_a[t] = E.cnt["vV"]
        v_a_all = E.cnt["vV"]

        st_tok = [None, None]
        v_b = [0] * 16
        for t in range(16):
            if t == 0:
                E.w("sync", "vV", v_a_all)
            if t >= 2:
                E.w("sync", "vV", v_b[t - 2])
            tok_x = PL.dma("sync", lambda e, t=t: e.dma_start(out=xt[t % 2][:], in_=xin[t * 128:(t + 1) * 128, :]))
            if t == 0:
                E.w("tensor", "vV", v_a_all)
                E.wt("tensor", tok_ao)
            else:
                E.w("tensor", "vV", v_b[t - 1])
            for c in range(8):
                for hh in range(2):
                    E.op("tensor", lambda e, c=c, hh=hh, t=t: e.matmul(
                        out=PSC[:, hh * 512:(hh + 1) * 512], lhsT=qlhsT16[:, t * 1024 + c * 128: t * 1024 + (c + 1) * 128],
                        rhs=qrhs[:, c * 1024 + hh * 512: c * 1024 + (hh + 1) * 512],
                        start=(c == 0), stop=(c == 7)), inc="tT" if (c == 7 and hh == 1) else None)
            t_ffn = E.cnt["tT"]
            for pp in range(8):
                for hh in range(2):
                    E.op("tensor", lambda e, pp=pp, hh=hh, t=t: e.matmul(
                        out=PSC[:, 1024 + hh * 512: 1024 + (hh + 1) * 512],
                        lhsT=aoTs[:, pp * 2048 + t * 128: pp * 2048 + (t + 1) * 128],
                        rhs=woutb[:, pp * 1024 + hh * 512: pp * 1024 + (hh + 1) * 512],
                        start=(pp == 0), stop=(pp == 7)), inc="tT" if (pp == 7 and hh == 1) else None)
            t_op = E.cnt["tT"]

            E.w("vector", "tT", t_ffn)
            E.opd("vector", lambda e: e.tensor_mul(out=p3t[:], in0=PSC[:, 0:1024], in1=rsb[:]), inc="vV")
            v_g1 = E.cnt["vV"]
            E.w("scalar", "vV", v_g1)
            E.op("scalar", lambda e, t=t: e.activation(out=gbuf[:], in_=p3t[:], func=AF.Silu, scale=ls16[:, t:t + 1]), inc="sS")
            s_g = E.cnt["sS"]
            E.w("vector", "tT", t_op)
            E.wt("vector", tok_x)
            E.opd("vector", lambda e: e.tensor_add(out=sbf[:], in0=PSC[:, 1024:2048], in1=bouts[:]))
            E.w("vector", "sS", s_g)
            if t >= 2:
                E.wt("vector", st_tok[t % 2])
            E.opd("vector", lambda e: e.tensor_mul(out=sbf[:], in0=sbf[:], in1=gbuf[:]))
            E.op("vector", lambda e, t=t: e.tensor_add(out=outb[t % 2][:], in0=sbf[:], in1=xt[t % 2][:]), inc="vV")
            v_b[t] = E.cnt["vV"]
            E.w("sync", "vV", v_b[t])
            st_tok[t % 2] = PS.dma("sync", lambda e, t=t: e.dma_start(out=out_d[t * 128:(t + 1) * 128, :], in_=outb[t % 2][:]))

        E.w("sync", "st0", E.cnt["st0"])
        E.w("sync", "st1", E.cnt["st1"])

        @block.sync
        def _(e):
            for fn in E.ops["sync"]:
                fn(e)

        @block.tensor
        def _(e):
            for fn in E.ops["tensor"]:
                fn(e)

        @block.vector
        def _(e):
            for fn in E.ops["vector"]:
                fn(e)

        @block.scalar
        def _(e):
            for fn in E.ops["scalar"]:
                fn(e)

        @block.gpsimd
        def _(e):
            for fn in E.ops["gpsimd"]:
                fn(e)

    return nc



def _prep_shared(w_qkv, w_out, b_out, rhs):
    wp = np.empty((NPAIR, 128, 3072), dtype=np.float32)
    for p in range(NPAIR):
        wq = w_qkv[:, p * 128:(p + 1) * 128]
        wk = w_qkv[:, 1024 + p * 128:1024 + (p + 1) * 128]
        wv = w_qkv[:, 2048 + p * 128:2048 + (p + 1) * 128]
        cat = np.concatenate([wq, wk, wv], axis=1)
        wp[p] = np.ascontiguousarray(cat.reshape(8, 128, 384).transpose(1, 0, 2).reshape(128, 3072))
    o2 = np.zeros((128, 2), dtype=np.float32)
    o2[0:64, 0] = 1.0
    o2[64:128, 1] = 1.0
    rhsT = np.ascontiguousarray(rhs.T)
    consts = {
        "wout": np.ascontiguousarray(w_out.reshape(8, 128, 1024).transpose(1, 0, 2).reshape(128, 8192)).astype(np.float32),
        "rhsT": np.ascontiguousarray(rhsT.reshape(8, 128, 1024).transpose(1, 0, 2).reshape(128, 8192)).astype(np.float32),
        "bout": np.ascontiguousarray(np.broadcast_to(b_out[None, :], (128, 1024))).astype(np.float32),
        "ident": np.eye(128, dtype=np.float32).astype(bfloat16),
        "identf": np.eye(128, dtype=np.float32),
        "ident2": np.eye(2, dtype=np.float32).astype(bfloat16),
        "onescol": np.ones((1, 128), dtype=np.float32),
        "onescolb": np.ones((1, 128), dtype=np.float32).astype(bfloat16),
        "ones2": o2.astype(bfloat16),
    }
    return wp, consts


def make_in_maps(x, w_qkv, w_out, b_out, rhs):
    wp, consts = _prep_shared(w_qkv, w_out, b_out, rhs)
    tab64 = _host_tables()
    in_maps = []
    for core in range(8):
        b, half = core // 2, core % 2
        perm = np.concatenate([np.arange(half * NQ, (half + 1) * NQ),
                               np.arange((1 - half) * NQ, (2 - half) * NQ)])
        xp = np.ascontiguousarray(x[b][perm])
        tabs = np.ascontiguousarray(
            tab64.reshape(128, 2, N)[:, :, perm].reshape(128, 2 * N)).astype(bfloat16)
        m = {"xin": xp, "wpair": wp, "tabs": tabs}
        m.update(consts)
        in_maps.append(m)
    return in_maps


def kernel(L0_attn_x, L0_attn_w_qkv, L0_attn_w_out, L0_attn_b_out, L0_ffn_rhs):
    x = np.asarray(L0_attn_x, dtype=np.float32)
    in_maps = make_in_maps(x,
                           np.asarray(L0_attn_w_qkv, dtype=np.float32),
                           np.asarray(L0_attn_w_out, dtype=np.float32),
                           np.asarray(L0_attn_b_out, dtype=np.float32),
                           np.asarray(L0_ffn_rhs, dtype=np.float32))
    nc = build()
    res = run_bass_kernel_spmd(nc, in_maps, core_ids=list(range(8)))
    out = np.empty((B, N, D), dtype=np.float32)
    for core in range(8):
        b, half = core // 2, core % 2
        out[b, half * NQ:(half + 1) * NQ] = res.results[core]["out"]
    return out


# version 11
# speedup vs baseline: 1.2428x; 1.2428x over previous
import numpy as np
import ml_dtypes
import concourse.bass as bass
import concourse.mybir as mybir
from concourse import library_config
from concourse.bass_utils import run_bass_kernel_spmd

F32 = mybir.dt.float32
BF16 = mybir.dt.bfloat16
AF = mybir.ActivationFunctionType
ALU = mybir.AluOpType
AX = mybir.AxisListType
bfloat16 = ml_dtypes.bfloat16

B, N, D = 4, 4096, 1024
HEADS, DH, HALF = 16, 64, 32
NQ = 2048
NPAIR = 8
SCALE = float(DH) ** -0.5
MAGIC = 12582912.0
EPS = 1e-5
INT8_MAX = 127.0


def _host_tables():
    j = np.arange(HALF, dtype=np.float64)
    theta = 10000.0 ** (-j / HALF)
    pos = np.arange(N, dtype=np.float64)
    ang = np.outer(theta, pos)
    cos = np.cos(ang)
    sin = np.sin(ang)
    CCt = np.concatenate([cos, cos, cos, cos], axis=0)
    SSt = np.concatenate([-sin, sin, -sin, sin], axis=0)
    return np.concatenate([CCt, SSt], axis=1)


class _Emit:
    ENGS = ("sync", "tensor", "vector", "scalar", "gpsimd")

    def __init__(self, sems):
        self.ops = {e: [] for e in self.ENGS}
        self.cnt = {k: 0 for k in sems}
        self.sems = sems

    def wt(self, eng, tok):
        if tok is not None:
            self.w(eng, tok[0], tok[1])

    def w(self, eng, sem, val):
        if val <= 0:
            return
        assert val <= self.cnt[sem], f"wait {sem}>={val} emitted={self.cnt[sem]}"
        s = self.sems[sem]
        self.ops[eng].append(lambda e, s=s, v=val: e.wait_ge(s, v))

    def op(self, eng, fn, inc=None, n=1):
        if inc is None:
            self.ops[eng].append(fn)
            return None
        s = self.sems[inc]
        self.cnt[inc] += n
        self.ops[eng].append(lambda e, fn=fn, s=s, n=n: fn(e).then_inc(s, n))
        return self.cnt[inc]

    def opd(self, eng, fn, inc=None, n=1):
        r = self.op(eng, fn, inc=inc, n=n)
        self.ops[eng].append(lambda e: e.drain())
        return r


class _DPool:

    def __init__(self, E, names):
        self.E = E
        self.names = names
        self.i = 0

    def dma(self, eng, fn):
        s = self.names[self.i % len(self.names)]
        self.i += 1
        self.E.w(eng, s, self.E.cnt[s])
        v = self.E.op(eng, fn, inc=s, n=16)
        return (s, v)


def build(debug=False):
    nc = bass.Bass("TRN2", target_bir_lowering=False)

    xin = nc.dram_tensor("xin", [N, D], F32, kind="ExternalInput").ap()
    wpair = nc.dram_tensor("wpair", [NPAIR, 128, 3072], F32, kind="ExternalInput").ap()
    wout_d = nc.dram_tensor("wout", [128, 8192], F32, kind="ExternalInput").ap()
    rhsT_d = nc.dram_tensor("rhsT", [128, 8192], F32, kind="ExternalInput").ap()
    bout_d = nc.dram_tensor("bout", [128, 1024], F32, kind="ExternalInput").ap()
    tabs_d = nc.dram_tensor("tabs", [128, 8192], BF16, kind="ExternalInput").ap()
    ident_d = nc.dram_tensor("ident", [128, 128], BF16, kind="ExternalInput").ap()
    identf_d = nc.dram_tensor("identf", [128, 128], F32, kind="ExternalInput").ap()
    ones2_d = nc.dram_tensor("ones2", [128, 2], BF16, kind="ExternalInput").ap()
    ident2_d = nc.dram_tensor("ident2", [2, 2], BF16, kind="ExternalInput").ap()
    onescol_d = nc.dram_tensor("onescol", [1, 128], F32, kind="ExternalInput").ap()
    onescolb_d = nc.dram_tensor("onescolb", [1, 128], BF16, kind="ExternalInput").ap()
    out_d = nc.dram_tensor("out", [NQ, D], F32, kind="ExternalOutput").ap()
    aoT_d = nc.dram_tensor("aoT", [NPAIR, 128, NQ], BF16, kind="Internal").ap()

    from contextlib import ExitStack
    with ExitStack() as _stk:
        en = _stk.enter_context
        ident_sb = en(nc.sbuf_tensor([128, 128], BF16))
        identf_sb = en(nc.sbuf_tensor([128, 128], F32))
        ones2_sb = en(nc.sbuf_tensor([128, 2], BF16))
        ident2_sb = en(nc.sbuf_tensor([2, 2], BF16))
        onescol_sb = en(nc.sbuf_tensor([1, 128], F32))
        zc2_sb = en(nc.sbuf_tensor([2, 128], BF16))
        xt0 = en(nc.sbuf_tensor([128, 1024], F32))
        xt1 = en(nc.sbuf_tensor([128, 1024], F32))
        st_sb = en(nc.sbuf_tensor([128, 16], F32))
        nm_sb = en(nc.sbuf_tensor([2, 8], F32))
        biasAB = en(nc.sbuf_tensor([128, 4], F32))
        rbc = en(nc.sbuf_tensor([64, 512], F32))
        rbch = en(nc.sbuf_tensor([1, 512], BF16))
        rbcl = en(nc.sbuf_tensor([1, 512], BF16))
        onescolb_sb = en(nc.sbuf_tensor([1, 128], BF16))
        rsT_sb = en(nc.sbuf_tensor([8, 128], F32))
        ls16 = en(nc.sbuf_tensor([128, 16], F32))
        abf = en(nc.sbuf_tensor([128, 77952], BF16))
        af32 = en(nc.sbuf_tensor([128, 10272], F32))
        PSC = en(nc.psum_tensor([128, 3072], F32))
        OA = en(nc.psum_tensor([128, 512], F32))
        PJ = en(nc.psum_tensor([128, 512], F32))
        sems = {}
        for name in (["ld%d" % i for i in range(8)] + ["st0", "st1", "dg0", "dg1",
                                                       "tT", "vV", "sS", "gG"]):
            sems[name] = en(nc.semaphore(name="sem_" + name))
        block = en(nc.Block())
        E = _Emit(sems)
        PL = _DPool(E, ["ld%d" % i for i in range(8)])
        PS = _DPool(E, ["st0", "st1"])
        PG = _DPool(E, ["dg0", "dg1"])

        xnT = abf[:, 0:32768]
        tabs = abf[:, 32768:40960]
        CC = tabs[:, 0:4096]
        SS = tabs[:, 4096:8192]
        wbf = abf[:, 40960:44032]
        QTb = (abf[:, 44032:46080], abf[:, 46080:48128])
        KTb = (abf[:, 48128:52224], abf[:, 52224:56320])
        Vpb = (abf[:, 56320:60480], abf[:, 60480:64640])
        Tstg = abf[:, 64640:68736]
        Tsw = abf[:, 68736:72832]
        stag = (abf[:, 72832:73856], abf[:, 73856:74880], abf[:, 74880:75904])
        aopair = abf[:, 75904:77952]

        aoTs = abf[:, 0:16384]
        woutb = abf[:, 16384:24576]
        qrhs = abf[:, 24576:32768]
        qlhsT16 = abf[:, 44032:60416]
        tr_bf = tabs[:, 2048:3072]

        wstg = af32[:, 0:3072]
        oaev = af32[:, 3072:3584]
        vstg = af32[:, 3584:4096]
        xnb32 = af32[:, 5120:6144]
        rstg = af32[:, 0:4096]
        trash = af32[:, 4096:5120]
        rsb = af32[:, 5120:6144]
        bouts = af32[:, 6144:7168]
        gbuf = af32[:, 7168:8192]
        sbf = af32[:, 8192:9216]
        p3t = af32[:, 9216:10240]
        rsv = af32[:, 10240:10248]
        rcv = af32[:, 10248:10256]
        rsrow = p3t[0:1, 0:1024]
        outb = (rstg[:, 0:1024], rstg[:, 1024:2048])

        st_sum = st_sb[:, 0:1]; st_ssq = st_sb[:, 1:2]; st_mean = st_sb[:, 2:3]
        st_ex2 = st_sb[:, 3:4]; st_m2 = st_sb[:, 4:5]; st_var = st_sb[:, 5:6]
        st_vep = st_sb[:, 6:7]; st_s0 = st_sb[:, 7:8]; st_r0 = st_sb[:, 8:9]
        st_a = st_sb[:, 9:10]; st_b = st_sb[:, 10:11]; st_c = st_sb[:, 11:12]
        st_rstd = st_sb[:, 12:13]; st_amax = st_sb[:, 13:14]
        st_lrcp = st_sb[:, 15:16]

        toks = {}
        for name, dst, src in (("ident", ident_sb, ident_d), ("identf", identf_sb, identf_d),
                               ("ones2", ones2_sb, ones2_d), ("ident2", ident2_sb, ident2_d),
                               ("onescol", onescol_sb, onescol_d), ("onescolb", onescolb_sb, onescolb_d), ("tabs", tabs, tabs_d)):
            toks[name] = PL.dma("sync", lambda e, d=dst, s=src: e.dma_start(out=d[:], in_=s[:]))

        E.op("vector", lambda e: e.memset(zc2_sb[:], 0.0))
        for vb in Vpb:
            vo = vb.rearrange("p (t h c) -> p t h c", t=32, h=2)[:, :, :, 64:65]
            E.op("vector", lambda e, vo=vo: e.memset(vo, 1.0))
        E.op("vector", lambda e: e.nop(), inc="vV")

        xt = (xt0, xt1)
        v_tile_done = [0] * 33
        t_tr_done = [0] * 33
        first_tr = True

        def emit_ln(buf, out_ap):
            E.op("vector", lambda e, b=buf: e.reduce_sum(out=st_sum, in_=b[:], axis=AX.X))
            E.op("vector", lambda e: e.drain())
            E.opd("vector", lambda e: e.tensor_scalar_mul(out=st_mean, in0=st_sum, scalar1=1.0 / D))
            E.opd("vector", lambda e: e.tensor_scalar_mul(out=st_ex2, in0=st_ssq, scalar1=1.0 / D))
            E.opd("vector", lambda e: e.tensor_mul(out=st_m2, in0=st_mean, in1=st_mean))
            E.opd("vector", lambda e: e.tensor_sub(out=st_var, in0=st_ex2, in1=st_m2))
            E.opd("vector", lambda e: e.tensor_scalar_add(out=st_vep, in0=st_var, scalar1=EPS), inc="vV")
            v_vep = E.cnt["vV"]
            E.w("scalar", "vV", v_vep)
            E.opd("scalar", lambda e: e.activation(out=st_s0, in_=st_vep, func=AF.Ln))
            E.op("scalar", lambda e: e.activation(out=st_r0, in_=st_s0, func=AF.Exp, scale=-0.5), inc="sS")
            s_r0 = E.cnt["sS"]
            E.w("vector", "sS", s_r0)
            E.opd("vector", lambda e: e.tensor_mul(out=st_a, in0=st_r0, in1=st_r0))
            E.opd("vector", lambda e: e.tensor_mul(out=st_b, in0=st_a, in1=st_vep))
            E.opd("vector", lambda e: e.tensor_scalar(out=st_c, in0=st_b, scalar1=-0.5, scalar2=1.5, op0=ALU.mult, op1=ALU.add))
            E.opd("vector", lambda e: e.tensor_mul(out=st_rstd, in0=st_r0, in1=st_c))
            E.opd("vector", lambda e, o=out_ap, b=buf: e.tensor_scalar(out=o, in0=b[:], scalar1=st_mean, scalar2=st_rstd, op0=ALU.subtract, op1=ALU.mult), inc="vV")
            return E.cnt["vV"]

        for i in range(32):
            if i >= 2:
                E.w("sync", "vV", v_tile_done[i - 2])
            tok_x = PL.dma("sync", lambda e, i=i: e.dma_start(out=xt[i % 2][:], in_=xin[i * 128:(i + 1) * 128, :]))
            E.wt("scalar", tok_x)
            E.op("scalar", lambda e, i=i: e.activation(out=trash[:], in_=xt[i % 2][:], func=AF.Square, accum_out=st_ssq), inc="sS")
            s_sq = E.cnt["sS"]
            E.wt("vector", tok_x)
            E.w("vector", "sS", s_sq)
            if i >= 1:
                E.w("vector", "tT", t_tr_done[i - 1])
            v_xn = emit_ln(xt[i % 2], xnb32[:])
            E.w("tensor", "vV", v_xn)
            if first_tr:
                E.wt("tensor", toks["identf"])
                first_tr = False
            for rnd in range(2):
                if rnd == 1:
                    E.w("tensor", "vV", E.cnt["vV"])
                for c in range(4):
                    cc = rnd * 4 + c
                    E.op("tensor", lambda e, cc=cc, c=c: e.transpose(out=PJ[:, c * 128:(c + 1) * 128], in_=xnb32[:, cc * 128:(cc + 1) * 128], identity=identf_sb[:]), inc="tT" if c == 3 else None)
                t_r = E.cnt["tT"]
                E.w("vector", "tT", t_r)
                dst = xnT.rearrange("p (c t) -> p c t", c=8)[:, rnd * 4:rnd * 4 + 4, i * 128:(i + 1) * 128]
                E.op("vector", lambda e, dst=dst: e.tensor_copy(out=dst, in_=PJ[:, 0:512]), inc="vV")
            t_tr_done[i] = E.cnt["tT"]
            v_tile_done[i] = E.cnt["vV"]
        v_xnT_all = E.cnt["vV"]

        state = {"wconv": 0, "last_proj_t": 0, "pj_free": [0], "rope_ready": [0, 0],
                 "vp_ready": [0, 0], "bias_ready": [0, 0],
                 "prev_exp": [0, 0, 0], "prev_av": [0, 0, 0]}
        tok_spill = [None] * NPAIR

        def gen_proj(p):
            par = p % 2
            QT, KT, Vp = QTb[par], KTb[par], Vpb[par]
            E.w("sync", "vV", state["wconv"])
            tok_w = PL.dma("sync", lambda e, p=p: e.dma_start(out=wstg[:], in_=wpair[p]))
            E.wt("vector", tok_w)
            E.w("vector", "tT", state["last_proj_t"])
            if p == 0:
                E.wt("vector", toks["tabs"])
            E.op("vector", lambda e: e.tensor_copy(out=wbf[:], in_=wstg[:]), inc="vV")
            state["wconv"] = E.cnt["vV"]
            wconv = state["wconv"]
            yield

            evac_marks = {}
            pj_free = state["pj_free"]

            rope_v = [0]

            def proj_slices(sel_off, nslices):
                for s in range(nslices):
                    E.w("tensor", "vV", max(v_xnT_all, wconv))
                    E.w("tensor", "vV", pj_free[0])
                    for c in range(8):
                        lhsT = wbf[:, c * 384 + sel_off: c * 384 + sel_off + 128]
                        rhs = xnT[:, c * 4096 + s * 512: c * 4096 + (s + 1) * 512]
                        E.op("tensor", lambda e, l=lhsT, r=rhs, c=c:
                             e.matmul(out=PJ[:], lhsT=l, rhs=r, start=(c == 0), stop=(c == 7)),
                             inc="tT" if c == 7 else None)
                    t_mm = E.cnt["tT"]
                    E.w("vector", "tT", t_mm)
                    if sel_off == 256:
                        E.opd("vector", lambda e: e.tensor_copy(out=vstg[:], in_=PJ[:]), inc="vV")
                    else:
                        E.opd("vector", lambda e, s=s:
                              e.tensor_copy(out=Tstg[:, s * 512:(s + 1) * 512], in_=PJ[:]), inc="vV")
                    evac_marks[(sel_off, s)] = E.cnt["vV"]
                    pj_free[0] = E.cnt["vV"]
                    yield

            def rope(n, dst):
                if p == 0:
                    E.wt("vector", toks["tabs"])
                for c0 in range(0, n, 1024):
                    c1 = c0 + 1024
                    for sb, db in ((32, 0), (0, 32), (96, 64), (64, 96)):
                        E.op("vector", lambda e, c0=c0, c1=c1, sb=sb, db=db: e.tensor_copy(out=Tsw[db:db + 32, c0:c1], in_=Tstg[sb:sb + 32, c0:c1]))
                    E.op("vector", lambda e: e.drain())
                    yield
                    E.op("vector", lambda e, c0=c0, c1=c1: e.tensor_mul(out=Tsw[:, c0:c1], in0=Tsw[:, c0:c1], in1=SS[:, c0:c1]))
                    E.opd("vector", lambda e, c0=c0, c1=c1: e.tensor_mul(out=Tstg[:, c0:c1], in0=Tstg[:, c0:c1], in1=CC[:, c0:c1]))
                    E.opd("vector", lambda e, c0=c0, c1=c1, dst=dst: e.tensor_add(out=dst[:, c0:c1], in0=Tstg[:, c0:c1], in1=Tsw[:, c0:c1]), inc="vV")
                    yield

            def norms(src, ncols, out_col):
                for c0 in range(0, ncols, 1024):
                    E.op("vector", lambda e, src=src, c0=c0: e.tensor_mul(out=Tsw[:, c0:c0 + 1024], in0=src[:, c0:c0 + 1024], in1=src[:, c0:c0 + 1024]), inc="vV")
                    yield
                v_sq = E.cnt["vV"]
                E.w("tensor", "vV", v_sq)
                E.w("tensor", "vV", pj_free[0])
                nsl = ncols // 512
                for s in range(nsl):
                    E.op("tensor", lambda e, s=s, nsl=nsl: e.matmul(
                        out=PJ[0:2, :], lhsT=ones2_sb[:],
                        rhs=Tsw[:, s * 512:(s + 1) * 512], start=(s == 0), stop=(s == nsl - 1)),
                        inc="tT" if s == nsl - 1 else None)
                t_nm = E.cnt["tT"]
                E.w("vector", "tT", t_nm)
                E.op("vector", lambda e: e.reduce_max(out=st_sb[0:2, 0:1], in_=PJ[0:2, :], axis=AX.X))
                E.op("vector", lambda e: e.drain())
                E.opd("vector", lambda e, oc=out_col: e.tensor_copy(out=nm_sb[:, oc:oc + 1], in_=st_sb[0:2, 0:1]), inc="vV")
                pj_free[0] = E.cnt["vV"]

            yield from proj_slices(0, 4)
            yield from rope(2048, QT)
            rope_v[0] = E.cnt["vV"]
            yield from norms(QT, 2048, 0)
            yield
            yield from proj_slices(128, 8)
            yield from rope(4096, KT)
            rope_v[0] = E.cnt["vV"]
            yield from norms(KT, 4096, 1)
            yield
            E.opd("vector", lambda e: e.tensor_add(out=nm_sb[:, 2:3], in0=nm_sb[:, 0:1], in1=nm_sb[:, 1:2]), inc="vV")
            E.opd("vector", lambda e: e.tensor_scalar_mul(out=nm_sb[:, 4:5], in0=nm_sb[:, 2:3], scalar1=SCALE * 0.5))
            E.op("vector", lambda e: e.tensor_scalar(out=zc2_sb[:], in0=zc2_sb[:], scalar1=nm_sb[:, 4:5], scalar2=None, op0=ALU.add), inc="vV")
            v_mb = E.cnt["vV"]
            E.w("tensor", "vV", v_mb)
            E.w("tensor", "vV", pj_free[0])
            if p == 0:
                E.wt("tensor", toks["ident2"])
                E.wt("tensor", toks["ones2"])
            E.op("tensor", lambda e: e.matmul(out=PJ[:, 0:2], lhsT=zc2_sb[:], rhs=ident2_sb[:], start=True, stop=True), inc="tT")
            t_mb = E.cnt["tT"]
            E.w("vector", "tT", t_mb)
            E.op("vector", lambda e, par=par: e.tensor_scalar_mul(out=biasAB[:, par * 2:par * 2 + 2], in0=PJ[:, 0:2], scalar1=-1.0))
            E.opd("vector", lambda e: e.memset(zc2_sb[:], 0.0), inc="vV")
            state["bias_ready"][par] = E.cnt["vV"]
            pj_free[0] = E.cnt["vV"]
            yield
            gen_v = proj_slices(256, 8)
            for s in range(8):
                next(gen_v)
                E.w("tensor", "vV", evac_marks[(256, s)])
                E.w("tensor", "vV", pj_free[0])
                for j in range(4):
                    E.op("tensor", lambda e, j=j: e.transpose(
                        out=PJ[:, j * 128:(j + 1) * 128],
                        in_=vstg[:, j * 128:(j + 1) * 128],
                        identity=identf_sb[:]), inc="tT" if j == 3 else None)
                t_vt = E.cnt["tT"]
                E.w("vector", "tT", t_vt)
                dst = Vp.rearrange("p (t h c) -> p t h c", t=32, h=2)[:, 4 * s:4 * s + 4, :, 0:64]
                src = PJ.rearrange("p (j h c) -> p j h c", j=4, h=2, c=64)[:, :, :, 0:64]
                E.opd("vector", lambda e, dst=dst, src=src: e.tensor_copy(out=dst, in_=src), inc="vV")
                pj_free[0] = E.cnt["vV"]
                evac_marks[(256, s)] = E.cnt["vV"]
                yield
            state["last_proj_t"] = E.cnt["tT"]
            state["vp_ready"][par] = E.cnt["vV"]

        def run_gen(g):
            if g is not None:
                for _ in g:
                    pass

        run_gen(gen_proj(0))

        for p in range(NPAIR):
            par = p % 2
            QT, KT, Vp = QTb[par], KTb[par], Vpb[par]
            bias = biasAB[:, par * 2:par * 2 + 2]
            gen = gen_proj(p + 1) if p + 1 < NPAIR else None

            import os as _os
            if _os.environ.get("KDBG"):
                print(f"pair {p}: vp_ready={state['vp_ready']}, bias={state['bias_ready']}, vV={E.cnt['vV']}")
            E.w("tensor", "vV", state["vp_ready"][par])
            GRP = [(2 * i, 2) for i in range(16)]
            exp_done = dict()
            av_done = dict()
            G = 0
            v_oa_free = state.get("prev_oa", 0)
            v_norm_prev = state.get("prev_norm", 0)
            pending_norm = []
            pending_tail = []
            gen_budget = 0.0
            first_qhip = True
            for q in range(4):
                for hip in range(2):
                    g0 = G
                    gl = g0 + len(GRP) - 1
                    for gi, (base, ln) in enumerate(GRP):
                        g = g0 + gi
                        bank = PSC[:, (g % 3) * 1024:(g % 3) * 1024 + 1024]
                        if g >= 3:
                            E.w("tensor", "sS", exp_done[g - 3])
                        elif state["prev_exp"][g % 3]:
                            E.w("tensor", "sS", state["prev_exp"][g % 3])
                        for k in range(ln):
                            ci = base + k
                            E.op("tensor", lambda e, ci=ci, k=k, q=q, hip=hip, bank=bank, KT=KT, QT=QT: e.matmul(
                                out=bank[:, k * 512:(k + 1) * 512],
                                lhsT=KT[hip * 64:(hip + 1) * 64, ci * 128:(ci + 1) * 128],
                                rhs=QT[hip * 64:(hip + 1) * 64, q * 512:(q + 1) * 512],
                                start=True, stop=True), inc="tT" if k == ln - 1 else None)
                        t_sc = E.cnt["tT"]
                        E.w("scalar", "tT", t_sc)
                        if g >= 3:
                            E.w("scalar", "tT", av_done[g - 3])
                        elif state["prev_av"][g % 3]:
                            E.w("scalar", "tT", state["prev_av"][g % 3])
                        if first_qhip and gi == 0:
                            E.w("scalar", "vV", state["bias_ready"][par])
                        E.op("scalar", lambda e, ln=ln, g=g, bank=bank, bias=bias, hip=hip: e.activation(
                            out=stag[g % 3][:, 0:ln * 512], in_=bank[:, 0:ln * 512], func=AF.Exp,
                            bias=bias[:, hip:hip + 1], scale=SCALE), inc="sS")
                        exp_done[g] = E.cnt["sS"]
                        if gi == 0 and pending_tail:
                            v_oa_free = pending_tail.pop(0)()
                        if gi == 2 and pending_norm:
                            v_norm_prev = pending_norm.pop(0)()
                        gen_budget += 44.0 / 128.0
                        while gen is not None and gen_budget >= 1.0:
                            gen_budget -= 1.0
                            try:
                                next(gen)
                            except StopIteration:
                                gen = None
                        if gi >= 1:
                            gp = g - 1
                            E.w("tensor", "sS", exp_done[gp])
                            if gi == 1:
                                E.w("tensor", "vV", v_oa_free)
                            bb, bl = GRP[gi - 1]
                            for k in range(bl):
                                ci = bb + k
                                E.op("tensor", lambda e, ci=ci, k=k, gp=gp, hip=hip, Vp=Vp,
                                     st_=(gp == g0 and k == 0): e.matmul(
                                    out=OA[0:65, :], lhsT=Vp[:, ci * 130 + hip * 65: ci * 130 + hip * 65 + 65],
                                    rhs=stag[gp % 3][:, k * 512:(k + 1) * 512],
                                    start=st_, stop=False), inc="tT" if k == bl - 1 else None)
                            av_done[gp] = E.cnt["tT"]
                    G = gl + 1
                    first_qhip = False

                    def _deferred_tail(gl=gl, hip=hip, Vp=Vp, q=q, spill_wait=(p >= 1 and q == 0 and hip == 0)):
                        E.w("tensor", "sS", exp_done[gl])
                        bb, bl = GRP[-1]
                        for k in range(bl):
                            ci = bb + k
                            E.op("tensor", lambda e, ci=ci, k=k, gl=gl, hip=hip, Vp=Vp,
                                 sp_=(k == bl - 1): e.matmul(
                                out=OA[0:65, :], lhsT=Vp[:, ci * 130 + hip * 65: ci * 130 + hip * 65 + 65],
                                rhs=stag[gl % 3][:, k * 512:(k + 1) * 512],
                                start=False, stop=sp_), inc="tT" if k == bl - 1 else None)
                        av_done[gl] = E.cnt["tT"]
                        E.w("vector", "tT", av_done[gl])
                        if spill_wait:
                            E.wt("vector", tok_spill[p - 1])
                        E.opd("vector", lambda e: e.reciprocal(out=rbc[0:1, :], in_=OA[64:65, :]))
                        E.opd("vector", lambda e: e.tensor_copy(out=rbch[0:1, :], in_=rbc[0:1, :]))
                        E.opd("vector", lambda e: e.scalar_tensor_tensor(out=rbcl[0:1, :], in0=rbch[0:1, :], scalar=-1.0, in1=rbc[0:1, :], op0=ALU.mult, op1=ALU.add))
                        E.opd("vector", lambda e: e.tensor_copy(out=oaev[0:64, :], in_=OA[0:64, :]), inc="vV")
                        return E.cnt["vV"]
                    pending_tail.append(_deferred_tail)

                    def _deferred_norm(q=q, hip=hip, first=(p == 0 and q == 0 and hip == 0)):
                        E.w("tensor", "vV", v_oa_free)
                        E.w("tensor", "vV", state["pj_free"][0])
                        if first:
                            E.wt("tensor", toks["onescol"])
                        E.op("tensor", lambda e: e.matmul(out=PJ[0:64, :], lhsT=onescolb_sb[0:1, 0:64], rhs=rbch[0:1, :], start=True, stop=False))
                        E.op("tensor", lambda e: e.matmul(out=PJ[0:64, :], lhsT=onescolb_sb[0:1, 0:64], rhs=rbcl[0:1, :], start=False, stop=True), inc="tT")
                        t_bc = E.cnt["tT"]
                        E.w("vector", "tT", t_bc)
                        E.opd("vector", lambda e, q=q, hip=hip: e.tensor_mul(
                            out=aopair[hip * 64:(hip + 1) * 64, q * 512:(q + 1) * 512],
                            in0=oaev[0:64, :], in1=PJ[0:64, :]), inc="vV")
                        state["pj_free"][0] = E.cnt["vV"]
                        return E.cnt["vV"]
                    pending_norm.append(_deferred_norm)
            while pending_tail:
                v_oa_free = pending_tail.pop(0)()
            while pending_norm:
                v_norm_prev = pending_norm.pop(0)()
            run_gen(gen)
            state["prev_oa"] = v_oa_free
            state["prev_norm"] = v_norm_prev
            pe = [0, 0, 0]; pa = [0, 0, 0]
            for g in (125, 126, 127):
                pe[g % 3] = exp_done[g]
                pa[g % 3] = av_done[g]
            state["prev_exp"] = pe
            state["prev_av"] = pa
            v_ao = E.cnt["vV"]
            E.w("gpsimd", "vV", v_ao)
            tok_spill[p] = PG.dma("gpsimd", lambda e, p=p: e.dma_start(out=aoT_d[p], in_=aopair[:]))

        p2_v = E.cnt["vV"]; p2_t = E.cnt["tT"]; p2_s = E.cnt["sS"]

        E.w("sync", "vV", p2_v)
        E.wt("sync", tok_spill[NPAIR - 2])
        E.wt("sync", tok_spill[NPAIR - 1])
        v_prev = 0
        for h in range(2):
            if h == 1:
                E.w("sync", "vV", v_prev)
            tok_wo = PL.dma("sync", lambda e, h=h: e.dma_start(out=rstg[:], in_=wout_d[:, h * 4096:(h + 1) * 4096]))
            E.wt("vector", tok_wo)
            if h == 0:
                E.w("vector", "tT", p2_t)
            E.op("vector", lambda e, h=h: e.tensor_copy(out=woutb[:, h * 4096:(h + 1) * 4096], in_=rstg[:]), inc="vV")
            v_prev = E.cnt["vV"]
        v_qq = 0
        for h in range(2):
            E.w("sync", "vV", v_prev if h == 0 else v_qq)
            tok_rh = PL.dma("sync", lambda e, h=h: e.dma_start(out=rstg[:], in_=rhsT_d[:, h * 4096:(h + 1) * 4096]))
            for c4 in range(4):
                cc = h * 4 + c4
                E.wt("vector", tok_rh)
                E.op("vector", lambda e, c4=c4: e.reduce_max(out=st_amax, in_=rstg[:, c4 * 1024:(c4 + 1) * 1024], axis=AX.X, apply_absolute_value=True))
                E.op("vector", lambda e: e.drain())
                E.opd("vector", lambda e: e.tensor_scalar_max(out=st_amax, in0=st_amax, scalar1=1e-12))
                E.opd("vector", lambda e, cc=cc: e.tensor_scalar_mul(out=ls16[:, cc:cc + 1], in0=st_amax, scalar1=1.0 / INT8_MAX))
                E.opd("vector", lambda e, cc=cc: e.reciprocal(out=st_lrcp, in_=ls16[:, cc:cc + 1]))
                E.opd("vector", lambda e, c4=c4: e.tensor_scalar(out=p3t[:], in0=rstg[:, c4 * 1024:(c4 + 1) * 1024], scalar1=st_lrcp, scalar2=MAGIC, op0=ALU.mult, op1=ALU.add))
                E.opd("vector", lambda e: e.tensor_scalar(out=p3t[:], in0=p3t[:], scalar1=MAGIC, scalar2=None, op0=ALU.subtract), inc="vV")
                v_q8 = E.cnt["vV"]
                for rnd in range(2):
                    E.w("tensor", "vV", E.cnt["vV"])
                    for fc in range(4):
                        fcc = rnd * 4 + fc
                        E.op("tensor", lambda e, fcc=fcc, fc=fc: e.transpose(
                            out=PJ[:, fc * 128:(fc + 1) * 128], in_=p3t[:, fcc * 128:(fcc + 1) * 128],
                            identity=identf_sb[:]), inc="tT" if fc == 3 else None)
                    t_qb = E.cnt["tT"]
                    E.w("vector", "tT", t_qb)
                    dst = qrhs.rearrange("p (f c) -> p f c", f=8)[:, rnd * 4:rnd * 4 + 4, cc * 128:(cc + 1) * 128]
                    src = PJ.rearrange("p (f c) -> p f c", f=4)
                    E.op("vector", lambda e, dst=dst, src=src: e.tensor_copy(out=dst, in_=src), inc="vV")
                v_qq = E.cnt["vV"]
        v_qrhs = E.cnt["vV"]
        E.opd("vector", lambda e: e.tensor_copy(out=rsv[:, 0:8], in_=ls16[:, 0:8]), inc="vV")
        v_rsv = E.cnt["vV"]
        E.w("tensor", "vV", max(v_qrhs, v_rsv))
        E.op("tensor", lambda e: e.transpose(out=PJ[0:8, 0:128], in_=rsv[:], identity=identf_sb[:]), inc="tT")
        t_rs = E.cnt["tT"]
        E.w("vector", "tT", t_rs)
        E.op("vector", lambda e: e.tensor_copy(out=rsT_sb[:], in_=PJ[0:8, 0:128]), inc="vV")
        v_rsT = E.cnt["vV"]
        E.w("gpsimd", "vV", v_rsT)
        tok_rr = PG.dma("gpsimd", lambda e: e.dma_start(out=rsrow, in_=rsT_sb[:]))
        E.wt("tensor", tok_rr)
        E.op("tensor", lambda e: e.matmul(out=PSC[:, 0:512], lhsT=onescol_sb[:], rhs=rsrow[0:1, 0:512], start=True, stop=True))
        E.op("tensor", lambda e: e.matmul(out=PSC[:, 512:1024], lhsT=onescol_sb[:], rhs=rsrow[0:1, 512:1024], start=True, stop=True), inc="tT")
        t_rsb = E.cnt["tT"]
        E.w("vector", "tT", t_rsb)
        E.op("vector", lambda e: e.tensor_copy(out=rsb[:, 0:512], in_=PSC[:, 0:512]))
        E.opd("vector", lambda e: e.tensor_copy(out=rsb[:, 512:1024], in_=PSC[:, 512:1024]), inc="vV")
        v_rsb = E.cnt["vV"]
        tok_bo = PL.dma("sync", lambda e: e.dma_start(out=bouts[:], in_=bout_d[:]))
        aoT_src = aoT_d.rearrange("p q c -> q p c")
        aoT_dst = aoTs.rearrange("q (p c) -> q p c", p=NPAIR)
        tok_ao = PL.dma("sync", lambda e, s=aoT_src, d=aoT_dst: e.dma_start(out=d, in_=s))

        v_a = [0] * 16
        for t in range(16):
            if t >= 2:
                E.w("sync", "vV", v_a[t - 2])
            tok_x = PL.dma("sync", lambda e, t=t: e.dma_start(out=xt[t % 2][:], in_=xin[t * 128:(t + 1) * 128, :]))
            E.wt("scalar", tok_x)
            E.op("scalar", lambda e, t=t: e.activation(out=tr_bf[:], in_=xt[t % 2][:], func=AF.Square, accum_out=st_ssq), inc="sS")
            s_sq = E.cnt["sS"]
            E.wt("vector", tok_x)
            E.w("vector", "sS", s_sq)
            if t == 0:
                E.w("vector", "vV", v_rsb)
            emit_ln(xt[t % 2], sbf[:])
            E.op("vector", lambda e: e.reduce_max(out=st_amax, in_=sbf[:], axis=AX.X, apply_absolute_value=True))
            E.op("vector", lambda e: e.drain())
            E.opd("vector", lambda e: e.tensor_scalar_max(out=st_amax, in0=st_amax, scalar1=1e-12))
            E.opd("vector", lambda e, t=t: e.tensor_scalar_mul(out=ls16[:, t:t + 1], in0=st_amax, scalar1=1.0 / INT8_MAX))
            E.opd("vector", lambda e, t=t: e.reciprocal(out=st_lrcp, in_=ls16[:, t:t + 1]))
            E.opd("vector", lambda e: e.tensor_scalar(out=p3t[:], in0=sbf[:], scalar1=st_lrcp, scalar2=MAGIC, op0=ALU.mult, op1=ALU.add))
            E.opd("vector", lambda e: e.tensor_scalar(out=p3t[:], in0=p3t[:], scalar1=MAGIC, scalar2=None, op0=ALU.subtract), inc="vV")
            v_q = E.cnt["vV"]
            for rnd in range(2):
                E.w("tensor", "vV", E.cnt["vV"])
                if t == 0 and rnd == 0:
                    pass
                for c in range(4):
                    cc = rnd * 4 + c
                    E.op("tensor", lambda e, cc=cc, c=c: e.transpose(
                        out=PJ[:, c * 128:(c + 1) * 128], in_=p3t[:, cc * 128:(cc + 1) * 128],
                        identity=identf_sb[:]), inc="tT" if c == 3 else None)
                t_r = E.cnt["tT"]
                E.w("vector", "tT", t_r)
                dst = qlhsT16[:, t * 1024 + rnd * 512: t * 1024 + rnd * 512 + 512]
                E.op("vector", lambda e, dst=dst: e.tensor_copy(out=dst, in_=PJ[:, 0:512]), inc="vV")
            v_a[t] = E.cnt["vV"]
        v_a_all = E.cnt["vV"]

        st_tok = [None, None]
        v_b = [0] * 16
        for t in range(16):
            if t == 0:
                E.w("sync", "vV", v_a_all)
            if t >= 2:
                E.w("sync", "vV", v_b[t - 2])
            tok_x = PL.dma("sync", lambda e, t=t: e.dma_start(out=xt[t % 2][:], in_=xin[t * 128:(t + 1) * 128, :]))
            if t == 0:
                E.w("tensor", "vV", v_a_all)
                E.wt("tensor", tok_ao)
            else:
                E.w("tensor", "vV", v_b[t - 1])
            for c in range(8):
                for hh in range(2):
                    E.op("tensor", lambda e, c=c, hh=hh, t=t: e.matmul(
                        out=PSC[:, hh * 512:(hh + 1) * 512], lhsT=qlhsT16[:, t * 1024 + c * 128: t * 1024 + (c + 1) * 128],
                        rhs=qrhs[:, c * 1024 + hh * 512: c * 1024 + (hh + 1) * 512],
                        start=(c == 0), stop=(c == 7)), inc="tT" if (c == 7 and hh == 1) else None)
            t_ffn = E.cnt["tT"]
            for pp in range(8):
                for hh in range(2):
                    E.op("tensor", lambda e, pp=pp, hh=hh, t=t: e.matmul(
                        out=PSC[:, 1024 + hh * 512: 1024 + (hh + 1) * 512],
                        lhsT=aoTs[:, pp * 2048 + t * 128: pp * 2048 + (t + 1) * 128],
                        rhs=woutb[:, pp * 1024 + hh * 512: pp * 1024 + (hh + 1) * 512],
                        start=(pp == 0), stop=(pp == 7)), inc="tT" if (pp == 7 and hh == 1) else None)
            t_op = E.cnt["tT"]

            E.w("vector", "tT", t_ffn)
            E.opd("vector", lambda e: e.tensor_mul(out=p3t[:], in0=PSC[:, 0:1024], in1=rsb[:]), inc="vV")
            v_g1 = E.cnt["vV"]
            E.w("scalar", "vV", v_g1)
            E.op("scalar", lambda e, t=t: e.activation(out=gbuf[:], in_=p3t[:], func=AF.Silu, scale=ls16[:, t:t + 1]), inc="sS")
            s_g = E.cnt["sS"]
            E.w("vector", "tT", t_op)
            E.wt("vector", tok_x)
            E.opd("vector", lambda e: e.tensor_add(out=sbf[:], in0=PSC[:, 1024:2048], in1=bouts[:]))
            E.w("vector", "sS", s_g)
            if t >= 2:
                E.wt("vector", st_tok[t % 2])
            E.opd("vector", lambda e: e.tensor_mul(out=sbf[:], in0=sbf[:], in1=gbuf[:]))
            E.op("vector", lambda e, t=t: e.tensor_add(out=outb[t % 2][:], in0=sbf[:], in1=xt[t % 2][:]), inc="vV")
            v_b[t] = E.cnt["vV"]
            E.w("sync", "vV", v_b[t])
            st_tok[t % 2] = PS.dma("sync", lambda e, t=t: e.dma_start(out=out_d[t * 128:(t + 1) * 128, :], in_=outb[t % 2][:]))

        E.w("sync", "st0", E.cnt["st0"])
        E.w("sync", "st1", E.cnt["st1"])

        @block.sync
        def _(e):
            for fn in E.ops["sync"]:
                fn(e)

        @block.tensor
        def _(e):
            for fn in E.ops["tensor"]:
                fn(e)

        @block.vector
        def _(e):
            for fn in E.ops["vector"]:
                fn(e)

        @block.scalar
        def _(e):
            for fn in E.ops["scalar"]:
                fn(e)

        @block.gpsimd
        def _(e):
            for fn in E.ops["gpsimd"]:
                fn(e)

    return nc



def _prep_shared(w_qkv, w_out, b_out, rhs):
    wp = np.empty((NPAIR, 128, 3072), dtype=np.float32)
    for p in range(NPAIR):
        wq = w_qkv[:, p * 128:(p + 1) * 128]
        wk = w_qkv[:, 1024 + p * 128:1024 + (p + 1) * 128]
        wv = w_qkv[:, 2048 + p * 128:2048 + (p + 1) * 128]
        cat = np.concatenate([wq, wk, wv], axis=1)
        wp[p] = np.ascontiguousarray(cat.reshape(8, 128, 384).transpose(1, 0, 2).reshape(128, 3072))
    o2 = np.zeros((128, 2), dtype=np.float32)
    o2[0:64, 0] = 1.0
    o2[64:128, 1] = 1.0
    rhsT = np.ascontiguousarray(rhs.T)
    consts = {
        "wout": np.ascontiguousarray(w_out.reshape(8, 128, 1024).transpose(1, 0, 2).reshape(128, 8192)).astype(np.float32),
        "rhsT": np.ascontiguousarray(rhsT.reshape(8, 128, 1024).transpose(1, 0, 2).reshape(128, 8192)).astype(np.float32),
        "bout": np.ascontiguousarray(np.broadcast_to(b_out[None, :], (128, 1024))).astype(np.float32),
        "ident": np.eye(128, dtype=np.float32).astype(bfloat16),
        "identf": np.eye(128, dtype=np.float32),
        "ident2": np.eye(2, dtype=np.float32).astype(bfloat16),
        "onescol": np.ones((1, 128), dtype=np.float32),
        "onescolb": np.ones((1, 128), dtype=np.float32).astype(bfloat16),
        "ones2": o2.astype(bfloat16),
    }
    return wp, consts


def make_in_maps(x, w_qkv, w_out, b_out, rhs):
    wp, consts = _prep_shared(w_qkv, w_out, b_out, rhs)
    tab64 = _host_tables()
    in_maps = []
    for core in range(8):
        b, half = core // 2, core % 2
        perm = np.concatenate([np.arange(half * NQ, (half + 1) * NQ),
                               np.arange((1 - half) * NQ, (2 - half) * NQ)])
        xp = np.ascontiguousarray(x[b][perm])
        tabs = np.ascontiguousarray(
            tab64.reshape(128, 2, N)[:, :, perm].reshape(128, 2 * N)).astype(bfloat16)
        m = {"xin": xp, "wpair": wp, "tabs": tabs}
        m.update(consts)
        in_maps.append(m)
    return in_maps


def kernel(L0_attn_x, L0_attn_w_qkv, L0_attn_w_out, L0_attn_b_out, L0_ffn_rhs):
    x = np.asarray(L0_attn_x, dtype=np.float32)
    in_maps = make_in_maps(x,
                           np.asarray(L0_attn_w_qkv, dtype=np.float32),
                           np.asarray(L0_attn_w_out, dtype=np.float32),
                           np.asarray(L0_attn_b_out, dtype=np.float32),
                           np.asarray(L0_ffn_rhs, dtype=np.float32))
    nc = build()
    res = run_bass_kernel_spmd(nc, in_maps, core_ids=list(range(8)))
    out = np.empty((B, N, D), dtype=np.float32)
    for core in range(8):
        b, half = core // 2, core % 2
        out[b, half * NQ:(half + 1) * NQ] = res.results[core]["out"]
    return out


# version 12
# speedup vs baseline: 1.3632x; 1.0968x over previous
import numpy as np
import ml_dtypes
import concourse.bass as bass
import concourse.mybir as mybir
from concourse import library_config
from concourse.bass_utils import run_bass_kernel_spmd

F32 = mybir.dt.float32
BF16 = mybir.dt.bfloat16
AF = mybir.ActivationFunctionType
ALU = mybir.AluOpType
AX = mybir.AxisListType
bfloat16 = ml_dtypes.bfloat16

B, N, D = 4, 4096, 1024
HEADS, DH, HALF = 16, 64, 32
NQ = 2048
NPAIR = 8
SCALE = float(DH) ** -0.5
MAGIC = 12582912.0
EPS = 1e-5
INT8_MAX = 127.0


def _host_tables():
    j = np.arange(HALF, dtype=np.float64)
    theta = 10000.0 ** (-j / HALF)
    pos = np.arange(N, dtype=np.float64)
    ang = np.outer(theta, pos)
    cos = np.cos(ang)
    sin = np.sin(ang)
    CCt = np.concatenate([cos, cos, cos, cos], axis=0)
    SSt = np.concatenate([-sin, sin, -sin, sin], axis=0)
    return np.concatenate([CCt, SSt], axis=1)


class _Emit:
    ENGS = ("sync", "tensor", "vector", "scalar", "gpsimd")

    def __init__(self, sems):
        self.ops = {e: [] for e in self.ENGS}
        self.cnt = {k: 0 for k in sems}
        self.sems = sems

    def wt(self, eng, tok):
        if tok is not None:
            self.w(eng, tok[0], tok[1])

    def w(self, eng, sem, val):
        if val <= 0:
            return
        assert val <= self.cnt[sem], f"wait {sem}>={val} emitted={self.cnt[sem]}"
        s = self.sems[sem]
        self.ops[eng].append(lambda e, s=s, v=val: e.wait_ge(s, v))

    def op(self, eng, fn, inc=None, n=1):
        if inc is None:
            self.ops[eng].append(fn)
            return None
        s = self.sems[inc]
        self.cnt[inc] += n
        self.ops[eng].append(lambda e, fn=fn, s=s, n=n: fn(e).then_inc(s, n))
        return self.cnt[inc]

    def opd(self, eng, fn, inc=None, n=1):
        r = self.op(eng, fn, inc=inc, n=n)
        self.ops[eng].append(lambda e: e.drain())
        return r


class _DPool:

    def __init__(self, E, names):
        self.E = E
        self.names = names
        self.i = 0

    def dma(self, eng, fn):
        s = self.names[self.i % len(self.names)]
        self.i += 1
        self.E.w(eng, s, self.E.cnt[s])
        v = self.E.op(eng, fn, inc=s, n=16)
        return (s, v)


def build(debug=False):
    nc = bass.Bass("TRN2", target_bir_lowering=False)

    xin = nc.dram_tensor("xin", [N, D], F32, kind="ExternalInput").ap()
    wpair = nc.dram_tensor("wpair", [NPAIR, 128, 3072], F32, kind="ExternalInput").ap()
    wout_d = nc.dram_tensor("wout", [128, 8192], F32, kind="ExternalInput").ap()
    rhsT_d = nc.dram_tensor("rhsT", [128, 8192], F32, kind="ExternalInput").ap()
    bout_d = nc.dram_tensor("bout", [128, 1024], F32, kind="ExternalInput").ap()
    tabs_d = nc.dram_tensor("tabs", [128, 8192], BF16, kind="ExternalInput").ap()
    ident_d = nc.dram_tensor("ident", [128, 128], BF16, kind="ExternalInput").ap()
    identf_d = nc.dram_tensor("identf", [128, 128], F32, kind="ExternalInput").ap()
    ones2_d = nc.dram_tensor("ones2", [128, 2], BF16, kind="ExternalInput").ap()
    ident2_d = nc.dram_tensor("ident2", [2, 2], BF16, kind="ExternalInput").ap()
    onescol_d = nc.dram_tensor("onescol", [1, 128], F32, kind="ExternalInput").ap()
    onescolb_d = nc.dram_tensor("onescolb", [1, 128], BF16, kind="ExternalInput").ap()
    out_d = nc.dram_tensor("out", [NQ, D], F32, kind="ExternalOutput").ap()
    aoT_d = nc.dram_tensor("aoT", [NPAIR, 128, NQ], BF16, kind="Internal").ap()

    from contextlib import ExitStack
    with ExitStack() as _stk:
        en = _stk.enter_context
        ident_sb = en(nc.sbuf_tensor([128, 128], BF16))
        identf_sb = en(nc.sbuf_tensor([128, 128], F32))
        ones2_sb = en(nc.sbuf_tensor([128, 2], BF16))
        ident2_sb = en(nc.sbuf_tensor([2, 2], BF16))
        onescol_sb = en(nc.sbuf_tensor([1, 128], F32))
        zc2_sb = en(nc.sbuf_tensor([2, 128], BF16))
        xt0 = en(nc.sbuf_tensor([128, 1024], F32))
        xt1 = en(nc.sbuf_tensor([128, 1024], F32))
        st_sb = en(nc.sbuf_tensor([128, 16], F32))
        nm_sb = en(nc.sbuf_tensor([2, 8], F32))
        biasAB = en(nc.sbuf_tensor([128, 4], F32))
        rbc = en(nc.sbuf_tensor([1, 512], F32))
        rbch = en(nc.sbuf_tensor([1, 512], BF16))
        rbcl = en(nc.sbuf_tensor([1, 512], BF16))
        onescolb_sb = en(nc.sbuf_tensor([1, 128], BF16))
        rsT_sb = en(nc.sbuf_tensor([8, 128], F32))
        ls16 = en(nc.sbuf_tensor([128, 16], F32))
        abf = en(nc.sbuf_tensor([128, 76928], BF16))
        af32 = en(nc.sbuf_tensor([128, 10272], F32))
        SD0 = en(nc.psum_tensor([128, 1536], F32))
        SD1 = en(nc.psum_tensor([128, 1024], F32))
        SDB = en(nc.psum_tensor([128, 1024], BF16))
        OA = en(nc.psum_tensor([128, 512], F32))
        MX = en(nc.psum_tensor([128, 512], F32))
        SD = (SD0, SD1)
        sems = {}
        for name in (["ld%d" % i for i in range(8)] + ["st0", "st1", "dg0", "dg1",
                                                       "tT", "vV", "sS", "gG"]):
            sems[name] = en(nc.semaphore(name="sem_" + name))
        block = en(nc.Block())
        E = _Emit(sems)
        PL = _DPool(E, ["ld%d" % i for i in range(8)])
        PS = _DPool(E, ["st0", "st1"])
        PG = _DPool(E, ["dg0", "dg1"])

        xnT = abf[:, 0:32768]
        tabs = abf[:, 32768:40960]
        CC = tabs[:, 0:4096]
        SS = tabs[:, 4096:8192]
        wbf = abf[:, 40960:44032]
        QTb = (abf[:, 44032:46080], abf[:, 46080:48128])
        KTb = (abf[:, 48128:52224], abf[:, 52224:56320])
        Vpb = (abf[:, 56320:60480], abf[:, 60480:64640])
        Tstg = abf[:, 64640:68736]
        Tsw = abf[:, 68736:72832]
        stag = (abf[:, 72832:73856], abf[:, 73856:74880])
        aopair = abf[:, 74880:76928]
        xnb = Tstg[:, 0:1024]
        aoTs = abf[:, 0:16384]
        woutb = abf[:, 16384:24576]
        qrhs = abf[:, 24576:32768]
        qlhs16 = abf[:, 44032:60416]
        qlT = tabs[:, 0:1024]
        qtstg = tabs[:, 1024:2048]
        tr_bf = tabs[:, 2048:3072]

        wstg = af32[:, 0:3072]
        oaev = af32[:, 3072:3584]
        rstg = af32[:, 0:4096]
        trash = af32[:, 4096:5120]
        rsb = af32[:, 5120:6144]
        bouts = af32[:, 6144:7168]
        gbuf = af32[:, 7168:8192]
        sbf = af32[:, 8192:9216]
        p3t = af32[:, 9216:10240]
        rsv = af32[:, 10240:10248]
        rcv = af32[:, 10248:10256]
        rsrow = p3t[0:1, 0:1024]
        outb = (rstg[:, 0:1024], rstg[:, 1024:2048])

        st_sum = st_sb[:, 0:1]; st_ssq = st_sb[:, 1:2]; st_mean = st_sb[:, 2:3]
        st_ex2 = st_sb[:, 3:4]; st_m2 = st_sb[:, 4:5]; st_var = st_sb[:, 5:6]
        st_vep = st_sb[:, 6:7]; st_s0 = st_sb[:, 7:8]; st_r0 = st_sb[:, 8:9]
        st_a = st_sb[:, 9:10]; st_b = st_sb[:, 10:11]; st_c = st_sb[:, 11:12]
        st_rstd = st_sb[:, 12:13]; st_amax = st_sb[:, 13:14]
        st_lrcp = st_sb[:, 15:16]

        toks = {}
        for name, dst, src in (("ident", ident_sb, ident_d), ("identf", identf_sb, identf_d),
                               ("ones2", ones2_sb, ones2_d), ("ident2", ident2_sb, ident2_d),
                               ("onescol", onescol_sb, onescol_d), ("onescolb", onescolb_sb, onescolb_d), ("tabs", tabs, tabs_d)):
            toks[name] = PL.dma("sync", lambda e, d=dst, s=src: e.dma_start(out=d[:], in_=s[:]))

        E.op("vector", lambda e: e.memset(zc2_sb[:], 0.0))
        for vb in Vpb:
            vo = vb.rearrange("p (t h c) -> p t h c", t=32, h=2)[:, :, :, 64:65]
            E.op("vector", lambda e, vo=vo: e.memset(vo, 1.0))
        E.op("vector", lambda e: e.nop(), inc="vV")

        xt = (xt0, xt1)
        v_tile_done = [0] * 33
        t_tr_done = [0] * 33
        first_tr = True

        def emit_ln(buf, out_ap):
            E.op("vector", lambda e, b=buf: e.reduce_sum(out=st_sum, in_=b[:], axis=AX.X))
            E.op("vector", lambda e: e.drain())
            E.opd("vector", lambda e: e.tensor_scalar_mul(out=st_mean, in0=st_sum, scalar1=1.0 / D))
            E.opd("vector", lambda e: e.tensor_scalar_mul(out=st_ex2, in0=st_ssq, scalar1=1.0 / D))
            E.opd("vector", lambda e: e.tensor_mul(out=st_m2, in0=st_mean, in1=st_mean))
            E.opd("vector", lambda e: e.tensor_sub(out=st_var, in0=st_ex2, in1=st_m2))
            E.opd("vector", lambda e: e.tensor_scalar_add(out=st_vep, in0=st_var, scalar1=EPS), inc="vV")
            v_vep = E.cnt["vV"]
            E.w("scalar", "vV", v_vep)
            E.opd("scalar", lambda e: e.activation(out=st_s0, in_=st_vep, func=AF.Ln))
            E.op("scalar", lambda e: e.activation(out=st_r0, in_=st_s0, func=AF.Exp, scale=-0.5), inc="sS")
            s_r0 = E.cnt["sS"]
            E.w("vector", "sS", s_r0)
            E.opd("vector", lambda e: e.tensor_mul(out=st_a, in0=st_r0, in1=st_r0))
            E.opd("vector", lambda e: e.tensor_mul(out=st_b, in0=st_a, in1=st_vep))
            E.opd("vector", lambda e: e.tensor_scalar(out=st_c, in0=st_b, scalar1=-0.5, scalar2=1.5, op0=ALU.mult, op1=ALU.add))
            E.opd("vector", lambda e: e.tensor_mul(out=st_rstd, in0=st_r0, in1=st_c))
            E.opd("vector", lambda e, o=out_ap, b=buf: e.tensor_scalar(out=o, in0=b[:], scalar1=st_mean, scalar2=st_rstd, op0=ALU.subtract, op1=ALU.mult), inc="vV")
            return E.cnt["vV"]

        for i in range(32):
            if i >= 2:
                E.w("sync", "vV", v_tile_done[i - 2])
            tok_x = PL.dma("sync", lambda e, i=i: e.dma_start(out=xt[i % 2][:], in_=xin[i * 128:(i + 1) * 128, :]))
            E.wt("scalar", tok_x)
            E.op("scalar", lambda e, i=i: e.activation(out=trash[:], in_=xt[i % 2][:], func=AF.Square, accum_out=st_ssq), inc="sS")
            s_sq = E.cnt["sS"]
            E.wt("vector", tok_x)
            E.w("vector", "sS", s_sq)
            if i >= 1:
                E.w("vector", "tT", t_tr_done[i - 1])
            v_xn = emit_ln(xt[i % 2], xnb[:])
            E.w("tensor", "vV", v_xn)
            if first_tr:
                E.wt("tensor", toks["ident"])
                first_tr = False
            for c in range(8):
                E.op("tensor", lambda e, c=c: e.transpose(out=SDB[:, c * 128:(c + 1) * 128], in_=xnb[:, c * 128:(c + 1) * 128], identity=ident_sb[:]), inc="tT" if c == 7 else None)
            t_tr_done[i] = E.cnt["tT"]
            E.w("vector", "tT", t_tr_done[i])
            dst = xnT.rearrange("p (c t) -> p c t", c=8)[:, :, i * 128:(i + 1) * 128]
            E.op("vector", lambda e, dst=dst: e.tensor_copy(out=dst, in_=SDB[:, 0:1024]), inc="vV")
            v_tile_done[i] = E.cnt["vV"]
        v_xnT_all = E.cnt["vV"]

        state = {"wconv": 0, "last_proj_t": 0,
                 "vp_ready": [0, 0], "bias_ready": [0, 0],
                 "prev_exp": [0, 0], "prev_av": [0, 0]}
        tok_spill = [None] * NPAIR

        def gen_proj(p):
            par = p % 2
            QT, KT, Vp = QTb[par], KTb[par], Vpb[par]
            E.w("sync", "vV", state["wconv"])
            tok_w = PL.dma("sync", lambda e, p=p: e.dma_start(out=wstg[:], in_=wpair[p]))
            E.wt("vector", tok_w)
            E.w("vector", "tT", state["last_proj_t"])
            if p == 0:
                E.wt("vector", toks["tabs"])
            E.op("vector", lambda e: e.tensor_copy(out=wbf[:], in_=wstg[:]), inc="vV")
            state["wconv"] = E.cnt["vV"]
            wconv = state["wconv"]
            yield

            evac_marks = {}

            def proj_slices(sel_off, nslices):
                for s in range(nslices):
                    E.w("tensor", "vV", max(v_xnT_all, wconv))
                    if (sel_off, s - 1) in evac_marks:
                        E.w("tensor", "vV", evac_marks[(sel_off, s - 1)])
                    for c in range(8):
                        lhsT = wbf[:, c * 384 + sel_off: c * 384 + sel_off + 128]
                        rhs = xnT[:, c * 4096 + s * 512: c * 4096 + (s + 1) * 512]
                        E.op("tensor", lambda e, l=lhsT, r=rhs, c=c:
                             e.matmul(out=SD0[:, 1024:1536], lhsT=l, rhs=r, start=(c == 0), stop=(c == 7)),
                             inc="tT" if c == 7 else None)
                    t_mm = E.cnt["tT"]
                    E.w("vector", "tT", t_mm)
                    E.opd("vector", lambda e, s=s:
                          e.tensor_copy(out=Tstg[:, s * 512:(s + 1) * 512], in_=SD0[:, 1024:1536]), inc="vV")
                    evac_marks[(sel_off, s)] = E.cnt["vV"]
                    yield

            def rope(n, dst):
                for sb, db in ((32, 0), (0, 32), (96, 64), (64, 96)):
                    E.op("vector", lambda e, n=n, sb=sb, db=db: e.tensor_copy(out=Tsw[db:db + 32, 0:n], in_=Tstg[sb:sb + 32, 0:n]))
                E.op("vector", lambda e: e.drain())
                E.op("vector", lambda e, n=n: e.tensor_mul(out=Tsw[:, 0:n], in0=Tsw[:, 0:n], in1=SS[:, 0:n]))
                E.opd("vector", lambda e, n=n: e.tensor_mul(out=Tstg[:, 0:n], in0=Tstg[:, 0:n], in1=CC[:, 0:n]))
                E.opd("vector", lambda e, n=n, dst=dst: e.tensor_add(out=dst, in0=Tstg[:, 0:n], in1=Tsw[:, 0:n]), inc="vV")

            def norms(src, ncols, out_col):
                E.op("vector", lambda e, src=src, n=ncols: e.tensor_mul(out=Tsw[:, 0:n], in0=src, in1=src), inc="vV")
                v_sq = E.cnt["vV"]
                E.w("tensor", "vV", v_sq)
                nsl = ncols // 512
                for s in range(nsl):
                    E.op("tensor", lambda e, s=s, nsl=nsl: e.matmul(
                        out=MX[0:2, :], lhsT=ones2_sb[:],
                        rhs=Tsw[:, s * 512:(s + 1) * 512], start=(s == 0), stop=(s == nsl - 1)),
                        inc="tT" if s == nsl - 1 else None)
                t_nm = E.cnt["tT"]
                E.w("vector", "tT", t_nm)
                E.op("vector", lambda e: e.reduce_max(out=st_sb[0:2, 0:1], in_=MX[0:2, :], axis=AX.X))
                E.op("vector", lambda e: e.drain())
                E.opd("vector", lambda e, oc=out_col: e.tensor_copy(out=nm_sb[:, oc:oc + 1], in_=st_sb[0:2, 0:1]), inc="vV")

            yield from proj_slices(0, 4)
            rope(2048, QT[:, :])
            yield
            norms(QT[:, :], 2048, 0)
            yield
            yield from proj_slices(128, 8)
            rope(4096, KT[:, :])
            yield
            norms(KT[:, :], 4096, 1)
            yield
            E.opd("vector", lambda e: e.tensor_add(out=nm_sb[:, 2:3], in0=nm_sb[:, 0:1], in1=nm_sb[:, 1:2]), inc="vV")
            E.opd("vector", lambda e: e.tensor_scalar_mul(out=nm_sb[:, 4:5], in0=nm_sb[:, 2:3], scalar1=SCALE * 0.5))
            E.op("vector", lambda e: e.tensor_scalar(out=zc2_sb[:], in0=zc2_sb[:], scalar1=nm_sb[:, 4:5], scalar2=None, op0=ALU.add), inc="vV")
            v_mb = E.cnt["vV"]
            E.w("tensor", "vV", v_mb)
            if p == 0:
                E.wt("tensor", toks["ident2"])
                E.wt("tensor", toks["ones2"])
            E.op("tensor", lambda e: e.matmul(out=MX[:, 0:2], lhsT=zc2_sb[:], rhs=ident2_sb[:], start=True, stop=True), inc="tT")
            t_mb = E.cnt["tT"]
            E.w("vector", "tT", t_mb)
            E.op("vector", lambda e, par=par: e.tensor_scalar_mul(out=biasAB[:, par * 2:par * 2 + 2], in0=MX[:, 0:2], scalar1=-1.0))
            E.opd("vector", lambda e: e.memset(zc2_sb[:], 0.0), inc="vV")
            state["bias_ready"][par] = E.cnt["vV"]
            yield
            yield from proj_slices(256, 8)
            vp_prev = 0
            for s in range(8):
                E.w("tensor", "vV", evac_marks[(256, s)])
                if vp_prev:
                    E.w("tensor", "vV", vp_prev)
                for j in range(4):
                    E.op("tensor", lambda e, s=s, j=j: e.transpose(
                        out=SDB[:, j * 128:(j + 1) * 128],
                        in_=Tstg[:, s * 512 + j * 128: s * 512 + (j + 1) * 128],
                        identity=ident_sb[:]), inc="tT" if j == 3 else None)
                t_vt = E.cnt["tT"]
                E.w("vector", "tT", t_vt)
                dst = Vp.rearrange("p (t h c) -> p t h c", t=32, h=2)[:, 4 * s:4 * s + 4, :, 0:64]
                src = SDB[:, 0:512].rearrange("p (j h c) -> p j h c", j=4, h=2, c=64)[:, :, :, 0:64]
                E.op("vector", lambda e, dst=dst, src=src: e.tensor_copy(out=dst, in_=src), inc="vV")
                vp_prev = E.cnt["vV"]
                yield
            state["last_proj_t"] = E.cnt["tT"]
            state["vp_ready"][par] = E.cnt["vV"]

        def run_gen(g):
            if g is not None:
                for _ in g:
                    pass

        run_gen(gen_proj(0))

        for p in range(NPAIR):
            par = p % 2
            QT, KT, Vp = QTb[par], KTb[par], Vpb[par]
            bias = biasAB[:, par * 2:par * 2 + 2]
            gen = gen_proj(p + 1) if p + 1 < NPAIR else None

            import os as _os
            if _os.environ.get("KDBG"):
                print(f"pair {p}: vp_ready={state['vp_ready']}, bias={state['bias_ready']}, vV={E.cnt['vV']}")
            E.w("tensor", "vV", state["vp_ready"][par])
            GRP = [(2 * i, 2) for i in range(16)]
            exp_done = dict()
            av_done = dict()
            G = 0
            v_oa_free = state.get("prev_oa", 0)
            v_norm_prev = state.get("prev_norm", 0)
            pending = []

            def pop_pending():
                nonlocal v_oa_free, v_norm_prev
                fn = pending.pop(0)
                r = fn()
                if fn.kind == "tail":
                    v_oa_free = r
                else:
                    v_norm_prev = r
            gen_budget = 0.0
            first_qhip = True
            for q in range(4):
                for hip in range(2):
                    g0 = G
                    gl = g0 + len(GRP) - 1
                    for gi, (base, ln) in enumerate(GRP):
                        g = g0 + gi
                        bank = SD[g % 2]
                        if g >= 2:
                            E.w("tensor", "sS", exp_done[g - 2])
                        elif state["prev_exp"][g % 2]:
                            E.w("tensor", "sS", state["prev_exp"][g % 2])
                        for k in range(ln):
                            ci = base + k
                            E.op("tensor", lambda e, ci=ci, k=k, q=q, hip=hip, bank=bank, KT=KT, QT=QT: e.matmul(
                                out=bank[:, k * 512:(k + 1) * 512],
                                lhsT=KT[hip * 64:(hip + 1) * 64, ci * 128:(ci + 1) * 128],
                                rhs=QT[hip * 64:(hip + 1) * 64, q * 512:(q + 1) * 512],
                                start=True, stop=True), inc="tT" if k == ln - 1 else None)
                        t_sc = E.cnt["tT"]
                        E.w("scalar", "tT", t_sc)
                        if g >= 2:
                            E.w("scalar", "tT", av_done[g - 2])
                        elif state["prev_av"][g % 2]:
                            E.w("scalar", "tT", state["prev_av"][g % 2])
                        if first_qhip and gi == 0:
                            E.w("scalar", "vV", state["bias_ready"][par])
                        E.op("scalar", lambda e, ln=ln, g=g, bank=bank, bias=bias, hip=hip: e.activation(
                            out=stag[g % 2][:, 0:ln * 512], in_=bank[:, 0:ln * 512], func=AF.Exp,
                            bias=bias[:, hip:hip + 1], scale=SCALE), inc="sS")
                        exp_done[g] = E.cnt["sS"]
                        if gi in (0, 2) and pending:
                            pop_pending()
                        gen_budget += 34.0 / 128.0
                        while gen is not None and gen_budget >= 1.0:
                            gen_budget -= 1.0
                            try:
                                next(gen)
                            except StopIteration:
                                gen = None
                        if gi >= 1:
                            gp = g - 1
                            E.w("tensor", "sS", exp_done[gp])
                            if gi == 1:
                                E.w("tensor", "vV", v_oa_free)
                                E.w("tensor", "vV", v_norm_prev)
                            bb, bl = GRP[gi - 1]
                            for k in range(bl):
                                ci = bb + k
                                E.op("tensor", lambda e, ci=ci, k=k, gp=gp, hip=hip, Vp=Vp,
                                     st_=(gp == g0 and k == 0): e.matmul(
                                    out=OA[0:65, :], lhsT=Vp[:, ci * 130 + hip * 65: ci * 130 + hip * 65 + 65],
                                    rhs=stag[gp % 2][:, k * 512:(k + 1) * 512],
                                    start=st_, stop=False), inc="tT" if k == bl - 1 else None)
                            av_done[gp] = E.cnt["tT"]
                    G = gl + 1
                    first_qhip = False

                    def _tail(gl=gl, hip=hip, Vp=Vp,
                              spill_wait=(p >= 1 and q == 0 and hip == 0)):
                        E.w("tensor", "sS", exp_done[gl])
                        bb, bl = GRP[-1]
                        for k in range(bl):
                            ci = bb + k
                            E.op("tensor", lambda e, ci=ci, k=k, gl=gl, hip=hip, Vp=Vp,
                                 sp_=(k == bl - 1): e.matmul(
                                out=OA[0:65, :], lhsT=Vp[:, ci * 130 + hip * 65: ci * 130 + hip * 65 + 65],
                                rhs=stag[gl % 2][:, k * 512:(k + 1) * 512],
                                start=False, stop=sp_), inc="tT" if k == bl - 1 else None)
                        av_done[gl] = E.cnt["tT"]
                        E.w("vector", "tT", av_done[gl])
                        if spill_wait:
                            E.wt("vector", tok_spill[p - 1])
                        E.opd("vector", lambda e: e.reciprocal(out=rbc[0:1, :], in_=OA[64:65, :]))
                        E.opd("vector", lambda e: e.tensor_copy(out=rbch[0:1, :], in_=rbc[0:1, :]))
                        E.opd("vector", lambda e: e.scalar_tensor_tensor(out=rbcl[0:1, :], in0=rbch[0:1, :], scalar=-1.0, in1=rbc[0:1, :], op0=ALU.mult, op1=ALU.add))
                        E.opd("vector", lambda e: e.tensor_copy(out=oaev[0:64, :], in_=OA[0:64, :]), inc="vV")
                        return E.cnt["vV"]
                    _tail.kind = "tail"
                    pending.append(_tail)

                    def _norm(q=q, hip=hip, first=(p == 0 and q == 0 and hip == 0)):
                        E.w("tensor", "vV", v_oa_free)
                        if first:
                            E.wt("tensor", toks["onescolb"])
                        E.op("tensor", lambda e: e.matmul(out=MX[0:64, :], lhsT=onescolb_sb[0:1, 0:64], rhs=rbch[0:1, :], start=True, stop=False))
                        E.op("tensor", lambda e: e.matmul(out=MX[0:64, :], lhsT=onescolb_sb[0:1, 0:64], rhs=rbcl[0:1, :], start=False, stop=True), inc="tT")
                        t_bc = E.cnt["tT"]
                        E.w("vector", "tT", t_bc)
                        E.opd("vector", lambda e, q=q, hip=hip: e.tensor_mul(
                            out=aopair[hip * 64:(hip + 1) * 64, q * 512:(q + 1) * 512],
                            in0=oaev[0:64, :], in1=MX[0:64, :]), inc="vV")
                        return E.cnt["vV"]
                    _norm.kind = "norm"
                    pending.append(_norm)
            while pending:
                pop_pending()
            run_gen(gen)
            state["prev_oa"] = v_oa_free
            state["prev_norm"] = v_norm_prev
            state["prev_exp"] = [exp_done[126], exp_done[127]]
            state["prev_av"] = [av_done[126], av_done[127]]
            v_ao = E.cnt["vV"]
            E.w("gpsimd", "vV", v_ao)
            tok_spill[p] = PG.dma("gpsimd", lambda e, p=p: e.dma_start(out=aoT_d[p], in_=aopair[:]))

        p2_v = E.cnt["vV"]; p2_t = E.cnt["tT"]; p2_s = E.cnt["sS"]

        E.w("sync", "vV", p2_v)
        E.wt("sync", tok_spill[NPAIR - 2])
        E.wt("sync", tok_spill[NPAIR - 1])
        v_prev = 0
        for h in range(2):
            if h == 1:
                E.w("sync", "vV", v_prev)
            tok_wo = PL.dma("sync", lambda e, h=h: e.dma_start(out=rstg[:], in_=wout_d[:, h * 4096:(h + 1) * 4096]))
            E.wt("vector", tok_wo)
            if h == 0:
                E.w("vector", "tT", p2_t)
            E.op("vector", lambda e, h=h: e.tensor_copy(out=woutb[:, h * 4096:(h + 1) * 4096], in_=rstg[:]), inc="vV")
            v_prev = E.cnt["vV"]
        v_qq = 0
        for h in range(2):
            E.w("sync", "vV", v_prev if h == 0 else v_qq)
            tok_rh = PL.dma("sync", lambda e, h=h: e.dma_start(out=rstg[:], in_=rhsT_d[:, h * 4096:(h + 1) * 4096]))
            for c4 in range(4):
                cc = h * 4 + c4
                E.wt("vector", tok_rh)
                E.op("vector", lambda e, c4=c4: e.reduce_max(out=st_amax, in_=rstg[:, c4 * 1024:(c4 + 1) * 1024], axis=AX.X, apply_absolute_value=True))
                E.op("vector", lambda e: e.drain())
                E.opd("vector", lambda e: e.tensor_scalar_max(out=st_amax, in0=st_amax, scalar1=1e-12))
                E.opd("vector", lambda e, cc=cc: e.tensor_scalar_mul(out=ls16[:, cc:cc + 1], in0=st_amax, scalar1=1.0 / INT8_MAX))
                E.opd("vector", lambda e, cc=cc: e.reciprocal(out=st_lrcp, in_=ls16[:, cc:cc + 1]))
                E.opd("vector", lambda e, c4=c4: e.tensor_scalar(out=p3t[:], in0=rstg[:, c4 * 1024:(c4 + 1) * 1024], scalar1=st_lrcp, scalar2=MAGIC, op0=ALU.mult, op1=ALU.add))
                E.op("vector", lambda e: e.tensor_scalar(out=qtstg[:], in0=p3t[:], scalar1=MAGIC, scalar2=None, op0=ALU.subtract), inc="vV")
                v_q8 = E.cnt["vV"]
                E.w("tensor", "vV", v_q8)
                if cc == 0:
                    E.wt("tensor", toks["identf"])
                for fc in range(8):
                    E.op("tensor", lambda e, fc=fc: e.transpose(
                        out=SDB[:, fc * 128:(fc + 1) * 128], in_=qtstg[:, fc * 128:(fc + 1) * 128],
                        identity=ident_sb[:]), inc="tT" if fc == 7 else None)
                t_qb = E.cnt["tT"]
                E.w("vector", "tT", t_qb)
                dst = qrhs.rearrange("p (f c) -> p f c", f=8)[:, :, cc * 128:(cc + 1) * 128]
                src = SDB.rearrange("p (f c) -> p f c", f=8)
                E.op("vector", lambda e, dst=dst, src=src: e.tensor_copy(out=dst, in_=src), inc="vV")
                v_qq = E.cnt["vV"]
        v_qrhs = E.cnt["vV"]
        E.opd("vector", lambda e: e.tensor_copy(out=rsv[:, 0:8], in_=ls16[:, 0:8]), inc="vV")
        v_rsv = E.cnt["vV"]
        E.w("tensor", "vV", max(v_qrhs, v_rsv))
        E.op("tensor", lambda e: e.transpose(out=MX[0:8, 0:128], in_=rsv[:], identity=identf_sb[:]), inc="tT")
        t_rs = E.cnt["tT"]
        E.w("vector", "tT", t_rs)
        E.op("vector", lambda e: e.tensor_copy(out=rsT_sb[:], in_=MX[0:8, 0:128]), inc="vV")
        v_rsT = E.cnt["vV"]
        E.w("gpsimd", "vV", v_rsT)
        tok_rr = PG.dma("gpsimd", lambda e: e.dma_start(out=rsrow, in_=rsT_sb[:]))
        E.wt("tensor", tok_rr)
        E.op("tensor", lambda e: e.matmul(out=SD0[:, 0:512], lhsT=onescol_sb[:], rhs=rsrow[0:1, 0:512], start=True, stop=True))
        E.op("tensor", lambda e: e.matmul(out=SD0[:, 512:1024], lhsT=onescol_sb[:], rhs=rsrow[0:1, 512:1024], start=True, stop=True), inc="tT")
        t_rsb = E.cnt["tT"]
        E.w("vector", "tT", t_rsb)
        E.op("vector", lambda e: e.tensor_copy(out=rsb[:, 0:512], in_=SD0[:, 0:512]))
        E.opd("vector", lambda e: e.tensor_copy(out=rsb[:, 512:1024], in_=SD0[:, 512:1024]), inc="vV")
        v_rsb = E.cnt["vV"]
        tok_bo = PL.dma("sync", lambda e: e.dma_start(out=bouts[:], in_=bout_d[:]))
        aoT_src = aoT_d.rearrange("p q c -> q p c")
        aoT_dst = aoTs.rearrange("q (p c) -> q p c", p=NPAIR)
        tok_ao = PL.dma("sync", lambda e, s=aoT_src, d=aoT_dst: e.dma_start(out=d, in_=s))

        v_a = [0] * 16
        for t in range(16):
            if t >= 2:
                E.w("sync", "vV", v_a[t - 2])
            tok_x = PL.dma("sync", lambda e, t=t: e.dma_start(out=xt[t % 2][:], in_=xin[t * 128:(t + 1) * 128, :]))
            E.wt("scalar", tok_x)
            E.op("scalar", lambda e, t=t: e.activation(out=tr_bf[:], in_=xt[t % 2][:], func=AF.Square, accum_out=st_ssq), inc="sS")
            s_sq = E.cnt["sS"]
            E.wt("vector", tok_x)
            E.w("vector", "sS", s_sq)
            if t == 0:
                E.w("vector", "vV", v_rsb)
            emit_ln(xt[t % 2], sbf[:])
            E.op("vector", lambda e: e.reduce_max(out=st_amax, in_=sbf[:], axis=AX.X, apply_absolute_value=True))
            E.op("vector", lambda e: e.drain())
            E.opd("vector", lambda e: e.tensor_scalar_max(out=st_amax, in0=st_amax, scalar1=1e-12))
            E.opd("vector", lambda e, t=t: e.tensor_scalar_mul(out=ls16[:, t:t + 1], in0=st_amax, scalar1=1.0 / INT8_MAX))
            E.opd("vector", lambda e, t=t: e.reciprocal(out=st_lrcp, in_=ls16[:, t:t + 1]))
            E.opd("vector", lambda e: e.tensor_scalar(out=p3t[:], in0=sbf[:], scalar1=st_lrcp, scalar2=MAGIC, op0=ALU.mult, op1=ALU.add))
            E.op("vector", lambda e, t=t: e.tensor_scalar(out=qlhs16[:, t * 1024:(t + 1) * 1024], in0=p3t[:], scalar1=MAGIC, scalar2=None, op0=ALU.subtract), inc="vV")
            v_a[t] = E.cnt["vV"]
        v_a_all = E.cnt["vV"]

        st_tok = [None, None]
        v_b = [0] * 16
        for t in range(16):
            if t == 0:
                E.w("sync", "vV", v_a_all)
            if t >= 2:
                E.w("sync", "vV", v_b[t - 2])
            tok_x = PL.dma("sync", lambda e, t=t: e.dma_start(out=xt[t % 2][:], in_=xin[t * 128:(t + 1) * 128, :]))
            if t == 0:
                E.w("tensor", "vV", v_a_all)
                E.wt("tensor", tok_ao)
            else:
                E.w("tensor", "vV", v_b[t - 1])
            for c in range(8):
                E.op("tensor", lambda e, c=c, t=t: e.transpose(out=SDB[:, c * 128:(c + 1) * 128], in_=qlhs16[:, t * 1024 + c * 128: t * 1024 + (c + 1) * 128], identity=ident_sb[:]), inc="tT" if c == 7 else None)
            t_ql = E.cnt["tT"]
            E.w("vector", "tT", t_ql)
            E.op("vector", lambda e: e.tensor_copy(out=qlT[:], in_=SDB[:, 0:1024]), inc="vV")
            v_qlT = E.cnt["vV"]

            E.w("tensor", "vV", v_qlT)
            for c in range(8):
                for hh in range(2):
                    E.op("tensor", lambda e, c=c, hh=hh: e.matmul(
                        out=SD0[:, hh * 512:(hh + 1) * 512], lhsT=qlT[:, c * 128:(c + 1) * 128],
                        rhs=qrhs[:, c * 1024 + hh * 512: c * 1024 + (hh + 1) * 512],
                        start=(c == 0), stop=(c == 7)), inc="tT" if (c == 7 and hh == 1) else None)
            t_ffn = E.cnt["tT"]
            for pp in range(8):
                for hh in range(2):
                    E.op("tensor", lambda e, pp=pp, hh=hh, t=t: e.matmul(
                        out=SD1[:, hh * 512:(hh + 1) * 512],
                        lhsT=aoTs[:, pp * 2048 + t * 128: pp * 2048 + (t + 1) * 128],
                        rhs=woutb[:, pp * 1024 + hh * 512: pp * 1024 + (hh + 1) * 512],
                        start=(pp == 0), stop=(pp == 7)), inc="tT" if (pp == 7 and hh == 1) else None)
            t_op = E.cnt["tT"]

            E.w("vector", "tT", t_ffn)
            E.opd("vector", lambda e: e.tensor_mul(out=p3t[:], in0=SD0[:, 0:1024], in1=rsb[:]), inc="vV")
            v_g1 = E.cnt["vV"]
            E.w("scalar", "vV", v_g1)
            E.op("scalar", lambda e, t=t: e.activation(out=gbuf[:], in_=p3t[:], func=AF.Silu, scale=ls16[:, t:t + 1]), inc="sS")
            s_g = E.cnt["sS"]
            E.w("vector", "tT", t_op)
            E.wt("vector", tok_x)
            E.opd("vector", lambda e: e.tensor_add(out=sbf[:], in0=SD1[:, 0:1024], in1=bouts[:]))
            E.w("vector", "sS", s_g)
            if t >= 2:
                E.wt("vector", st_tok[t % 2])
            E.opd("vector", lambda e: e.tensor_mul(out=sbf[:], in0=sbf[:], in1=gbuf[:]))
            E.op("vector", lambda e, t=t: e.tensor_add(out=outb[t % 2][:], in0=sbf[:], in1=xt[t % 2][:]), inc="vV")
            v_b[t] = E.cnt["vV"]
            E.w("sync", "vV", v_b[t])
            st_tok[t % 2] = PS.dma("sync", lambda e, t=t: e.dma_start(out=out_d[t * 128:(t + 1) * 128, :], in_=outb[t % 2][:]))

        E.w("sync", "st0", E.cnt["st0"])
        E.w("sync", "st1", E.cnt["st1"])

        @block.sync
        def _(e):
            for fn in E.ops["sync"]:
                fn(e)

        @block.tensor
        def _(e):
            for fn in E.ops["tensor"]:
                fn(e)

        @block.vector
        def _(e):
            for fn in E.ops["vector"]:
                fn(e)

        @block.scalar
        def _(e):
            for fn in E.ops["scalar"]:
                fn(e)

        @block.gpsimd
        def _(e):
            for fn in E.ops["gpsimd"]:
                fn(e)

    return nc



def _prep_shared(w_qkv, w_out, b_out, rhs):
    wp = np.empty((NPAIR, 128, 3072), dtype=np.float32)
    for p in range(NPAIR):
        wq = w_qkv[:, p * 128:(p + 1) * 128]
        wk = w_qkv[:, 1024 + p * 128:1024 + (p + 1) * 128]
        wv = w_qkv[:, 2048 + p * 128:2048 + (p + 1) * 128]
        cat = np.concatenate([wq, wk, wv], axis=1)
        wp[p] = np.ascontiguousarray(cat.reshape(8, 128, 384).transpose(1, 0, 2).reshape(128, 3072))
    o2 = np.zeros((128, 2), dtype=np.float32)
    o2[0:64, 0] = 1.0
    o2[64:128, 1] = 1.0
    rhsT = np.ascontiguousarray(rhs.T)
    consts = {
        "wout": np.ascontiguousarray(w_out.reshape(8, 128, 1024).transpose(1, 0, 2).reshape(128, 8192)).astype(np.float32),
        "rhsT": np.ascontiguousarray(rhsT.reshape(8, 128, 1024).transpose(1, 0, 2).reshape(128, 8192)).astype(np.float32),
        "bout": np.ascontiguousarray(np.broadcast_to(b_out[None, :], (128, 1024))).astype(np.float32),
        "ident": np.eye(128, dtype=np.float32).astype(bfloat16),
        "identf": np.eye(128, dtype=np.float32),
        "ident2": np.eye(2, dtype=np.float32).astype(bfloat16),
        "onescol": np.ones((1, 128), dtype=np.float32),
        "onescolb": np.ones((1, 128), dtype=np.float32).astype(bfloat16),
        "ones2": o2.astype(bfloat16),
    }
    return wp, consts


def make_in_maps(x, w_qkv, w_out, b_out, rhs):
    wp, consts = _prep_shared(w_qkv, w_out, b_out, rhs)
    tab64 = _host_tables()
    in_maps = []
    for core in range(8):
        b, half = core // 2, core % 2
        perm = np.concatenate([np.arange(half * NQ, (half + 1) * NQ),
                               np.arange((1 - half) * NQ, (2 - half) * NQ)])
        xp = np.ascontiguousarray(x[b][perm])
        tabs = np.ascontiguousarray(
            tab64.reshape(128, 2, N)[:, :, perm].reshape(128, 2 * N)).astype(bfloat16)
        m = {"xin": xp, "wpair": wp, "tabs": tabs}
        m.update(consts)
        in_maps.append(m)
    return in_maps


def kernel(L0_attn_x, L0_attn_w_qkv, L0_attn_w_out, L0_attn_b_out, L0_ffn_rhs):
    x = np.asarray(L0_attn_x, dtype=np.float32)
    in_maps = make_in_maps(x,
                           np.asarray(L0_attn_w_qkv, dtype=np.float32),
                           np.asarray(L0_attn_w_out, dtype=np.float32),
                           np.asarray(L0_attn_b_out, dtype=np.float32),
                           np.asarray(L0_ffn_rhs, dtype=np.float32))
    nc = build()
    res = run_bass_kernel_spmd(nc, in_maps, core_ids=list(range(8)))
    out = np.empty((B, N, D), dtype=np.float32)
    for core in range(8):
        b, half = core // 2, core % 2
        out[b, half * NQ:(half + 1) * NQ] = res.results[core]["out"]
    return out


# version 13
# speedup vs baseline: 1.4256x; 1.0458x over previous
import numpy as np
import ml_dtypes
import concourse.bass as bass
import concourse.mybir as mybir
from concourse import library_config
from concourse.bass_utils import run_bass_kernel_spmd

F32 = mybir.dt.float32
BF16 = mybir.dt.bfloat16
AF = mybir.ActivationFunctionType
ALU = mybir.AluOpType
AX = mybir.AxisListType
bfloat16 = ml_dtypes.bfloat16

B, N, D = 4, 4096, 1024
HEADS, DH, HALF = 16, 64, 32
NQ = 2048
NPAIR = 8
SCALE = float(DH) ** -0.5
MAGIC = 12582912.0
EPS = 1e-5
INT8_MAX = 127.0


def _host_tables():
    j = np.arange(HALF, dtype=np.float64)
    theta = 10000.0 ** (-j / HALF)
    pos = np.arange(N, dtype=np.float64)
    ang = np.outer(theta, pos)
    cos = np.cos(ang)
    sin = np.sin(ang)
    CCt = np.concatenate([cos, cos, cos, cos], axis=0)
    SSt = np.concatenate([-sin, sin, -sin, sin], axis=0)
    return np.concatenate([CCt, SSt], axis=1)


class _Emit:
    ENGS = ("sync", "tensor", "vector", "scalar", "gpsimd")

    def __init__(self, sems):
        self.ops = {e: [] for e in self.ENGS}
        self.cnt = {k: 0 for k in sems}
        self.sems = sems

    def wt(self, eng, tok):
        if tok is not None:
            self.w(eng, tok[0], tok[1])

    def w(self, eng, sem, val):
        if val <= 0:
            return
        assert val <= self.cnt[sem], f"wait {sem}>={val} emitted={self.cnt[sem]}"
        s = self.sems[sem]
        self.ops[eng].append(lambda e, s=s, v=val: e.wait_ge(s, v))

    def op(self, eng, fn, inc=None, n=1):
        if inc is None:
            self.ops[eng].append(fn)
            return None
        s = self.sems[inc]
        self.cnt[inc] += n
        self.ops[eng].append(lambda e, fn=fn, s=s, n=n: fn(e).then_inc(s, n))
        return self.cnt[inc]

    def opd(self, eng, fn, inc=None, n=1):
        r = self.op(eng, fn, inc=inc, n=n)
        self.ops[eng].append(lambda e: e.drain())
        return r


class _DPool:

    def __init__(self, E, names):
        self.E = E
        self.names = names
        self.i = 0

    def dma(self, eng, fn):
        s = self.names[self.i % len(self.names)]
        self.i += 1
        self.E.w(eng, s, self.E.cnt[s])
        v = self.E.op(eng, fn, inc=s, n=16)
        return (s, v)


def build(debug=False):
    nc = bass.Bass("TRN2", target_bir_lowering=False)

    xin = nc.dram_tensor("xin", [N, D], F32, kind="ExternalInput").ap()
    wpair = nc.dram_tensor("wpair", [NPAIR, 128, 3072], F32, kind="ExternalInput").ap()
    wout_d = nc.dram_tensor("wout", [128, 8192], F32, kind="ExternalInput").ap()
    rhsT_d = nc.dram_tensor("rhsT", [128, 8192], F32, kind="ExternalInput").ap()
    bout_d = nc.dram_tensor("bout", [128, 1024], F32, kind="ExternalInput").ap()
    tabs_d = nc.dram_tensor("tabs", [128, 8192], BF16, kind="ExternalInput").ap()
    ident_d = nc.dram_tensor("ident", [128, 128], BF16, kind="ExternalInput").ap()
    identf_d = nc.dram_tensor("identf", [128, 128], F32, kind="ExternalInput").ap()
    ones2_d = nc.dram_tensor("ones2", [128, 2], BF16, kind="ExternalInput").ap()
    ident2_d = nc.dram_tensor("ident2", [2, 2], BF16, kind="ExternalInput").ap()
    onescol_d = nc.dram_tensor("onescol", [1, 128], F32, kind="ExternalInput").ap()
    onescolb_d = nc.dram_tensor("onescolb", [1, 128], BF16, kind="ExternalInput").ap()
    out_d = nc.dram_tensor("out", [NQ, D], F32, kind="ExternalOutput").ap()
    aoT_d = nc.dram_tensor("aoT", [NPAIR, 128, NQ], BF16, kind="Internal").ap()

    from contextlib import ExitStack
    with ExitStack() as _stk:
        en = _stk.enter_context
        ident_sb = en(nc.sbuf_tensor([128, 128], BF16))
        identf_sb = en(nc.sbuf_tensor([128, 128], F32))
        ones2_sb = en(nc.sbuf_tensor([128, 2], BF16))
        ident2_sb = en(nc.sbuf_tensor([2, 2], BF16))
        onescol_sb = en(nc.sbuf_tensor([1, 128], F32))
        zc2_sb = en(nc.sbuf_tensor([2, 128], BF16))
        xt0 = en(nc.sbuf_tensor([128, 1024], F32))
        xt1 = en(nc.sbuf_tensor([128, 1024], F32))
        st_sb = en(nc.sbuf_tensor([128, 16], F32))
        nm_sb = en(nc.sbuf_tensor([2, 8], F32))
        biasAB = en(nc.sbuf_tensor([128, 4], F32))
        rbc = en(nc.sbuf_tensor([1, 512], F32))
        rbch = en(nc.sbuf_tensor([1, 512], BF16))
        rbcl = en(nc.sbuf_tensor([1, 512], BF16))
        onescolb_sb = en(nc.sbuf_tensor([1, 128], BF16))
        rsT_sb = en(nc.sbuf_tensor([8, 128], F32))
        ls16 = en(nc.sbuf_tensor([128, 16], F32))
        abf = en(nc.sbuf_tensor([128, 76928], BF16))
        af32 = en(nc.sbuf_tensor([128, 10272], F32))
        SD0 = en(nc.psum_tensor([128, 1536], F32))
        SD1 = en(nc.psum_tensor([128, 1024], F32))
        SDB = en(nc.psum_tensor([128, 1024], BF16))
        OA = en(nc.psum_tensor([128, 512], F32))
        MX = en(nc.psum_tensor([128, 512], F32))
        SD = (SD0, SD1)
        sems = {}
        for name in (["ld%d" % i for i in range(8)] + ["st0", "st1", "dg0", "dg1",
                                                       "tT", "vV", "sS", "gG"]):
            sems[name] = en(nc.semaphore(name="sem_" + name))
        block = en(nc.Block())
        E = _Emit(sems)
        PL = _DPool(E, ["ld%d" % i for i in range(8)])
        PS = _DPool(E, ["st0", "st1"])
        PG = _DPool(E, ["dg0", "dg1"])

        xnT = abf[:, 0:32768]
        tabs = abf[:, 32768:40960]
        CC = tabs[:, 0:4096]
        SS = tabs[:, 4096:8192]
        wbf = abf[:, 40960:44032]
        QTb = (abf[:, 44032:46080], abf[:, 46080:48128])
        KTb = (abf[:, 48128:52224], abf[:, 52224:56320])
        Vpb = (abf[:, 56320:60480], abf[:, 60480:64640])
        Tstg = abf[:, 64640:68736]
        Tsw = abf[:, 68736:72832]
        stag = (abf[:, 72832:73856], abf[:, 73856:74880])
        aopair = abf[:, 74880:76928]
        xnb = Tstg[:, 0:1024]
        aoTs = abf[:, 0:16384]
        woutb = abf[:, 16384:24576]
        qrhs = abf[:, 24576:32768]
        qlhs16 = abf[:, 44032:60416]
        qlT = tabs[:, 0:1024]
        qtstg = tabs[:, 1024:2048]
        tr_bf = tabs[:, 2048:3072]

        wstg = af32[:, 0:3072]
        oaev = af32[:, 3072:3584]
        rstg = af32[:, 0:4096]
        trash = af32[:, 4096:5120]
        rsb = af32[:, 5120:6144]
        bouts = af32[:, 6144:7168]
        gbuf = af32[:, 7168:8192]
        sbf = af32[:, 8192:9216]
        p3t = af32[:, 9216:10240]
        rsv = af32[:, 10240:10248]
        rcv = af32[:, 10248:10256]
        rsrow = p3t[0:1, 0:1024]
        outb = (rstg[:, 0:1024], rstg[:, 1024:2048])

        st_sum = st_sb[:, 0:1]; st_ssq = st_sb[:, 1:2]; st_mean = st_sb[:, 2:3]
        st_ex2 = st_sb[:, 3:4]; st_m2 = st_sb[:, 4:5]; st_var = st_sb[:, 5:6]
        st_vep = st_sb[:, 6:7]; st_s0 = st_sb[:, 7:8]; st_r0 = st_sb[:, 8:9]
        st_a = st_sb[:, 9:10]; st_b = st_sb[:, 10:11]; st_c = st_sb[:, 11:12]
        st_rstd = st_sb[:, 12:13]; st_amax = st_sb[:, 13:14]
        st_lrcp = st_sb[:, 15:16]

        toks = {}
        for name, dst, src in (("ident", ident_sb, ident_d), ("identf", identf_sb, identf_d),
                               ("ones2", ones2_sb, ones2_d), ("ident2", ident2_sb, ident2_d),
                               ("onescol", onescol_sb, onescol_d), ("onescolb", onescolb_sb, onescolb_d), ("tabs", tabs, tabs_d)):
            toks[name] = PL.dma("sync", lambda e, d=dst, s=src: e.dma_start(out=d[:], in_=s[:]))

        E.op("vector", lambda e: e.memset(zc2_sb[:], 0.0))
        for vb in Vpb:
            vo = vb.rearrange("p (t h c) -> p t h c", t=32, h=2)[:, :, :, 64:65]
            E.op("vector", lambda e, vo=vo: e.memset(vo, 1.0))
        E.op("vector", lambda e: e.nop(), inc="vV")

        xt = (xt0, xt1)
        v_tile_done = [0] * 33
        t_tr_done = [0] * 33
        first_tr = True

        def emit_ln(buf, out_ap):
            E.op("vector", lambda e, b=buf: e.reduce_sum(out=st_sum, in_=b[:], axis=AX.X))
            E.op("vector", lambda e: e.drain())
            E.opd("vector", lambda e: e.tensor_scalar_mul(out=st_mean, in0=st_sum, scalar1=1.0 / D))
            E.opd("vector", lambda e: e.tensor_scalar_mul(out=st_ex2, in0=st_ssq, scalar1=1.0 / D))
            E.opd("vector", lambda e: e.tensor_mul(out=st_m2, in0=st_mean, in1=st_mean))
            E.opd("vector", lambda e: e.tensor_sub(out=st_var, in0=st_ex2, in1=st_m2))
            E.opd("vector", lambda e: e.tensor_scalar_add(out=st_vep, in0=st_var, scalar1=EPS), inc="vV")
            v_vep = E.cnt["vV"]
            E.w("scalar", "vV", v_vep)
            E.opd("scalar", lambda e: e.activation(out=st_s0, in_=st_vep, func=AF.Ln))
            E.op("scalar", lambda e: e.activation(out=st_r0, in_=st_s0, func=AF.Exp, scale=-0.5), inc="sS")
            s_r0 = E.cnt["sS"]
            E.w("vector", "sS", s_r0)
            E.opd("vector", lambda e: e.tensor_mul(out=st_a, in0=st_r0, in1=st_r0))
            E.opd("vector", lambda e: e.tensor_mul(out=st_b, in0=st_a, in1=st_vep))
            E.opd("vector", lambda e: e.tensor_scalar(out=st_c, in0=st_b, scalar1=-0.5, scalar2=1.5, op0=ALU.mult, op1=ALU.add))
            E.opd("vector", lambda e: e.tensor_mul(out=st_rstd, in0=st_r0, in1=st_c))
            E.opd("vector", lambda e, o=out_ap, b=buf: e.tensor_scalar(out=o, in0=b[:], scalar1=st_mean, scalar2=st_rstd, op0=ALU.subtract, op1=ALU.mult), inc="vV")
            return E.cnt["vV"]

        for i in range(32):
            if i >= 2:
                E.w("sync", "vV", v_tile_done[i - 2])
            tok_x = PL.dma("sync", lambda e, i=i: e.dma_start(out=xt[i % 2][:], in_=xin[i * 128:(i + 1) * 128, :]))
            E.wt("scalar", tok_x)
            E.op("scalar", lambda e, i=i: e.activation(out=trash[:], in_=xt[i % 2][:], func=AF.Square, accum_out=st_ssq), inc="sS")
            s_sq = E.cnt["sS"]
            E.wt("vector", tok_x)
            E.w("vector", "sS", s_sq)
            if i >= 1:
                E.w("vector", "tT", t_tr_done[i - 1])
            v_xn = emit_ln(xt[i % 2], xnb[:])
            E.w("tensor", "vV", v_xn)
            if first_tr:
                E.wt("tensor", toks["ident"])
                first_tr = False
            for c in range(8):
                E.op("tensor", lambda e, c=c: e.transpose(out=SDB[:, c * 128:(c + 1) * 128], in_=xnb[:, c * 128:(c + 1) * 128], identity=ident_sb[:]), inc="tT" if c == 7 else None)
            t_tr_done[i] = E.cnt["tT"]
            E.w("vector", "tT", t_tr_done[i])
            dst = xnT.rearrange("p (c t) -> p c t", c=8)[:, :, i * 128:(i + 1) * 128]
            E.op("vector", lambda e, dst=dst: e.tensor_copy(out=dst, in_=SDB[:, 0:1024]), inc="vV")
            v_tile_done[i] = E.cnt["vV"]
        v_xnT_all = E.cnt["vV"]

        state = {"wconv": 0, "last_proj_t": 0,
                 "vp_ready": [0, 0], "bias_ready": [0, 0],
                 "prev_exp": [0, 0], "prev_av": [0, 0]}
        tok_spill = [None] * NPAIR

        def gen_proj(p):
            par = p % 2
            QT, KT, Vp = QTb[par], KTb[par], Vpb[par]
            E.w("sync", "vV", state["wconv"])
            tok_w = PL.dma("sync", lambda e, p=p: e.dma_start(out=wstg[:], in_=wpair[p]))
            E.wt("vector", tok_w)
            E.w("vector", "tT", state["last_proj_t"])
            if p == 0:
                E.wt("vector", toks["tabs"])
            E.op("vector", lambda e: e.tensor_copy(out=wbf[:], in_=wstg[:]), inc="vV")
            state["wconv"] = E.cnt["vV"]
            wconv = state["wconv"]
            yield

            evac_marks = {}

            def proj_slices(sel_off, nslices):
                for s in range(nslices):
                    E.w("tensor", "vV", max(v_xnT_all, wconv))
                    if (sel_off, s - 1) in evac_marks:
                        E.w("tensor", "vV", evac_marks[(sel_off, s - 1)])
                    for c in range(8):
                        lhsT = wbf[:, c * 384 + sel_off: c * 384 + sel_off + 128]
                        rhs = xnT[:, c * 4096 + s * 512: c * 4096 + (s + 1) * 512]
                        E.op("tensor", lambda e, l=lhsT, r=rhs, c=c:
                             e.matmul(out=SD0[:, 1024:1536], lhsT=l, rhs=r, start=(c == 0), stop=(c == 7)),
                             inc="tT" if c == 7 else None)
                    t_mm = E.cnt["tT"]
                    E.w("vector", "tT", t_mm)
                    E.opd("vector", lambda e, s=s:
                          e.tensor_copy(out=Tstg[:, s * 512:(s + 1) * 512], in_=SD0[:, 1024:1536]), inc="vV")
                    evac_marks[(sel_off, s)] = E.cnt["vV"]
                    yield

            def rope(n, dst):
                for sb, db in ((32, 0), (0, 32), (96, 64), (64, 96)):
                    E.op("vector", lambda e, n=n, sb=sb, db=db: e.tensor_copy(out=Tsw[db:db + 32, 0:n], in_=Tstg[sb:sb + 32, 0:n]))
                E.op("vector", lambda e: e.drain())
                E.op("vector", lambda e, n=n: e.tensor_mul(out=Tsw[:, 0:n], in0=Tsw[:, 0:n], in1=SS[:, 0:n]))
                E.opd("vector", lambda e, n=n: e.tensor_mul(out=Tstg[:, 0:n], in0=Tstg[:, 0:n], in1=CC[:, 0:n]))
                E.opd("vector", lambda e, n=n, dst=dst: e.tensor_add(out=dst, in0=Tstg[:, 0:n], in1=Tsw[:, 0:n]), inc="vV")

            def norms_sq(src, ncols):
                E.op("vector", lambda e, src=src, n=ncols: e.tensor_mul(out=Tsw[:, 0:n], in0=src, in1=src), inc="vV")
                return E.cnt["vV"]

            def norms_mm(v_sq, ncols, out_col):
                E.w("tensor", "vV", v_sq)
                nsl = ncols // 512
                for s in range(nsl):
                    E.op("tensor", lambda e, s=s, nsl=nsl: e.matmul(
                        out=MX[0:2, :], lhsT=ones2_sb[:],
                        rhs=Tsw[:, s * 512:(s + 1) * 512], start=(s == 0), stop=(s == nsl - 1)),
                        inc="tT" if s == nsl - 1 else None)
                t_nm = E.cnt["tT"]
                E.w("vector", "tT", t_nm)
                E.op("vector", lambda e: e.reduce_max(out=st_sb[0:2, 0:1], in_=MX[0:2, :], axis=AX.X))
                E.op("vector", lambda e: e.drain())
                E.opd("vector", lambda e, oc=out_col: e.tensor_copy(out=nm_sb[:, oc:oc + 1], in_=st_sb[0:2, 0:1]), inc="vV")

            yield from proj_slices(0, 4)
            rope(2048, QT[:, :])
            v_sq_q = norms_sq(QT[:, :], 2048)
            yield
            yield
            norms_mm(v_sq_q, 2048, 0)
            yield
            yield from proj_slices(128, 8)
            rope(4096, KT[:, :])
            v_sq_k = norms_sq(KT[:, :], 4096)
            yield
            yield
            norms_mm(v_sq_k, 4096, 1)
            yield
            E.opd("vector", lambda e: e.tensor_add(out=nm_sb[:, 2:3], in0=nm_sb[:, 0:1], in1=nm_sb[:, 1:2]), inc="vV")
            E.opd("vector", lambda e: e.tensor_scalar_mul(out=nm_sb[:, 4:5], in0=nm_sb[:, 2:3], scalar1=SCALE * 0.5))
            E.op("vector", lambda e: e.tensor_scalar(out=zc2_sb[:], in0=zc2_sb[:], scalar1=nm_sb[:, 4:5], scalar2=None, op0=ALU.add), inc="vV")
            v_mb = E.cnt["vV"]
            E.w("tensor", "vV", v_mb)
            if p == 0:
                E.wt("tensor", toks["ident2"])
                E.wt("tensor", toks["ones2"])
            E.op("tensor", lambda e: e.matmul(out=MX[:, 0:2], lhsT=zc2_sb[:], rhs=ident2_sb[:], start=True, stop=True), inc="tT")
            t_mb = E.cnt["tT"]
            E.w("vector", "tT", t_mb)
            E.op("vector", lambda e, par=par: e.tensor_scalar_mul(out=biasAB[:, par * 2:par * 2 + 2], in0=MX[:, 0:2], scalar1=-1.0))
            E.opd("vector", lambda e: e.memset(zc2_sb[:], 0.0), inc="vV")
            state["bias_ready"][par] = E.cnt["vV"]
            yield
            yield from proj_slices(256, 8)
            vp_prev = 0
            for s in range(8):
                E.w("tensor", "vV", evac_marks[(256, s)])
                if vp_prev:
                    E.w("tensor", "vV", vp_prev)
                for j in range(4):
                    E.op("tensor", lambda e, s=s, j=j: e.transpose(
                        out=SDB[:, j * 128:(j + 1) * 128],
                        in_=Tstg[:, s * 512 + j * 128: s * 512 + (j + 1) * 128],
                        identity=ident_sb[:]), inc="tT" if j == 3 else None)
                t_vt = E.cnt["tT"]
                E.w("vector", "tT", t_vt)
                dst = Vp.rearrange("p (t h c) -> p t h c", t=32, h=2)[:, 4 * s:4 * s + 4, :, 0:64]
                src = SDB[:, 0:512].rearrange("p (j h c) -> p j h c", j=4, h=2, c=64)[:, :, :, 0:64]
                E.op("vector", lambda e, dst=dst, src=src: e.tensor_copy(out=dst, in_=src), inc="vV")
                vp_prev = E.cnt["vV"]
                yield
            state["last_proj_t"] = E.cnt["tT"]
            state["vp_ready"][par] = E.cnt["vV"]

        def run_gen(g):
            if g is not None:
                for _ in g:
                    pass

        run_gen(gen_proj(0))

        for p in range(NPAIR):
            par = p % 2
            QT, KT, Vp = QTb[par], KTb[par], Vpb[par]
            bias = biasAB[:, par * 2:par * 2 + 2]
            gen = gen_proj(p + 1) if p + 1 < NPAIR else None

            import os as _os
            if _os.environ.get("KDBG"):
                print(f"pair {p}: vp_ready={state['vp_ready']}, bias={state['bias_ready']}, vV={E.cnt['vV']}")
            E.w("tensor", "vV", state["vp_ready"][par])
            GRP = [(2 * i, 2) for i in range(16)]
            exp_done = dict()
            av_done = dict()
            G = 0
            v_oa_free = state.get("prev_oa", 0)
            v_rb = state.get("prev_rb", 0)
            v_norm_prev = state.get("prev_norm", 0)
            pending = []

            def pop_pending():
                nonlocal v_oa_free, v_norm_prev, v_rb
                fn = pending.pop(0)
                r = fn()
                if fn.kind == "tail":
                    v_oa_free, v_rb = r
                else:
                    v_norm_prev = r
            gen_budget = 0.0
            first_qhip = True
            for q in range(4):
                for hip in range(2):
                    g0 = G
                    gl = g0 + len(GRP) - 1
                    for gi, (base, ln) in enumerate(GRP):
                        g = g0 + gi
                        bank = SD[g % 2]
                        if g >= 2:
                            E.w("tensor", "sS", exp_done[g - 2])
                        elif state["prev_exp"][g % 2]:
                            E.w("tensor", "sS", state["prev_exp"][g % 2])
                        for k in range(ln):
                            ci = base + k
                            E.op("tensor", lambda e, ci=ci, k=k, q=q, hip=hip, bank=bank, KT=KT, QT=QT: e.matmul(
                                out=bank[:, k * 512:(k + 1) * 512],
                                lhsT=KT[hip * 64:(hip + 1) * 64, ci * 128:(ci + 1) * 128],
                                rhs=QT[hip * 64:(hip + 1) * 64, q * 512:(q + 1) * 512],
                                start=True, stop=True), inc="tT" if k == ln - 1 else None)
                        t_sc = E.cnt["tT"]
                        E.w("scalar", "tT", t_sc)
                        if g >= 2:
                            E.w("scalar", "tT", av_done[g - 2])
                        elif state["prev_av"][g % 2]:
                            E.w("scalar", "tT", state["prev_av"][g % 2])
                        if first_qhip and gi == 0:
                            E.w("scalar", "vV", state["bias_ready"][par])
                        E.op("scalar", lambda e, ln=ln, g=g, bank=bank, bias=bias, hip=hip: e.activation(
                            out=stag[g % 2][:, 0:ln * 512], in_=bank[:, 0:ln * 512], func=AF.Exp,
                            bias=bias[:, hip:hip + 1], scale=SCALE), inc="sS")
                        exp_done[g] = E.cnt["sS"]
                        if gi in (0, 2) and pending:
                            pop_pending()
                        gen_budget += 34.0 / 128.0
                        while gen is not None and gen_budget >= 1.0:
                            gen_budget -= 1.0
                            try:
                                next(gen)
                            except StopIteration:
                                gen = None
                        if gi >= 1:
                            gp = g - 1
                            E.w("tensor", "sS", exp_done[gp])
                            if gi == 1:
                                E.w("tensor", "vV", v_oa_free)
                                E.w("tensor", "vV", v_norm_prev)
                            bb, bl = GRP[gi - 1]
                            for k in range(bl):
                                ci = bb + k
                                E.op("tensor", lambda e, ci=ci, k=k, gp=gp, hip=hip, Vp=Vp,
                                     st_=(gp == g0 and k == 0): e.matmul(
                                    out=OA[0:65, :], lhsT=Vp[:, ci * 130 + hip * 65: ci * 130 + hip * 65 + 65],
                                    rhs=stag[gp % 2][:, k * 512:(k + 1) * 512],
                                    start=st_, stop=False), inc="tT" if k == bl - 1 else None)
                            av_done[gp] = E.cnt["tT"]
                    G = gl + 1
                    first_qhip = False

                    def _tail(gl=gl, hip=hip, Vp=Vp,
                              spill_wait=(p >= 1 and q == 0 and hip == 0)):
                        E.w("tensor", "sS", exp_done[gl])
                        bb, bl = GRP[-1]
                        for k in range(bl):
                            ci = bb + k
                            E.op("tensor", lambda e, ci=ci, k=k, gl=gl, hip=hip, Vp=Vp,
                                 sp_=(k == bl - 1): e.matmul(
                                out=OA[0:65, :], lhsT=Vp[:, ci * 130 + hip * 65: ci * 130 + hip * 65 + 65],
                                rhs=stag[gl % 2][:, k * 512:(k + 1) * 512],
                                start=False, stop=sp_), inc="tT" if k == bl - 1 else None)
                        av_done[gl] = E.cnt["tT"]
                        E.w("vector", "tT", av_done[gl])
                        if spill_wait:
                            E.wt("vector", tok_spill[p - 1])
                        E.opd("vector", lambda e: e.tensor_copy(out=oaev[0:65, :], in_=OA[0:65, :]), inc="vV")
                        oa_free = E.cnt["vV"]
                        E.opd("vector", lambda e: e.reciprocal(out=rbc[0:1, :], in_=oaev[64:65, :]))
                        E.opd("vector", lambda e: e.tensor_copy(out=rbch[0:1, :], in_=rbc[0:1, :]))
                        E.opd("vector", lambda e: e.scalar_tensor_tensor(out=rbcl[0:1, :], in0=rbch[0:1, :], scalar=-1.0, in1=rbc[0:1, :], op0=ALU.mult, op1=ALU.add), inc="vV")
                        return (oa_free, E.cnt["vV"])
                    _tail.kind = "tail"
                    pending.append(_tail)

                    def _norm(q=q, hip=hip, first=(p == 0 and q == 0 and hip == 0)):
                        E.w("tensor", "vV", v_rb)
                        if first:
                            E.wt("tensor", toks["onescolb"])
                        E.op("tensor", lambda e: e.matmul(out=MX[0:64, :], lhsT=onescolb_sb[0:1, 0:64], rhs=rbch[0:1, :], start=True, stop=False))
                        E.op("tensor", lambda e: e.matmul(out=MX[0:64, :], lhsT=onescolb_sb[0:1, 0:64], rhs=rbcl[0:1, :], start=False, stop=True), inc="tT")
                        t_bc = E.cnt["tT"]
                        E.w("vector", "tT", t_bc)
                        E.opd("vector", lambda e, q=q, hip=hip: e.tensor_mul(
                            out=aopair[hip * 64:(hip + 1) * 64, q * 512:(q + 1) * 512],
                            in0=oaev[0:64, :], in1=MX[0:64, :]), inc="vV")
                        return E.cnt["vV"]
                    _norm.kind = "norm"
                    pending.append(_norm)
            while pending:
                pop_pending()
            run_gen(gen)
            state["prev_oa"] = v_oa_free
            state["prev_rb"] = v_rb
            state["prev_norm"] = v_norm_prev
            state["prev_exp"] = [exp_done[126], exp_done[127]]
            state["prev_av"] = [av_done[126], av_done[127]]
            v_ao = E.cnt["vV"]
            E.w("gpsimd", "vV", v_ao)
            tok_spill[p] = PG.dma("gpsimd", lambda e, p=p: e.dma_start(out=aoT_d[p], in_=aopair[:]))

        p2_v = E.cnt["vV"]; p2_t = E.cnt["tT"]; p2_s = E.cnt["sS"]

        E.w("sync", "vV", p2_v)
        E.wt("sync", tok_spill[NPAIR - 2])
        E.wt("sync", tok_spill[NPAIR - 1])
        v_prev = 0
        for h in range(2):
            if h == 1:
                E.w("sync", "vV", v_prev)
            tok_wo = PL.dma("sync", lambda e, h=h: e.dma_start(out=rstg[:], in_=wout_d[:, h * 4096:(h + 1) * 4096]))
            E.wt("vector", tok_wo)
            if h == 0:
                E.w("vector", "tT", p2_t)
            E.op("vector", lambda e, h=h: e.tensor_copy(out=woutb[:, h * 4096:(h + 1) * 4096], in_=rstg[:]), inc="vV")
            v_prev = E.cnt["vV"]
        v_qq = 0
        for h in range(2):
            E.w("sync", "vV", v_prev if h == 0 else v_qq)
            tok_rh = PL.dma("sync", lambda e, h=h: e.dma_start(out=rstg[:], in_=rhsT_d[:, h * 4096:(h + 1) * 4096]))
            for c4 in range(4):
                cc = h * 4 + c4
                E.wt("vector", tok_rh)
                E.op("vector", lambda e, c4=c4: e.reduce_max(out=st_amax, in_=rstg[:, c4 * 1024:(c4 + 1) * 1024], axis=AX.X, apply_absolute_value=True))
                E.op("vector", lambda e: e.drain())
                E.opd("vector", lambda e: e.tensor_scalar_max(out=st_amax, in0=st_amax, scalar1=1e-12))
                E.opd("vector", lambda e, cc=cc: e.tensor_scalar_mul(out=ls16[:, cc:cc + 1], in0=st_amax, scalar1=1.0 / INT8_MAX))
                E.opd("vector", lambda e, cc=cc: e.reciprocal(out=st_lrcp, in_=ls16[:, cc:cc + 1]))
                E.opd("vector", lambda e, c4=c4: e.tensor_scalar(out=p3t[:], in0=rstg[:, c4 * 1024:(c4 + 1) * 1024], scalar1=st_lrcp, scalar2=MAGIC, op0=ALU.mult, op1=ALU.add))
                E.op("vector", lambda e: e.tensor_scalar(out=qtstg[:], in0=p3t[:], scalar1=MAGIC, scalar2=None, op0=ALU.subtract), inc="vV")
                v_q8 = E.cnt["vV"]
                E.w("tensor", "vV", v_q8)
                if cc == 0:
                    E.wt("tensor", toks["identf"])
                for fc in range(8):
                    E.op("tensor", lambda e, fc=fc: e.transpose(
                        out=SDB[:, fc * 128:(fc + 1) * 128], in_=qtstg[:, fc * 128:(fc + 1) * 128],
                        identity=ident_sb[:]), inc="tT" if fc == 7 else None)
                t_qb = E.cnt["tT"]
                E.w("vector", "tT", t_qb)
                dst = qrhs.rearrange("p (f c) -> p f c", f=8)[:, :, cc * 128:(cc + 1) * 128]
                src = SDB.rearrange("p (f c) -> p f c", f=8)
                E.op("vector", lambda e, dst=dst, src=src: e.tensor_copy(out=dst, in_=src), inc="vV")
                v_qq = E.cnt["vV"]
        v_qrhs = E.cnt["vV"]
        E.opd("vector", lambda e: e.tensor_copy(out=rsv[:, 0:8], in_=ls16[:, 0:8]), inc="vV")
        v_rsv = E.cnt["vV"]
        E.w("tensor", "vV", max(v_qrhs, v_rsv))
        E.op("tensor", lambda e: e.transpose(out=MX[0:8, 0:128], in_=rsv[:], identity=identf_sb[:]), inc="tT")
        t_rs = E.cnt["tT"]
        E.w("vector", "tT", t_rs)
        E.op("vector", lambda e: e.tensor_copy(out=rsT_sb[:], in_=MX[0:8, 0:128]), inc="vV")
        v_rsT = E.cnt["vV"]
        E.w("gpsimd", "vV", v_rsT)
        tok_rr = PG.dma("gpsimd", lambda e: e.dma_start(out=rsrow, in_=rsT_sb[:]))
        E.wt("tensor", tok_rr)
        E.op("tensor", lambda e: e.matmul(out=SD0[:, 0:512], lhsT=onescol_sb[:], rhs=rsrow[0:1, 0:512], start=True, stop=True))
        E.op("tensor", lambda e: e.matmul(out=SD0[:, 512:1024], lhsT=onescol_sb[:], rhs=rsrow[0:1, 512:1024], start=True, stop=True), inc="tT")
        t_rsb = E.cnt["tT"]
        E.w("vector", "tT", t_rsb)
        E.op("vector", lambda e: e.tensor_copy(out=rsb[:, 0:512], in_=SD0[:, 0:512]))
        E.opd("vector", lambda e: e.tensor_copy(out=rsb[:, 512:1024], in_=SD0[:, 512:1024]), inc="vV")
        v_rsb = E.cnt["vV"]
        tok_bo = PL.dma("sync", lambda e: e.dma_start(out=bouts[:], in_=bout_d[:]))
        aoT_src = aoT_d.rearrange("p q c -> q p c")
        aoT_dst = aoTs.rearrange("q (p c) -> q p c", p=NPAIR)
        tok_ao = PL.dma("sync", lambda e, s=aoT_src, d=aoT_dst: e.dma_start(out=d, in_=s))

        v_a = [0] * 16
        for t in range(16):
            if t >= 2:
                E.w("sync", "vV", v_a[t - 2])
            tok_x = PL.dma("sync", lambda e, t=t: e.dma_start(out=xt[t % 2][:], in_=xin[t * 128:(t + 1) * 128, :]))
            E.wt("scalar", tok_x)
            E.op("scalar", lambda e, t=t: e.activation(out=tr_bf[:], in_=xt[t % 2][:], func=AF.Square, accum_out=st_ssq), inc="sS")
            s_sq = E.cnt["sS"]
            E.wt("vector", tok_x)
            E.w("vector", "sS", s_sq)
            if t == 0:
                E.w("vector", "vV", v_rsb)
            emit_ln(xt[t % 2], sbf[:])
            E.op("vector", lambda e: e.reduce_max(out=st_amax, in_=sbf[:], axis=AX.X, apply_absolute_value=True))
            E.op("vector", lambda e: e.drain())
            E.opd("vector", lambda e: e.tensor_scalar_max(out=st_amax, in0=st_amax, scalar1=1e-12))
            E.opd("vector", lambda e, t=t: e.tensor_scalar_mul(out=ls16[:, t:t + 1], in0=st_amax, scalar1=1.0 / INT8_MAX))
            E.opd("vector", lambda e, t=t: e.reciprocal(out=st_lrcp, in_=ls16[:, t:t + 1]))
            E.opd("vector", lambda e: e.tensor_scalar(out=p3t[:], in0=sbf[:], scalar1=st_lrcp, scalar2=MAGIC, op0=ALU.mult, op1=ALU.add))
            E.op("vector", lambda e, t=t: e.tensor_scalar(out=qlhs16[:, t * 1024:(t + 1) * 1024], in0=p3t[:], scalar1=MAGIC, scalar2=None, op0=ALU.subtract), inc="vV")
            v_a[t] = E.cnt["vV"]
        v_a_all = E.cnt["vV"]

        st_tok = [None, None]
        v_b = [0] * 16
        for t in range(16):
            if t == 0:
                E.w("sync", "vV", v_a_all)
            if t >= 2:
                E.w("sync", "vV", v_b[t - 2])
            tok_x = PL.dma("sync", lambda e, t=t: e.dma_start(out=xt[t % 2][:], in_=xin[t * 128:(t + 1) * 128, :]))
            if t == 0:
                E.w("tensor", "vV", v_a_all)
                E.wt("tensor", tok_ao)
            else:
                E.w("tensor", "vV", v_b[t - 1])
            for c in range(8):
                E.op("tensor", lambda e, c=c, t=t: e.transpose(out=SDB[:, c * 128:(c + 1) * 128], in_=qlhs16[:, t * 1024 + c * 128: t * 1024 + (c + 1) * 128], identity=ident_sb[:]), inc="tT" if c == 7 else None)
            t_ql = E.cnt["tT"]
            E.w("vector", "tT", t_ql)
            E.op("vector", lambda e: e.tensor_copy(out=qlT[:], in_=SDB[:, 0:1024]), inc="vV")
            v_qlT = E.cnt["vV"]

            E.w("tensor", "vV", v_qlT)
            for c in range(8):
                for hh in range(2):
                    E.op("tensor", lambda e, c=c, hh=hh: e.matmul(
                        out=SD0[:, hh * 512:(hh + 1) * 512], lhsT=qlT[:, c * 128:(c + 1) * 128],
                        rhs=qrhs[:, c * 1024 + hh * 512: c * 1024 + (hh + 1) * 512],
                        start=(c == 0), stop=(c == 7)), inc="tT" if (c == 7 and hh == 1) else None)
            t_ffn = E.cnt["tT"]
            for pp in range(8):
                for hh in range(2):
                    E.op("tensor", lambda e, pp=pp, hh=hh, t=t: e.matmul(
                        out=SD1[:, hh * 512:(hh + 1) * 512],
                        lhsT=aoTs[:, pp * 2048 + t * 128: pp * 2048 + (t + 1) * 128],
                        rhs=woutb[:, pp * 1024 + hh * 512: pp * 1024 + (hh + 1) * 512],
                        start=(pp == 0), stop=(pp == 7)), inc="tT" if (pp == 7 and hh == 1) else None)
            t_op = E.cnt["tT"]

            E.w("vector", "tT", t_ffn)
            E.opd("vector", lambda e: e.tensor_mul(out=p3t[:], in0=SD0[:, 0:1024], in1=rsb[:]), inc="vV")
            v_g1 = E.cnt["vV"]
            E.w("scalar", "vV", v_g1)
            E.op("scalar", lambda e, t=t: e.activation(out=gbuf[:], in_=p3t[:], func=AF.Silu, scale=ls16[:, t:t + 1]), inc="sS")
            s_g = E.cnt["sS"]
            E.w("vector", "tT", t_op)
            E.wt("vector", tok_x)
            E.opd("vector", lambda e: e.tensor_add(out=sbf[:], in0=SD1[:, 0:1024], in1=bouts[:]))
            E.w("vector", "sS", s_g)
            if t >= 2:
                E.wt("vector", st_tok[t % 2])
            E.opd("vector", lambda e: e.tensor_mul(out=sbf[:], in0=sbf[:], in1=gbuf[:]))
            E.op("vector", lambda e, t=t: e.tensor_add(out=outb[t % 2][:], in0=sbf[:], in1=xt[t % 2][:]), inc="vV")
            v_b[t] = E.cnt["vV"]
            E.w("sync", "vV", v_b[t])
            st_tok[t % 2] = PS.dma("sync", lambda e, t=t: e.dma_start(out=out_d[t * 128:(t + 1) * 128, :], in_=outb[t % 2][:]))

        E.w("sync", "st0", E.cnt["st0"])
        E.w("sync", "st1", E.cnt["st1"])

        @block.sync
        def _(e):
            for fn in E.ops["sync"]:
                fn(e)

        @block.tensor
        def _(e):
            for fn in E.ops["tensor"]:
                fn(e)

        @block.vector
        def _(e):
            for fn in E.ops["vector"]:
                fn(e)

        @block.scalar
        def _(e):
            for fn in E.ops["scalar"]:
                fn(e)

        @block.gpsimd
        def _(e):
            for fn in E.ops["gpsimd"]:
                fn(e)

    return nc



def _prep_shared(w_qkv, w_out, b_out, rhs):
    wp = np.empty((NPAIR, 128, 3072), dtype=np.float32)
    for p in range(NPAIR):
        wq = w_qkv[:, p * 128:(p + 1) * 128]
        wk = w_qkv[:, 1024 + p * 128:1024 + (p + 1) * 128]
        wv = w_qkv[:, 2048 + p * 128:2048 + (p + 1) * 128]
        cat = np.concatenate([wq, wk, wv], axis=1)
        wp[p] = np.ascontiguousarray(cat.reshape(8, 128, 384).transpose(1, 0, 2).reshape(128, 3072))
    o2 = np.zeros((128, 2), dtype=np.float32)
    o2[0:64, 0] = 1.0
    o2[64:128, 1] = 1.0
    rhsT = np.ascontiguousarray(rhs.T)
    consts = {
        "wout": np.ascontiguousarray(w_out.reshape(8, 128, 1024).transpose(1, 0, 2).reshape(128, 8192)).astype(np.float32),
        "rhsT": np.ascontiguousarray(rhsT.reshape(8, 128, 1024).transpose(1, 0, 2).reshape(128, 8192)).astype(np.float32),
        "bout": np.ascontiguousarray(np.broadcast_to(b_out[None, :], (128, 1024))).astype(np.float32),
        "ident": np.eye(128, dtype=np.float32).astype(bfloat16),
        "identf": np.eye(128, dtype=np.float32),
        "ident2": np.eye(2, dtype=np.float32).astype(bfloat16),
        "onescol": np.ones((1, 128), dtype=np.float32),
        "onescolb": np.ones((1, 128), dtype=np.float32).astype(bfloat16),
        "ones2": o2.astype(bfloat16),
    }
    return wp, consts


def make_in_maps(x, w_qkv, w_out, b_out, rhs):
    wp, consts = _prep_shared(w_qkv, w_out, b_out, rhs)
    tab64 = _host_tables()
    in_maps = []
    for core in range(8):
        b, half = core // 2, core % 2
        perm = np.concatenate([np.arange(half * NQ, (half + 1) * NQ),
                               np.arange((1 - half) * NQ, (2 - half) * NQ)])
        xp = np.ascontiguousarray(x[b][perm])
        tabs = np.ascontiguousarray(
            tab64.reshape(128, 2, N)[:, :, perm].reshape(128, 2 * N)).astype(bfloat16)
        m = {"xin": xp, "wpair": wp, "tabs": tabs}
        m.update(consts)
        in_maps.append(m)
    return in_maps


def kernel(L0_attn_x, L0_attn_w_qkv, L0_attn_w_out, L0_attn_b_out, L0_ffn_rhs):
    x = np.asarray(L0_attn_x, dtype=np.float32)
    in_maps = make_in_maps(x,
                           np.asarray(L0_attn_w_qkv, dtype=np.float32),
                           np.asarray(L0_attn_w_out, dtype=np.float32),
                           np.asarray(L0_attn_b_out, dtype=np.float32),
                           np.asarray(L0_ffn_rhs, dtype=np.float32))
    nc = build()
    res = run_bass_kernel_spmd(nc, in_maps, core_ids=list(range(8)))
    out = np.empty((B, N, D), dtype=np.float32)
    for core in range(8):
        b, half = core // 2, core % 2
        out[b, half * NQ:(half + 1) * NQ] = res.results[core]["out"]
    return out


# version 15
# speedup vs baseline: 1.6005x; 1.1227x over previous
import numpy as np
import ml_dtypes
import concourse.bass as bass
import concourse.mybir as mybir
from concourse import library_config
from concourse.bass_utils import run_bass_kernel_spmd

F32 = mybir.dt.float32
BF16 = mybir.dt.bfloat16
AF = mybir.ActivationFunctionType
ALU = mybir.AluOpType
AX = mybir.AxisListType
bfloat16 = ml_dtypes.bfloat16

B, N, D = 4, 4096, 1024
HEADS, DH, HALF = 16, 64, 32
NQ = 2048
NPAIR = 8
SCALE = float(DH) ** -0.5
MAGIC = 12582912.0
EPS = 1e-5
INT8_MAX = 127.0


def _host_tables():
    j = np.arange(HALF, dtype=np.float64)
    theta = 10000.0 ** (-j / HALF)
    pos = np.arange(N, dtype=np.float64)
    ang = np.outer(theta, pos)
    cos = np.cos(ang)
    sin = np.sin(ang)
    CCt = np.concatenate([cos, cos, cos, cos], axis=0)
    SSt = np.concatenate([-sin, sin, -sin, sin], axis=0)
    return np.concatenate([CCt, SSt], axis=1)


class _Emit:
    ENGS = ("sync", "tensor", "vector", "scalar", "gpsimd")

    def __init__(self, sems):
        self.ops = {e: [] for e in self.ENGS}
        self.cnt = {k: 0 for k in sems}
        self.sems = sems

    def wt(self, eng, tok):
        if tok is not None:
            self.w(eng, tok[0], tok[1])

    def w(self, eng, sem, val):
        if val <= 0:
            return
        assert val <= self.cnt[sem], f"wait {sem}>={val} emitted={self.cnt[sem]}"
        s = self.sems[sem]
        self.ops[eng].append(lambda e, s=s, v=val: e.wait_ge(s, v))

    def op(self, eng, fn, inc=None, n=1):
        if inc is None:
            self.ops[eng].append(fn)
            return None
        s = self.sems[inc]
        self.cnt[inc] += n
        self.ops[eng].append(lambda e, fn=fn, s=s, n=n: fn(e).then_inc(s, n))
        return self.cnt[inc]

    def opd(self, eng, fn, inc=None, n=1):
        r = self.op(eng, fn, inc=inc, n=n)
        self.ops[eng].append(lambda e: e.drain())
        return r


class _DPool:

    def __init__(self, E, names):
        self.E = E
        self.names = names
        self.i = 0

    def dma(self, eng, fn):
        s = self.names[self.i % len(self.names)]
        self.i += 1
        self.E.w(eng, s, self.E.cnt[s])
        v = self.E.op(eng, fn, inc=s, n=16)
        return (s, v)


def build(debug=False):
    nc = bass.Bass("TRN2", target_bir_lowering=False)

    xin = nc.dram_tensor("xin", [N, D], F32, kind="ExternalInput").ap()
    wpair = nc.dram_tensor("wpair", [NPAIR, 128, 3072], F32, kind="ExternalInput").ap()
    wout_d = nc.dram_tensor("wout", [128, 8192], F32, kind="ExternalInput").ap()
    rhsT_d = nc.dram_tensor("rhsT", [128, 8192], F32, kind="ExternalInput").ap()
    bout_d = nc.dram_tensor("bout", [128, 1024], F32, kind="ExternalInput").ap()
    tabs_d = nc.dram_tensor("tabs", [128, 8192], BF16, kind="ExternalInput").ap()
    ident_d = nc.dram_tensor("ident", [128, 128], BF16, kind="ExternalInput").ap()
    identf_d = nc.dram_tensor("identf", [128, 128], F32, kind="ExternalInput").ap()
    ones2_d = nc.dram_tensor("ones2", [128, 2], BF16, kind="ExternalInput").ap()
    ident2_d = nc.dram_tensor("ident2", [2, 2], BF16, kind="ExternalInput").ap()
    onescol_d = nc.dram_tensor("onescol", [1, 128], F32, kind="ExternalInput").ap()
    onescolb_d = nc.dram_tensor("onescolb", [1, 128], BF16, kind="ExternalInput").ap()
    out_d = nc.dram_tensor("out", [NQ, D], F32, kind="ExternalOutput").ap()
    aoT_d = nc.dram_tensor("aoT", [NPAIR, 128, NQ], BF16, kind="Internal").ap()

    from contextlib import ExitStack
    with ExitStack() as _stk:
        en = _stk.enter_context
        ident_sb = en(nc.sbuf_tensor([128, 128], BF16))
        identf_sb = en(nc.sbuf_tensor([128, 128], F32))
        ones2_sb = en(nc.sbuf_tensor([128, 2], BF16))
        ident2_sb = en(nc.sbuf_tensor([2, 2], BF16))
        onescol_sb = en(nc.sbuf_tensor([1, 128], F32))
        zc2_sb = en(nc.sbuf_tensor([2, 128], BF16))
        xt0 = en(nc.sbuf_tensor([128, 1024], F32))
        xt1 = en(nc.sbuf_tensor([128, 1024], F32))
        st_sb = en(nc.sbuf_tensor([128, 16], F32))
        nm_sb = en(nc.sbuf_tensor([2, 8], F32))
        biasAB = en(nc.sbuf_tensor([128, 4], F32))
        rbc = en(nc.sbuf_tensor([1, 512], F32))
        rbch = en(nc.sbuf_tensor([1, 512], BF16))
        rbcl = en(nc.sbuf_tensor([1, 512], BF16))
        onescolb_sb = en(nc.sbuf_tensor([1, 128], BF16))
        rsT_sb = en(nc.sbuf_tensor([8, 128], F32))
        ls16 = en(nc.sbuf_tensor([128, 16], F32))
        abf = en(nc.sbuf_tensor([128, 76928], BF16))
        af32 = en(nc.sbuf_tensor([128, 10272], F32))
        SD0 = en(nc.psum_tensor([128, 1536], F32))
        SD1 = en(nc.psum_tensor([128, 1024], F32))
        SDB = en(nc.psum_tensor([128, 1024], BF16))
        OA = en(nc.psum_tensor([128, 512], F32))
        MX = en(nc.psum_tensor([128, 512], F32))
        SD = (SD0, SD1)
        sems = {}
        for name in (["ld%d" % i for i in range(8)] + ["st0", "st1", "dg0", "dg1",
                                                       "tT", "vV", "sS", "gG"]):
            sems[name] = en(nc.semaphore(name="sem_" + name))
        block = en(nc.Block())
        E = _Emit(sems)
        PL = _DPool(E, ["ld%d" % i for i in range(8)])
        PS = _DPool(E, ["st0", "st1"])
        PG = _DPool(E, ["dg0", "dg1"])

        xnT = abf[:, 0:32768]
        tabs = abf[:, 32768:40960]
        CC = tabs[:, 0:4096]
        SS = tabs[:, 4096:8192]
        wbf = abf[:, 40960:44032]
        QTb = (abf[:, 44032:46080], abf[:, 46080:48128])
        KTb = (abf[:, 48128:52224], abf[:, 52224:56320])
        Vpb = (abf[:, 56320:60480], abf[:, 60480:64640])
        Tstg = abf[:, 64640:68736]
        Tsw = abf[:, 68736:72832]
        stag = (abf[:, 72832:73856], abf[:, 73856:74880])
        aopair = abf[:, 74880:76928]
        xnb = Tstg[:, 0:1024]
        aoTs = abf[:, 0:16384]
        woutb = abf[:, 16384:24576]
        qrhs = abf[:, 24576:32768]
        qlhs16 = abf[:, 44032:60416]
        qlT = tabs[:, 0:1024]
        qtstg = tabs[:, 1024:2048]
        tr_bf = tabs[:, 2048:3072]

        wstg = af32[:, 0:3072]
        oaev = af32[:, 3072:3584]
        rstg = af32[:, 0:4096]
        trash = af32[:, 4096:5120]
        rsb = af32[:, 5120:6144]
        bouts = af32[:, 6144:7168]
        gbuf = af32[:, 7168:8192]
        sbf = af32[:, 8192:9216]
        p3t = af32[:, 9216:10240]
        rsv = af32[:, 10240:10248]
        rcv = af32[:, 10248:10256]
        rsrow = p3t[0:1, 0:1024]
        outb = (rstg[:, 0:1024], rstg[:, 1024:2048])

        st_sum = st_sb[:, 0:1]; st_ssq = st_sb[:, 1:2]; st_mean = st_sb[:, 2:3]
        st_ex2 = st_sb[:, 3:4]; st_m2 = st_sb[:, 4:5]; st_var = st_sb[:, 5:6]
        st_vep = st_sb[:, 6:7]; st_s0 = st_sb[:, 7:8]; st_r0 = st_sb[:, 8:9]
        st_a = st_sb[:, 9:10]; st_b = st_sb[:, 10:11]; st_c = st_sb[:, 11:12]
        st_rstd = st_sb[:, 12:13]; st_amax = st_sb[:, 13:14]
        st_lrcp = st_sb[:, 15:16]

        toks = {}
        for name, dst, src in (("ident", ident_sb, ident_d), ("identf", identf_sb, identf_d),
                               ("ones2", ones2_sb, ones2_d), ("ident2", ident2_sb, ident2_d),
                               ("onescol", onescol_sb, onescol_d), ("onescolb", onescolb_sb, onescolb_d), ("tabs", tabs, tabs_d)):
            toks[name] = PL.dma("sync", lambda e, d=dst, s=src: e.dma_start(out=d[:], in_=s[:]))

        E.op("vector", lambda e: e.memset(zc2_sb[:], 0.0))
        for vb in Vpb:
            vo = vb.rearrange("p (t h c) -> p t h c", t=32, h=2)[:, :, :, 64:65]
            E.op("vector", lambda e, vo=vo: e.memset(vo, 1.0))
        E.op("vector", lambda e: e.nop(), inc="vV")

        xt = (xt0, xt1)
        v_tile_done = [0] * 33
        t_tr_done = [0] * 33
        first_tr = True

        def emit_ln(buf, out_ap):
            E.op("vector", lambda e, b=buf: e.reduce_sum(out=st_sum, in_=b[:], axis=AX.X))
            E.op("vector", lambda e: e.drain())
            E.opd("vector", lambda e: e.tensor_scalar_mul(out=st_mean, in0=st_sum, scalar1=1.0 / D))
            E.opd("vector", lambda e: e.tensor_scalar_mul(out=st_ex2, in0=st_ssq, scalar1=1.0 / D))
            E.opd("vector", lambda e: e.tensor_mul(out=st_m2, in0=st_mean, in1=st_mean))
            E.opd("vector", lambda e: e.tensor_sub(out=st_var, in0=st_ex2, in1=st_m2))
            E.opd("vector", lambda e: e.tensor_scalar_add(out=st_vep, in0=st_var, scalar1=EPS), inc="vV")
            v_vep = E.cnt["vV"]
            E.w("scalar", "vV", v_vep)
            E.opd("scalar", lambda e: e.activation(out=st_s0, in_=st_vep, func=AF.Ln))
            E.op("scalar", lambda e: e.activation(out=st_r0, in_=st_s0, func=AF.Exp, scale=-0.5), inc="sS")
            s_r0 = E.cnt["sS"]
            E.w("vector", "sS", s_r0)
            E.opd("vector", lambda e: e.tensor_mul(out=st_a, in0=st_r0, in1=st_r0))
            E.opd("vector", lambda e: e.tensor_mul(out=st_b, in0=st_a, in1=st_vep))
            E.opd("vector", lambda e: e.tensor_scalar(out=st_c, in0=st_b, scalar1=-0.5, scalar2=1.5, op0=ALU.mult, op1=ALU.add))
            E.opd("vector", lambda e: e.tensor_mul(out=st_rstd, in0=st_r0, in1=st_c))
            E.opd("vector", lambda e, o=out_ap, b=buf: e.tensor_scalar(out=o, in0=b[:], scalar1=st_mean, scalar2=st_rstd, op0=ALU.subtract, op1=ALU.mult), inc="vV")
            return E.cnt["vV"]

        for i in range(32):
            if i >= 2:
                E.w("sync", "vV", v_tile_done[i - 2])
            tok_x = PL.dma("sync", lambda e, i=i: e.dma_start(out=xt[i % 2][:], in_=xin[i * 128:(i + 1) * 128, :]))
            E.wt("scalar", tok_x)
            E.op("scalar", lambda e, i=i: e.activation(out=trash[:], in_=xt[i % 2][:], func=AF.Square, accum_out=st_ssq), inc="sS")
            s_sq = E.cnt["sS"]
            E.wt("vector", tok_x)
            E.w("vector", "sS", s_sq)
            if i >= 1:
                E.w("vector", "tT", t_tr_done[i - 1])
            v_xn = emit_ln(xt[i % 2], xnb[:])
            E.w("tensor", "vV", v_xn)
            if first_tr:
                E.wt("tensor", toks["ident"])
                first_tr = False
            for c in range(8):
                E.op("tensor", lambda e, c=c: e.transpose(out=SDB[:, c * 128:(c + 1) * 128], in_=xnb[:, c * 128:(c + 1) * 128], identity=ident_sb[:]), inc="tT" if c == 7 else None)
            t_tr_done[i] = E.cnt["tT"]
            E.w("vector", "tT", t_tr_done[i])
            dst = xnT.rearrange("p (c t) -> p c t", c=8)[:, :, i * 128:(i + 1) * 128]
            E.op("vector", lambda e, dst=dst: e.tensor_copy(out=dst, in_=SDB[:, 0:1024]), inc="vV")
            v_tile_done[i] = E.cnt["vV"]
        v_xnT_all = E.cnt["vV"]

        state = {"wconv": 0, "last_proj_t": 0, "mx_rd": 0,
                 "vp_ready": [0, 0], "bias_ready": [0, 0],
                 "prev_exp": [0, 0], "prev_av": [0, 0]}
        tok_spill = [None] * NPAIR

        def gen_proj(p):
            par = p % 2
            QT, KT, Vp = QTb[par], KTb[par], Vpb[par]
            E.w("sync", "vV", state["wconv"])
            tok_w = PL.dma("sync", lambda e, p=p: e.dma_start(out=wstg[:], in_=wpair[p]))
            E.wt("vector", tok_w)
            E.w("vector", "tT", state["last_proj_t"])
            if p == 0:
                E.wt("vector", toks["tabs"])
            E.op("vector", lambda e: e.tensor_copy(out=wbf[:], in_=wstg[:]), inc="vV")
            state["wconv"] = E.cnt["vV"]
            wconv = state["wconv"]
            yield

            evac_marks = {}

            def proj_slices(sel_off, nslices):
                for s in range(nslices):
                    E.w("tensor", "vV", max(v_xnT_all, wconv))
                    if (sel_off, s - 1) in evac_marks:
                        E.w("tensor", "vV", evac_marks[(sel_off, s - 1)])
                    for c in range(8):
                        lhsT = wbf[:, c * 384 + sel_off: c * 384 + sel_off + 128]
                        rhs = xnT[:, c * 4096 + s * 512: c * 4096 + (s + 1) * 512]
                        E.op("tensor", lambda e, l=lhsT, r=rhs, c=c:
                             e.matmul(out=SD0[:, 1024:1536], lhsT=l, rhs=r, start=(c == 0), stop=(c == 7)),
                             inc="tT" if c == 7 else None)
                    t_mm = E.cnt["tT"]
                    E.w("vector", "tT", t_mm)
                    E.opd("vector", lambda e, s=s:
                          e.tensor_copy(out=Tstg[:, s * 512:(s + 1) * 512], in_=SD0[:, 1024:1536]), inc="vV")
                    evac_marks[(sel_off, s)] = E.cnt["vV"]
                    yield

            def rope(n, dst):
                for sb, db in ((32, 0), (0, 32), (96, 64), (64, 96)):
                    E.op("vector", lambda e, n=n, sb=sb, db=db: e.tensor_copy(out=Tsw[db:db + 32, 0:n], in_=Tstg[sb:sb + 32, 0:n]))
                E.op("vector", lambda e: e.drain())
                E.op("vector", lambda e, n=n: e.tensor_mul(out=Tsw[:, 0:n], in0=Tsw[:, 0:n], in1=SS[:, 0:n]))
                E.opd("vector", lambda e, n=n: e.tensor_mul(out=Tstg[:, 0:n], in0=Tstg[:, 0:n], in1=CC[:, 0:n]))
                E.opd("vector", lambda e, n=n, dst=dst: e.tensor_add(out=dst, in0=Tstg[:, 0:n], in1=Tsw[:, 0:n]), inc="vV")

            def norms_sq(src, ncols):
                E.op("vector", lambda e, src=src, n=ncols: e.tensor_mul(out=Tsw[:, 0:n], in0=src, in1=src), inc="vV")
                return E.cnt["vV"]

            def norms_mm(v_sq, ncols, out_col):
                E.w("tensor", "vV", max(v_sq, state["mx_rd"]))
                nsl = ncols // 512
                for s in range(nsl):
                    E.op("tensor", lambda e, s=s, nsl=nsl: e.matmul(
                        out=MX[0:2, :], lhsT=ones2_sb[:],
                        rhs=Tsw[:, s * 512:(s + 1) * 512], start=(s == 0), stop=(s == nsl - 1)),
                        inc="tT" if s == nsl - 1 else None)
                t_nm = E.cnt["tT"]
                E.w("vector", "tT", t_nm)
                E.op("vector", lambda e: e.reduce_max(out=st_sb[0:2, 0:1], in_=MX[0:2, :], axis=AX.X))
                E.op("vector", lambda e: e.drain())
                E.opd("vector", lambda e, oc=out_col: e.tensor_copy(out=nm_sb[:, oc:oc + 1], in_=st_sb[0:2, 0:1]), inc="vV")
                state["mx_rd"] = E.cnt["vV"]

            yield from proj_slices(0, 4)
            rope(2048, QT[:, :])
            v_sq_q = norms_sq(QT[:, :], 2048)
            yield
            yield
            norms_mm(v_sq_q, 2048, 0)
            yield
            yield from proj_slices(128, 8)
            rope(4096, KT[:, :])
            v_sq_k = norms_sq(KT[:, :], 4096)
            yield
            yield
            norms_mm(v_sq_k, 4096, 1)
            yield
            E.opd("vector", lambda e: e.tensor_add(out=nm_sb[:, 2:3], in0=nm_sb[:, 0:1], in1=nm_sb[:, 1:2]), inc="vV")
            E.opd("vector", lambda e: e.tensor_scalar_mul(out=nm_sb[:, 4:5], in0=nm_sb[:, 2:3], scalar1=SCALE * 0.5))
            E.op("vector", lambda e: e.tensor_scalar(out=zc2_sb[:], in0=zc2_sb[:], scalar1=nm_sb[:, 4:5], scalar2=None, op0=ALU.add), inc="vV")
            v_mb = E.cnt["vV"]
            E.w("tensor", "vV", max(v_mb, state["mx_rd"]))
            if p == 0:
                E.wt("tensor", toks["ident2"])
                E.wt("tensor", toks["ones2"])
            E.op("tensor", lambda e: e.matmul(out=MX[:, 0:2], lhsT=zc2_sb[:], rhs=ident2_sb[:], start=True, stop=True), inc="tT")
            t_mb = E.cnt["tT"]
            E.w("vector", "tT", t_mb)
            E.op("vector", lambda e, par=par: e.tensor_scalar_mul(out=biasAB[:, par * 2:par * 2 + 2], in0=MX[:, 0:2], scalar1=-1.0))
            E.opd("vector", lambda e: e.memset(zc2_sb[:], 0.0), inc="vV")
            state["bias_ready"][par] = E.cnt["vV"]
            state["mx_rd"] = E.cnt["vV"]
            yield
            yield from proj_slices(256, 8)
            vp_prev = 0
            for s in range(8):
                E.w("tensor", "vV", evac_marks[(256, s)])
                if vp_prev:
                    E.w("tensor", "vV", vp_prev)
                for j in range(4):
                    E.op("tensor", lambda e, s=s, j=j: e.transpose(
                        out=SDB[:, j * 128:(j + 1) * 128],
                        in_=Tstg[:, s * 512 + j * 128: s * 512 + (j + 1) * 128],
                        identity=ident_sb[:]), inc="tT" if j == 3 else None)
                t_vt = E.cnt["tT"]
                E.w("vector", "tT", t_vt)
                dst = Vp.rearrange("p (t h c) -> p t h c", t=32, h=2)[:, 4 * s:4 * s + 4, :, 0:64]
                src = SDB[:, 0:512].rearrange("p (j h c) -> p j h c", j=4, h=2, c=64)[:, :, :, 0:64]
                E.op("vector", lambda e, dst=dst, src=src: e.tensor_copy(out=dst, in_=src), inc="vV")
                vp_prev = E.cnt["vV"]
                yield
            state["last_proj_t"] = E.cnt["tT"]
            state["vp_ready"][par] = E.cnt["vV"]

        def run_gen(g):
            if g is not None:
                for _ in g:
                    pass

        run_gen(gen_proj(0))

        for p in range(NPAIR):
            par = p % 2
            QT, KT, Vp = QTb[par], KTb[par], Vpb[par]
            bias = biasAB[:, par * 2:par * 2 + 2]
            gen = gen_proj(p + 1) if p + 1 < NPAIR else None

            import os as _os
            if _os.environ.get("KDBG"):
                print(f"pair {p}: vp_ready={state['vp_ready']}, bias={state['bias_ready']}, vV={E.cnt['vV']}")
            E.w("tensor", "vV", state["vp_ready"][par])
            GRP = [(2 * i, 2) for i in range(16)]
            exp_done = dict()
            av_done = dict()
            G = 0
            v_oa_free = state.get("prev_oa", 0)
            v_rb = state.get("prev_rb", 0)
            v_norm_prev = state.get("prev_norm", 0)
            pending = []

            def pop_pending():
                nonlocal v_oa_free, v_norm_prev, v_rb
                fn = pending.pop(0)
                r = fn()
                if fn.kind == "tail":
                    v_oa_free, v_rb = r
                else:
                    v_norm_prev = r
            gen_budget = 0.0
            first_qhip = True
            for q in range(4):
                for hip in range(2):
                    g0 = G
                    gl = g0 + len(GRP) - 1
                    for gi, (base, ln) in enumerate(GRP):
                        g = g0 + gi
                        bank = SD[g % 2]
                        if g >= 2:
                            E.w("tensor", "sS", exp_done[g - 2])
                        elif state["prev_exp"][g % 2]:
                            E.w("tensor", "sS", state["prev_exp"][g % 2])
                        for k in range(ln):
                            ci = base + k
                            E.op("tensor", lambda e, ci=ci, k=k, q=q, hip=hip, bank=bank, KT=KT, QT=QT: e.matmul(
                                out=bank[:, k * 512:(k + 1) * 512],
                                lhsT=KT[hip * 64:(hip + 1) * 64, ci * 128:(ci + 1) * 128],
                                rhs=QT[hip * 64:(hip + 1) * 64, q * 512:(q + 1) * 512],
                                start=True, stop=True), inc="tT" if k == ln - 1 else None)
                        t_sc = E.cnt["tT"]
                        E.w("scalar", "tT", t_sc)
                        if g >= 2:
                            E.w("scalar", "tT", av_done[g - 2])
                        elif state["prev_av"][g % 2]:
                            E.w("scalar", "tT", state["prev_av"][g % 2])
                        if first_qhip and gi == 0:
                            E.w("scalar", "vV", state["bias_ready"][par])
                        E.op("scalar", lambda e, ln=ln, g=g, bank=bank, bias=bias, hip=hip: e.activation(
                            out=stag[g % 2][:, 0:ln * 512], in_=bank[:, 0:ln * 512], func=AF.Exp,
                            bias=bias[:, hip:hip + 1], scale=SCALE), inc="sS")
                        exp_done[g] = E.cnt["sS"]
                        while pending and (pending[0].kind == "tail" or gi >= 5):
                            pop_pending()
                        gen_budget += 39.0 / 128.0
                        while gen is not None and gen_budget >= 1.0:
                            gen_budget -= 1.0
                            try:
                                next(gen)
                            except StopIteration:
                                gen = None
                        if gi >= 1:
                            gp = g - 1
                            E.w("tensor", "sS", exp_done[gp])
                            if gi == 1:
                                E.w("tensor", "vV", v_oa_free)
                                E.w("tensor", "vV", v_norm_prev)
                            bb, bl = GRP[gi - 1]
                            for k in range(bl):
                                ci = bb + k
                                E.op("tensor", lambda e, ci=ci, k=k, gp=gp, hip=hip, Vp=Vp,
                                     st_=(gp == g0 and k == 0): e.matmul(
                                    out=OA[0:65, :], lhsT=Vp[:, ci * 130 + hip * 65: ci * 130 + hip * 65 + 65],
                                    rhs=stag[gp % 2][:, k * 512:(k + 1) * 512],
                                    start=st_, stop=False), inc="tT" if k == bl - 1 else None)
                            av_done[gp] = E.cnt["tT"]
                    G = gl + 1
                    first_qhip = False

                    def _tail(gl=gl, hip=hip, Vp=Vp,
                              spill_wait=(p >= 1 and q == 0 and hip == 0)):
                        E.w("tensor", "sS", exp_done[gl])
                        bb, bl = GRP[-1]
                        for k in range(bl):
                            ci = bb + k
                            E.op("tensor", lambda e, ci=ci, k=k, gl=gl, hip=hip, Vp=Vp,
                                 sp_=(k == bl - 1): e.matmul(
                                out=OA[0:65, :], lhsT=Vp[:, ci * 130 + hip * 65: ci * 130 + hip * 65 + 65],
                                rhs=stag[gl % 2][:, k * 512:(k + 1) * 512],
                                start=False, stop=sp_), inc="tT" if k == bl - 1 else None)
                        av_done[gl] = E.cnt["tT"]
                        E.w("vector", "tT", av_done[gl])
                        if spill_wait:
                            E.wt("vector", tok_spill[p - 1])
                        E.opd("vector", lambda e: e.tensor_copy(out=oaev[0:65, :], in_=OA[0:65, :]), inc="vV")
                        oa_free = E.cnt["vV"]
                        E.opd("vector", lambda e: e.reciprocal(out=rbc[0:1, :], in_=oaev[64:65, :]))
                        E.opd("vector", lambda e: e.tensor_copy(out=rbch[0:1, :], in_=rbc[0:1, :]))
                        E.opd("vector", lambda e: e.scalar_tensor_tensor(out=rbcl[0:1, :], in0=rbch[0:1, :], scalar=-1.0, in1=rbc[0:1, :], op0=ALU.mult, op1=ALU.add), inc="vV")
                        return (oa_free, E.cnt["vV"])
                    _tail.kind = "tail"
                    pending.append(_tail)

                    def _norm(q=q, hip=hip, first=(p == 0 and q == 0 and hip == 0)):
                        E.w("tensor", "vV", max(v_rb, state["mx_rd"]))
                        if first:
                            E.wt("tensor", toks["onescolb"])
                        E.op("tensor", lambda e: e.matmul(out=MX[0:64, :], lhsT=onescolb_sb[0:1, 0:64], rhs=rbch[0:1, :], start=True, stop=False))
                        E.op("tensor", lambda e: e.matmul(out=MX[0:64, :], lhsT=onescolb_sb[0:1, 0:64], rhs=rbcl[0:1, :], start=False, stop=True), inc="tT")
                        t_bc = E.cnt["tT"]
                        E.w("vector", "tT", t_bc)
                        E.opd("vector", lambda e, q=q, hip=hip: e.tensor_mul(
                            out=aopair[hip * 64:(hip + 1) * 64, q * 512:(q + 1) * 512],
                            in0=oaev[0:64, :], in1=MX[0:64, :]), inc="vV")
                        state["mx_rd"] = E.cnt["vV"]
                        return E.cnt["vV"]
                    _norm.kind = "norm"
                    pending.append(_norm)
            while pending:
                pop_pending()
            run_gen(gen)
            state["prev_oa"] = v_oa_free
            state["prev_rb"] = v_rb
            state["prev_norm"] = v_norm_prev
            state["prev_exp"] = [exp_done[126], exp_done[127]]
            state["prev_av"] = [av_done[126], av_done[127]]
            v_ao = E.cnt["vV"]
            E.w("gpsimd", "vV", v_ao)
            tok_spill[p] = PG.dma("gpsimd", lambda e, p=p: e.dma_start(out=aoT_d[p], in_=aopair[:]))

        p2_v = E.cnt["vV"]; p2_t = E.cnt["tT"]; p2_s = E.cnt["sS"]

        E.w("sync", "vV", p2_v)
        E.wt("sync", tok_spill[NPAIR - 2])
        E.wt("sync", tok_spill[NPAIR - 1])
        v_prev = 0
        for h in range(2):
            if h == 1:
                E.w("sync", "vV", v_prev)
            tok_wo = PL.dma("sync", lambda e, h=h: e.dma_start(out=rstg[:], in_=wout_d[:, h * 4096:(h + 1) * 4096]))
            E.wt("vector", tok_wo)
            if h == 0:
                E.w("vector", "tT", p2_t)
            E.op("vector", lambda e, h=h: e.tensor_copy(out=woutb[:, h * 4096:(h + 1) * 4096], in_=rstg[:]), inc="vV")
            v_prev = E.cnt["vV"]
        v_qq = 0
        for h in range(2):
            E.w("sync", "vV", v_prev if h == 0 else v_qq)
            tok_rh = PL.dma("sync", lambda e, h=h: e.dma_start(out=rstg[:], in_=rhsT_d[:, h * 4096:(h + 1) * 4096]))
            for c4 in range(4):
                cc = h * 4 + c4
                E.wt("vector", tok_rh)
                E.op("vector", lambda e, c4=c4: e.reduce_max(out=st_amax, in_=rstg[:, c4 * 1024:(c4 + 1) * 1024], axis=AX.X, apply_absolute_value=True))
                E.op("vector", lambda e: e.drain())
                E.opd("vector", lambda e: e.tensor_scalar_max(out=st_amax, in0=st_amax, scalar1=1e-12))
                E.opd("vector", lambda e, cc=cc: e.tensor_scalar_mul(out=ls16[:, cc:cc + 1], in0=st_amax, scalar1=1.0 / INT8_MAX))
                E.opd("vector", lambda e, cc=cc: e.reciprocal(out=st_lrcp, in_=ls16[:, cc:cc + 1]))
                E.opd("vector", lambda e, c4=c4: e.tensor_scalar(out=p3t[:], in0=rstg[:, c4 * 1024:(c4 + 1) * 1024], scalar1=st_lrcp, scalar2=MAGIC, op0=ALU.mult, op1=ALU.add))
                E.op("vector", lambda e: e.tensor_scalar(out=qtstg[:], in0=p3t[:], scalar1=MAGIC, scalar2=None, op0=ALU.subtract), inc="vV")
                v_q8 = E.cnt["vV"]
                E.w("tensor", "vV", v_q8)
                if cc == 0:
                    E.wt("tensor", toks["identf"])
                for fc in range(8):
                    E.op("tensor", lambda e, fc=fc: e.transpose(
                        out=SDB[:, fc * 128:(fc + 1) * 128], in_=qtstg[:, fc * 128:(fc + 1) * 128],
                        identity=ident_sb[:]), inc="tT" if fc == 7 else None)
                t_qb = E.cnt["tT"]
                E.w("vector", "tT", t_qb)
                dst = qrhs.rearrange("p (f c) -> p f c", f=8)[:, :, cc * 128:(cc + 1) * 128]
                src = SDB.rearrange("p (f c) -> p f c", f=8)
                E.op("vector", lambda e, dst=dst, src=src: e.tensor_copy(out=dst, in_=src), inc="vV")
                v_qq = E.cnt["vV"]
        v_qrhs = E.cnt["vV"]
        E.opd("vector", lambda e: e.tensor_copy(out=rsv[:, 0:8], in_=ls16[:, 0:8]), inc="vV")
        v_rsv = E.cnt["vV"]
        E.w("tensor", "vV", max(v_qrhs, v_rsv))
        E.op("tensor", lambda e: e.transpose(out=MX[0:8, 0:128], in_=rsv[:], identity=identf_sb[:]), inc="tT")
        t_rs = E.cnt["tT"]
        E.w("vector", "tT", t_rs)
        E.op("vector", lambda e: e.tensor_copy(out=rsT_sb[:], in_=MX[0:8, 0:128]), inc="vV")
        v_rsT = E.cnt["vV"]
        E.w("gpsimd", "vV", v_rsT)
        tok_rr = PG.dma("gpsimd", lambda e: e.dma_start(out=rsrow, in_=rsT_sb[:]))
        E.wt("tensor", tok_rr)
        E.op("tensor", lambda e: e.matmul(out=SD0[:, 0:512], lhsT=onescol_sb[:], rhs=rsrow[0:1, 0:512], start=True, stop=True))
        E.op("tensor", lambda e: e.matmul(out=SD0[:, 512:1024], lhsT=onescol_sb[:], rhs=rsrow[0:1, 512:1024], start=True, stop=True), inc="tT")
        t_rsb = E.cnt["tT"]
        E.w("vector", "tT", t_rsb)
        E.op("vector", lambda e: e.tensor_copy(out=rsb[:, 0:512], in_=SD0[:, 0:512]))
        E.opd("vector", lambda e: e.tensor_copy(out=rsb[:, 512:1024], in_=SD0[:, 512:1024]), inc="vV")
        v_rsb = E.cnt["vV"]
        tok_bo = PL.dma("sync", lambda e: e.dma_start(out=bouts[:], in_=bout_d[:]))
        aoT_src = aoT_d.rearrange("p q c -> q p c")
        aoT_dst = aoTs.rearrange("q (p c) -> q p c", p=NPAIR)
        tok_ao = PL.dma("sync", lambda e, s=aoT_src, d=aoT_dst: e.dma_start(out=d, in_=s))

        v_a = [0] * 16
        for t in range(16):
            if t >= 2:
                E.w("sync", "vV", v_a[t - 2])
            tok_x = PL.dma("sync", lambda e, t=t: e.dma_start(out=xt[t % 2][:], in_=xin[t * 128:(t + 1) * 128, :]))
            E.wt("scalar", tok_x)
            E.op("scalar", lambda e, t=t: e.activation(out=tr_bf[:], in_=xt[t % 2][:], func=AF.Square, accum_out=st_ssq), inc="sS")
            s_sq = E.cnt["sS"]
            E.wt("vector", tok_x)
            E.w("vector", "sS", s_sq)
            if t == 0:
                E.w("vector", "vV", v_rsb)
            emit_ln(xt[t % 2], sbf[:])
            E.op("vector", lambda e: e.reduce_max(out=st_amax, in_=sbf[:], axis=AX.X, apply_absolute_value=True))
            E.op("vector", lambda e: e.drain())
            E.opd("vector", lambda e: e.tensor_scalar_max(out=st_amax, in0=st_amax, scalar1=1e-12))
            E.opd("vector", lambda e, t=t: e.tensor_scalar_mul(out=ls16[:, t:t + 1], in0=st_amax, scalar1=1.0 / INT8_MAX))
            E.opd("vector", lambda e, t=t: e.reciprocal(out=st_lrcp, in_=ls16[:, t:t + 1]))
            E.opd("vector", lambda e: e.tensor_scalar(out=p3t[:], in0=sbf[:], scalar1=st_lrcp, scalar2=MAGIC, op0=ALU.mult, op1=ALU.add))
            E.op("vector", lambda e, t=t: e.tensor_scalar(out=qlhs16[:, t * 1024:(t + 1) * 1024], in0=p3t[:], scalar1=MAGIC, scalar2=None, op0=ALU.subtract), inc="vV")
            v_a[t] = E.cnt["vV"]
        v_a_all = E.cnt["vV"]

        st_tok = [None, None]
        v_b = [0] * 16
        for t in range(16):
            if t == 0:
                E.w("sync", "vV", v_a_all)
            if t >= 2:
                E.w("sync", "vV", v_b[t - 2])
            tok_x = PL.dma("sync", lambda e, t=t: e.dma_start(out=xt[t % 2][:], in_=xin[t * 128:(t + 1) * 128, :]))
            if t == 0:
                E.w("tensor", "vV", v_a_all)
                E.wt("tensor", tok_ao)
            else:
                E.w("tensor", "vV", v_b[t - 1])
            for c in range(8):
                E.op("tensor", lambda e, c=c, t=t: e.transpose(out=SDB[:, c * 128:(c + 1) * 128], in_=qlhs16[:, t * 1024 + c * 128: t * 1024 + (c + 1) * 128], identity=ident_sb[:]), inc="tT" if c == 7 else None)
            t_ql = E.cnt["tT"]
            E.w("vector", "tT", t_ql)
            E.op("vector", lambda e: e.tensor_copy(out=qlT[:], in_=SDB[:, 0:1024]), inc="vV")
            v_qlT = E.cnt["vV"]

            E.w("tensor", "vV", v_qlT)
            for c in range(8):
                for hh in range(2):
                    E.op("tensor", lambda e, c=c, hh=hh: e.matmul(
                        out=SD0[:, hh * 512:(hh + 1) * 512], lhsT=qlT[:, c * 128:(c + 1) * 128],
                        rhs=qrhs[:, c * 1024 + hh * 512: c * 1024 + (hh + 1) * 512],
                        start=(c == 0), stop=(c == 7)), inc="tT" if (c == 7 and hh == 1) else None)
            t_ffn = E.cnt["tT"]
            for pp in range(8):
                for hh in range(2):
                    E.op("tensor", lambda e, pp=pp, hh=hh, t=t: e.matmul(
                        out=SD1[:, hh * 512:(hh + 1) * 512],
                        lhsT=aoTs[:, pp * 2048 + t * 128: pp * 2048 + (t + 1) * 128],
                        rhs=woutb[:, pp * 1024 + hh * 512: pp * 1024 + (hh + 1) * 512],
                        start=(pp == 0), stop=(pp == 7)), inc="tT" if (pp == 7 and hh == 1) else None)
            t_op = E.cnt["tT"]

            E.w("vector", "tT", t_ffn)
            E.opd("vector", lambda e: e.tensor_mul(out=p3t[:], in0=SD0[:, 0:1024], in1=rsb[:]), inc="vV")
            v_g1 = E.cnt["vV"]
            E.w("scalar", "vV", v_g1)
            E.op("scalar", lambda e, t=t: e.activation(out=gbuf[:], in_=p3t[:], func=AF.Silu, scale=ls16[:, t:t + 1]), inc="sS")
            s_g = E.cnt["sS"]
            E.w("vector", "tT", t_op)
            E.wt("vector", tok_x)
            E.opd("vector", lambda e: e.tensor_add(out=sbf[:], in0=SD1[:, 0:1024], in1=bouts[:]))
            E.w("vector", "sS", s_g)
            if t >= 2:
                E.wt("vector", st_tok[t % 2])
            E.opd("vector", lambda e: e.tensor_mul(out=sbf[:], in0=sbf[:], in1=gbuf[:]))
            E.op("vector", lambda e, t=t: e.tensor_add(out=outb[t % 2][:], in0=sbf[:], in1=xt[t % 2][:]), inc="vV")
            v_b[t] = E.cnt["vV"]
            E.w("sync", "vV", v_b[t])
            st_tok[t % 2] = PS.dma("sync", lambda e, t=t: e.dma_start(out=out_d[t * 128:(t + 1) * 128, :], in_=outb[t % 2][:]))

        E.w("sync", "st0", E.cnt["st0"])
        E.w("sync", "st1", E.cnt["st1"])

        @block.sync
        def _(e):
            for fn in E.ops["sync"]:
                fn(e)

        @block.tensor
        def _(e):
            for fn in E.ops["tensor"]:
                fn(e)

        @block.vector
        def _(e):
            for fn in E.ops["vector"]:
                fn(e)

        @block.scalar
        def _(e):
            for fn in E.ops["scalar"]:
                fn(e)

        @block.gpsimd
        def _(e):
            for fn in E.ops["gpsimd"]:
                fn(e)

    return nc



def _prep_shared(w_qkv, w_out, b_out, rhs):
    wp = np.empty((NPAIR, 128, 3072), dtype=np.float32)
    for p in range(NPAIR):
        wq = w_qkv[:, p * 128:(p + 1) * 128]
        wk = w_qkv[:, 1024 + p * 128:1024 + (p + 1) * 128]
        wv = w_qkv[:, 2048 + p * 128:2048 + (p + 1) * 128]
        cat = np.concatenate([wq, wk, wv], axis=1)
        wp[p] = np.ascontiguousarray(cat.reshape(8, 128, 384).transpose(1, 0, 2).reshape(128, 3072))
    o2 = np.zeros((128, 2), dtype=np.float32)
    o2[0:64, 0] = 1.0
    o2[64:128, 1] = 1.0
    rhsT = np.ascontiguousarray(rhs.T)
    consts = {
        "wout": np.ascontiguousarray(w_out.reshape(8, 128, 1024).transpose(1, 0, 2).reshape(128, 8192)).astype(np.float32),
        "rhsT": np.ascontiguousarray(rhsT.reshape(8, 128, 1024).transpose(1, 0, 2).reshape(128, 8192)).astype(np.float32),
        "bout": np.ascontiguousarray(np.broadcast_to(b_out[None, :], (128, 1024))).astype(np.float32),
        "ident": np.eye(128, dtype=np.float32).astype(bfloat16),
        "identf": np.eye(128, dtype=np.float32),
        "ident2": np.eye(2, dtype=np.float32).astype(bfloat16),
        "onescol": np.ones((1, 128), dtype=np.float32),
        "onescolb": np.ones((1, 128), dtype=np.float32).astype(bfloat16),
        "ones2": o2.astype(bfloat16),
    }
    return wp, consts


def make_in_maps(x, w_qkv, w_out, b_out, rhs):
    wp, consts = _prep_shared(w_qkv, w_out, b_out, rhs)
    tab64 = _host_tables()
    in_maps = []
    for core in range(8):
        b, half = core // 2, core % 2
        perm = np.concatenate([np.arange(half * NQ, (half + 1) * NQ),
                               np.arange((1 - half) * NQ, (2 - half) * NQ)])
        xp = np.ascontiguousarray(x[b][perm])
        tabs = np.ascontiguousarray(
            tab64.reshape(128, 2, N)[:, :, perm].reshape(128, 2 * N)).astype(bfloat16)
        m = {"xin": xp, "wpair": wp, "tabs": tabs}
        m.update(consts)
        in_maps.append(m)
    return in_maps


def kernel(L0_attn_x, L0_attn_w_qkv, L0_attn_w_out, L0_attn_b_out, L0_ffn_rhs):
    x = np.asarray(L0_attn_x, dtype=np.float32)
    in_maps = make_in_maps(x,
                           np.asarray(L0_attn_w_qkv, dtype=np.float32),
                           np.asarray(L0_attn_w_out, dtype=np.float32),
                           np.asarray(L0_attn_b_out, dtype=np.float32),
                           np.asarray(L0_ffn_rhs, dtype=np.float32))
    nc = build()
    res = run_bass_kernel_spmd(nc, in_maps, core_ids=list(range(8)))
    out = np.empty((B, N, D), dtype=np.float32)
    for core in range(8):
        b, half = core // 2, core % 2
        out[b, half * NQ:(half + 1) * NQ] = res.results[core]["out"]
    return out


# version 16
# speedup vs baseline: 1.6249x; 1.0152x over previous
import numpy as np
import ml_dtypes
import concourse.bass as bass
import concourse.mybir as mybir
from concourse import library_config
from concourse.bass_utils import run_bass_kernel_spmd

F32 = mybir.dt.float32
BF16 = mybir.dt.bfloat16
AF = mybir.ActivationFunctionType
ALU = mybir.AluOpType
AX = mybir.AxisListType
bfloat16 = ml_dtypes.bfloat16

B, N, D = 4, 4096, 1024
HEADS, DH, HALF = 16, 64, 32
NQ = 2048
NPAIR = 8
SCALE = float(DH) ** -0.5
MAGIC = 12582912.0
EPS = 1e-5
INT8_MAX = 127.0


def _host_tables():
    j = np.arange(HALF, dtype=np.float64)
    theta = 10000.0 ** (-j / HALF)
    pos = np.arange(N, dtype=np.float64)
    ang = np.outer(theta, pos)
    cos = np.cos(ang)
    sin = np.sin(ang)
    CCt = np.concatenate([cos, cos, cos, cos], axis=0)
    SSt = np.concatenate([-sin, sin, -sin, sin], axis=0)
    return np.concatenate([CCt, SSt], axis=1)


class _Emit:
    ENGS = ("sync", "tensor", "vector", "scalar", "gpsimd")

    def __init__(self, sems):
        self.ops = {e: [] for e in self.ENGS}
        self.cnt = {k: 0 for k in sems}
        self.sems = sems

    def wt(self, eng, tok):
        if tok is not None:
            self.w(eng, tok[0], tok[1])

    def w(self, eng, sem, val):
        if val <= 0:
            return
        assert val <= self.cnt[sem], f"wait {sem}>={val} emitted={self.cnt[sem]}"
        s = self.sems[sem]
        self.ops[eng].append(lambda e, s=s, v=val: e.wait_ge(s, v))

    def op(self, eng, fn, inc=None, n=1):
        if inc is None:
            self.ops[eng].append(fn)
            return None
        s = self.sems[inc]
        self.cnt[inc] += n
        self.ops[eng].append(lambda e, fn=fn, s=s, n=n: fn(e).then_inc(s, n))
        return self.cnt[inc]

    def opd(self, eng, fn, inc=None, n=1):
        r = self.op(eng, fn, inc=inc, n=n)
        self.ops[eng].append(lambda e: e.drain())
        return r


class _DPool:

    def __init__(self, E, names):
        self.E = E
        self.names = names
        self.i = 0

    def dma(self, eng, fn):
        s = self.names[self.i % len(self.names)]
        self.i += 1
        self.E.w(eng, s, self.E.cnt[s])
        v = self.E.op(eng, fn, inc=s, n=16)
        return (s, v)


def build(debug=False):
    nc = bass.Bass("TRN2", target_bir_lowering=False)

    xin = nc.dram_tensor("xin", [N, D], F32, kind="ExternalInput").ap()
    wpair = nc.dram_tensor("wpair", [NPAIR, 128, 3072], F32, kind="ExternalInput").ap()
    wout_d = nc.dram_tensor("wout", [128, 8192], F32, kind="ExternalInput").ap()
    rhsT_d = nc.dram_tensor("rhsT", [128, 8192], F32, kind="ExternalInput").ap()
    bout_d = nc.dram_tensor("bout", [128, 1024], F32, kind="ExternalInput").ap()
    tabs_d = nc.dram_tensor("tabs", [128, 8192], BF16, kind="ExternalInput").ap()
    ident_d = nc.dram_tensor("ident", [128, 128], BF16, kind="ExternalInput").ap()
    identf_d = nc.dram_tensor("identf", [128, 128], F32, kind="ExternalInput").ap()
    ones2_d = nc.dram_tensor("ones2", [128, 2], BF16, kind="ExternalInput").ap()
    ident2_d = nc.dram_tensor("ident2", [2, 2], BF16, kind="ExternalInput").ap()
    onescol_d = nc.dram_tensor("onescol", [1, 128], F32, kind="ExternalInput").ap()
    onescolb_d = nc.dram_tensor("onescolb", [1, 128], BF16, kind="ExternalInput").ap()
    out_d = nc.dram_tensor("out", [NQ, D], F32, kind="ExternalOutput").ap()
    aoT_d = nc.dram_tensor("aoT", [NPAIR, 128, NQ], BF16, kind="Internal").ap()

    from contextlib import ExitStack
    with ExitStack() as _stk:
        en = _stk.enter_context
        ident_sb = en(nc.sbuf_tensor([128, 128], BF16))
        identf_sb = en(nc.sbuf_tensor([128, 128], F32))
        ones2_sb = en(nc.sbuf_tensor([128, 2], BF16))
        ident2_sb = en(nc.sbuf_tensor([2, 2], BF16))
        onescol_sb = en(nc.sbuf_tensor([1, 128], F32))
        zc2_sb = en(nc.sbuf_tensor([2, 128], BF16))
        xt0 = en(nc.sbuf_tensor([128, 1024], F32))
        xt1 = en(nc.sbuf_tensor([128, 1024], F32))
        st_sb = en(nc.sbuf_tensor([128, 16], F32))
        nm_sb = en(nc.sbuf_tensor([2, 8], F32))
        biasAB = en(nc.sbuf_tensor([128, 4], F32))
        rbc = en(nc.sbuf_tensor([1, 512], F32))
        rbch = en(nc.sbuf_tensor([1, 512], BF16))
        rbcl = en(nc.sbuf_tensor([1, 512], BF16))
        onescolb_sb = en(nc.sbuf_tensor([1, 128], BF16))
        rsT_sb = en(nc.sbuf_tensor([8, 128], F32))
        ls16 = en(nc.sbuf_tensor([128, 16], F32))
        abf = en(nc.sbuf_tensor([128, 76928], BF16))
        af32 = en(nc.sbuf_tensor([128, 10400], F32))
        SD0 = en(nc.psum_tensor([128, 1536], F32))
        SD1 = en(nc.psum_tensor([128, 1024], F32))
        SDB = en(nc.psum_tensor([128, 1024], BF16))
        OA = en(nc.psum_tensor([128, 512], F32))
        MX = en(nc.psum_tensor([128, 512], F32))
        SD = (SD0, SD1)
        sems = {}
        for name in (["ld%d" % i for i in range(8)] + ["st0", "st1", "dg0", "dg1",
                                                       "tT", "vV", "sS", "gG"]):
            sems[name] = en(nc.semaphore(name="sem_" + name))
        block = en(nc.Block())
        E = _Emit(sems)
        PL = _DPool(E, ["ld%d" % i for i in range(8)])
        PS = _DPool(E, ["st0", "st1"])
        PG = _DPool(E, ["dg0", "dg1"])

        xnT = abf[:, 0:32768]
        tabs = abf[:, 32768:40960]
        CC = tabs[:, 0:4096]
        SS = tabs[:, 4096:8192]
        wbf = abf[:, 40960:44032]
        QTb = (abf[:, 44032:46080], abf[:, 46080:48128])
        KTb = (abf[:, 48128:52224], abf[:, 52224:56320])
        Vpb = (abf[:, 56320:60480], abf[:, 60480:64640])
        Tstg = abf[:, 64640:68736]
        Tsw = abf[:, 68736:72832]
        stag = (abf[:, 72832:73856], abf[:, 73856:74880])
        aopair = abf[:, 74880:76928]
        xnb = Tstg[:, 0:1024]
        aoTs = abf[:, 0:16384]
        woutb = abf[:, 16384:24576]
        qrhs = abf[:, 24576:32768]
        qlhs16 = abf[:, 44032:60416]
        qlT = tabs[:, 0:1024]
        qtstg = tabs[:, 1024:2048]
        tr_bf = tabs[:, 2048:3072]

        wstg = af32[:, 0:3072]
        oaev = af32[:, 3072:3584]
        rstg = af32[:, 0:4096]
        trash = af32[:, 4096:5120]
        rsb = af32[:, 5120:6144]
        bouts = af32[:, 6144:7168]
        gbuf = af32[:, 7168:8192]
        sbf = af32[:, 8192:9216]
        p3t = af32[:, 9216:10240]
        rsv = af32[:, 10240:10248]
        pst = af32[:, 10272:10400]
        rcv = af32[:, 10248:10256]
        rsrow = p3t[0:1, 0:1024]
        outb = (rstg[:, 0:1024], rstg[:, 1024:2048])

        st_sum = st_sb[:, 0:1]; st_ssq = st_sb[:, 1:2]; st_mean = st_sb[:, 2:3]
        st_ex2 = st_sb[:, 3:4]; st_m2 = st_sb[:, 4:5]; st_var = st_sb[:, 5:6]
        st_vep = st_sb[:, 6:7]; st_s0 = st_sb[:, 7:8]; st_r0 = st_sb[:, 8:9]
        st_a = st_sb[:, 9:10]; st_b = st_sb[:, 10:11]; st_c = st_sb[:, 11:12]
        st_rstd = st_sb[:, 12:13]; st_amax = st_sb[:, 13:14]
        st_lrcp = st_sb[:, 15:16]

        toks = {}
        for name, dst, src in (("ident", ident_sb, ident_d), ("identf", identf_sb, identf_d),
                               ("ones2", ones2_sb, ones2_d), ("ident2", ident2_sb, ident2_d),
                               ("onescol", onescol_sb, onescol_d), ("onescolb", onescolb_sb, onescolb_d), ("tabs", tabs, tabs_d)):
            toks[name] = PL.dma("sync", lambda e, d=dst, s=src: e.dma_start(out=d[:], in_=s[:]))

        E.op("vector", lambda e: e.memset(zc2_sb[:], 0.0))
        for vb in Vpb:
            vo = vb.rearrange("p (t h c) -> p t h c", t=32, h=2)[:, :, :, 64:65]
            E.op("vector", lambda e, vo=vo: e.memset(vo, 1.0))
        E.op("vector", lambda e: e.nop(), inc="vV")

        xt = (xt0, xt1)
        v_tile_done = [0] * 33
        t_tr_done = [0] * 33
        first_tr = True

        def emit_ln(buf, out_ap):
            E.op("vector", lambda e, b=buf: e.reduce_sum(out=st_sum, in_=b[:], axis=AX.X))
            E.op("vector", lambda e: e.drain())
            E.opd("vector", lambda e: e.tensor_scalar_mul(out=st_mean, in0=st_sum, scalar1=1.0 / D))
            E.opd("vector", lambda e: e.tensor_scalar_mul(out=st_ex2, in0=st_ssq, scalar1=1.0 / D))
            E.opd("vector", lambda e: e.tensor_mul(out=st_m2, in0=st_mean, in1=st_mean))
            E.opd("vector", lambda e: e.tensor_sub(out=st_var, in0=st_ex2, in1=st_m2))
            E.opd("vector", lambda e: e.tensor_scalar_add(out=st_vep, in0=st_var, scalar1=EPS), inc="vV")
            v_vep = E.cnt["vV"]
            E.w("scalar", "vV", v_vep)
            E.opd("scalar", lambda e: e.activation(out=st_s0, in_=st_vep, func=AF.Ln))
            E.op("scalar", lambda e: e.activation(out=st_r0, in_=st_s0, func=AF.Exp, scale=-0.5), inc="sS")
            s_r0 = E.cnt["sS"]
            E.w("vector", "sS", s_r0)
            E.opd("vector", lambda e: e.tensor_mul(out=st_a, in0=st_r0, in1=st_r0))
            E.opd("vector", lambda e: e.tensor_mul(out=st_b, in0=st_a, in1=st_vep))
            E.opd("vector", lambda e: e.tensor_scalar(out=st_c, in0=st_b, scalar1=-0.5, scalar2=1.5, op0=ALU.mult, op1=ALU.add))
            E.opd("vector", lambda e: e.tensor_mul(out=st_rstd, in0=st_r0, in1=st_c))
            E.opd("vector", lambda e, o=out_ap, b=buf: e.tensor_scalar(out=o, in0=b[:], scalar1=st_mean, scalar2=st_rstd, op0=ALU.subtract, op1=ALU.mult), inc="vV")
            return E.cnt["vV"]

        sums = pst[:, 0:32]; ssqs = pst[:, 32:64]
        rstds = pst[:, 64:96]; means = pst[:, 96:128]
        s_sq = [0] * 32
        v_sum = [0] * 32
        for i in range(32):
            if i >= 2:
                E.w("sync", "sS", s_sq[i - 2])
                E.w("sync", "vV", v_sum[i - 2])
            tok_x = PL.dma("sync", lambda e, i=i: e.dma_start(out=xt[i % 2][:], in_=xin[i * 128:(i + 1) * 128, :]))
            E.wt("scalar", tok_x)
            E.op("scalar", lambda e, i=i: e.activation(out=trash[:], in_=xt[i % 2][:], func=AF.Square, accum_out=ssqs[:, i:i + 1]), inc="sS")
            s_sq[i] = E.cnt["sS"]
            E.wt("vector", tok_x)
            E.op("vector", lambda e, i=i: e.reduce_sum(out=sums[:, i:i + 1], in_=xt[i % 2][:], axis=AX.X), inc="vV")
            v_sum[i] = E.cnt["vV"]
        E.w("vector", "sS", s_sq[31])
        E.op("vector", lambda e: e.drain())
        E.opd("vector", lambda e: e.tensor_scalar_mul(out=means[:], in0=sums[:], scalar1=1.0 / D))
        E.opd("vector", lambda e: e.tensor_scalar_mul(out=ssqs[:], in0=ssqs[:], scalar1=1.0 / D))
        E.opd("vector", lambda e: e.tensor_mul(out=sums[:], in0=means[:], in1=means[:]))
        E.opd("vector", lambda e: e.tensor_sub(out=ssqs[:], in0=ssqs[:], in1=sums[:]))
        E.opd("vector", lambda e: e.tensor_scalar_add(out=ssqs[:], in0=ssqs[:], scalar1=EPS), inc="vV")
        v_vep = E.cnt["vV"]
        E.w("scalar", "vV", v_vep)
        E.opd("scalar", lambda e: e.activation(out=sums[:], in_=ssqs[:], func=AF.Ln))
        E.op("scalar", lambda e: e.activation(out=rstds[:], in_=sums[:], func=AF.Exp, scale=-0.5), inc="sS")
        s_r0 = E.cnt["sS"]
        E.w("vector", "sS", s_r0)
        E.opd("vector", lambda e: e.tensor_mul(out=sums[:], in0=rstds[:], in1=rstds[:]))
        E.opd("vector", lambda e: e.tensor_mul(out=sums[:], in0=sums[:], in1=ssqs[:]))
        E.opd("vector", lambda e: e.tensor_scalar(out=sums[:], in0=sums[:], scalar1=-0.5, scalar2=1.5, op0=ALU.mult, op1=ALU.add))
        E.opd("vector", lambda e: e.tensor_mul(out=rstds[:], in0=rstds[:], in1=sums[:]), inc="vV")
        v_stats = E.cnt["vV"]
        state = {"wconv": 0, "last_proj_t": 0, "mx_rd": 0,
                 "vp_ready": [0, 0], "bias_ready": [0, 0],
                 "prev_exp": [0, 0], "prev_av": [0, 0]}
        tok_spill = [None] * NPAIR

        def gen_proj(p):
            par = p % 2
            QT, KT, Vp = QTb[par], KTb[par], Vpb[par]
            E.w("sync", "vV", state["wconv"])
            tok_w = PL.dma("sync", lambda e, p=p: e.dma_start(out=wstg[:], in_=wpair[p]))
            E.wt("vector", tok_w)
            E.w("vector", "tT", state["last_proj_t"])
            if p == 0:
                E.wt("vector", toks["tabs"])
            E.op("vector", lambda e: e.tensor_copy(out=wbf[:], in_=wstg[:]), inc="vV")
            state["wconv"] = E.cnt["vV"]
            wconv = state["wconv"]
            yield

            evac_marks = {}

            def proj_slices(sel_off, nslices):
                for s in range(nslices):
                    if p == 0:
                        rdy = v_tile_done[min(4 * s + 3, 31)]
                        assert rdy > 0
                        E.w("tensor", "vV", max(rdy, wconv))
                    else:
                        E.w("tensor", "vV", max(v_xnT_all, wconv))
                    if (sel_off, s - 1) in evac_marks:
                        E.w("tensor", "vV", evac_marks[(sel_off, s - 1)])
                    for c in range(8):
                        lhsT = wbf[:, c * 384 + sel_off: c * 384 + sel_off + 128]
                        rhs = xnT[:, c * 4096 + s * 512: c * 4096 + (s + 1) * 512]
                        E.op("tensor", lambda e, l=lhsT, r=rhs, c=c:
                             e.matmul(out=SD0[:, 1024:1536], lhsT=l, rhs=r, start=(c == 0), stop=(c == 7)),
                             inc="tT" if c == 7 else None)
                    t_mm = E.cnt["tT"]
                    E.w("vector", "tT", t_mm)
                    E.opd("vector", lambda e, s=s:
                          e.tensor_copy(out=Tstg[:, s * 512:(s + 1) * 512], in_=SD0[:, 1024:1536]), inc="vV")
                    evac_marks[(sel_off, s)] = E.cnt["vV"]
                    yield

            def rope(n, dst):
                for sb, db in ((32, 0), (0, 32), (96, 64), (64, 96)):
                    E.op("vector", lambda e, n=n, sb=sb, db=db: e.tensor_copy(out=Tsw[db:db + 32, 0:n], in_=Tstg[sb:sb + 32, 0:n]))
                E.op("vector", lambda e: e.drain())
                E.op("vector", lambda e, n=n: e.tensor_mul(out=Tsw[:, 0:n], in0=Tsw[:, 0:n], in1=SS[:, 0:n]))
                E.opd("vector", lambda e, n=n: e.tensor_mul(out=Tstg[:, 0:n], in0=Tstg[:, 0:n], in1=CC[:, 0:n]))
                E.opd("vector", lambda e, n=n, dst=dst: e.tensor_add(out=dst, in0=Tstg[:, 0:n], in1=Tsw[:, 0:n]), inc="vV")

            def norms_sq(src, ncols):
                E.op("vector", lambda e, src=src, n=ncols: e.tensor_mul(out=Tsw[:, 0:n], in0=src, in1=src), inc="vV")
                return E.cnt["vV"]

            def norms_mm(v_sq, ncols, out_col):
                E.w("tensor", "vV", max(v_sq, state["mx_rd"]))
                nsl = ncols // 512
                for s in range(nsl):
                    E.op("tensor", lambda e, s=s, nsl=nsl: e.matmul(
                        out=MX[0:2, :], lhsT=ones2_sb[:],
                        rhs=Tsw[:, s * 512:(s + 1) * 512], start=(s == 0), stop=(s == nsl - 1)),
                        inc="tT" if s == nsl - 1 else None)
                t_nm = E.cnt["tT"]
                E.w("vector", "tT", t_nm)
                E.op("vector", lambda e: e.reduce_max(out=st_sb[0:2, 0:1], in_=MX[0:2, :], axis=AX.X))
                E.op("vector", lambda e: e.drain())
                E.opd("vector", lambda e, oc=out_col: e.tensor_copy(out=nm_sb[:, oc:oc + 1], in_=st_sb[0:2, 0:1]), inc="vV")
                state["mx_rd"] = E.cnt["vV"]

            yield from proj_slices(0, 4)
            rope(2048, QT[:, :])
            v_sq_q = norms_sq(QT[:, :], 2048)
            yield
            yield
            norms_mm(v_sq_q, 2048, 0)
            yield
            yield from proj_slices(128, 8)
            rope(4096, KT[:, :])
            v_sq_k = norms_sq(KT[:, :], 4096)
            yield
            yield
            norms_mm(v_sq_k, 4096, 1)
            yield
            E.opd("vector", lambda e: e.tensor_add(out=nm_sb[:, 2:3], in0=nm_sb[:, 0:1], in1=nm_sb[:, 1:2]), inc="vV")
            E.opd("vector", lambda e: e.tensor_scalar_mul(out=nm_sb[:, 4:5], in0=nm_sb[:, 2:3], scalar1=SCALE * 0.5))
            E.op("vector", lambda e: e.tensor_scalar(out=zc2_sb[:], in0=zc2_sb[:], scalar1=nm_sb[:, 4:5], scalar2=None, op0=ALU.add), inc="vV")
            v_mb = E.cnt["vV"]
            E.w("tensor", "vV", max(v_mb, state["mx_rd"]))
            if p == 0:
                E.wt("tensor", toks["ident2"])
                E.wt("tensor", toks["ones2"])
            E.op("tensor", lambda e: e.matmul(out=MX[:, 0:2], lhsT=zc2_sb[:], rhs=ident2_sb[:], start=True, stop=True), inc="tT")
            t_mb = E.cnt["tT"]
            E.w("vector", "tT", t_mb)
            E.op("vector", lambda e, par=par: e.tensor_scalar_mul(out=biasAB[:, par * 2:par * 2 + 2], in0=MX[:, 0:2], scalar1=-1.0))
            E.opd("vector", lambda e: e.memset(zc2_sb[:], 0.0), inc="vV")
            state["bias_ready"][par] = E.cnt["vV"]
            state["mx_rd"] = E.cnt["vV"]
            yield
            vs_inner = proj_slices(256, 8)
            vp_prev = 0

            def vp_tr(s):
                nonlocal vp_prev
                E.w("tensor", "vV", evac_marks[(256, s)])
                if vp_prev:
                    E.w("tensor", "vV", vp_prev)
                for j in range(4):
                    E.op("tensor", lambda e, s=s, j=j: e.transpose(
                        out=SDB[:, j * 128:(j + 1) * 128],
                        in_=Tstg[:, s * 512 + j * 128: s * 512 + (j + 1) * 128],
                        identity=ident_sb[:]), inc="tT" if j == 3 else None)
                t_vt = E.cnt["tT"]
                E.w("vector", "tT", t_vt)
                dst = Vp.rearrange("p (t h c) -> p t h c", t=32, h=2)[:, 4 * s:4 * s + 4, :, 0:64]
                src = SDB[:, 0:512].rearrange("p (j h c) -> p j h c", j=4, h=2, c=64)[:, :, :, 0:64]
                E.op("vector", lambda e, dst=dst, src=src: e.tensor_copy(out=dst, in_=src), inc="vV")
                vp_prev = E.cnt["vV"]

            for s in range(8):
                next(vs_inner)
                if s >= 1:
                    vp_tr(s - 1)
                yield
            vp_tr(7)
            state["last_proj_t"] = E.cnt["tT"]
            state["vp_ready"][par] = E.cnt["vV"]

        def run_gen(g):
            if g is not None:
                for _ in g:
                    pass

        gen0 = gen_proj(0)
        PUMP0 = {3: 1, 4: 1, 8: 1, 12: 1, 16: 1, 17: 1, 18: 1, 19: 1,
                 20: 1, 21: 1, 22: 1, 23: 1, 24: 1, 25: 1, 28: 1}
        xnb2 = (Tsw[:, 2048:3072], Tsw[:, 3072:4096])
        v_nrm = [0] * 32
        for i in range(32):
            if i < 2:
                E.w("sync", "sS", s_sq[30 + i])
                E.w("sync", "vV", v_sum[30 + i])
            else:
                E.w("sync", "vV", v_nrm[i - 2])
            tok_x = PL.dma("sync", lambda e, i=i: e.dma_start(out=xt[i % 2][:], in_=xin[i * 128:(i + 1) * 128, :]))
            E.wt("vector", tok_x)
            if i >= 2:
                E.w("vector", "tT", t_tr_done[i - 2])
            E.op("vector", lambda e, i=i: e.tensor_scalar(out=xnb2[i % 2][:], in0=xt[i % 2][:], scalar1=means[:, i:i + 1], scalar2=rstds[:, i:i + 1], op0=ALU.subtract, op1=ALU.mult), inc="vV")
            v_nrm[i] = E.cnt["vV"]
            if i >= 1:
                E.w("vector", "tT", t_tr_done[i - 1])
                dst = xnT.rearrange("p (c t) -> p c t", c=8)[:, :, (i - 1) * 128:i * 128]
                E.op("vector", lambda e, dst=dst: e.tensor_copy(out=dst, in_=SDB[:, 0:1024]), inc="vV")
                v_tile_done[i - 1] = E.cnt["vV"]
            E.w("tensor", "vV", v_nrm[i])
            if i >= 1:
                E.w("tensor", "vV", v_tile_done[i - 1])
            if first_tr:
                E.wt("tensor", toks["ident"])
                first_tr = False
            for c in range(8):
                E.op("tensor", lambda e, c=c, i=i: e.transpose(out=SDB[:, c * 128:(c + 1) * 128], in_=xnb2[i % 2][:, c * 128:(c + 1) * 128], identity=ident_sb[:]), inc="tT" if c == 7 else None)
            t_tr_done[i] = E.cnt["tT"]
            for _ in range(PUMP0.get(i, 0)):
                next(gen0, None)
        E.w("vector", "tT", t_tr_done[31])
        dst = xnT.rearrange("p (c t) -> p c t", c=8)[:, :, 31 * 128:32 * 128]
        E.op("vector", lambda e, dst=dst: e.tensor_copy(out=dst, in_=SDB[:, 0:1024]), inc="vV")
        v_tile_done[31] = E.cnt["vV"]
        v_xnT_all = E.cnt["vV"]
        run_gen(gen0)


        for p in range(NPAIR):
            par = p % 2
            QT, KT, Vp = QTb[par], KTb[par], Vpb[par]
            bias = biasAB[:, par * 2:par * 2 + 2]
            gen = gen_proj(p + 1) if p + 1 < NPAIR else None

            import os as _os
            if _os.environ.get("KDBG"):
                print(f"pair {p}: vp_ready={state['vp_ready']}, bias={state['bias_ready']}, vV={E.cnt['vV']}")
            E.w("tensor", "vV", state["vp_ready"][par])
            GRP = [(2 * i, 2) for i in range(16)]
            exp_done = dict()
            av_done = dict()
            G = 0
            v_oa_free = state.get("prev_oa", 0)
            v_rb = state.get("prev_rb", 0)
            v_norm_prev = state.get("prev_norm", 0)
            pending = []

            def pop_pending():
                nonlocal v_oa_free, v_norm_prev, v_rb
                fn = pending.pop(0)
                r = fn()
                if fn.kind == "tail":
                    v_oa_free, v_rb = r
                else:
                    v_norm_prev = r
            gen_budget = 0.0
            first_qhip = True
            for q in range(4):
                for hip in range(2):
                    g0 = G
                    gl = g0 + len(GRP) - 1
                    for gi, (base, ln) in enumerate(GRP):
                        g = g0 + gi
                        bank = SD[g % 2]
                        if g >= 2:
                            E.w("tensor", "sS", exp_done[g - 2])
                        elif state["prev_exp"][g % 2]:
                            E.w("tensor", "sS", state["prev_exp"][g % 2])
                        for k in range(ln):
                            ci = base + k
                            E.op("tensor", lambda e, ci=ci, k=k, q=q, hip=hip, bank=bank, KT=KT, QT=QT: e.matmul(
                                out=bank[:, k * 512:(k + 1) * 512],
                                lhsT=KT[hip * 64:(hip + 1) * 64, ci * 128:(ci + 1) * 128],
                                rhs=QT[hip * 64:(hip + 1) * 64, q * 512:(q + 1) * 512],
                                start=True, stop=True), inc="tT" if k == ln - 1 else None)
                        t_sc = E.cnt["tT"]
                        E.w("scalar", "tT", t_sc)
                        if g >= 2:
                            E.w("scalar", "tT", av_done[g - 2])
                        elif state["prev_av"][g % 2]:
                            E.w("scalar", "tT", state["prev_av"][g % 2])
                        if first_qhip and gi == 0:
                            E.w("scalar", "vV", state["bias_ready"][par])
                        E.op("scalar", lambda e, ln=ln, g=g, bank=bank, bias=bias, hip=hip: e.activation(
                            out=stag[g % 2][:, 0:ln * 512], in_=bank[:, 0:ln * 512], func=AF.Exp,
                            bias=bias[:, hip:hip + 1], scale=SCALE), inc="sS")
                        exp_done[g] = E.cnt["sS"]
                        while pending and (pending[0].kind == "tail" or gi >= 5):
                            pop_pending()
                        gen_budget += 39.0 / 128.0
                        while gen is not None and gen_budget >= 1.0:
                            gen_budget -= 1.0
                            try:
                                next(gen)
                            except StopIteration:
                                gen = None
                        if gi >= 1:
                            gp = g - 1
                            E.w("tensor", "sS", exp_done[gp])
                            if gi == 1:
                                E.w("tensor", "vV", v_oa_free)
                                E.w("tensor", "vV", v_norm_prev)
                            bb, bl = GRP[gi - 1]
                            for k in range(bl):
                                ci = bb + k
                                E.op("tensor", lambda e, ci=ci, k=k, gp=gp, hip=hip, Vp=Vp,
                                     st_=(gp == g0 and k == 0): e.matmul(
                                    out=OA[0:65, :], lhsT=Vp[:, ci * 130 + hip * 65: ci * 130 + hip * 65 + 65],
                                    rhs=stag[gp % 2][:, k * 512:(k + 1) * 512],
                                    start=st_, stop=False), inc="tT" if k == bl - 1 else None)
                            av_done[gp] = E.cnt["tT"]
                    G = gl + 1
                    first_qhip = False

                    def _tail(gl=gl, hip=hip, Vp=Vp,
                              spill_wait=(p >= 1 and q == 0 and hip == 0)):
                        E.w("tensor", "sS", exp_done[gl])
                        bb, bl = GRP[-1]
                        for k in range(bl):
                            ci = bb + k
                            E.op("tensor", lambda e, ci=ci, k=k, gl=gl, hip=hip, Vp=Vp,
                                 sp_=(k == bl - 1): e.matmul(
                                out=OA[0:65, :], lhsT=Vp[:, ci * 130 + hip * 65: ci * 130 + hip * 65 + 65],
                                rhs=stag[gl % 2][:, k * 512:(k + 1) * 512],
                                start=False, stop=sp_), inc="tT" if k == bl - 1 else None)
                        av_done[gl] = E.cnt["tT"]
                        E.w("vector", "tT", av_done[gl])
                        if spill_wait:
                            E.wt("vector", tok_spill[p - 1])
                        E.opd("vector", lambda e: e.tensor_copy(out=oaev[0:65, :], in_=OA[0:65, :]), inc="vV")
                        oa_free = E.cnt["vV"]
                        E.opd("vector", lambda e: e.reciprocal(out=rbc[0:1, :], in_=oaev[64:65, :]))
                        E.opd("vector", lambda e: e.tensor_copy(out=rbch[0:1, :], in_=rbc[0:1, :]))
                        E.opd("vector", lambda e: e.scalar_tensor_tensor(out=rbcl[0:1, :], in0=rbch[0:1, :], scalar=-1.0, in1=rbc[0:1, :], op0=ALU.mult, op1=ALU.add), inc="vV")
                        return (oa_free, E.cnt["vV"])
                    _tail.kind = "tail"
                    pending.append(_tail)

                    def _norm(q=q, hip=hip, first=(p == 0 and q == 0 and hip == 0)):
                        E.w("tensor", "vV", max(v_rb, state["mx_rd"]))
                        if first:
                            E.wt("tensor", toks["onescolb"])
                        E.op("tensor", lambda e: e.matmul(out=MX[0:64, :], lhsT=onescolb_sb[0:1, 0:64], rhs=rbch[0:1, :], start=True, stop=False))
                        E.op("tensor", lambda e: e.matmul(out=MX[0:64, :], lhsT=onescolb_sb[0:1, 0:64], rhs=rbcl[0:1, :], start=False, stop=True), inc="tT")
                        t_bc = E.cnt["tT"]
                        E.w("vector", "tT", t_bc)
                        E.opd("vector", lambda e, q=q, hip=hip: e.tensor_mul(
                            out=aopair[hip * 64:(hip + 1) * 64, q * 512:(q + 1) * 512],
                            in0=oaev[0:64, :], in1=MX[0:64, :]), inc="vV")
                        state["mx_rd"] = E.cnt["vV"]
                        return E.cnt["vV"]
                    _norm.kind = "norm"
                    pending.append(_norm)
            while pending:
                pop_pending()
            run_gen(gen)
            state["prev_oa"] = v_oa_free
            state["prev_rb"] = v_rb
            state["prev_norm"] = v_norm_prev
            state["prev_exp"] = [exp_done[126], exp_done[127]]
            state["prev_av"] = [av_done[126], av_done[127]]
            v_ao = E.cnt["vV"]
            E.w("gpsimd", "vV", v_ao)
            tok_spill[p] = PG.dma("gpsimd", lambda e, p=p: e.dma_start(out=aoT_d[p], in_=aopair[:]))

        p2_v = E.cnt["vV"]; p2_t = E.cnt["tT"]; p2_s = E.cnt["sS"]

        E.w("sync", "vV", p2_v)
        E.wt("sync", tok_spill[NPAIR - 2])
        E.wt("sync", tok_spill[NPAIR - 1])
        v_prev = 0
        for h in range(2):
            if h == 1:
                E.w("sync", "vV", v_prev)
            tok_wo = PL.dma("sync", lambda e, h=h: e.dma_start(out=rstg[:], in_=wout_d[:, h * 4096:(h + 1) * 4096]))
            E.wt("vector", tok_wo)
            if h == 0:
                E.w("vector", "tT", p2_t)
            E.op("vector", lambda e, h=h: e.tensor_copy(out=woutb[:, h * 4096:(h + 1) * 4096], in_=rstg[:]), inc="vV")
            v_prev = E.cnt["vV"]
        v_qq = 0
        for h in range(2):
            E.w("sync", "vV", v_prev if h == 0 else v_qq)
            tok_rh = PL.dma("sync", lambda e, h=h: e.dma_start(out=rstg[:], in_=rhsT_d[:, h * 4096:(h + 1) * 4096]))
            for c4 in range(4):
                cc = h * 4 + c4
                E.wt("vector", tok_rh)
                E.op("vector", lambda e, c4=c4: e.reduce_max(out=st_amax, in_=rstg[:, c4 * 1024:(c4 + 1) * 1024], axis=AX.X, apply_absolute_value=True))
                E.op("vector", lambda e: e.drain())
                E.opd("vector", lambda e: e.tensor_scalar_max(out=st_amax, in0=st_amax, scalar1=1e-12))
                E.opd("vector", lambda e, cc=cc: e.tensor_scalar_mul(out=ls16[:, cc:cc + 1], in0=st_amax, scalar1=1.0 / INT8_MAX))
                E.opd("vector", lambda e, cc=cc: e.reciprocal(out=st_lrcp, in_=ls16[:, cc:cc + 1]))
                E.opd("vector", lambda e, c4=c4: e.tensor_scalar(out=p3t[:], in0=rstg[:, c4 * 1024:(c4 + 1) * 1024], scalar1=st_lrcp, scalar2=MAGIC, op0=ALU.mult, op1=ALU.add))
                E.op("vector", lambda e: e.tensor_scalar(out=qtstg[:], in0=p3t[:], scalar1=MAGIC, scalar2=None, op0=ALU.subtract), inc="vV")
                v_q8 = E.cnt["vV"]
                E.w("tensor", "vV", v_q8)
                if cc == 0:
                    E.wt("tensor", toks["identf"])
                for fc in range(8):
                    E.op("tensor", lambda e, fc=fc: e.transpose(
                        out=SDB[:, fc * 128:(fc + 1) * 128], in_=qtstg[:, fc * 128:(fc + 1) * 128],
                        identity=ident_sb[:]), inc="tT" if fc == 7 else None)
                t_qb = E.cnt["tT"]
                E.w("vector", "tT", t_qb)
                dst = qrhs.rearrange("p (f c) -> p f c", f=8)[:, :, cc * 128:(cc + 1) * 128]
                src = SDB.rearrange("p (f c) -> p f c", f=8)
                E.op("vector", lambda e, dst=dst, src=src: e.tensor_copy(out=dst, in_=src), inc="vV")
                v_qq = E.cnt["vV"]
        v_qrhs = E.cnt["vV"]
        E.opd("vector", lambda e: e.tensor_copy(out=rsv[:, 0:8], in_=ls16[:, 0:8]), inc="vV")
        v_rsv = E.cnt["vV"]
        E.w("tensor", "vV", max(v_qrhs, v_rsv))
        E.op("tensor", lambda e: e.transpose(out=MX[0:8, 0:128], in_=rsv[:], identity=identf_sb[:]), inc="tT")
        t_rs = E.cnt["tT"]
        E.w("vector", "tT", t_rs)
        E.op("vector", lambda e: e.tensor_copy(out=rsT_sb[:], in_=MX[0:8, 0:128]), inc="vV")
        v_rsT = E.cnt["vV"]
        E.w("gpsimd", "vV", v_rsT)
        tok_rr = PG.dma("gpsimd", lambda e: e.dma_start(out=rsrow, in_=rsT_sb[:]))
        E.wt("tensor", tok_rr)
        E.op("tensor", lambda e: e.matmul(out=SD0[:, 0:512], lhsT=onescol_sb[:], rhs=rsrow[0:1, 0:512], start=True, stop=True))
        E.op("tensor", lambda e: e.matmul(out=SD0[:, 512:1024], lhsT=onescol_sb[:], rhs=rsrow[0:1, 512:1024], start=True, stop=True), inc="tT")
        t_rsb = E.cnt["tT"]
        E.w("vector", "tT", t_rsb)
        E.op("vector", lambda e: e.tensor_copy(out=rsb[:, 0:512], in_=SD0[:, 0:512]))
        E.opd("vector", lambda e: e.tensor_copy(out=rsb[:, 512:1024], in_=SD0[:, 512:1024]), inc="vV")
        v_rsb = E.cnt["vV"]
        tok_bo = PL.dma("sync", lambda e: e.dma_start(out=bouts[:], in_=bout_d[:]))
        aoT_src = aoT_d.rearrange("p q c -> q p c")
        aoT_dst = aoTs.rearrange("q (p c) -> q p c", p=NPAIR)
        tok_ao = PL.dma("sync", lambda e, s=aoT_src, d=aoT_dst: e.dma_start(out=d, in_=s))

        v_a = [0] * 16
        for t in range(16):
            if t >= 2:
                E.w("sync", "vV", v_a[t - 2])
            tok_x = PL.dma("sync", lambda e, t=t: e.dma_start(out=xt[t % 2][:], in_=xin[t * 128:(t + 1) * 128, :]))
            E.wt("scalar", tok_x)
            E.op("scalar", lambda e, t=t: e.activation(out=tr_bf[:], in_=xt[t % 2][:], func=AF.Square, accum_out=st_ssq), inc="sS")
            s_sq = E.cnt["sS"]
            E.wt("vector", tok_x)
            E.w("vector", "sS", s_sq)
            if t == 0:
                E.w("vector", "vV", v_rsb)
            emit_ln(xt[t % 2], sbf[:])
            E.op("vector", lambda e: e.reduce_max(out=st_amax, in_=sbf[:], axis=AX.X, apply_absolute_value=True))
            E.op("vector", lambda e: e.drain())
            E.opd("vector", lambda e: e.tensor_scalar_max(out=st_amax, in0=st_amax, scalar1=1e-12))
            E.opd("vector", lambda e, t=t: e.tensor_scalar_mul(out=ls16[:, t:t + 1], in0=st_amax, scalar1=1.0 / INT8_MAX))
            E.opd("vector", lambda e, t=t: e.reciprocal(out=st_lrcp, in_=ls16[:, t:t + 1]))
            E.opd("vector", lambda e: e.tensor_scalar(out=p3t[:], in0=sbf[:], scalar1=st_lrcp, scalar2=MAGIC, op0=ALU.mult, op1=ALU.add))
            E.op("vector", lambda e, t=t: e.tensor_scalar(out=qlhs16[:, t * 1024:(t + 1) * 1024], in0=p3t[:], scalar1=MAGIC, scalar2=None, op0=ALU.subtract), inc="vV")
            v_a[t] = E.cnt["vV"]
        v_a_all = E.cnt["vV"]

        st_tok = [None, None]
        v_b = [0] * 16
        for t in range(16):
            if t == 0:
                E.w("sync", "vV", v_a_all)
            if t >= 2:
                E.w("sync", "vV", v_b[t - 2])
            tok_x = PL.dma("sync", lambda e, t=t: e.dma_start(out=xt[t % 2][:], in_=xin[t * 128:(t + 1) * 128, :]))
            if t == 0:
                E.w("tensor", "vV", v_a_all)
                E.wt("tensor", tok_ao)
            else:
                E.w("tensor", "vV", v_b[t - 1])
            for c in range(8):
                E.op("tensor", lambda e, c=c, t=t: e.transpose(out=SDB[:, c * 128:(c + 1) * 128], in_=qlhs16[:, t * 1024 + c * 128: t * 1024 + (c + 1) * 128], identity=ident_sb[:]), inc="tT" if c == 7 else None)
            t_ql = E.cnt["tT"]
            E.w("vector", "tT", t_ql)
            E.op("vector", lambda e: e.tensor_copy(out=qlT[:], in_=SDB[:, 0:1024]), inc="vV")
            v_qlT = E.cnt["vV"]

            E.w("tensor", "vV", v_qlT)
            for c in range(8):
                for hh in range(2):
                    E.op("tensor", lambda e, c=c, hh=hh: e.matmul(
                        out=SD0[:, hh * 512:(hh + 1) * 512], lhsT=qlT[:, c * 128:(c + 1) * 128],
                        rhs=qrhs[:, c * 1024 + hh * 512: c * 1024 + (hh + 1) * 512],
                        start=(c == 0), stop=(c == 7)), inc="tT" if (c == 7 and hh == 1) else None)
            t_ffn = E.cnt["tT"]
            for pp in range(8):
                for hh in range(2):
                    E.op("tensor", lambda e, pp=pp, hh=hh, t=t: e.matmul(
                        out=SD1[:, hh * 512:(hh + 1) * 512],
                        lhsT=aoTs[:, pp * 2048 + t * 128: pp * 2048 + (t + 1) * 128],
                        rhs=woutb[:, pp * 1024 + hh * 512: pp * 1024 + (hh + 1) * 512],
                        start=(pp == 0), stop=(pp == 7)), inc="tT" if (pp == 7 and hh == 1) else None)
            t_op = E.cnt["tT"]

            E.w("vector", "tT", t_ffn)
            E.opd("vector", lambda e: e.tensor_mul(out=p3t[:], in0=SD0[:, 0:1024], in1=rsb[:]), inc="vV")
            v_g1 = E.cnt["vV"]
            E.w("scalar", "vV", v_g1)
            E.op("scalar", lambda e, t=t: e.activation(out=gbuf[:], in_=p3t[:], func=AF.Silu, scale=ls16[:, t:t + 1]), inc="sS")
            s_g = E.cnt["sS"]
            E.w("vector", "tT", t_op)
            E.wt("vector", tok_x)
            E.opd("vector", lambda e: e.tensor_add(out=sbf[:], in0=SD1[:, 0:1024], in1=bouts[:]))
            E.w("vector", "sS", s_g)
            if t >= 2:
                E.wt("vector", st_tok[t % 2])
            E.opd("vector", lambda e: e.tensor_mul(out=sbf[:], in0=sbf[:], in1=gbuf[:]))
            E.op("vector", lambda e, t=t: e.tensor_add(out=outb[t % 2][:], in0=sbf[:], in1=xt[t % 2][:]), inc="vV")
            v_b[t] = E.cnt["vV"]
            E.w("sync", "vV", v_b[t])
            st_tok[t % 2] = PS.dma("sync", lambda e, t=t: e.dma_start(out=out_d[t * 128:(t + 1) * 128, :], in_=outb[t % 2][:]))

        E.w("sync", "st0", E.cnt["st0"])
        E.w("sync", "st1", E.cnt["st1"])

        @block.sync
        def _(e):
            for fn in E.ops["sync"]:
                fn(e)

        @block.tensor
        def _(e):
            for fn in E.ops["tensor"]:
                fn(e)

        @block.vector
        def _(e):
            for fn in E.ops["vector"]:
                fn(e)

        @block.scalar
        def _(e):
            for fn in E.ops["scalar"]:
                fn(e)

        @block.gpsimd
        def _(e):
            for fn in E.ops["gpsimd"]:
                fn(e)

    return nc



def _prep_shared(w_qkv, w_out, b_out, rhs):
    wp = np.empty((NPAIR, 128, 3072), dtype=np.float32)
    for p in range(NPAIR):
        wq = w_qkv[:, p * 128:(p + 1) * 128]
        wk = w_qkv[:, 1024 + p * 128:1024 + (p + 1) * 128]
        wv = w_qkv[:, 2048 + p * 128:2048 + (p + 1) * 128]
        cat = np.concatenate([wq, wk, wv], axis=1)
        wp[p] = np.ascontiguousarray(cat.reshape(8, 128, 384).transpose(1, 0, 2).reshape(128, 3072))
    o2 = np.zeros((128, 2), dtype=np.float32)
    o2[0:64, 0] = 1.0
    o2[64:128, 1] = 1.0
    rhsT = np.ascontiguousarray(rhs.T)
    consts = {
        "wout": np.ascontiguousarray(w_out.reshape(8, 128, 1024).transpose(1, 0, 2).reshape(128, 8192)).astype(np.float32),
        "rhsT": np.ascontiguousarray(rhsT.reshape(8, 128, 1024).transpose(1, 0, 2).reshape(128, 8192)).astype(np.float32),
        "bout": np.ascontiguousarray(np.broadcast_to(b_out[None, :], (128, 1024))).astype(np.float32),
        "ident": np.eye(128, dtype=np.float32).astype(bfloat16),
        "identf": np.eye(128, dtype=np.float32),
        "ident2": np.eye(2, dtype=np.float32).astype(bfloat16),
        "onescol": np.ones((1, 128), dtype=np.float32),
        "onescolb": np.ones((1, 128), dtype=np.float32).astype(bfloat16),
        "ones2": o2.astype(bfloat16),
    }
    return wp, consts


def make_in_maps(x, w_qkv, w_out, b_out, rhs):
    wp, consts = _prep_shared(w_qkv, w_out, b_out, rhs)
    tab64 = _host_tables()
    in_maps = []
    for core in range(8):
        b, half = core // 2, core % 2
        perm = np.concatenate([np.arange(half * NQ, (half + 1) * NQ),
                               np.arange((1 - half) * NQ, (2 - half) * NQ)])
        xp = np.ascontiguousarray(x[b][perm])
        tabs = np.ascontiguousarray(
            tab64.reshape(128, 2, N)[:, :, perm].reshape(128, 2 * N)).astype(bfloat16)
        m = {"xin": xp, "wpair": wp, "tabs": tabs}
        m.update(consts)
        in_maps.append(m)
    return in_maps


def kernel(L0_attn_x, L0_attn_w_qkv, L0_attn_w_out, L0_attn_b_out, L0_ffn_rhs):
    x = np.asarray(L0_attn_x, dtype=np.float32)
    in_maps = make_in_maps(x,
                           np.asarray(L0_attn_w_qkv, dtype=np.float32),
                           np.asarray(L0_attn_w_out, dtype=np.float32),
                           np.asarray(L0_attn_b_out, dtype=np.float32),
                           np.asarray(L0_ffn_rhs, dtype=np.float32))
    nc = build()
    res = run_bass_kernel_spmd(nc, in_maps, core_ids=list(range(8)))
    out = np.empty((B, N, D), dtype=np.float32)
    for core in range(8):
        b, half = core // 2, core % 2
        out[b, half * NQ:(half + 1) * NQ] = res.results[core]["out"]
    return out


# version 17
# speedup vs baseline: 1.7017x; 1.0473x over previous
import numpy as np
import ml_dtypes
import concourse.bass as bass
import concourse.mybir as mybir
from concourse import library_config
from concourse.bass_utils import run_bass_kernel_spmd

F32 = mybir.dt.float32
BF16 = mybir.dt.bfloat16
AF = mybir.ActivationFunctionType
ALU = mybir.AluOpType
AX = mybir.AxisListType
bfloat16 = ml_dtypes.bfloat16

B, N, D = 4, 4096, 1024
HEADS, DH, HALF = 16, 64, 32
NQ = 2048
NPAIR = 8
SCALE = float(DH) ** -0.5
MAGIC = 12582912.0
EPS = 1e-5
INT8_MAX = 127.0


def _host_tables():
    j = np.arange(HALF, dtype=np.float64)
    theta = 10000.0 ** (-j / HALF)
    pos = np.arange(N, dtype=np.float64)
    ang = np.outer(theta, pos)
    cos = np.cos(ang)
    sin = np.sin(ang)
    CCt = np.concatenate([cos, cos, cos, cos], axis=0)
    SSt = np.concatenate([-sin, sin, -sin, sin], axis=0)
    return np.concatenate([CCt, SSt], axis=1)


class _Emit:
    ENGS = ("sync", "tensor", "vector", "scalar", "gpsimd")

    def __init__(self, sems):
        self.ops = {e: [] for e in self.ENGS}
        self.cnt = {k: 0 for k in sems}
        self.sems = sems

    def wt(self, eng, tok):
        if tok is not None:
            self.w(eng, tok[0], tok[1])

    def w(self, eng, sem, val):
        if val <= 0:
            return
        assert val <= self.cnt[sem], f"wait {sem}>={val} emitted={self.cnt[sem]}"
        s = self.sems[sem]
        self.ops[eng].append(lambda e, s=s, v=val: e.wait_ge(s, v))

    def op(self, eng, fn, inc=None, n=1):
        if inc is None:
            self.ops[eng].append(fn)
            return None
        s = self.sems[inc]
        self.cnt[inc] += n
        self.ops[eng].append(lambda e, fn=fn, s=s, n=n: fn(e).then_inc(s, n))
        return self.cnt[inc]

    def opd(self, eng, fn, inc=None, n=1):
        r = self.op(eng, fn, inc=inc, n=n)
        self.ops[eng].append(lambda e: e.drain())
        return r


class _DPool:

    def __init__(self, E, names):
        self.E = E
        self.names = names
        self.i = 0

    def dma(self, eng, fn):
        s = self.names[self.i % len(self.names)]
        self.i += 1
        self.E.w(eng, s, self.E.cnt[s])
        v = self.E.op(eng, fn, inc=s, n=16)
        return (s, v)


def build(debug=False):
    nc = bass.Bass("TRN2", target_bir_lowering=False)

    xin = nc.dram_tensor("xin", [N, D], F32, kind="ExternalInput").ap()
    wpair = nc.dram_tensor("wpair", [NPAIR, 128, 3072], F32, kind="ExternalInput").ap()
    wout_d = nc.dram_tensor("wout", [128, 8192], F32, kind="ExternalInput").ap()
    rhsT_d = nc.dram_tensor("rhsT", [128, 8192], F32, kind="ExternalInput").ap()
    bout_d = nc.dram_tensor("bout", [128, 1024], F32, kind="ExternalInput").ap()
    tabs_d = nc.dram_tensor("tabs", [128, 8192], BF16, kind="ExternalInput").ap()
    ident_d = nc.dram_tensor("ident", [128, 128], BF16, kind="ExternalInput").ap()
    identf_d = nc.dram_tensor("identf", [128, 128], F32, kind="ExternalInput").ap()
    ones2_d = nc.dram_tensor("ones2", [128, 2], BF16, kind="ExternalInput").ap()
    ident2_d = nc.dram_tensor("ident2", [2, 2], BF16, kind="ExternalInput").ap()
    onescol_d = nc.dram_tensor("onescol", [1, 128], F32, kind="ExternalInput").ap()
    onescolb_d = nc.dram_tensor("onescolb", [1, 128], BF16, kind="ExternalInput").ap()
    out_d = nc.dram_tensor("out", [NQ, D], F32, kind="ExternalOutput").ap()
    aoT_d = nc.dram_tensor("aoT", [NPAIR, 128, NQ], BF16, kind="Internal").ap()

    from contextlib import ExitStack
    with ExitStack() as _stk:
        en = _stk.enter_context
        ident_sb = en(nc.sbuf_tensor([128, 128], BF16))
        identf_sb = en(nc.sbuf_tensor([128, 128], F32))
        ones2_sb = en(nc.sbuf_tensor([128, 2], BF16))
        ident2_sb = en(nc.sbuf_tensor([2, 2], BF16))
        onescol_sb = en(nc.sbuf_tensor([1, 128], F32))
        zc2_sb = en(nc.sbuf_tensor([2, 128], BF16))
        xt0 = en(nc.sbuf_tensor([128, 1024], F32))
        xt1 = en(nc.sbuf_tensor([128, 1024], F32))
        st_sb = en(nc.sbuf_tensor([128, 16], F32))
        nm_sb = en(nc.sbuf_tensor([2, 8], F32))
        biasAB = en(nc.sbuf_tensor([128, 4], F32))
        rbc = en(nc.sbuf_tensor([1, 512], F32))
        rbch = en(nc.sbuf_tensor([1, 512], BF16))
        rbcl = en(nc.sbuf_tensor([1, 512], BF16))
        onescolb_sb = en(nc.sbuf_tensor([1, 128], BF16))
        rsT_sb = en(nc.sbuf_tensor([8, 128], F32))
        ls16 = en(nc.sbuf_tensor([128, 32], F32))
        abf = en(nc.sbuf_tensor([128, 76928], BF16))
        af32 = en(nc.sbuf_tensor([128, 10400], F32))
        SD0 = en(nc.psum_tensor([128, 1536], F32))
        SD1 = en(nc.psum_tensor([128, 1024], F32))
        SDB = en(nc.psum_tensor([128, 1024], BF16))
        OA = en(nc.psum_tensor([128, 512], F32))
        MX = en(nc.psum_tensor([128, 512], F32))
        SD = (SD0, SD1)
        sems = {}
        for name in (["ld%d" % i for i in range(8)] + ["st0", "st1", "dg0", "dg1",
                                                       "tT", "vV", "sS", "gG"]):
            sems[name] = en(nc.semaphore(name="sem_" + name))
        block = en(nc.Block())
        E = _Emit(sems)
        PL = _DPool(E, ["ld%d" % i for i in range(8)])
        PS = _DPool(E, ["st0", "st1"])
        PG = _DPool(E, ["dg0", "dg1"])

        xnT = abf[:, 0:32768]
        tabs = abf[:, 32768:40960]
        CC = tabs[:, 0:4096]
        SS = tabs[:, 4096:8192]
        wbf = abf[:, 40960:44032]
        QTb = (abf[:, 44032:46080], abf[:, 46080:48128])
        KTb = (abf[:, 48128:52224], abf[:, 52224:56320])
        Vpb = (abf[:, 56320:60480], abf[:, 60480:64640])
        Tstg = abf[:, 64640:68736]
        Tsw = abf[:, 68736:72832]
        stag = (abf[:, 72832:73856], abf[:, 73856:74880])
        aopair = abf[:, 74880:76928]
        xnb = Tstg[:, 0:1024]
        aoTs = abf[:, 0:16384]
        woutb = abf[:, 16384:24576]
        qrhs = abf[:, 24576:32768]
        qlhs16 = abf[:, 44032:60416]
        qlT = tabs[:, 0:1024]
        qtstg = tabs[:, 1024:2048]
        tr_bf = tabs[:, 2048:3072]

        wstg = af32[:, 0:3072]
        oaev = af32[:, 3072:3584]
        rstg = af32[:, 0:4096]
        trash = af32[:, 4096:5120]
        rsb = af32[:, 5120:6144]
        bouts = af32[:, 6144:7168]
        gbuf = af32[:, 7168:8192]
        sbf = af32[:, 8192:9216]
        p3t = af32[:, 9216:10240]
        rsv = af32[:, 10240:10248]
        pst = af32[:, 10272:10400]
        rcv = af32[:, 10248:10256]
        rsrow = p3t[0:1, 0:1024]
        outb = (rstg[:, 0:1024], rstg[:, 1024:2048])

        st_sum = st_sb[:, 0:1]; st_ssq = st_sb[:, 1:2]; st_mean = st_sb[:, 2:3]
        st_ex2 = st_sb[:, 3:4]; st_m2 = st_sb[:, 4:5]; st_var = st_sb[:, 5:6]
        st_vep = st_sb[:, 6:7]; st_s0 = st_sb[:, 7:8]; st_r0 = st_sb[:, 8:9]
        st_a = st_sb[:, 9:10]; st_b = st_sb[:, 10:11]; st_c = st_sb[:, 11:12]
        st_rstd = st_sb[:, 12:13]; st_amax = st_sb[:, 13:14]
        st_lrcp = st_sb[:, 15:16]

        toks = {}
        for name, dst, src in (("ident", ident_sb, ident_d), ("identf", identf_sb, identf_d),
                               ("ones2", ones2_sb, ones2_d), ("ident2", ident2_sb, ident2_d),
                               ("onescol", onescol_sb, onescol_d), ("onescolb", onescolb_sb, onescolb_d), ("tabs", tabs, tabs_d)):
            toks[name] = PL.dma("sync", lambda e, d=dst, s=src: e.dma_start(out=d[:], in_=s[:]))

        E.op("vector", lambda e: e.memset(zc2_sb[:], 0.0))
        for vb in Vpb:
            vo = vb.rearrange("p (t h c) -> p t h c", t=32, h=2)[:, :, :, 64:65]
            E.op("vector", lambda e, vo=vo: e.memset(vo, 1.0))
        E.op("vector", lambda e: e.nop(), inc="vV")

        xt = (xt0, xt1)
        v_tile_done = [0] * 33
        t_tr_done = [0] * 33
        first_tr = True

        def emit_ln(buf, out_ap):
            E.op("vector", lambda e, b=buf: e.reduce_sum(out=st_sum, in_=b[:], axis=AX.X))
            E.op("vector", lambda e: e.drain())
            E.opd("vector", lambda e: e.tensor_scalar_mul(out=st_mean, in0=st_sum, scalar1=1.0 / D))
            E.opd("vector", lambda e: e.tensor_scalar_mul(out=st_ex2, in0=st_ssq, scalar1=1.0 / D))
            E.opd("vector", lambda e: e.tensor_mul(out=st_m2, in0=st_mean, in1=st_mean))
            E.opd("vector", lambda e: e.tensor_sub(out=st_var, in0=st_ex2, in1=st_m2))
            E.opd("vector", lambda e: e.tensor_scalar_add(out=st_vep, in0=st_var, scalar1=EPS), inc="vV")
            v_vep = E.cnt["vV"]
            E.w("scalar", "vV", v_vep)
            E.opd("scalar", lambda e: e.activation(out=st_s0, in_=st_vep, func=AF.Ln))
            E.op("scalar", lambda e: e.activation(out=st_r0, in_=st_s0, func=AF.Exp, scale=-0.5), inc="sS")
            s_r0 = E.cnt["sS"]
            E.w("vector", "sS", s_r0)
            E.opd("vector", lambda e: e.tensor_mul(out=st_a, in0=st_r0, in1=st_r0))
            E.opd("vector", lambda e: e.tensor_mul(out=st_b, in0=st_a, in1=st_vep))
            E.opd("vector", lambda e: e.tensor_scalar(out=st_c, in0=st_b, scalar1=-0.5, scalar2=1.5, op0=ALU.mult, op1=ALU.add))
            E.opd("vector", lambda e: e.tensor_mul(out=st_rstd, in0=st_r0, in1=st_c))
            E.opd("vector", lambda e, o=out_ap, b=buf: e.tensor_scalar(out=o, in0=b[:], scalar1=st_mean, scalar2=st_rstd, op0=ALU.subtract, op1=ALU.mult), inc="vV")
            return E.cnt["vV"]

        sums = pst[:, 0:32]; ssqs = pst[:, 32:64]
        rstds = pst[:, 64:96]; means = pst[:, 96:128]
        s_sq = [0] * 32
        v_sum = [0] * 32
        for i in range(32):
            if i >= 2:
                E.w("sync", "sS", s_sq[i - 2])
                E.w("sync", "vV", v_sum[i - 2])
            tok_x = PL.dma("sync", lambda e, i=i: e.dma_start(out=xt[i % 2][:], in_=xin[i * 128:(i + 1) * 128, :]))
            E.wt("scalar", tok_x)
            E.op("scalar", lambda e, i=i: e.activation(out=trash[:], in_=xt[i % 2][:], func=AF.Square, accum_out=ssqs[:, i:i + 1]), inc="sS")
            s_sq[i] = E.cnt["sS"]
            E.wt("vector", tok_x)
            E.op("vector", lambda e, i=i: e.reduce_sum(out=sums[:, i:i + 1], in_=xt[i % 2][:], axis=AX.X), inc="vV")
            v_sum[i] = E.cnt["vV"]
        E.w("vector", "sS", s_sq[31])
        E.op("vector", lambda e: e.drain())
        E.opd("vector", lambda e: e.tensor_scalar_mul(out=means[:], in0=sums[:], scalar1=1.0 / D))
        E.opd("vector", lambda e: e.tensor_scalar_mul(out=ssqs[:], in0=ssqs[:], scalar1=1.0 / D))
        E.opd("vector", lambda e: e.tensor_mul(out=sums[:], in0=means[:], in1=means[:]))
        E.opd("vector", lambda e: e.tensor_sub(out=ssqs[:], in0=ssqs[:], in1=sums[:]))
        E.opd("vector", lambda e: e.tensor_scalar_add(out=ssqs[:], in0=ssqs[:], scalar1=EPS), inc="vV")
        v_vep = E.cnt["vV"]
        E.w("scalar", "vV", v_vep)
        E.opd("scalar", lambda e: e.activation(out=sums[:], in_=ssqs[:], func=AF.Ln))
        E.op("scalar", lambda e: e.activation(out=rstds[:], in_=sums[:], func=AF.Exp, scale=-0.5), inc="sS")
        s_r0 = E.cnt["sS"]
        E.w("vector", "sS", s_r0)
        E.opd("vector", lambda e: e.tensor_mul(out=sums[:], in0=rstds[:], in1=rstds[:]))
        E.opd("vector", lambda e: e.tensor_mul(out=sums[:], in0=sums[:], in1=ssqs[:]))
        E.opd("vector", lambda e: e.tensor_scalar(out=sums[:], in0=sums[:], scalar1=-0.5, scalar2=1.5, op0=ALU.mult, op1=ALU.add))
        E.opd("vector", lambda e: e.tensor_mul(out=rstds[:], in0=rstds[:], in1=sums[:]), inc="vV")
        v_stats = E.cnt["vV"]
        state = {"wconv": 0, "last_proj_t": 0, "mx_rd": 0,
                 "vp_ready": [0, 0], "bias_ready": [0, 0],
                 "prev_exp": [0, 0], "prev_av": [0, 0]}
        tok_spill = [None] * NPAIR

        def gen_proj(p):
            par = p % 2
            QT, KT, Vp = QTb[par], KTb[par], Vpb[par]
            E.w("sync", "vV", state["wconv"])
            tok_w = PL.dma("sync", lambda e, p=p: e.dma_start(out=wstg[:], in_=wpair[p]))
            E.wt("vector", tok_w)
            E.w("vector", "tT", state["last_proj_t"])
            if p == 0:
                E.wt("vector", toks["tabs"])
            E.op("vector", lambda e: e.tensor_copy(out=wbf[:], in_=wstg[:]), inc="vV")
            state["wconv"] = E.cnt["vV"]
            wconv = state["wconv"]
            yield

            evac_marks = {}

            def proj_slices(sel_off, nslices):
                for s in range(nslices):
                    if p == 0:
                        rdy = v_tile_done[min(4 * s + 3, 31)]
                        assert rdy > 0
                        E.w("tensor", "vV", max(rdy, wconv))
                    else:
                        E.w("tensor", "vV", max(v_xnT_all, wconv))
                    if (sel_off, s - 1) in evac_marks:
                        E.w("tensor", "vV", evac_marks[(sel_off, s - 1)])
                    for c in range(8):
                        lhsT = wbf[:, c * 384 + sel_off: c * 384 + sel_off + 128]
                        rhs = xnT[:, c * 4096 + s * 512: c * 4096 + (s + 1) * 512]
                        E.op("tensor", lambda e, l=lhsT, r=rhs, c=c:
                             e.matmul(out=SD0[:, 1024:1536], lhsT=l, rhs=r, start=(c == 0), stop=(c == 7)),
                             inc="tT" if c == 7 else None)
                    t_mm = E.cnt["tT"]
                    E.w("vector", "tT", t_mm)
                    E.opd("vector", lambda e, s=s:
                          e.tensor_copy(out=Tstg[:, s * 512:(s + 1) * 512], in_=SD0[:, 1024:1536]), inc="vV")
                    evac_marks[(sel_off, s)] = E.cnt["vV"]
                    yield

            def rope(n, dst):
                for sb, db in ((32, 0), (0, 32), (96, 64), (64, 96)):
                    E.op("vector", lambda e, n=n, sb=sb, db=db: e.tensor_copy(out=Tsw[db:db + 32, 0:n], in_=Tstg[sb:sb + 32, 0:n]))
                E.op("vector", lambda e: e.drain())
                E.op("vector", lambda e, n=n: e.tensor_mul(out=Tsw[:, 0:n], in0=Tsw[:, 0:n], in1=SS[:, 0:n]))
                E.opd("vector", lambda e, n=n: e.tensor_mul(out=Tstg[:, 0:n], in0=Tstg[:, 0:n], in1=CC[:, 0:n]))
                E.opd("vector", lambda e, n=n, dst=dst: e.tensor_add(out=dst, in0=Tstg[:, 0:n], in1=Tsw[:, 0:n]), inc="vV")

            def norms_sq(src, ncols):
                E.op("vector", lambda e, src=src, n=ncols: e.tensor_mul(out=Tsw[:, 0:n], in0=src, in1=src), inc="vV")
                return E.cnt["vV"]

            def norms_mm(v_sq, ncols, out_col):
                E.w("tensor", "vV", max(v_sq, state["mx_rd"]))
                nsl = ncols // 512
                for s in range(nsl):
                    E.op("tensor", lambda e, s=s, nsl=nsl: e.matmul(
                        out=MX[0:2, :], lhsT=ones2_sb[:],
                        rhs=Tsw[:, s * 512:(s + 1) * 512], start=(s == 0), stop=(s == nsl - 1)),
                        inc="tT" if s == nsl - 1 else None)
                t_nm = E.cnt["tT"]
                E.w("vector", "tT", t_nm)
                E.op("vector", lambda e: e.reduce_max(out=st_sb[0:2, 0:1], in_=MX[0:2, :], axis=AX.X))
                E.op("vector", lambda e: e.drain())
                E.opd("vector", lambda e, oc=out_col: e.tensor_copy(out=nm_sb[:, oc:oc + 1], in_=st_sb[0:2, 0:1]), inc="vV")
                state["mx_rd"] = E.cnt["vV"]

            yield from proj_slices(0, 4)
            rope(2048, QT[:, :])
            v_sq_q = norms_sq(QT[:, :], 2048)
            yield
            yield
            norms_mm(v_sq_q, 2048, 0)
            yield
            yield from proj_slices(128, 8)
            rope(4096, KT[:, :])
            v_sq_k = norms_sq(KT[:, :], 4096)
            yield
            yield
            norms_mm(v_sq_k, 4096, 1)
            yield
            E.opd("vector", lambda e: e.tensor_add(out=nm_sb[:, 2:3], in0=nm_sb[:, 0:1], in1=nm_sb[:, 1:2]), inc="vV")
            E.opd("vector", lambda e: e.tensor_scalar_mul(out=nm_sb[:, 4:5], in0=nm_sb[:, 2:3], scalar1=SCALE * 0.5))
            E.op("vector", lambda e: e.tensor_scalar(out=zc2_sb[:], in0=zc2_sb[:], scalar1=nm_sb[:, 4:5], scalar2=None, op0=ALU.add), inc="vV")
            v_mb = E.cnt["vV"]
            E.w("tensor", "vV", max(v_mb, state["mx_rd"]))
            if p == 0:
                E.wt("tensor", toks["ident2"])
                E.wt("tensor", toks["ones2"])
            E.op("tensor", lambda e: e.matmul(out=MX[:, 0:2], lhsT=zc2_sb[:], rhs=ident2_sb[:], start=True, stop=True), inc="tT")
            t_mb = E.cnt["tT"]
            E.w("vector", "tT", t_mb)
            E.op("vector", lambda e, par=par: e.tensor_scalar_mul(out=biasAB[:, par * 2:par * 2 + 2], in0=MX[:, 0:2], scalar1=-1.0))
            E.opd("vector", lambda e: e.memset(zc2_sb[:], 0.0), inc="vV")
            state["bias_ready"][par] = E.cnt["vV"]
            state["mx_rd"] = E.cnt["vV"]
            yield
            vs_inner = proj_slices(256, 8)
            vp_prev = 0

            def vp_tr(s):
                nonlocal vp_prev
                E.w("tensor", "vV", evac_marks[(256, s)])
                if vp_prev:
                    E.w("tensor", "vV", vp_prev)
                for j in range(4):
                    E.op("tensor", lambda e, s=s, j=j: e.transpose(
                        out=SDB[:, j * 128:(j + 1) * 128],
                        in_=Tstg[:, s * 512 + j * 128: s * 512 + (j + 1) * 128],
                        identity=ident_sb[:]), inc="tT" if j == 3 else None)
                t_vt = E.cnt["tT"]
                E.w("vector", "tT", t_vt)
                dst = Vp.rearrange("p (t h c) -> p t h c", t=32, h=2)[:, 4 * s:4 * s + 4, :, 0:64]
                src = SDB[:, 0:512].rearrange("p (j h c) -> p j h c", j=4, h=2, c=64)[:, :, :, 0:64]
                E.op("vector", lambda e, dst=dst, src=src: e.tensor_copy(out=dst, in_=src), inc="vV")
                vp_prev = E.cnt["vV"]

            for s in range(8):
                next(vs_inner)
                if s >= 1:
                    vp_tr(s - 1)
                yield
            vp_tr(7)
            state["last_proj_t"] = E.cnt["tT"]
            state["vp_ready"][par] = E.cnt["vV"]

        def run_gen(g):
            if g is not None:
                for _ in g:
                    pass

        gen0 = gen_proj(0)
        PUMP0 = {3: 1, 4: 1, 8: 1, 12: 1, 16: 1, 17: 1, 18: 1, 19: 1,
                 20: 1, 21: 1, 22: 1, 23: 1, 24: 1, 25: 1, 28: 1}
        xnb2 = (Tsw[:, 2048:3072], Tsw[:, 3072:4096])
        v_nrm = [0] * 32
        for i in range(32):
            if i < 2:
                E.w("sync", "sS", s_sq[30 + i])
                E.w("sync", "vV", v_sum[30 + i])
            else:
                E.w("sync", "vV", v_nrm[i - 2])
            tok_x = PL.dma("sync", lambda e, i=i: e.dma_start(out=xt[i % 2][:], in_=xin[i * 128:(i + 1) * 128, :]))
            E.wt("vector", tok_x)
            if i >= 2:
                E.w("vector", "tT", t_tr_done[i - 2])
            E.op("vector", lambda e, i=i: e.tensor_scalar(out=xnb2[i % 2][:], in0=xt[i % 2][:], scalar1=means[:, i:i + 1], scalar2=rstds[:, i:i + 1], op0=ALU.subtract, op1=ALU.mult), inc="vV")
            v_nrm[i] = E.cnt["vV"]
            if i >= 1:
                E.w("vector", "tT", t_tr_done[i - 1])
                dst = xnT.rearrange("p (c t) -> p c t", c=8)[:, :, (i - 1) * 128:i * 128]
                E.op("vector", lambda e, dst=dst: e.tensor_copy(out=dst, in_=SDB[:, 0:1024]), inc="vV")
                v_tile_done[i - 1] = E.cnt["vV"]
            E.w("tensor", "vV", v_nrm[i])
            if i >= 1:
                E.w("tensor", "vV", v_tile_done[i - 1])
            if first_tr:
                E.wt("tensor", toks["ident"])
                first_tr = False
            for c in range(8):
                E.op("tensor", lambda e, c=c, i=i: e.transpose(out=SDB[:, c * 128:(c + 1) * 128], in_=xnb2[i % 2][:, c * 128:(c + 1) * 128], identity=ident_sb[:]), inc="tT" if c == 7 else None)
            t_tr_done[i] = E.cnt["tT"]
            for _ in range(PUMP0.get(i, 0)):
                next(gen0, None)
        E.w("vector", "tT", t_tr_done[31])
        dst = xnT.rearrange("p (c t) -> p c t", c=8)[:, :, 31 * 128:32 * 128]
        E.op("vector", lambda e, dst=dst: e.tensor_copy(out=dst, in_=SDB[:, 0:1024]), inc="vV")
        v_tile_done[31] = E.cnt["vV"]
        v_xnT_all = E.cnt["vV"]
        run_gen(gen0)


        def gen_p3prep():
            W2 = af32[:, 4096:8192]
            lpt = state["last_proj_t"]
            v_wc = 0
            for h in range(2):
                if h == 1:
                    E.w("sync", "vV", v_wc)
                tok_wo = PL.dma("sync", lambda e, h=h: e.dma_start(out=W2[:], in_=wout_d[:, h * 4096:(h + 1) * 4096]))
                E.wt("vector", tok_wo)
                if h == 0:
                    E.w("vector", "tT", lpt)
                E.op("vector", lambda e, h=h: e.tensor_copy(out=woutb[:, h * 4096:(h + 1) * 4096], in_=W2[:]), inc="vV")
                v_wc = E.cnt["vV"]
                yield
            v_qq = 0
            first_t = True
            for h in range(2):
                E.w("sync", "vV", v_wc if h == 0 else v_qq)
                tok_rh = PL.dma("sync", lambda e, h=h: e.dma_start(out=W2[:], in_=rhsT_d[:, h * 4096:(h + 1) * 4096]))
                for c4 in range(4):
                    cc = h * 4 + c4
                    E.wt("vector", tok_rh)
                    E.op("vector", lambda e, c4=c4: e.reduce_max(out=st_amax, in_=W2[:, c4 * 1024:(c4 + 1) * 1024], axis=AX.X, apply_absolute_value=True))
                    E.op("vector", lambda e: e.drain())
                    E.opd("vector", lambda e: e.tensor_scalar_max(out=st_amax, in0=st_amax, scalar1=1e-12))
                    E.opd("vector", lambda e, cc=cc: e.tensor_scalar_mul(out=ls16[:, 16 + cc:17 + cc], in0=st_amax, scalar1=1.0 / INT8_MAX))
                    E.opd("vector", lambda e, cc=cc: e.reciprocal(out=st_lrcp, in_=ls16[:, 16 + cc:17 + cc]))
                    E.opd("vector", lambda e, c4=c4: e.tensor_scalar(out=p3t[:], in0=W2[:, c4 * 1024:(c4 + 1) * 1024], scalar1=st_lrcp, scalar2=MAGIC, op0=ALU.mult, op1=ALU.add))
                    E.op("vector", lambda e: e.tensor_scalar(out=qtstg[:], in0=p3t[:], scalar1=MAGIC, scalar2=None, op0=ALU.subtract), inc="vV")
                    v_q8 = E.cnt["vV"]
                    yield
                    E.w("tensor", "vV", v_q8)
                    if first_t:
                        E.w("tensor", "vV", state["vp_ready"][1])
                        first_t = False
                    for fc in range(8):
                        E.op("tensor", lambda e, fc=fc: e.transpose(
                            out=SDB[:, fc * 128:(fc + 1) * 128], in_=qtstg[:, fc * 128:(fc + 1) * 128],
                            identity=ident_sb[:]), inc="tT" if fc == 7 else None)
                    t_qb = E.cnt["tT"]
                    E.w("vector", "tT", t_qb)
                    if cc == 0:
                        E.w("vector", "tT", lpt)
                    dst = qrhs.rearrange("p (f c) -> p f c", f=8)[:, :, cc * 128:(cc + 1) * 128]
                    src2 = SDB.rearrange("p (f c) -> p f c", f=8)
                    E.op("vector", lambda e, dst=dst, src2=src2: e.tensor_copy(out=dst, in_=src2), inc="vV")
                    v_qq = E.cnt["vV"]
                    yield

        for p in range(NPAIR):
            par = p % 2
            QT, KT, Vp = QTb[par], KTb[par], Vpb[par]
            bias = biasAB[:, par * 2:par * 2 + 2]
            gen = gen_proj(p + 1) if p + 1 < NPAIR else gen_p3prep()

            import os as _os
            if _os.environ.get("KDBG"):
                print(f"pair {p}: vp_ready={state['vp_ready']}, bias={state['bias_ready']}, vV={E.cnt['vV']}")
            E.w("tensor", "vV", state["vp_ready"][par])
            GRP = [(2 * i, 2) for i in range(16)]
            exp_done = dict()
            av_done = dict()
            G = 0
            v_oa_free = state.get("prev_oa", 0)
            v_rb = state.get("prev_rb", 0)
            v_norm_prev = state.get("prev_norm", 0)
            pending = []

            def pop_pending():
                nonlocal v_oa_free, v_norm_prev, v_rb
                fn = pending.pop(0)
                r = fn()
                if fn.kind == "tail":
                    v_oa_free, v_rb = r
                else:
                    v_norm_prev = r
            gen_budget = 0.0
            first_qhip = True
            for q in range(4):
                for hip in range(2):
                    g0 = G
                    gl = g0 + len(GRP) - 1
                    for gi, (base, ln) in enumerate(GRP):
                        g = g0 + gi
                        bank = SD[g % 2]
                        if g >= 2:
                            E.w("tensor", "sS", exp_done[g - 2])
                        elif state["prev_exp"][g % 2]:
                            E.w("tensor", "sS", state["prev_exp"][g % 2])
                        for k in range(ln):
                            ci = base + k
                            E.op("tensor", lambda e, ci=ci, k=k, q=q, hip=hip, bank=bank, KT=KT, QT=QT: e.matmul(
                                out=bank[:, k * 512:(k + 1) * 512],
                                lhsT=KT[hip * 64:(hip + 1) * 64, ci * 128:(ci + 1) * 128],
                                rhs=QT[hip * 64:(hip + 1) * 64, q * 512:(q + 1) * 512],
                                start=True, stop=True), inc="tT" if k == ln - 1 else None)
                        t_sc = E.cnt["tT"]
                        E.w("scalar", "tT", t_sc)
                        if g >= 2:
                            E.w("scalar", "tT", av_done[g - 2])
                        elif state["prev_av"][g % 2]:
                            E.w("scalar", "tT", state["prev_av"][g % 2])
                        if first_qhip and gi == 0:
                            E.w("scalar", "vV", state["bias_ready"][par])
                        E.op("scalar", lambda e, ln=ln, g=g, bank=bank, bias=bias, hip=hip: e.activation(
                            out=stag[g % 2][:, 0:ln * 512], in_=bank[:, 0:ln * 512], func=AF.Exp,
                            bias=bias[:, hip:hip + 1], scale=SCALE), inc="sS")
                        exp_done[g] = E.cnt["sS"]
                        while pending and (pending[0].kind == "tail" or gi >= 5):
                            pop_pending()
                        gen_budget += 39.0 / 128.0
                        while gen is not None and gen_budget >= 1.0:
                            gen_budget -= 1.0
                            try:
                                next(gen)
                            except StopIteration:
                                gen = None
                        if gi >= 1:
                            gp = g - 1
                            E.w("tensor", "sS", exp_done[gp])
                            if gi == 1:
                                E.w("tensor", "vV", v_oa_free)
                                E.w("tensor", "vV", v_norm_prev)
                            bb, bl = GRP[gi - 1]
                            for k in range(bl):
                                ci = bb + k
                                E.op("tensor", lambda e, ci=ci, k=k, gp=gp, hip=hip, Vp=Vp,
                                     st_=(gp == g0 and k == 0): e.matmul(
                                    out=OA[0:65, :], lhsT=Vp[:, ci * 130 + hip * 65: ci * 130 + hip * 65 + 65],
                                    rhs=stag[gp % 2][:, k * 512:(k + 1) * 512],
                                    start=st_, stop=False), inc="tT" if k == bl - 1 else None)
                            av_done[gp] = E.cnt["tT"]
                    G = gl + 1
                    first_qhip = False

                    def _tail(gl=gl, hip=hip, Vp=Vp,
                              spill_wait=(p >= 1 and q == 0 and hip == 0)):
                        E.w("tensor", "sS", exp_done[gl])
                        bb, bl = GRP[-1]
                        for k in range(bl):
                            ci = bb + k
                            E.op("tensor", lambda e, ci=ci, k=k, gl=gl, hip=hip, Vp=Vp,
                                 sp_=(k == bl - 1): e.matmul(
                                out=OA[0:65, :], lhsT=Vp[:, ci * 130 + hip * 65: ci * 130 + hip * 65 + 65],
                                rhs=stag[gl % 2][:, k * 512:(k + 1) * 512],
                                start=False, stop=sp_), inc="tT" if k == bl - 1 else None)
                        av_done[gl] = E.cnt["tT"]
                        E.w("vector", "tT", av_done[gl])
                        if spill_wait:
                            E.wt("vector", tok_spill[p - 1])
                        E.opd("vector", lambda e: e.tensor_copy(out=oaev[0:65, :], in_=OA[0:65, :]), inc="vV")
                        oa_free = E.cnt["vV"]
                        E.opd("vector", lambda e: e.reciprocal(out=rbc[0:1, :], in_=oaev[64:65, :]))
                        E.opd("vector", lambda e: e.tensor_copy(out=rbch[0:1, :], in_=rbc[0:1, :]))
                        E.opd("vector", lambda e: e.scalar_tensor_tensor(out=rbcl[0:1, :], in0=rbch[0:1, :], scalar=-1.0, in1=rbc[0:1, :], op0=ALU.mult, op1=ALU.add), inc="vV")
                        return (oa_free, E.cnt["vV"])
                    _tail.kind = "tail"
                    pending.append(_tail)

                    def _norm(q=q, hip=hip, first=(p == 0 and q == 0 and hip == 0)):
                        E.w("tensor", "vV", max(v_rb, state["mx_rd"]))
                        if first:
                            E.wt("tensor", toks["onescolb"])
                        E.op("tensor", lambda e: e.matmul(out=MX[0:64, :], lhsT=onescolb_sb[0:1, 0:64], rhs=rbch[0:1, :], start=True, stop=False))
                        E.op("tensor", lambda e: e.matmul(out=MX[0:64, :], lhsT=onescolb_sb[0:1, 0:64], rhs=rbcl[0:1, :], start=False, stop=True), inc="tT")
                        t_bc = E.cnt["tT"]
                        E.w("vector", "tT", t_bc)
                        E.opd("vector", lambda e, q=q, hip=hip: e.tensor_mul(
                            out=aopair[hip * 64:(hip + 1) * 64, q * 512:(q + 1) * 512],
                            in0=oaev[0:64, :], in1=MX[0:64, :]), inc="vV")
                        state["mx_rd"] = E.cnt["vV"]
                        return E.cnt["vV"]
                    _norm.kind = "norm"
                    pending.append(_norm)
            while pending:
                pop_pending()
            run_gen(gen)
            state["prev_oa"] = v_oa_free
            state["prev_rb"] = v_rb
            state["prev_norm"] = v_norm_prev
            state["prev_exp"] = [exp_done[126], exp_done[127]]
            state["prev_av"] = [av_done[126], av_done[127]]
            v_ao = E.cnt["vV"]
            E.w("gpsimd", "vV", v_ao)
            tok_spill[p] = PG.dma("gpsimd", lambda e, p=p: e.dma_start(out=aoT_d[p], in_=aopair[:]))

        p2_v = E.cnt["vV"]; p2_t = E.cnt["tT"]; p2_s = E.cnt["sS"]

        E.wt("sync", tok_spill[NPAIR - 2])
        E.wt("sync", tok_spill[NPAIR - 1])
        E.opd("vector", lambda e: e.tensor_copy(out=rsv[:, 0:8], in_=ls16[:, 16:24]), inc="vV")
        v_rsv = E.cnt["vV"]
        E.w("tensor", "vV", max(v_rsv, state["mx_rd"]))
        E.wt("tensor", toks["identf"])
        E.op("tensor", lambda e: e.transpose(out=MX[0:8, 0:128], in_=rsv[:], identity=identf_sb[:]), inc="tT")
        t_rs = E.cnt["tT"]
        E.w("vector", "tT", t_rs)
        E.op("vector", lambda e: e.tensor_copy(out=rsT_sb[:], in_=MX[0:8, 0:128]), inc="vV")
        v_rsT = E.cnt["vV"]
        E.w("gpsimd", "vV", v_rsT)
        tok_rr = PG.dma("gpsimd", lambda e: e.dma_start(out=rsrow, in_=rsT_sb[:]))
        E.wt("tensor", tok_rr)
        E.w("tensor", "sS", p2_s)
        E.op("tensor", lambda e: e.matmul(out=SD0[:, 0:512], lhsT=onescol_sb[:], rhs=rsrow[0:1, 0:512], start=True, stop=True))
        E.op("tensor", lambda e: e.matmul(out=SD0[:, 512:1024], lhsT=onescol_sb[:], rhs=rsrow[0:1, 512:1024], start=True, stop=True), inc="tT")
        t_rsb = E.cnt["tT"]
        E.w("vector", "tT", t_rsb)
        E.op("vector", lambda e: e.tensor_copy(out=rsb[:, 0:512], in_=SD0[:, 0:512]))
        E.opd("vector", lambda e: e.tensor_copy(out=rsb[:, 512:1024], in_=SD0[:, 512:1024]), inc="vV")
        v_rsb = E.cnt["vV"]
        tok_bo = PL.dma("sync", lambda e: e.dma_start(out=bouts[:], in_=bout_d[:]))
        aoT_src = aoT_d.rearrange("p q c -> q p c")
        aoT_dst = aoTs.rearrange("q (p c) -> q p c", p=NPAIR)
        tok_ao = PL.dma("sync", lambda e, s=aoT_src, d=aoT_dst: e.dma_start(out=d, in_=s))

        v_a = [0] * 16
        for t in range(16):
            if t >= 2:
                E.w("sync", "vV", v_a[t - 2])
            tok_x = PL.dma("sync", lambda e, t=t: e.dma_start(out=xt[t % 2][:], in_=xin[t * 128:(t + 1) * 128, :]))
            E.wt("vector", tok_x)
            if t == 0:
                E.w("vector", "vV", v_rsb)
            E.opd("vector", lambda e, t=t: e.tensor_scalar(out=sbf[:], in0=xt[t % 2][:], scalar1=means[:, t:t + 1], scalar2=rstds[:, t:t + 1], op0=ALU.subtract, op1=ALU.mult))
            E.op("vector", lambda e: e.reduce_max(out=st_amax, in_=sbf[:], axis=AX.X, apply_absolute_value=True))
            E.op("vector", lambda e: e.drain())
            E.opd("vector", lambda e: e.tensor_scalar_max(out=st_amax, in0=st_amax, scalar1=1e-12))
            E.opd("vector", lambda e, t=t: e.tensor_scalar_mul(out=ls16[:, t:t + 1], in0=st_amax, scalar1=1.0 / INT8_MAX))
            E.opd("vector", lambda e, t=t: e.reciprocal(out=st_lrcp, in_=ls16[:, t:t + 1]))
            E.opd("vector", lambda e: e.tensor_scalar(out=p3t[:], in0=sbf[:], scalar1=st_lrcp, scalar2=MAGIC, op0=ALU.mult, op1=ALU.add))
            E.op("vector", lambda e, t=t: e.tensor_scalar(out=qlhs16[:, t * 1024:(t + 1) * 1024], in0=p3t[:], scalar1=MAGIC, scalar2=None, op0=ALU.subtract), inc="vV")
            v_a[t] = E.cnt["vV"]
        v_a_all = E.cnt["vV"]

        st_tok = [None, None]
        v_b = [0] * 16
        v_ql = [0] * 16
        v_g1l = [0] * 16
        v_s1l = [0] * 16
        for t in range(16):
            if t == 0:
                E.w("sync", "vV", v_a_all)
            if t >= 2:
                E.w("sync", "vV", v_b[t - 2])
            tok_x = PL.dma("sync", lambda e, t=t: e.dma_start(out=xt[t % 2][:], in_=xin[t * 128:(t + 1) * 128, :]))
            if t == 0:
                E.w("tensor", "vV", v_a_all)
                E.wt("tensor", tok_ao)
            else:
                E.w("tensor", "vV", v_ql[t - 1])
            for c in range(8):
                E.op("tensor", lambda e, c=c, t=t: e.transpose(out=SDB[:, c * 128:(c + 1) * 128], in_=qlhs16[:, t * 1024 + c * 128: t * 1024 + (c + 1) * 128], identity=ident_sb[:]), inc="tT" if c == 7 else None)
            t_ql = E.cnt["tT"]
            E.w("vector", "tT", t_ql)
            E.op("vector", lambda e: e.tensor_copy(out=qlT[:], in_=SDB[:, 0:1024]), inc="vV")
            v_qlT = E.cnt["vV"]
            v_ql[t] = v_qlT

            E.w("tensor", "vV", v_qlT)
            if t >= 1:
                E.w("tensor", "vV", v_g1l[t - 1])
            for c in range(8):
                for hh in range(2):
                    E.op("tensor", lambda e, c=c, hh=hh: e.matmul(
                        out=SD0[:, hh * 512:(hh + 1) * 512], lhsT=qlT[:, c * 128:(c + 1) * 128],
                        rhs=qrhs[:, c * 1024 + hh * 512: c * 1024 + (hh + 1) * 512],
                        start=(c == 0), stop=(c == 7)), inc="tT" if (c == 7 and hh == 1) else None)
            t_ffn = E.cnt["tT"]
            if t >= 1:
                E.w("tensor", "vV", v_s1l[t - 1])
            for pp in range(8):
                for hh in range(2):
                    E.op("tensor", lambda e, pp=pp, hh=hh, t=t: e.matmul(
                        out=SD1[:, hh * 512:(hh + 1) * 512],
                        lhsT=aoTs[:, pp * 2048 + t * 128: pp * 2048 + (t + 1) * 128],
                        rhs=woutb[:, pp * 1024 + hh * 512: pp * 1024 + (hh + 1) * 512],
                        start=(pp == 0), stop=(pp == 7)), inc="tT" if (pp == 7 and hh == 1) else None)
            t_op = E.cnt["tT"]

            E.w("vector", "tT", t_ffn)
            E.opd("vector", lambda e: e.tensor_mul(out=p3t[:], in0=SD0[:, 0:1024], in1=rsb[:]), inc="vV")
            v_g1 = E.cnt["vV"]
            v_g1l[t] = v_g1
            E.w("scalar", "vV", v_g1)
            E.op("scalar", lambda e, t=t: e.activation(out=gbuf[:], in_=p3t[:], func=AF.Silu, scale=ls16[:, t:t + 1]), inc="sS")
            s_g = E.cnt["sS"]
            E.w("vector", "tT", t_op)
            E.wt("vector", tok_x)
            E.opd("vector", lambda e: e.tensor_add(out=sbf[:], in0=SD1[:, 0:1024], in1=bouts[:]), inc="vV")
            v_s1l[t] = E.cnt["vV"]
            E.w("vector", "sS", s_g)
            if t >= 2:
                E.wt("vector", st_tok[t % 2])
            E.opd("vector", lambda e: e.tensor_mul(out=sbf[:], in0=sbf[:], in1=gbuf[:]))
            E.op("vector", lambda e, t=t: e.tensor_add(out=outb[t % 2][:], in0=sbf[:], in1=xt[t % 2][:]), inc="vV")
            v_b[t] = E.cnt["vV"]
            E.w("sync", "vV", v_b[t])
            st_tok[t % 2] = PS.dma("sync", lambda e, t=t: e.dma_start(out=out_d[t * 128:(t + 1) * 128, :], in_=outb[t % 2][:]))

        E.w("sync", "st0", E.cnt["st0"])
        E.w("sync", "st1", E.cnt["st1"])

        @block.sync
        def _(e):
            for fn in E.ops["sync"]:
                fn(e)

        @block.tensor
        def _(e):
            for fn in E.ops["tensor"]:
                fn(e)

        @block.vector
        def _(e):
            for fn in E.ops["vector"]:
                fn(e)

        @block.scalar
        def _(e):
            for fn in E.ops["scalar"]:
                fn(e)

        @block.gpsimd
        def _(e):
            for fn in E.ops["gpsimd"]:
                fn(e)

    return nc



def _prep_shared(w_qkv, w_out, b_out, rhs):
    wp = np.empty((NPAIR, 128, 3072), dtype=np.float32)
    for p in range(NPAIR):
        wq = w_qkv[:, p * 128:(p + 1) * 128]
        wk = w_qkv[:, 1024 + p * 128:1024 + (p + 1) * 128]
        wv = w_qkv[:, 2048 + p * 128:2048 + (p + 1) * 128]
        cat = np.concatenate([wq, wk, wv], axis=1)
        wp[p] = np.ascontiguousarray(cat.reshape(8, 128, 384).transpose(1, 0, 2).reshape(128, 3072))
    o2 = np.zeros((128, 2), dtype=np.float32)
    o2[0:64, 0] = 1.0
    o2[64:128, 1] = 1.0
    rhsT = np.ascontiguousarray(rhs.T)
    consts = {
        "wout": np.ascontiguousarray(w_out.reshape(8, 128, 1024).transpose(1, 0, 2).reshape(128, 8192)).astype(np.float32),
        "rhsT": np.ascontiguousarray(rhsT.reshape(8, 128, 1024).transpose(1, 0, 2).reshape(128, 8192)).astype(np.float32),
        "bout": np.ascontiguousarray(np.broadcast_to(b_out[None, :], (128, 1024))).astype(np.float32),
        "ident": np.eye(128, dtype=np.float32).astype(bfloat16),
        "identf": np.eye(128, dtype=np.float32),
        "ident2": np.eye(2, dtype=np.float32).astype(bfloat16),
        "onescol": np.ones((1, 128), dtype=np.float32),
        "onescolb": np.ones((1, 128), dtype=np.float32).astype(bfloat16),
        "ones2": o2.astype(bfloat16),
    }
    return wp, consts


def make_in_maps(x, w_qkv, w_out, b_out, rhs):
    wp, consts = _prep_shared(w_qkv, w_out, b_out, rhs)
    tab64 = _host_tables()
    in_maps = []
    for core in range(8):
        b, half = core // 2, core % 2
        perm = np.concatenate([np.arange(half * NQ, (half + 1) * NQ),
                               np.arange((1 - half) * NQ, (2 - half) * NQ)])
        xp = np.ascontiguousarray(x[b][perm])
        tabs = np.ascontiguousarray(
            tab64.reshape(128, 2, N)[:, :, perm].reshape(128, 2 * N)).astype(bfloat16)
        m = {"xin": xp, "wpair": wp, "tabs": tabs}
        m.update(consts)
        in_maps.append(m)
    return in_maps


def kernel(L0_attn_x, L0_attn_w_qkv, L0_attn_w_out, L0_attn_b_out, L0_ffn_rhs):
    x = np.asarray(L0_attn_x, dtype=np.float32)
    in_maps = make_in_maps(x,
                           np.asarray(L0_attn_w_qkv, dtype=np.float32),
                           np.asarray(L0_attn_w_out, dtype=np.float32),
                           np.asarray(L0_attn_b_out, dtype=np.float32),
                           np.asarray(L0_ffn_rhs, dtype=np.float32))
    nc = build()
    res = run_bass_kernel_spmd(nc, in_maps, core_ids=list(range(8)))
    out = np.empty((B, N, D), dtype=np.float32)
    for core in range(8):
        b, half = core // 2, core % 2
        out[b, half * NQ:(half + 1) * NQ] = res.results[core]["out"]
    return out


# version 18
# speedup vs baseline: 1.8151x; 1.0666x over previous
import numpy as np
import ml_dtypes
import concourse.bass as bass
import concourse.mybir as mybir
from concourse import library_config
from concourse.bass_utils import run_bass_kernel_spmd

F32 = mybir.dt.float32
BF16 = mybir.dt.bfloat16
AF = mybir.ActivationFunctionType
ALU = mybir.AluOpType
AX = mybir.AxisListType
bfloat16 = ml_dtypes.bfloat16

B, N, D = 4, 4096, 1024
HEADS, DH, HALF = 16, 64, 32
NQ = 2048
NPAIR = 8
SCALE = float(DH) ** -0.5
MAGIC = 12582912.0
EPS = 1e-5
INT8_MAX = 127.0


def _host_tables():
    j = np.arange(HALF, dtype=np.float64)
    theta = 10000.0 ** (-j / HALF)
    pos = np.arange(N, dtype=np.float64)
    ang = np.outer(theta, pos)
    cos = np.cos(ang)
    sin = np.sin(ang)
    CCt = np.concatenate([cos, cos, cos, cos], axis=0)
    SSt = np.concatenate([-sin, sin, -sin, sin], axis=0)
    return np.concatenate([CCt, SSt], axis=1)


class _Emit:
    ENGS = ("sync", "tensor", "vector", "scalar", "gpsimd")

    def __init__(self, sems):
        self.ops = {e: [] for e in self.ENGS}
        self.cnt = {k: 0 for k in sems}
        self.sems = sems

    def wt(self, eng, tok):
        if tok is not None:
            self.w(eng, tok[0], tok[1])

    def w(self, eng, sem, val):
        if val <= 0:
            return
        assert val <= self.cnt[sem], f"wait {sem}>={val} emitted={self.cnt[sem]}"
        s = self.sems[sem]
        self.ops[eng].append(lambda e, s=s, v=val: e.wait_ge(s, v))

    def op(self, eng, fn, inc=None, n=1):
        if inc is None:
            self.ops[eng].append(fn)
            return None
        s = self.sems[inc]
        self.cnt[inc] += n
        self.ops[eng].append(lambda e, fn=fn, s=s, n=n: fn(e).then_inc(s, n))
        return self.cnt[inc]

    def opd(self, eng, fn, inc=None, n=1):
        r = self.op(eng, fn, inc=inc, n=n)
        self.ops[eng].append(lambda e: e.drain())
        return r


class _DPool:

    def __init__(self, E, names):
        self.E = E
        self.names = names
        self.i = 0

    def dma(self, eng, fn):
        s = self.names[self.i % len(self.names)]
        self.i += 1
        self.E.w(eng, s, self.E.cnt[s])
        v = self.E.op(eng, fn, inc=s, n=16)
        return (s, v)


def build(debug=False):
    nc = bass.Bass("TRN2", target_bir_lowering=False)

    xin = nc.dram_tensor("xin", [N, D], F32, kind="ExternalInput").ap()
    wpair = nc.dram_tensor("wpair", [NPAIR, 128, 3072], F32, kind="ExternalInput").ap()
    wout_d = nc.dram_tensor("wout", [128, 8192], F32, kind="ExternalInput").ap()
    rhsT_d = nc.dram_tensor("rhsT", [128, 8192], F32, kind="ExternalInput").ap()
    bout_d = nc.dram_tensor("bout", [128, 1024], F32, kind="ExternalInput").ap()
    tabs_d = nc.dram_tensor("tabs", [128, 8192], BF16, kind="ExternalInput").ap()
    ident_d = nc.dram_tensor("ident", [128, 128], BF16, kind="ExternalInput").ap()
    identf_d = nc.dram_tensor("identf", [128, 128], F32, kind="ExternalInput").ap()
    ones2_d = nc.dram_tensor("ones2", [128, 2], BF16, kind="ExternalInput").ap()
    ident2_d = nc.dram_tensor("ident2", [2, 2], BF16, kind="ExternalInput").ap()
    onescol_d = nc.dram_tensor("onescol", [1, 128], F32, kind="ExternalInput").ap()
    onescolb_d = nc.dram_tensor("onescolb", [1, 128], BF16, kind="ExternalInput").ap()
    out_d = nc.dram_tensor("out", [NQ, D], F32, kind="ExternalOutput").ap()
    aoT_d = nc.dram_tensor("aoT", [NPAIR, 128, NQ], BF16, kind="Internal").ap()

    from contextlib import ExitStack
    with ExitStack() as _stk:
        en = _stk.enter_context
        ident_sb = en(nc.sbuf_tensor([128, 128], BF16))
        identf_sb = en(nc.sbuf_tensor([128, 128], F32))
        ones2_sb = en(nc.sbuf_tensor([128, 2], BF16))
        ident2_sb = en(nc.sbuf_tensor([2, 2], BF16))
        onescol_sb = en(nc.sbuf_tensor([1, 128], F32))
        zc2_sb = en(nc.sbuf_tensor([2, 128], BF16))
        xt0 = en(nc.sbuf_tensor([128, 1024], F32))
        xt1 = en(nc.sbuf_tensor([128, 1024], F32))
        st_sb = en(nc.sbuf_tensor([128, 16], F32))
        nm_sb = en(nc.sbuf_tensor([2, 8], F32))
        biasAB = en(nc.sbuf_tensor([128, 4], F32))
        rbc = en(nc.sbuf_tensor([1, 512], F32))
        rbch = en(nc.sbuf_tensor([1, 512], BF16))
        rbcl = en(nc.sbuf_tensor([1, 512], BF16))
        onescolb_sb = en(nc.sbuf_tensor([1, 128], BF16))
        rsT_sb = en(nc.sbuf_tensor([8, 128], F32))
        ls16 = en(nc.sbuf_tensor([128, 32], F32))
        abf = en(nc.sbuf_tensor([128, 76928], BF16))
        af32 = en(nc.sbuf_tensor([128, 10400], F32))
        SD0 = en(nc.psum_tensor([128, 1536], F32))
        SD1 = en(nc.psum_tensor([128, 1024], F32))
        SDB = en(nc.psum_tensor([128, 1024], BF16))
        OA = en(nc.psum_tensor([128, 512], F32))
        MX = en(nc.psum_tensor([128, 512], F32))
        SD = (SD0, SD1)
        sems = {}
        for name in (["ld%d" % i for i in range(8)] + ["st0", "st1", "dg0", "dg1",
                                                       "tT", "vV", "sS", "gG"]):
            sems[name] = en(nc.semaphore(name="sem_" + name))
        block = en(nc.Block())
        E = _Emit(sems)
        PL = _DPool(E, ["ld%d" % i for i in range(8)])
        PS = _DPool(E, ["st0", "st1"])
        PG = _DPool(E, ["dg0", "dg1"])

        xnT = abf[:, 0:32768]
        tabs = abf[:, 32768:40960]
        CC = tabs[:, 0:4096]
        SS = tabs[:, 4096:8192]
        wbf = abf[:, 40960:44032]
        QTb = (abf[:, 44032:46080], abf[:, 46080:48128])
        KTb = (abf[:, 48128:52224], abf[:, 52224:56320])
        Vpb = (abf[:, 56320:60480], abf[:, 60480:64640])
        Tstg = abf[:, 64640:68736]
        Tsw = abf[:, 68736:72832]
        stag = (abf[:, 72832:73856], abf[:, 73856:74880])
        aopair = abf[:, 74880:76928]
        xnb = Tstg[:, 0:1024]
        aoTs = abf[:, 0:16384]
        woutb = abf[:, 16384:24576]
        qrhs = abf[:, 24576:32768]
        qlhs16 = abf[:, 44032:60416]
        qlT = tabs[:, 0:1024]
        qtstg = tabs[:, 1024:2048]
        tr_bf = tabs[:, 2048:3072]

        wstg = af32[:, 0:3072]
        oaev = af32[:, 3072:3584]
        rstg = af32[:, 0:4096]
        trash = af32[:, 4096:5120]
        rsb = af32[:, 5120:6144]
        bouts = af32[:, 6144:7168]
        gbuf = af32[:, 7168:8192]
        sbf = af32[:, 8192:9216]
        p3t = af32[:, 9216:10240]
        rsv = af32[:, 10240:10248]
        pst = af32[:, 10272:10400]
        rcv = af32[:, 10248:10256]
        rsrow = p3t[0:1, 0:1024]
        outb = (rstg[:, 0:1024], rstg[:, 1024:2048])

        st_sum = st_sb[:, 0:1]; st_ssq = st_sb[:, 1:2]; st_mean = st_sb[:, 2:3]
        st_ex2 = st_sb[:, 3:4]; st_m2 = st_sb[:, 4:5]; st_var = st_sb[:, 5:6]
        st_vep = st_sb[:, 6:7]; st_s0 = st_sb[:, 7:8]; st_r0 = st_sb[:, 8:9]
        st_a = st_sb[:, 9:10]; st_b = st_sb[:, 10:11]; st_c = st_sb[:, 11:12]
        st_rstd = st_sb[:, 12:13]; st_amax = st_sb[:, 13:14]
        st_lrcp = st_sb[:, 15:16]

        toks = {}
        for name, dst, src in (("ident", ident_sb, ident_d), ("identf", identf_sb, identf_d),
                               ("ones2", ones2_sb, ones2_d), ("ident2", ident2_sb, ident2_d),
                               ("onescol", onescol_sb, onescol_d), ("onescolb", onescolb_sb, onescolb_d), ("tabs", tabs, tabs_d)):
            toks[name] = PL.dma("sync", lambda e, d=dst, s=src: e.dma_start(out=d[:], in_=s[:]))

        E.op("vector", lambda e: e.memset(zc2_sb[:], 0.0))
        for vb in Vpb:
            vo = vb.rearrange("p (t h c) -> p t h c", t=32, h=2)[:, :, :, 64:65]
            E.op("vector", lambda e, vo=vo: e.memset(vo, 1.0))
        E.op("vector", lambda e: e.nop(), inc="vV")

        xt = (xt0, xt1)
        v_tile_done = [0] * 33
        t_tr_done = [0] * 33
        first_tr = True

        def emit_ln(buf, out_ap):
            E.op("vector", lambda e, b=buf: e.reduce_sum(out=st_sum, in_=b[:], axis=AX.X))
            E.op("vector", lambda e: e.drain())
            E.opd("vector", lambda e: e.tensor_scalar_mul(out=st_mean, in0=st_sum, scalar1=1.0 / D))
            E.opd("vector", lambda e: e.tensor_scalar_mul(out=st_ex2, in0=st_ssq, scalar1=1.0 / D))
            E.opd("vector", lambda e: e.tensor_mul(out=st_m2, in0=st_mean, in1=st_mean))
            E.opd("vector", lambda e: e.tensor_sub(out=st_var, in0=st_ex2, in1=st_m2))
            E.opd("vector", lambda e: e.tensor_scalar_add(out=st_vep, in0=st_var, scalar1=EPS), inc="vV")
            v_vep = E.cnt["vV"]
            E.w("scalar", "vV", v_vep)
            E.opd("scalar", lambda e: e.activation(out=st_s0, in_=st_vep, func=AF.Ln))
            E.op("scalar", lambda e: e.activation(out=st_r0, in_=st_s0, func=AF.Exp, scale=-0.5), inc="sS")
            s_r0 = E.cnt["sS"]
            E.w("vector", "sS", s_r0)
            E.opd("vector", lambda e: e.tensor_mul(out=st_a, in0=st_r0, in1=st_r0))
            E.opd("vector", lambda e: e.tensor_mul(out=st_b, in0=st_a, in1=st_vep))
            E.opd("vector", lambda e: e.tensor_scalar(out=st_c, in0=st_b, scalar1=-0.5, scalar2=1.5, op0=ALU.mult, op1=ALU.add))
            E.opd("vector", lambda e: e.tensor_mul(out=st_rstd, in0=st_r0, in1=st_c))
            E.opd("vector", lambda e, o=out_ap, b=buf: e.tensor_scalar(out=o, in0=b[:], scalar1=st_mean, scalar2=st_rstd, op0=ALU.subtract, op1=ALU.mult), inc="vV")
            return E.cnt["vV"]

        sums = pst[:, 0:32]; ssqs = pst[:, 32:64]
        rstds = pst[:, 64:96]; means = pst[:, 96:128]
        s_sq = [0] * 32
        v_sum = [0] * 32
        for i in range(32):
            if i >= 2:
                E.w("sync", "sS", s_sq[i - 2])
                E.w("sync", "vV", v_sum[i - 2])
            tok_x = PL.dma("sync", lambda e, i=i: e.dma_start(out=xt[i % 2][:], in_=xin[i * 128:(i + 1) * 128, :]))
            E.wt("scalar", tok_x)
            E.op("scalar", lambda e, i=i: e.activation(out=trash[:], in_=xt[i % 2][:], func=AF.Square, accum_out=ssqs[:, i:i + 1]), inc="sS")
            s_sq[i] = E.cnt["sS"]
            E.wt("vector", tok_x)
            E.op("vector", lambda e, i=i: e.reduce_sum(out=sums[:, i:i + 1], in_=xt[i % 2][:], axis=AX.X), inc="vV")
            v_sum[i] = E.cnt["vV"]
        E.w("vector", "sS", s_sq[31])
        E.op("vector", lambda e: e.drain())
        E.opd("vector", lambda e: e.tensor_scalar_mul(out=means[:], in0=sums[:], scalar1=1.0 / D))
        E.opd("vector", lambda e: e.tensor_scalar_mul(out=ssqs[:], in0=ssqs[:], scalar1=1.0 / D))
        E.opd("vector", lambda e: e.tensor_mul(out=sums[:], in0=means[:], in1=means[:]))
        E.opd("vector", lambda e: e.tensor_sub(out=ssqs[:], in0=ssqs[:], in1=sums[:]))
        E.opd("vector", lambda e: e.tensor_scalar_add(out=ssqs[:], in0=ssqs[:], scalar1=EPS), inc="vV")
        v_vep = E.cnt["vV"]
        E.w("scalar", "vV", v_vep)
        E.opd("scalar", lambda e: e.activation(out=sums[:], in_=ssqs[:], func=AF.Ln))
        E.op("scalar", lambda e: e.activation(out=rstds[:], in_=sums[:], func=AF.Exp, scale=-0.5), inc="sS")
        s_r0 = E.cnt["sS"]
        E.w("vector", "sS", s_r0)
        E.opd("vector", lambda e: e.tensor_mul(out=sums[:], in0=rstds[:], in1=rstds[:]))
        E.opd("vector", lambda e: e.tensor_mul(out=sums[:], in0=sums[:], in1=ssqs[:]))
        E.opd("vector", lambda e: e.tensor_scalar(out=sums[:], in0=sums[:], scalar1=-0.5, scalar2=1.5, op0=ALU.mult, op1=ALU.add))
        E.opd("vector", lambda e: e.tensor_mul(out=rstds[:], in0=rstds[:], in1=sums[:]), inc="vV")
        v_stats = E.cnt["vV"]
        state = {"wconv": 0, "last_proj_t": 0, "mx_rd": 0,
                 "vp_ready": [0, 0], "bias_ready": [0, 0],
                 "prev_exp": [0, 0], "prev_av": [0, 0]}
        tok_spill = [None] * NPAIR

        def gen_proj(p):
            par = p % 2
            QT, KT, Vp = QTb[par], KTb[par], Vpb[par]
            E.w("sync", "vV", state["wconv"])
            tok_w = PL.dma("sync", lambda e, p=p: e.dma_start(out=wstg[:], in_=wpair[p]))
            E.wt("vector", tok_w)
            E.w("vector", "tT", state["last_proj_t"])
            if p == 0:
                E.wt("vector", toks["tabs"])
            E.op("vector", lambda e: e.tensor_copy(out=wbf[:], in_=wstg[:]), inc="vV")
            state["wconv"] = E.cnt["vV"]
            wconv = state["wconv"]
            yield

            evac_marks = {}

            def proj_slices(sel_off, nslices):
                for s in range(nslices):
                    if p == 0:
                        rdy = v_tile_done[min(4 * s + 3, 31)]
                        assert rdy > 0
                        E.w("tensor", "vV", max(rdy, wconv))
                    else:
                        E.w("tensor", "vV", max(v_xnT_all, wconv))
                    if (sel_off, s - 1) in evac_marks:
                        E.w("tensor", "vV", evac_marks[(sel_off, s - 1)])
                    for c in range(8):
                        lhsT = wbf[:, c * 384 + sel_off: c * 384 + sel_off + 128]
                        rhs = xnT[:, c * 4096 + s * 512: c * 4096 + (s + 1) * 512]
                        E.op("tensor", lambda e, l=lhsT, r=rhs, c=c:
                             e.matmul(out=SD0[:, 1024:1536], lhsT=l, rhs=r, start=(c == 0), stop=(c == 7)),
                             inc="tT" if c == 7 else None)
                    t_mm = E.cnt["tT"]
                    E.w("vector", "tT", t_mm)
                    E.opd("vector", lambda e, s=s:
                          e.tensor_copy(out=Tstg[:, s * 512:(s + 1) * 512], in_=SD0[:, 1024:1536]), inc="vV")
                    evac_marks[(sel_off, s)] = E.cnt["vV"]
                    yield

            def rope(n, dst):
                for sb, db in ((32, 0), (0, 32), (96, 64), (64, 96)):
                    E.op("vector", lambda e, n=n, sb=sb, db=db: e.tensor_copy(out=Tsw[db:db + 32, 0:n], in_=Tstg[sb:sb + 32, 0:n]))
                E.op("vector", lambda e: e.drain())
                E.op("vector", lambda e, n=n: e.tensor_mul(out=Tsw[:, 0:n], in0=Tsw[:, 0:n], in1=SS[:, 0:n]))
                E.opd("vector", lambda e, n=n: e.tensor_mul(out=Tstg[:, 0:n], in0=Tstg[:, 0:n], in1=CC[:, 0:n]))
                E.opd("vector", lambda e, n=n, dst=dst: e.tensor_add(out=dst, in0=Tstg[:, 0:n], in1=Tsw[:, 0:n]), inc="vV")

            def norms_sq(src, ncols):
                E.op("vector", lambda e, src=src, n=ncols: e.tensor_mul(out=Tsw[:, 0:n], in0=src, in1=src), inc="vV")
                return E.cnt["vV"]

            def norms_mm(v_sq, ncols, out_col):
                E.w("tensor", "vV", max(v_sq, state["mx_rd"]))
                nsl = ncols // 512
                for s in range(nsl):
                    E.op("tensor", lambda e, s=s, nsl=nsl: e.matmul(
                        out=MX[0:2, :], lhsT=ones2_sb[:],
                        rhs=Tsw[:, s * 512:(s + 1) * 512], start=(s == 0), stop=(s == nsl - 1)),
                        inc="tT" if s == nsl - 1 else None)
                t_nm = E.cnt["tT"]
                E.w("vector", "tT", t_nm)
                E.op("vector", lambda e: e.reduce_max(out=st_sb[0:2, 0:1], in_=MX[0:2, :], axis=AX.X))
                E.op("vector", lambda e: e.drain())
                E.opd("vector", lambda e, oc=out_col: e.tensor_copy(out=nm_sb[:, oc:oc + 1], in_=st_sb[0:2, 0:1]), inc="vV")
                state["mx_rd"] = E.cnt["vV"]

            yield from proj_slices(0, 4)
            rope(2048, QT[:, :])
            v_sq_q = norms_sq(QT[:, :], 2048)
            yield
            yield
            yield
            norms_mm(v_sq_q, 2048, 0)
            yield
            yield from proj_slices(128, 8)
            rope(4096, KT[:, :])
            v_sq_k = norms_sq(KT[:, :], 4096)
            yield
            yield
            yield
            yield
            yield
            yield
            norms_mm(v_sq_k, 4096, 1)
            yield
            E.opd("vector", lambda e: e.tensor_add(out=nm_sb[:, 2:3], in0=nm_sb[:, 0:1], in1=nm_sb[:, 1:2]), inc="vV")
            E.opd("vector", lambda e: e.tensor_scalar_mul(out=nm_sb[:, 4:5], in0=nm_sb[:, 2:3], scalar1=SCALE * 0.5))
            E.op("vector", lambda e: e.tensor_scalar(out=zc2_sb[:], in0=zc2_sb[:], scalar1=nm_sb[:, 4:5], scalar2=None, op0=ALU.add), inc="vV")
            v_mb = E.cnt["vV"]
            yield
            yield
            E.w("tensor", "vV", max(v_mb, state["mx_rd"]))
            if p == 0:
                E.wt("tensor", toks["ident2"])
                E.wt("tensor", toks["ones2"])
            E.op("tensor", lambda e: e.matmul(out=MX[:, 0:2], lhsT=zc2_sb[:], rhs=ident2_sb[:], start=True, stop=True), inc="tT")
            t_mb = E.cnt["tT"]
            E.w("vector", "tT", t_mb)
            E.op("vector", lambda e, par=par: e.tensor_scalar_mul(out=biasAB[:, par * 2:par * 2 + 2], in0=MX[:, 0:2], scalar1=-1.0))
            E.opd("vector", lambda e: e.memset(zc2_sb[:], 0.0), inc="vV")
            state["bias_ready"][par] = E.cnt["vV"]
            state["mx_rd"] = E.cnt["vV"]
            yield
            vs_inner = proj_slices(256, 8)
            vp_prev = 0

            def vp_tr(s):
                nonlocal vp_prev
                E.w("tensor", "vV", evac_marks[(256, s)])
                if vp_prev:
                    E.w("tensor", "vV", vp_prev)
                for j in range(4):
                    E.op("tensor", lambda e, s=s, j=j: e.transpose(
                        out=SDB[:, j * 128:(j + 1) * 128],
                        in_=Tstg[:, s * 512 + j * 128: s * 512 + (j + 1) * 128],
                        identity=ident_sb[:]), inc="tT" if j == 3 else None)
                t_vt = E.cnt["tT"]
                E.w("vector", "tT", t_vt)
                dst = Vp.rearrange("p (t h c) -> p t h c", t=32, h=2)[:, 4 * s:4 * s + 4, :, 0:64]
                src = SDB[:, 0:512].rearrange("p (j h c) -> p j h c", j=4, h=2, c=64)[:, :, :, 0:64]
                E.op("vector", lambda e, dst=dst, src=src: e.tensor_copy(out=dst, in_=src), inc="vV")
                vp_prev = E.cnt["vV"]

            for s in range(8):
                next(vs_inner)
                if s >= 1:
                    vp_tr(s - 1)
                yield
            vp_tr(7)
            state["last_proj_t"] = E.cnt["tT"]
            state["vp_ready"][par] = E.cnt["vV"]

        def run_gen(g):
            if g is not None:
                for _ in g:
                    pass

        gen0 = gen_proj(0)
        PUMP0 = {3: 1, 4: 1, 8: 1, 12: 1, 16: 1, 17: 1, 18: 1, 19: 1,
                 20: 1, 21: 1, 22: 1, 23: 1, 24: 1, 25: 1, 28: 1}
        xnb2 = (Tsw[:, 2048:3072], Tsw[:, 3072:4096])
        v_nrm = [0] * 32
        for i in range(32):
            if i < 2:
                E.w("sync", "sS", s_sq[30 + i])
                E.w("sync", "vV", v_sum[30 + i])
            else:
                E.w("sync", "vV", v_nrm[i - 2])
            tok_x = PL.dma("sync", lambda e, i=i: e.dma_start(out=xt[i % 2][:], in_=xin[i * 128:(i + 1) * 128, :]))
            E.wt("vector", tok_x)
            if i >= 2:
                E.w("vector", "tT", t_tr_done[i - 2])
            E.op("vector", lambda e, i=i: e.tensor_scalar(out=xnb2[i % 2][:], in0=xt[i % 2][:], scalar1=means[:, i:i + 1], scalar2=rstds[:, i:i + 1], op0=ALU.subtract, op1=ALU.mult), inc="vV")
            v_nrm[i] = E.cnt["vV"]
            if i >= 1:
                E.w("vector", "tT", t_tr_done[i - 1])
                dst = xnT.rearrange("p (c t) -> p c t", c=8)[:, :, (i - 1) * 128:i * 128]
                E.op("vector", lambda e, dst=dst: e.tensor_copy(out=dst, in_=SDB[:, 0:1024]), inc="vV")
                v_tile_done[i - 1] = E.cnt["vV"]
            E.w("tensor", "vV", v_nrm[i])
            if i >= 1:
                E.w("tensor", "vV", v_tile_done[i - 1])
            if first_tr:
                E.wt("tensor", toks["ident"])
                first_tr = False
            for c in range(8):
                E.op("tensor", lambda e, c=c, i=i: e.transpose(out=SDB[:, c * 128:(c + 1) * 128], in_=xnb2[i % 2][:, c * 128:(c + 1) * 128], identity=ident_sb[:]), inc="tT" if c == 7 else None)
            t_tr_done[i] = E.cnt["tT"]
            for _ in range(PUMP0.get(i, 0)):
                next(gen0, None)
        E.w("vector", "tT", t_tr_done[31])
        dst = xnT.rearrange("p (c t) -> p c t", c=8)[:, :, 31 * 128:32 * 128]
        E.op("vector", lambda e, dst=dst: e.tensor_copy(out=dst, in_=SDB[:, 0:1024]), inc="vV")
        v_tile_done[31] = E.cnt["vV"]
        v_xnT_all = E.cnt["vV"]
        run_gen(gen0)


        def gen_p3prep():
            W2 = af32[:, 4096:8192]
            lpt = state["last_proj_t"]
            v_wc = 0
            for h in range(2):
                if h == 1:
                    E.w("sync", "vV", v_wc)
                tok_wo = PL.dma("sync", lambda e, h=h: e.dma_start(out=W2[:], in_=wout_d[:, h * 4096:(h + 1) * 4096]))
                E.wt("vector", tok_wo)
                if h == 0:
                    E.w("vector", "tT", lpt)
                E.op("vector", lambda e, h=h: e.tensor_copy(out=woutb[:, h * 4096:(h + 1) * 4096], in_=W2[:]), inc="vV")
                v_wc = E.cnt["vV"]
                yield
            v_qq = 0
            first_t = True
            for h in range(2):
                E.w("sync", "vV", v_wc if h == 0 else v_qq)
                tok_rh = PL.dma("sync", lambda e, h=h: e.dma_start(out=W2[:], in_=rhsT_d[:, h * 4096:(h + 1) * 4096]))
                for c4 in range(4):
                    cc = h * 4 + c4
                    E.wt("vector", tok_rh)
                    E.op("vector", lambda e, c4=c4: e.reduce_max(out=st_amax, in_=W2[:, c4 * 1024:(c4 + 1) * 1024], axis=AX.X, apply_absolute_value=True))
                    E.op("vector", lambda e: e.drain())
                    E.opd("vector", lambda e: e.tensor_scalar_max(out=st_amax, in0=st_amax, scalar1=1e-12))
                    E.opd("vector", lambda e, cc=cc: e.tensor_scalar_mul(out=ls16[:, 16 + cc:17 + cc], in0=st_amax, scalar1=1.0 / INT8_MAX))
                    E.opd("vector", lambda e, cc=cc: e.reciprocal(out=st_lrcp, in_=ls16[:, 16 + cc:17 + cc]))
                    E.opd("vector", lambda e, c4=c4: e.tensor_scalar(out=p3t[:], in0=W2[:, c4 * 1024:(c4 + 1) * 1024], scalar1=st_lrcp, scalar2=MAGIC, op0=ALU.mult, op1=ALU.add))
                    E.op("vector", lambda e: e.tensor_scalar(out=qtstg[:], in0=p3t[:], scalar1=MAGIC, scalar2=None, op0=ALU.subtract), inc="vV")
                    v_q8 = E.cnt["vV"]
                    yield
                    E.w("tensor", "vV", v_q8)
                    if first_t:
                        E.w("tensor", "vV", state["vp_ready"][1])
                        first_t = False
                    for fc in range(8):
                        E.op("tensor", lambda e, fc=fc: e.transpose(
                            out=SDB[:, fc * 128:(fc + 1) * 128], in_=qtstg[:, fc * 128:(fc + 1) * 128],
                            identity=ident_sb[:]), inc="tT" if fc == 7 else None)
                    t_qb = E.cnt["tT"]
                    E.w("vector", "tT", t_qb)
                    if cc == 0:
                        E.w("vector", "tT", lpt)
                    dst = qrhs.rearrange("p (f c) -> p f c", f=8)[:, :, cc * 128:(cc + 1) * 128]
                    src2 = SDB.rearrange("p (f c) -> p f c", f=8)
                    E.op("vector", lambda e, dst=dst, src2=src2: e.tensor_copy(out=dst, in_=src2), inc="vV")
                    v_qq = E.cnt["vV"]
                    yield

        pending = []
        v_oa_free = 0
        v_rb = 0
        v_norm_prev = 0

        def pop_pending():
            nonlocal v_oa_free, v_norm_prev, v_rb
            fn = pending.pop(0)
            r = fn()
            if fn.kind == "tail":
                v_oa_free, v_rb = r
            elif fn.kind == "norm":
                v_norm_prev = r

        for p in range(NPAIR):
            par = p % 2
            QT, KT, Vp = QTb[par], KTb[par], Vpb[par]
            bias = biasAB[:, par * 2:par * 2 + 2]
            gen = gen_proj(p + 1) if p + 1 < NPAIR else gen_p3prep()

            import os as _os
            if _os.environ.get("KDBG"):
                print(f"pair {p}: vp_ready={state['vp_ready']}, bias={state['bias_ready']}, vV={E.cnt['vV']}")
            E.w("tensor", "vV", state["vp_ready"][par])
            GRP = [(2 * i, 2) for i in range(16)]
            exp_done = dict()
            av_done = dict()
            G = 0
            gen_budget = 0.0
            first_qhip = True
            for q in range(4):
                for hip in range(2):
                    g0 = G
                    gl = g0 + len(GRP) - 1
                    for gi, (base, ln) in enumerate(GRP):
                        g = g0 + gi
                        bank = SD[g % 2]
                        if g >= 2:
                            E.w("tensor", "sS", exp_done[g - 2])
                        elif state["prev_exp"][g % 2]:
                            E.w("tensor", "sS", state["prev_exp"][g % 2])
                        for k in range(ln):
                            ci = base + k
                            E.op("tensor", lambda e, ci=ci, k=k, q=q, hip=hip, bank=bank, KT=KT, QT=QT: e.matmul(
                                out=bank[:, k * 512:(k + 1) * 512],
                                lhsT=KT[hip * 64:(hip + 1) * 64, ci * 128:(ci + 1) * 128],
                                rhs=QT[hip * 64:(hip + 1) * 64, q * 512:(q + 1) * 512],
                                start=True, stop=True), inc="tT" if k == ln - 1 else None)
                        t_sc = E.cnt["tT"]
                        E.w("scalar", "tT", t_sc)
                        if g >= 2:
                            E.w("scalar", "tT", av_done[g - 2])
                        elif state["prev_av"][g % 2]:
                            E.w("scalar", "tT", state["prev_av"][g % 2])
                        if first_qhip and gi == 0:
                            E.w("scalar", "vV", state["bias_ready"][par])
                        E.op("scalar", lambda e, ln=ln, g=g, bank=bank, bias=bias, hip=hip: e.activation(
                            out=stag[g % 2][:, 0:ln * 512], in_=bank[:, 0:ln * 512], func=AF.Exp,
                            bias=bias[:, hip:hip + 1], scale=SCALE), inc="sS")
                        exp_done[g] = E.cnt["sS"]
                        while pending and (pending[0].kind == "tail" or gi >= 5):
                            pop_pending()
                        gen_budget += 39.0 / 128.0
                        while gen is not None and gen_budget >= 1.0:
                            gen_budget -= 1.0
                            try:
                                next(gen)
                            except StopIteration:
                                gen = None
                        if gi >= 1:
                            gp = g - 1
                            E.w("tensor", "sS", exp_done[gp])
                            if gi == 1:
                                E.w("tensor", "vV", v_oa_free)
                                E.w("tensor", "vV", v_norm_prev)
                            bb, bl = GRP[gi - 1]
                            for k in range(bl):
                                ci = bb + k
                                E.op("tensor", lambda e, ci=ci, k=k, gp=gp, hip=hip, Vp=Vp,
                                     st_=(gp == g0 and k == 0): e.matmul(
                                    out=OA[0:65, :], lhsT=Vp[:, ci * 130 + hip * 65: ci * 130 + hip * 65 + 65],
                                    rhs=stag[gp % 2][:, k * 512:(k + 1) * 512],
                                    start=st_, stop=False), inc="tT" if k == bl - 1 else None)
                            av_done[gp] = E.cnt["tT"]
                    G = gl + 1
                    first_qhip = False

                    def _tail(gl=gl, hip=hip, Vp=Vp, exp_done=exp_done, av_done=av_done,
                              last=(q == 3 and hip == 1)):
                        E.w("tensor", "sS", exp_done[gl])
                        bb, bl = GRP[-1]
                        for k in range(bl):
                            ci = bb + k
                            E.op("tensor", lambda e, ci=ci, k=k, gl=gl, hip=hip, Vp=Vp,
                                 sp_=(k == bl - 1): e.matmul(
                                out=OA[0:65, :], lhsT=Vp[:, ci * 130 + hip * 65: ci * 130 + hip * 65 + 65],
                                rhs=stag[gl % 2][:, k * 512:(k + 1) * 512],
                                start=False, stop=sp_), inc="tT" if k == bl - 1 else None)
                        av_done[gl] = E.cnt["tT"]
                        if last:
                            state["prev_av"][1] = av_done[gl]
                        E.w("vector", "tT", av_done[gl])
                        E.opd("vector", lambda e: e.tensor_copy(out=oaev[0:65, :], in_=OA[0:65, :]), inc="vV")
                        oa_free = E.cnt["vV"]
                        E.opd("vector", lambda e: e.reciprocal(out=rbc[0:1, :], in_=oaev[64:65, :]))
                        E.opd("vector", lambda e: e.tensor_copy(out=rbch[0:1, :], in_=rbc[0:1, :]))
                        E.opd("vector", lambda e: e.scalar_tensor_tensor(out=rbcl[0:1, :], in0=rbch[0:1, :], scalar=-1.0, in1=rbc[0:1, :], op0=ALU.mult, op1=ALU.add), inc="vV")
                        return (oa_free, E.cnt["vV"])
                    _tail.kind = "tail"
                    pending.append(_tail)

                    def _norm(q=q, hip=hip, p=p, first=(p == 0 and q == 0 and hip == 0),
                              spill_wait=(p >= 1 and q == 0 and hip == 0)):
                        E.w("tensor", "vV", max(v_rb, state["mx_rd"]))
                        if first:
                            E.wt("tensor", toks["onescolb"])
                        E.op("tensor", lambda e: e.matmul(out=MX[0:64, :], lhsT=onescolb_sb[0:1, 0:64], rhs=rbch[0:1, :], start=True, stop=False))
                        E.op("tensor", lambda e: e.matmul(out=MX[0:64, :], lhsT=onescolb_sb[0:1, 0:64], rhs=rbcl[0:1, :], start=False, stop=True), inc="tT")
                        t_bc = E.cnt["tT"]
                        E.w("vector", "tT", t_bc)
                        if spill_wait:
                            E.wt("vector", tok_spill[p - 1])
                        E.opd("vector", lambda e, q=q, hip=hip: e.tensor_mul(
                            out=aopair[hip * 64:(hip + 1) * 64, q * 512:(q + 1) * 512],
                            in0=oaev[0:64, :], in1=MX[0:64, :]), inc="vV")
                        state["mx_rd"] = E.cnt["vV"]
                        return E.cnt["vV"]
                    _norm.kind = "norm"
                    pending.append(_norm)
            run_gen(gen)
            state["prev_exp"] = [exp_done[126], exp_done[127]]
            state["prev_av"] = [av_done[126], state["prev_av"][1]]

            def _spill(p=p):
                E.w("gpsimd", "vV", v_norm_prev)
                tok_spill[p] = PG.dma("gpsimd", lambda e, p=p: e.dma_start(out=aoT_d[p], in_=aopair[:]))
            _spill.kind = "spill"
            pending.append(_spill)
        while pending:
            pop_pending()

        p2_v = E.cnt["vV"]; p2_t = E.cnt["tT"]; p2_s = E.cnt["sS"]

        E.wt("sync", tok_spill[NPAIR - 2])
        E.wt("sync", tok_spill[NPAIR - 1])
        E.opd("vector", lambda e: e.tensor_copy(out=rsv[:, 0:8], in_=ls16[:, 16:24]), inc="vV")
        v_rsv = E.cnt["vV"]
        E.w("tensor", "vV", max(v_rsv, state["mx_rd"]))
        E.wt("tensor", toks["identf"])
        E.op("tensor", lambda e: e.transpose(out=MX[0:8, 0:128], in_=rsv[:], identity=identf_sb[:]), inc="tT")
        t_rs = E.cnt["tT"]
        E.w("vector", "tT", t_rs)
        E.op("vector", lambda e: e.tensor_copy(out=rsT_sb[:], in_=MX[0:8, 0:128]), inc="vV")
        v_rsT = E.cnt["vV"]
        E.w("gpsimd", "vV", v_rsT)
        tok_rr = PG.dma("gpsimd", lambda e: e.dma_start(out=rsrow, in_=rsT_sb[:]))
        E.wt("tensor", tok_rr)
        E.w("tensor", "sS", p2_s)
        E.op("tensor", lambda e: e.matmul(out=SD0[:, 0:512], lhsT=onescol_sb[:], rhs=rsrow[0:1, 0:512], start=True, stop=True))
        E.op("tensor", lambda e: e.matmul(out=SD0[:, 512:1024], lhsT=onescol_sb[:], rhs=rsrow[0:1, 512:1024], start=True, stop=True), inc="tT")
        t_rsb = E.cnt["tT"]
        E.w("vector", "tT", t_rsb)
        E.op("vector", lambda e: e.tensor_copy(out=rsb[:, 0:512], in_=SD0[:, 0:512]))
        E.opd("vector", lambda e: e.tensor_copy(out=rsb[:, 512:1024], in_=SD0[:, 512:1024]), inc="vV")
        v_rsb = E.cnt["vV"]
        tok_bo = PL.dma("sync", lambda e: e.dma_start(out=bouts[:], in_=bout_d[:]))
        aoT_src = aoT_d.rearrange("p q c -> q p c")
        aoT_dst = aoTs.rearrange("q (p c) -> q p c", p=NPAIR)
        tok_ao = PL.dma("sync", lambda e, s=aoT_src, d=aoT_dst: e.dma_start(out=d, in_=s))

        v_a = [0] * 16
        for t in range(16):
            if t >= 2:
                E.w("sync", "vV", v_a[t - 2])
            tok_x = PL.dma("sync", lambda e, t=t: e.dma_start(out=xt[t % 2][:], in_=xin[t * 128:(t + 1) * 128, :]))
            E.wt("vector", tok_x)
            if t == 0:
                E.w("vector", "vV", v_rsb)
            E.opd("vector", lambda e, t=t: e.tensor_scalar(out=sbf[:], in0=xt[t % 2][:], scalar1=means[:, t:t + 1], scalar2=rstds[:, t:t + 1], op0=ALU.subtract, op1=ALU.mult))
            E.op("vector", lambda e: e.reduce_max(out=st_amax, in_=sbf[:], axis=AX.X, apply_absolute_value=True))
            E.op("vector", lambda e: e.drain())
            E.opd("vector", lambda e: e.tensor_scalar_max(out=st_amax, in0=st_amax, scalar1=1e-12))
            E.opd("vector", lambda e, t=t: e.tensor_scalar_mul(out=ls16[:, t:t + 1], in0=st_amax, scalar1=1.0 / INT8_MAX))
            E.opd("vector", lambda e, t=t: e.reciprocal(out=st_lrcp, in_=ls16[:, t:t + 1]))
            E.opd("vector", lambda e: e.tensor_scalar(out=p3t[:], in0=sbf[:], scalar1=st_lrcp, scalar2=MAGIC, op0=ALU.mult, op1=ALU.add))
            E.op("vector", lambda e, t=t: e.tensor_scalar(out=qlhs16[:, t * 1024:(t + 1) * 1024], in0=p3t[:], scalar1=MAGIC, scalar2=None, op0=ALU.subtract), inc="vV")
            v_a[t] = E.cnt["vV"]
        v_a_all = E.cnt["vV"]

        st_tok = [None, None]
        v_b = [0] * 16
        v_ql = [0] * 16
        v_g1l = [0] * 16
        v_s1l = [0] * 16
        for t in range(16):
            if t == 0:
                E.w("sync", "vV", v_a_all)
            if t >= 2:
                E.w("sync", "vV", v_b[t - 2])
            tok_x = PL.dma("sync", lambda e, t=t: e.dma_start(out=xt[t % 2][:], in_=xin[t * 128:(t + 1) * 128, :]))
            if t == 0:
                E.w("tensor", "vV", v_a_all)
                E.wt("tensor", tok_ao)
            else:
                E.w("tensor", "vV", v_ql[t - 1])
            for c in range(8):
                E.op("tensor", lambda e, c=c, t=t: e.transpose(out=SDB[:, c * 128:(c + 1) * 128], in_=qlhs16[:, t * 1024 + c * 128: t * 1024 + (c + 1) * 128], identity=ident_sb[:]), inc="tT" if c == 7 else None)
            t_ql = E.cnt["tT"]
            E.w("vector", "tT", t_ql)
            E.op("vector", lambda e: e.tensor_copy(out=qlT[:], in_=SDB[:, 0:1024]), inc="vV")
            v_qlT = E.cnt["vV"]
            v_ql[t] = v_qlT

            E.w("tensor", "vV", v_qlT)
            if t >= 1:
                E.w("tensor", "vV", v_g1l[t - 1])
            for c in range(8):
                for hh in range(2):
                    E.op("tensor", lambda e, c=c, hh=hh: e.matmul(
                        out=SD0[:, hh * 512:(hh + 1) * 512], lhsT=qlT[:, c * 128:(c + 1) * 128],
                        rhs=qrhs[:, c * 1024 + hh * 512: c * 1024 + (hh + 1) * 512],
                        start=(c == 0), stop=(c == 7)), inc="tT" if (c == 7 and hh == 1) else None)
            t_ffn = E.cnt["tT"]
            if t >= 1:
                E.w("tensor", "vV", v_s1l[t - 1])
            for pp in range(8):
                for hh in range(2):
                    E.op("tensor", lambda e, pp=pp, hh=hh, t=t: e.matmul(
                        out=SD1[:, hh * 512:(hh + 1) * 512],
                        lhsT=aoTs[:, pp * 2048 + t * 128: pp * 2048 + (t + 1) * 128],
                        rhs=woutb[:, pp * 1024 + hh * 512: pp * 1024 + (hh + 1) * 512],
                        start=(pp == 0), stop=(pp == 7)), inc="tT" if (pp == 7 and hh == 1) else None)
            t_op = E.cnt["tT"]

            E.w("vector", "tT", t_ffn)
            E.opd("vector", lambda e: e.tensor_mul(out=p3t[:], in0=SD0[:, 0:1024], in1=rsb[:]), inc="vV")
            v_g1 = E.cnt["vV"]
            v_g1l[t] = v_g1
            E.w("scalar", "vV", v_g1)
            E.op("scalar", lambda e, t=t: e.activation(out=gbuf[:], in_=p3t[:], func=AF.Silu, scale=ls16[:, t:t + 1]), inc="sS")
            s_g = E.cnt["sS"]
            E.w("vector", "tT", t_op)
            E.wt("vector", tok_x)
            E.opd("vector", lambda e: e.tensor_add(out=sbf[:], in0=SD1[:, 0:1024], in1=bouts[:]), inc="vV")
            v_s1l[t] = E.cnt["vV"]
            E.w("vector", "sS", s_g)
            if t >= 2:
                E.wt("vector", st_tok[t % 2])
            E.opd("vector", lambda e: e.tensor_mul(out=sbf[:], in0=sbf[:], in1=gbuf[:]))
            E.op("vector", lambda e, t=t: e.tensor_add(out=outb[t % 2][:], in0=sbf[:], in1=xt[t % 2][:]), inc="vV")
            v_b[t] = E.cnt["vV"]
            E.w("sync", "vV", v_b[t])
            st_tok[t % 2] = PS.dma("sync", lambda e, t=t: e.dma_start(out=out_d[t * 128:(t + 1) * 128, :], in_=outb[t % 2][:]))

        E.w("sync", "st0", E.cnt["st0"])
        E.w("sync", "st1", E.cnt["st1"])

        @block.sync
        def _(e):
            for fn in E.ops["sync"]:
                fn(e)

        @block.tensor
        def _(e):
            for fn in E.ops["tensor"]:
                fn(e)

        @block.vector
        def _(e):
            for fn in E.ops["vector"]:
                fn(e)

        @block.scalar
        def _(e):
            for fn in E.ops["scalar"]:
                fn(e)

        @block.gpsimd
        def _(e):
            for fn in E.ops["gpsimd"]:
                fn(e)

    return nc



def _prep_shared(w_qkv, w_out, b_out, rhs):
    wp = np.empty((NPAIR, 128, 3072), dtype=np.float32)
    for p in range(NPAIR):
        wq = w_qkv[:, p * 128:(p + 1) * 128]
        wk = w_qkv[:, 1024 + p * 128:1024 + (p + 1) * 128]
        wv = w_qkv[:, 2048 + p * 128:2048 + (p + 1) * 128]
        cat = np.concatenate([wq, wk, wv], axis=1)
        wp[p] = np.ascontiguousarray(cat.reshape(8, 128, 384).transpose(1, 0, 2).reshape(128, 3072))
    o2 = np.zeros((128, 2), dtype=np.float32)
    o2[0:64, 0] = 1.0
    o2[64:128, 1] = 1.0
    rhsT = np.ascontiguousarray(rhs.T)
    consts = {
        "wout": np.ascontiguousarray(w_out.reshape(8, 128, 1024).transpose(1, 0, 2).reshape(128, 8192)).astype(np.float32),
        "rhsT": np.ascontiguousarray(rhsT.reshape(8, 128, 1024).transpose(1, 0, 2).reshape(128, 8192)).astype(np.float32),
        "bout": np.ascontiguousarray(np.broadcast_to(b_out[None, :], (128, 1024))).astype(np.float32),
        "ident": np.eye(128, dtype=np.float32).astype(bfloat16),
        "identf": np.eye(128, dtype=np.float32),
        "ident2": np.eye(2, dtype=np.float32).astype(bfloat16),
        "onescol": np.ones((1, 128), dtype=np.float32),
        "onescolb": np.ones((1, 128), dtype=np.float32).astype(bfloat16),
        "ones2": o2.astype(bfloat16),
    }
    return wp, consts


def make_in_maps(x, w_qkv, w_out, b_out, rhs):
    wp, consts = _prep_shared(w_qkv, w_out, b_out, rhs)
    tab64 = _host_tables()
    in_maps = []
    for core in range(8):
        b, half = core // 2, core % 2
        perm = np.concatenate([np.arange(half * NQ, (half + 1) * NQ),
                               np.arange((1 - half) * NQ, (2 - half) * NQ)])
        xp = np.ascontiguousarray(x[b][perm])
        tabs = np.ascontiguousarray(
            tab64.reshape(128, 2, N)[:, :, perm].reshape(128, 2 * N)).astype(bfloat16)
        m = {"xin": xp, "wpair": wp, "tabs": tabs}
        m.update(consts)
        in_maps.append(m)
    return in_maps


def kernel(L0_attn_x, L0_attn_w_qkv, L0_attn_w_out, L0_attn_b_out, L0_ffn_rhs):
    x = np.asarray(L0_attn_x, dtype=np.float32)
    in_maps = make_in_maps(x,
                           np.asarray(L0_attn_w_qkv, dtype=np.float32),
                           np.asarray(L0_attn_w_out, dtype=np.float32),
                           np.asarray(L0_attn_b_out, dtype=np.float32),
                           np.asarray(L0_ffn_rhs, dtype=np.float32))
    nc = build()
    res = run_bass_kernel_spmd(nc, in_maps, core_ids=list(range(8)))
    out = np.empty((B, N, D), dtype=np.float32)
    for core in range(8):
        b, half = core // 2, core % 2
        out[b, half * NQ:(half + 1) * NQ] = res.results[core]["out"]
    return out


# version 19
# speedup vs baseline: 1.9628x; 1.0814x over previous
import numpy as np
import ml_dtypes
import concourse.bass as bass
import concourse.mybir as mybir
from concourse import library_config
from concourse.bass_utils import run_bass_kernel_spmd

F32 = mybir.dt.float32
BF16 = mybir.dt.bfloat16
AF = mybir.ActivationFunctionType
ALU = mybir.AluOpType
AX = mybir.AxisListType
bfloat16 = ml_dtypes.bfloat16

B, N, D = 4, 4096, 1024
HEADS, DH, HALF = 16, 64, 32
NQ = 2048
NPAIR = 8
SCALE = float(DH) ** -0.5
MAGIC = 12582912.0
EPS = 1e-5
INT8_MAX = 127.0


def _host_tables():
    j = np.arange(HALF, dtype=np.float64)
    theta = 10000.0 ** (-j / HALF)
    pos = np.arange(N, dtype=np.float64)
    ang = np.outer(theta, pos)
    cos = np.cos(ang)
    sin = np.sin(ang)
    CCt = np.concatenate([cos, cos, cos, cos], axis=0)
    SSt = np.concatenate([-sin, sin, -sin, sin], axis=0)
    return np.concatenate([CCt, SSt], axis=1)


class _Emit:
    ENGS = ("sync", "tensor", "vector", "scalar", "gpsimd")

    def __init__(self, sems):
        self.ops = {e: [] for e in self.ENGS}
        self.cnt = {k: 0 for k in sems}
        self.sems = sems

    def wt(self, eng, tok):
        if tok is not None:
            self.w(eng, tok[0], tok[1])

    def w(self, eng, sem, val):
        if val <= 0:
            return
        assert val <= self.cnt[sem], f"wait {sem}>={val} emitted={self.cnt[sem]}"
        s = self.sems[sem]
        self.ops[eng].append(lambda e, s=s, v=val: e.wait_ge(s, v))

    def op(self, eng, fn, inc=None, n=1):
        if inc is None:
            self.ops[eng].append(fn)
            return None
        s = self.sems[inc]
        self.cnt[inc] += n
        self.ops[eng].append(lambda e, fn=fn, s=s, n=n: fn(e).then_inc(s, n))
        return self.cnt[inc]

    def opd(self, eng, fn, inc=None, n=1):
        r = self.op(eng, fn, inc=inc, n=n)
        self.ops[eng].append(lambda e: e.drain())
        return r


class _DPool:

    def __init__(self, E, names):
        self.E = E
        self.names = names
        self.i = 0

    def dma(self, eng, fn):
        s = self.names[self.i % len(self.names)]
        self.i += 1
        self.E.w(eng, s, self.E.cnt[s])
        v = self.E.op(eng, fn, inc=s, n=16)
        return (s, v)


def build(debug=False):
    nc = bass.Bass("TRN2", target_bir_lowering=False)

    xin = nc.dram_tensor("xin", [N, D], F32, kind="ExternalInput").ap()
    wpair = nc.dram_tensor("wpair", [NPAIR, 128, 3072], F32, kind="ExternalInput").ap()
    wout_d = nc.dram_tensor("wout", [128, 8192], F32, kind="ExternalInput").ap()
    rhsT_d = nc.dram_tensor("rhsT", [128, 8192], F32, kind="ExternalInput").ap()
    bout_d = nc.dram_tensor("bout", [128, 1024], F32, kind="ExternalInput").ap()
    tabs_d = nc.dram_tensor("tabs", [128, 8192], BF16, kind="ExternalInput").ap()
    ident_d = nc.dram_tensor("ident", [128, 128], BF16, kind="ExternalInput").ap()
    identf_d = nc.dram_tensor("identf", [128, 128], F32, kind="ExternalInput").ap()
    ones2_d = nc.dram_tensor("ones2", [128, 2], BF16, kind="ExternalInput").ap()
    ident2_d = nc.dram_tensor("ident2", [2, 2], BF16, kind="ExternalInput").ap()
    onescol_d = nc.dram_tensor("onescol", [1, 128], F32, kind="ExternalInput").ap()
    onescolb_d = nc.dram_tensor("onescolb", [1, 128], BF16, kind="ExternalInput").ap()
    out_d = nc.dram_tensor("out", [NQ, D], F32, kind="ExternalOutput").ap()
    aoT_d = nc.dram_tensor("aoT", [NPAIR, 128, NQ], BF16, kind="Internal").ap()

    from contextlib import ExitStack
    with ExitStack() as _stk:
        en = _stk.enter_context
        ident_sb = en(nc.sbuf_tensor([128, 128], BF16))
        identf_sb = en(nc.sbuf_tensor([128, 128], F32))
        ones2_sb = en(nc.sbuf_tensor([128, 2], BF16))
        ident2_sb = en(nc.sbuf_tensor([2, 2], BF16))
        onescol_sb = en(nc.sbuf_tensor([1, 128], F32))
        zc2_sb = en(nc.sbuf_tensor([2, 128], BF16))
        xt0 = en(nc.sbuf_tensor([128, 1024], F32))
        xt1 = en(nc.sbuf_tensor([128, 1024], F32))
        st_sb = en(nc.sbuf_tensor([128, 16], F32))
        nm_sb = en(nc.sbuf_tensor([2, 8], F32))
        biasAB = en(nc.sbuf_tensor([128, 4], F32))
        rbc = en(nc.sbuf_tensor([1, 512], F32))
        rbch = en(nc.sbuf_tensor([1, 512], BF16))
        rbcl = en(nc.sbuf_tensor([1, 512], BF16))
        onescolb_sb = en(nc.sbuf_tensor([1, 128], BF16))
        rsT_sb = en(nc.sbuf_tensor([8, 128], F32))
        ls16 = en(nc.sbuf_tensor([128, 32], F32))
        abf = en(nc.sbuf_tensor([128, 76928], BF16))
        af32 = en(nc.sbuf_tensor([128, 10400], F32))
        SD0 = en(nc.psum_tensor([128, 1536], F32))
        SD1 = en(nc.psum_tensor([128, 1024], F32))
        SDB = en(nc.psum_tensor([128, 1024], BF16))
        OA = en(nc.psum_tensor([128, 512], F32))
        MX = en(nc.psum_tensor([128, 512], F32))
        SD = (SD0, SD1)
        sems = {}
        for name in (["ld%d" % i for i in range(8)] + ["st0", "st1", "dg0", "dg1",
                                                       "tT", "vV", "sS", "gG"]):
            sems[name] = en(nc.semaphore(name="sem_" + name))
        block = en(nc.Block())
        E = _Emit(sems)
        PL = _DPool(E, ["ld%d" % i for i in range(8)])
        PS = _DPool(E, ["st0", "st1"])
        PG = _DPool(E, ["dg0", "dg1"])

        xnT = abf[:, 0:32768]
        tabs = abf[:, 32768:40960]
        CC = tabs[:, 0:4096]
        SS = tabs[:, 4096:8192]
        wbf = abf[:, 40960:44032]
        QTb = (abf[:, 44032:46080], abf[:, 46080:48128])
        KTb = (abf[:, 48128:52224], abf[:, 52224:56320])
        Vpb = (abf[:, 56320:60480], abf[:, 60480:64640])
        Tstg = abf[:, 64640:68736]
        Tsw = abf[:, 68736:72832]
        stag = (abf[:, 72832:73856], abf[:, 73856:74880])
        aopair = abf[:, 74880:76928]
        xnb = Tstg[:, 0:1024]
        aoTs = abf[:, 0:16384]
        woutb = abf[:, 16384:24576]
        qrhs = abf[:, 24576:32768]
        _ql16_off = ([44032 + 1024 * t for t in range(2)] +
                     [48128 + 1024 * t for t in range(4)] +
                     [56320 + 1024 * t for t in range(4)] +
                     [64640 + 1024 * t for t in range(4)] +
                     [68736 + 1024 * t for t in range(2)])

        def ql16(t):
            o = _ql16_off[t]
            return abf[:, o:o + 1024]
        qlT2 = (tabs[:, 0:1024], tabs[:, 1024:2048])
        qtstg = tabs[:, 2048:3072]

        wstg = af32[:, 0:3072]
        oaev = af32[:, 3072:3584]
        rstg = af32[:, 0:4096]
        trash = af32[:, 4096:5120]
        rsb = af32[:, 5120:6144]
        bouts = af32[:, 6144:7168]
        gbuf = af32[:, 7168:8192]
        sbf = af32[:, 8192:9216]
        p3t = af32[:, 9216:10240]
        rsv = af32[:, 10240:10248]
        pst = af32[:, 10272:10400]
        rcv = af32[:, 10248:10256]
        rsrow = p3t[0:1, 0:1024]
        outb = (rstg[:, 0:1024], rstg[:, 1024:2048])

        st_sum = st_sb[:, 0:1]; st_ssq = st_sb[:, 1:2]; st_mean = st_sb[:, 2:3]
        st_ex2 = st_sb[:, 3:4]; st_m2 = st_sb[:, 4:5]; st_var = st_sb[:, 5:6]
        st_vep = st_sb[:, 6:7]; st_s0 = st_sb[:, 7:8]; st_r0 = st_sb[:, 8:9]
        st_a = st_sb[:, 9:10]; st_b = st_sb[:, 10:11]; st_c = st_sb[:, 11:12]
        st_rstd = st_sb[:, 12:13]; st_amax = st_sb[:, 13:14]
        st_lrcp = st_sb[:, 15:16]

        toks = {}
        for name, dst, src in (("ident", ident_sb, ident_d), ("identf", identf_sb, identf_d),
                               ("ones2", ones2_sb, ones2_d), ("ident2", ident2_sb, ident2_d),
                               ("onescol", onescol_sb, onescol_d), ("onescolb", onescolb_sb, onescolb_d), ("tabs", tabs, tabs_d)):
            toks[name] = PL.dma("sync", lambda e, d=dst, s=src: e.dma_start(out=d[:], in_=s[:]))

        E.op("vector", lambda e: e.memset(zc2_sb[:], 0.0))
        for vb in Vpb:
            vo = vb.rearrange("p (t h c) -> p t h c", t=32, h=2)[:, :, :, 64:65]
            E.op("vector", lambda e, vo=vo: e.memset(vo, 1.0))
        E.op("vector", lambda e: e.nop(), inc="vV")

        xt = (xt0, xt1)
        v_tile_done = [0] * 33
        t_tr_done = [0] * 33
        first_tr = True

        def emit_ln(buf, out_ap):
            E.op("vector", lambda e, b=buf: e.reduce_sum(out=st_sum, in_=b[:], axis=AX.X))
            E.op("vector", lambda e: e.drain())
            E.opd("vector", lambda e: e.tensor_scalar_mul(out=st_mean, in0=st_sum, scalar1=1.0 / D))
            E.opd("vector", lambda e: e.tensor_scalar_mul(out=st_ex2, in0=st_ssq, scalar1=1.0 / D))
            E.opd("vector", lambda e: e.tensor_mul(out=st_m2, in0=st_mean, in1=st_mean))
            E.opd("vector", lambda e: e.tensor_sub(out=st_var, in0=st_ex2, in1=st_m2))
            E.opd("vector", lambda e: e.tensor_scalar_add(out=st_vep, in0=st_var, scalar1=EPS), inc="vV")
            v_vep = E.cnt["vV"]
            E.w("scalar", "vV", v_vep)
            E.opd("scalar", lambda e: e.activation(out=st_s0, in_=st_vep, func=AF.Ln))
            E.op("scalar", lambda e: e.activation(out=st_r0, in_=st_s0, func=AF.Exp, scale=-0.5), inc="sS")
            s_r0 = E.cnt["sS"]
            E.w("vector", "sS", s_r0)
            E.opd("vector", lambda e: e.tensor_mul(out=st_a, in0=st_r0, in1=st_r0))
            E.opd("vector", lambda e: e.tensor_mul(out=st_b, in0=st_a, in1=st_vep))
            E.opd("vector", lambda e: e.tensor_scalar(out=st_c, in0=st_b, scalar1=-0.5, scalar2=1.5, op0=ALU.mult, op1=ALU.add))
            E.opd("vector", lambda e: e.tensor_mul(out=st_rstd, in0=st_r0, in1=st_c))
            E.opd("vector", lambda e, o=out_ap, b=buf: e.tensor_scalar(out=o, in0=b[:], scalar1=st_mean, scalar2=st_rstd, op0=ALU.subtract, op1=ALU.mult), inc="vV")
            return E.cnt["vV"]

        sums = pst[:, 0:32]; ssqs = pst[:, 32:64]
        rstds = pst[:, 64:96]; means = pst[:, 96:128]
        s_sq = [0] * 32
        v_sum = [0] * 32
        for i in range(32):
            if i >= 2:
                E.w("sync", "sS", s_sq[i - 2])
                E.w("sync", "vV", v_sum[i - 2])
            tok_x = PL.dma("sync", lambda e, i=i: e.dma_start(out=xt[i % 2][:], in_=xin[i * 128:(i + 1) * 128, :]))
            E.wt("scalar", tok_x)
            E.op("scalar", lambda e, i=i: e.activation(out=trash[:], in_=xt[i % 2][:], func=AF.Square, accum_out=ssqs[:, i:i + 1]), inc="sS")
            s_sq[i] = E.cnt["sS"]
            E.wt("vector", tok_x)
            E.op("vector", lambda e, i=i: e.reduce_sum(out=sums[:, i:i + 1], in_=xt[i % 2][:], axis=AX.X), inc="vV")
            v_sum[i] = E.cnt["vV"]
        E.w("vector", "sS", s_sq[31])
        E.op("vector", lambda e: e.drain())
        E.opd("vector", lambda e: e.tensor_scalar_mul(out=means[:], in0=sums[:], scalar1=1.0 / D))
        E.opd("vector", lambda e: e.tensor_scalar_mul(out=ssqs[:], in0=ssqs[:], scalar1=1.0 / D))
        E.opd("vector", lambda e: e.tensor_mul(out=sums[:], in0=means[:], in1=means[:]))
        E.opd("vector", lambda e: e.tensor_sub(out=ssqs[:], in0=ssqs[:], in1=sums[:]))
        E.opd("vector", lambda e: e.tensor_scalar_add(out=ssqs[:], in0=ssqs[:], scalar1=EPS), inc="vV")
        v_vep = E.cnt["vV"]
        E.w("scalar", "vV", v_vep)
        E.opd("scalar", lambda e: e.activation(out=sums[:], in_=ssqs[:], func=AF.Ln))
        E.op("scalar", lambda e: e.activation(out=rstds[:], in_=sums[:], func=AF.Exp, scale=-0.5), inc="sS")
        s_r0 = E.cnt["sS"]
        E.w("vector", "sS", s_r0)
        E.opd("vector", lambda e: e.tensor_mul(out=sums[:], in0=rstds[:], in1=rstds[:]))
        E.opd("vector", lambda e: e.tensor_mul(out=sums[:], in0=sums[:], in1=ssqs[:]))
        E.opd("vector", lambda e: e.tensor_scalar(out=sums[:], in0=sums[:], scalar1=-0.5, scalar2=1.5, op0=ALU.mult, op1=ALU.add))
        E.opd("vector", lambda e: e.tensor_mul(out=rstds[:], in0=rstds[:], in1=sums[:]), inc="vV")
        v_stats = E.cnt["vV"]
        state = {"wconv": 0, "last_proj_t": 0, "mx_rd": 0,
                 "vp_ready": [0, 0], "bias_ready": [0, 0],
                 "prev_exp": [0, 0], "prev_av": [0, 0]}
        tok_spill = [None] * NPAIR

        def gen_proj(p):
            par = p % 2
            QT, KT, Vp = QTb[par], KTb[par], Vpb[par]
            E.w("sync", "vV", state["wconv"])
            tok_w = PL.dma("sync", lambda e, p=p: e.dma_start(out=wstg[:], in_=wpair[p]))
            E.wt("vector", tok_w)
            E.w("vector", "tT", state["last_proj_t"])
            if p == 0:
                E.wt("vector", toks["tabs"])
            E.op("vector", lambda e: e.tensor_copy(out=wbf[:], in_=wstg[:]), inc="vV")
            state["wconv"] = E.cnt["vV"]
            wconv = state["wconv"]
            yield

            evac_marks = {}

            def proj_slices(sel_off, nslices):
                for s in range(nslices):
                    if p == 0:
                        rdy = v_tile_done[min(4 * s + 3, 31)]
                        assert rdy > 0
                        E.w("tensor", "vV", max(rdy, wconv))
                    else:
                        E.w("tensor", "vV", max(v_xnT_all, wconv))
                    if (sel_off, s - 1) in evac_marks:
                        E.w("tensor", "vV", evac_marks[(sel_off, s - 1)])
                    for c in range(8):
                        lhsT = wbf[:, c * 384 + sel_off: c * 384 + sel_off + 128]
                        rhs = xnT[:, c * 4096 + s * 512: c * 4096 + (s + 1) * 512]
                        E.op("tensor", lambda e, l=lhsT, r=rhs, c=c:
                             e.matmul(out=SD0[:, 1024:1536], lhsT=l, rhs=r, start=(c == 0), stop=(c == 7)),
                             inc="tT" if c == 7 else None)
                    t_mm = E.cnt["tT"]
                    E.w("vector", "tT", t_mm)
                    E.opd("vector", lambda e, s=s:
                          e.tensor_copy(out=Tstg[:, s * 512:(s + 1) * 512], in_=SD0[:, 1024:1536]), inc="vV")
                    evac_marks[(sel_off, s)] = E.cnt["vV"]
                    yield

            def rope(n, dst):
                for sb, db in ((32, 0), (0, 32), (96, 64), (64, 96)):
                    E.op("vector", lambda e, n=n, sb=sb, db=db: e.tensor_copy(out=Tsw[db:db + 32, 0:n], in_=Tstg[sb:sb + 32, 0:n]))
                E.op("vector", lambda e: e.drain())
                E.op("vector", lambda e, n=n: e.tensor_mul(out=Tsw[:, 0:n], in0=Tsw[:, 0:n], in1=SS[:, 0:n]))
                E.opd("vector", lambda e, n=n: e.tensor_mul(out=Tstg[:, 0:n], in0=Tstg[:, 0:n], in1=CC[:, 0:n]))
                E.opd("vector", lambda e, n=n, dst=dst: e.tensor_add(out=dst, in0=Tstg[:, 0:n], in1=Tsw[:, 0:n]), inc="vV")

            def norms_sq(src, ncols):
                E.op("vector", lambda e, src=src, n=ncols: e.tensor_mul(out=Tsw[:, 0:n], in0=src, in1=src), inc="vV")
                return E.cnt["vV"]

            def norms_mm(v_sq, ncols, out_col):
                E.w("tensor", "vV", max(v_sq, state["mx_rd"]))
                nsl = ncols // 512
                for s in range(nsl):
                    E.op("tensor", lambda e, s=s, nsl=nsl: e.matmul(
                        out=MX[0:2, :], lhsT=ones2_sb[:],
                        rhs=Tsw[:, s * 512:(s + 1) * 512], start=(s == 0), stop=(s == nsl - 1)),
                        inc="tT" if s == nsl - 1 else None)
                t_nm = E.cnt["tT"]
                E.w("vector", "tT", t_nm)
                E.op("vector", lambda e: e.reduce_max(out=st_sb[0:2, 0:1], in_=MX[0:2, :], axis=AX.X))
                E.op("vector", lambda e: e.drain())
                E.opd("vector", lambda e, oc=out_col: e.tensor_copy(out=nm_sb[:, oc:oc + 1], in_=st_sb[0:2, 0:1]), inc="vV")
                state["mx_rd"] = E.cnt["vV"]

            yield from proj_slices(0, 4)
            rope(2048, QT[:, :])
            v_sq_q = norms_sq(QT[:, :], 2048)
            yield
            yield
            yield
            norms_mm(v_sq_q, 2048, 0)
            yield
            yield from proj_slices(128, 8)
            rope(4096, KT[:, :])
            v_sq_k = norms_sq(KT[:, :], 4096)
            yield
            yield
            yield
            yield
            yield
            yield
            norms_mm(v_sq_k, 4096, 1)
            yield
            E.opd("vector", lambda e: e.tensor_add(out=nm_sb[:, 2:3], in0=nm_sb[:, 0:1], in1=nm_sb[:, 1:2]), inc="vV")
            E.opd("vector", lambda e: e.tensor_scalar_mul(out=nm_sb[:, 4:5], in0=nm_sb[:, 2:3], scalar1=SCALE * 0.5))
            E.op("vector", lambda e: e.tensor_scalar(out=zc2_sb[:], in0=zc2_sb[:], scalar1=nm_sb[:, 4:5], scalar2=None, op0=ALU.add), inc="vV")
            v_mb = E.cnt["vV"]
            yield
            yield
            E.w("tensor", "vV", max(v_mb, state["mx_rd"]))
            if p == 0:
                E.wt("tensor", toks["ident2"])
                E.wt("tensor", toks["ones2"])
            E.op("tensor", lambda e: e.matmul(out=MX[:, 0:2], lhsT=zc2_sb[:], rhs=ident2_sb[:], start=True, stop=True), inc="tT")
            t_mb = E.cnt["tT"]
            E.w("vector", "tT", t_mb)
            E.op("vector", lambda e, par=par: e.tensor_scalar_mul(out=biasAB[:, par * 2:par * 2 + 2], in0=MX[:, 0:2], scalar1=-1.0))
            E.opd("vector", lambda e: e.memset(zc2_sb[:], 0.0), inc="vV")
            state["bias_ready"][par] = E.cnt["vV"]
            state["mx_rd"] = E.cnt["vV"]
            yield
            vs_inner = proj_slices(256, 8)
            vp_prev = 0

            def vp_tr(s):
                nonlocal vp_prev
                E.w("tensor", "vV", evac_marks[(256, s)])
                if vp_prev:
                    E.w("tensor", "vV", vp_prev)
                for j in range(4):
                    E.op("tensor", lambda e, s=s, j=j: e.transpose(
                        out=SDB[:, j * 128:(j + 1) * 128],
                        in_=Tstg[:, s * 512 + j * 128: s * 512 + (j + 1) * 128],
                        identity=ident_sb[:]), inc="tT" if j == 3 else None)
                t_vt = E.cnt["tT"]
                E.w("vector", "tT", t_vt)
                dst = Vp.rearrange("p (t h c) -> p t h c", t=32, h=2)[:, 4 * s:4 * s + 4, :, 0:64]
                src = SDB[:, 0:512].rearrange("p (j h c) -> p j h c", j=4, h=2, c=64)[:, :, :, 0:64]
                E.op("vector", lambda e, dst=dst, src=src: e.tensor_copy(out=dst, in_=src), inc="vV")
                vp_prev = E.cnt["vV"]

            for s in range(8):
                next(vs_inner)
                if s >= 1:
                    vp_tr(s - 1)
                yield
            vp_tr(7)
            state["last_proj_t"] = E.cnt["tT"]
            state["vp_ready"][par] = E.cnt["vV"]

        def run_gen(g):
            if g is not None:
                for _ in g:
                    pass

        gen0 = gen_proj(0)
        PUMP0 = {3: 1, 4: 1, 8: 1, 12: 1, 16: 1, 17: 1, 18: 1, 19: 1,
                 20: 1, 21: 1, 22: 1, 23: 1, 24: 1, 25: 1, 28: 1}
        xnb2 = (Tsw[:, 2048:3072], Tsw[:, 3072:4096])
        v_nrm = [0] * 32
        for i in range(32):
            if i < 2:
                E.w("sync", "sS", s_sq[30 + i])
                E.w("sync", "vV", v_sum[30 + i])
            else:
                E.w("sync", "vV", v_nrm[i - 2])
            tok_x = PL.dma("sync", lambda e, i=i: e.dma_start(out=xt[i % 2][:], in_=xin[i * 128:(i + 1) * 128, :]))
            E.wt("vector", tok_x)
            if i >= 2:
                E.w("vector", "tT", t_tr_done[i - 2])
            E.op("vector", lambda e, i=i: e.tensor_scalar(out=xnb2[i % 2][:], in0=xt[i % 2][:], scalar1=means[:, i:i + 1], scalar2=rstds[:, i:i + 1], op0=ALU.subtract, op1=ALU.mult), inc="vV")
            v_nrm[i] = E.cnt["vV"]
            if i >= 1:
                E.w("vector", "tT", t_tr_done[i - 1])
                dst = xnT.rearrange("p (c t) -> p c t", c=8)[:, :, (i - 1) * 128:i * 128]
                E.op("vector", lambda e, dst=dst: e.tensor_copy(out=dst, in_=SDB[:, 0:1024]), inc="vV")
                v_tile_done[i - 1] = E.cnt["vV"]
            E.w("tensor", "vV", v_nrm[i])
            if i >= 1:
                E.w("tensor", "vV", v_tile_done[i - 1])
            if first_tr:
                E.wt("tensor", toks["ident"])
                first_tr = False
            for c in range(8):
                E.op("tensor", lambda e, c=c, i=i: e.transpose(out=SDB[:, c * 128:(c + 1) * 128], in_=xnb2[i % 2][:, c * 128:(c + 1) * 128], identity=ident_sb[:]), inc="tT" if c == 7 else None)
            t_tr_done[i] = E.cnt["tT"]
            for _ in range(PUMP0.get(i, 0)):
                next(gen0, None)
        E.w("vector", "tT", t_tr_done[31])
        dst = xnT.rearrange("p (c t) -> p c t", c=8)[:, :, 31 * 128:32 * 128]
        E.op("vector", lambda e, dst=dst: e.tensor_copy(out=dst, in_=SDB[:, 0:1024]), inc="vV")
        v_tile_done[31] = E.cnt["vV"]
        v_xnT_all = E.cnt["vV"]
        run_gen(gen0)


        def gen_p3prep():
            W2 = af32[:, 4096:8192]
            lpt = state["last_proj_t"]
            v_a = [0] * 16
            for t in range(16):
                if t >= 2:
                    E.w("sync", "vV", v_a[t - 2])
                tok_x = PL.dma("sync", lambda e, t=t: e.dma_start(out=xt[t % 2][:], in_=xin[t * 128:(t + 1) * 128, :]))
                E.wt("vector", tok_x)
                if t == 0:
                    E.w("vector", "sS", state["prev_exp"][1])
                if t == 6:
                    E.w("vector", "tT", state["prev_av"][1])
                E.opd("vector", lambda e, t=t: e.tensor_scalar(out=sbf[:], in0=xt[t % 2][:], scalar1=means[:, t:t + 1], scalar2=rstds[:, t:t + 1], op0=ALU.subtract, op1=ALU.mult))
                E.op("vector", lambda e: e.reduce_max(out=st_amax, in_=sbf[:], axis=AX.X, apply_absolute_value=True))
                E.op("vector", lambda e: e.drain())
                E.opd("vector", lambda e: e.tensor_scalar_max(out=st_amax, in0=st_amax, scalar1=1e-12))
                E.opd("vector", lambda e, t=t: e.tensor_scalar_mul(out=ls16[:, t:t + 1], in0=st_amax, scalar1=1.0 / INT8_MAX))
                E.opd("vector", lambda e, t=t: e.reciprocal(out=st_lrcp, in_=ls16[:, t:t + 1]))
                E.opd("vector", lambda e: e.tensor_scalar(out=p3t[:], in0=sbf[:], scalar1=st_lrcp, scalar2=MAGIC, op0=ALU.mult, op1=ALU.add))
                E.op("vector", lambda e, t=t: e.tensor_scalar(out=ql16(t)[:], in0=p3t[:], scalar1=MAGIC, scalar2=None, op0=ALU.subtract), inc="vV")
                v_a[t] = E.cnt["vV"]
                yield
            state["v_a_all"] = E.cnt["vV"]
            v_wc = 0
            for h in range(2):
                if h == 1:
                    E.w("sync", "vV", v_wc)
                tok_wo = PL.dma("sync", lambda e, h=h: e.dma_start(out=W2[:], in_=wout_d[:, h * 4096:(h + 1) * 4096]))
                E.wt("vector", tok_wo)
                if h == 0:
                    E.w("vector", "tT", lpt)
                E.op("vector", lambda e, h=h: e.tensor_copy(out=woutb[:, h * 4096:(h + 1) * 4096], in_=W2[:]), inc="vV")
                v_wc = E.cnt["vV"]
                yield
            v_qq = 0
            first_t = True
            for h in range(2):
                E.w("sync", "vV", v_wc if h == 0 else v_qq)
                tok_rh = PL.dma("sync", lambda e, h=h: e.dma_start(out=W2[:], in_=rhsT_d[:, h * 4096:(h + 1) * 4096]))
                for c4 in range(4):
                    cc = h * 4 + c4
                    E.wt("vector", tok_rh)
                    E.op("vector", lambda e, c4=c4: e.reduce_max(out=st_amax, in_=W2[:, c4 * 1024:(c4 + 1) * 1024], axis=AX.X, apply_absolute_value=True))
                    E.op("vector", lambda e: e.drain())
                    E.opd("vector", lambda e: e.tensor_scalar_max(out=st_amax, in0=st_amax, scalar1=1e-12))
                    E.opd("vector", lambda e, cc=cc: e.tensor_scalar_mul(out=ls16[:, 16 + cc:17 + cc], in0=st_amax, scalar1=1.0 / INT8_MAX))
                    E.opd("vector", lambda e, cc=cc: e.reciprocal(out=st_lrcp, in_=ls16[:, 16 + cc:17 + cc]))
                    E.opd("vector", lambda e, c4=c4: e.tensor_scalar(out=p3t[:], in0=W2[:, c4 * 1024:(c4 + 1) * 1024], scalar1=st_lrcp, scalar2=MAGIC, op0=ALU.mult, op1=ALU.add))
                    E.op("vector", lambda e: e.tensor_scalar(out=qtstg[:], in0=p3t[:], scalar1=MAGIC, scalar2=None, op0=ALU.subtract), inc="vV")
                    v_q8 = E.cnt["vV"]
                    yield
                    E.w("tensor", "vV", v_q8)
                    if first_t:
                        E.w("tensor", "vV", state["vp_ready"][1])
                        first_t = False
                    for fc in range(8):
                        E.op("tensor", lambda e, fc=fc: e.transpose(
                            out=SDB[:, fc * 128:(fc + 1) * 128], in_=qtstg[:, fc * 128:(fc + 1) * 128],
                            identity=ident_sb[:]), inc="tT" if fc == 7 else None)
                    t_qb = E.cnt["tT"]
                    E.w("vector", "tT", t_qb)
                    if cc == 0:
                        E.w("vector", "tT", lpt)
                    dst = qrhs.rearrange("p (f c) -> p f c", f=8)[:, :, cc * 128:(cc + 1) * 128]
                    src2 = SDB.rearrange("p (f c) -> p f c", f=8)
                    E.op("vector", lambda e, dst=dst, src2=src2: e.tensor_copy(out=dst, in_=src2), inc="vV")
                    v_qq = E.cnt["vV"]
                    yield

        pending = []
        v_oa_free = 0
        v_rb = 0
        v_norm_prev = 0

        def pop_pending():
            nonlocal v_oa_free, v_norm_prev, v_rb
            fn = pending.pop(0)
            r = fn()
            if fn.kind == "tail":
                v_oa_free, v_rb = r
            elif fn.kind == "norm":
                v_norm_prev = r

        for p in range(NPAIR):
            par = p % 2
            QT, KT, Vp = QTb[par], KTb[par], Vpb[par]
            bias = biasAB[:, par * 2:par * 2 + 2]
            gen = gen_proj(p + 1) if p + 1 < NPAIR else gen_p3prep()

            import os as _os
            if _os.environ.get("KDBG"):
                print(f"pair {p}: vp_ready={state['vp_ready']}, bias={state['bias_ready']}, vV={E.cnt['vV']}")
            E.w("tensor", "vV", state["vp_ready"][par])
            GRP = [(2 * i, 2) for i in range(16)]
            exp_done = dict()
            av_done = dict()
            G = 0
            gen_budget = 0.0
            first_qhip = True
            for q in range(4):
                for hip in range(2):
                    g0 = G
                    gl = g0 + len(GRP) - 1
                    for gi, (base, ln) in enumerate(GRP):
                        g = g0 + gi
                        bank = SD[g % 2]
                        if g >= 2:
                            E.w("tensor", "sS", exp_done[g - 2])
                        elif state["prev_exp"][g % 2]:
                            E.w("tensor", "sS", state["prev_exp"][g % 2])
                        for k in range(ln):
                            ci = base + k
                            E.op("tensor", lambda e, ci=ci, k=k, q=q, hip=hip, bank=bank, KT=KT, QT=QT: e.matmul(
                                out=bank[:, k * 512:(k + 1) * 512],
                                lhsT=KT[hip * 64:(hip + 1) * 64, ci * 128:(ci + 1) * 128],
                                rhs=QT[hip * 64:(hip + 1) * 64, q * 512:(q + 1) * 512],
                                start=True, stop=True), inc="tT" if k == ln - 1 else None)
                        t_sc = E.cnt["tT"]
                        E.w("scalar", "tT", t_sc)
                        if g >= 2:
                            E.w("scalar", "tT", av_done[g - 2])
                        elif state["prev_av"][g % 2]:
                            E.w("scalar", "tT", state["prev_av"][g % 2])
                        if first_qhip and gi == 0:
                            E.w("scalar", "vV", state["bias_ready"][par])
                        E.op("scalar", lambda e, ln=ln, g=g, bank=bank, bias=bias, hip=hip: e.activation(
                            out=stag[g % 2][:, 0:ln * 512], in_=bank[:, 0:ln * 512], func=AF.Exp,
                            bias=bias[:, hip:hip + 1], scale=SCALE), inc="sS")
                        exp_done[g] = E.cnt["sS"]
                        while pending and (pending[0].kind == "tail" or gi >= 5):
                            pop_pending()
                        gen_budget += 39.0 / 128.0
                        while gen is not None and gen_budget >= 1.0:
                            gen_budget -= 1.0
                            try:
                                next(gen)
                            except StopIteration:
                                gen = None
                        if gi >= 1:
                            gp = g - 1
                            E.w("tensor", "sS", exp_done[gp])
                            if gi == 1:
                                E.w("tensor", "vV", v_oa_free)
                                E.w("tensor", "vV", v_norm_prev)
                            bb, bl = GRP[gi - 1]
                            for k in range(bl):
                                ci = bb + k
                                E.op("tensor", lambda e, ci=ci, k=k, gp=gp, hip=hip, Vp=Vp,
                                     st_=(gp == g0 and k == 0): e.matmul(
                                    out=OA[0:65, :], lhsT=Vp[:, ci * 130 + hip * 65: ci * 130 + hip * 65 + 65],
                                    rhs=stag[gp % 2][:, k * 512:(k + 1) * 512],
                                    start=st_, stop=False), inc="tT" if k == bl - 1 else None)
                            av_done[gp] = E.cnt["tT"]
                    G = gl + 1
                    first_qhip = False

                    def _tail(gl=gl, hip=hip, Vp=Vp, exp_done=exp_done, av_done=av_done,
                              last=(q == 3 and hip == 1)):
                        E.w("tensor", "sS", exp_done[gl])
                        bb, bl = GRP[-1]
                        for k in range(bl):
                            ci = bb + k
                            E.op("tensor", lambda e, ci=ci, k=k, gl=gl, hip=hip, Vp=Vp,
                                 sp_=(k == bl - 1): e.matmul(
                                out=OA[0:65, :], lhsT=Vp[:, ci * 130 + hip * 65: ci * 130 + hip * 65 + 65],
                                rhs=stag[gl % 2][:, k * 512:(k + 1) * 512],
                                start=False, stop=sp_), inc="tT" if k == bl - 1 else None)
                        av_done[gl] = E.cnt["tT"]
                        if last:
                            state["prev_av"][1] = av_done[gl]
                        E.w("vector", "tT", av_done[gl])
                        E.opd("vector", lambda e: e.tensor_copy(out=oaev[0:65, :], in_=OA[0:65, :]), inc="vV")
                        oa_free = E.cnt["vV"]
                        E.opd("vector", lambda e: e.reciprocal(out=rbc[0:1, :], in_=oaev[64:65, :]))
                        E.opd("vector", lambda e: e.tensor_copy(out=rbch[0:1, :], in_=rbc[0:1, :]))
                        E.opd("vector", lambda e: e.scalar_tensor_tensor(out=rbcl[0:1, :], in0=rbch[0:1, :], scalar=-1.0, in1=rbc[0:1, :], op0=ALU.mult, op1=ALU.add), inc="vV")
                        return (oa_free, E.cnt["vV"])
                    _tail.kind = "tail"
                    pending.append(_tail)

                    def _norm(q=q, hip=hip, p=p, first=(p == 0 and q == 0 and hip == 0),
                              spill_wait=(p >= 1 and q == 0 and hip == 0)):
                        E.w("tensor", "vV", max(v_rb, state["mx_rd"]))
                        if first:
                            E.wt("tensor", toks["onescolb"])
                        E.op("tensor", lambda e: e.matmul(out=MX[0:64, :], lhsT=onescolb_sb[0:1, 0:64], rhs=rbch[0:1, :], start=True, stop=False))
                        E.op("tensor", lambda e: e.matmul(out=MX[0:64, :], lhsT=onescolb_sb[0:1, 0:64], rhs=rbcl[0:1, :], start=False, stop=True), inc="tT")
                        t_bc = E.cnt["tT"]
                        E.w("vector", "tT", t_bc)
                        if spill_wait:
                            E.wt("vector", tok_spill[p - 1])
                        E.opd("vector", lambda e, q=q, hip=hip: e.tensor_mul(
                            out=aopair[hip * 64:(hip + 1) * 64, q * 512:(q + 1) * 512],
                            in0=oaev[0:64, :], in1=MX[0:64, :]), inc="vV")
                        state["mx_rd"] = E.cnt["vV"]
                        return E.cnt["vV"]
                    _norm.kind = "norm"
                    pending.append(_norm)
            run_gen(gen)
            state["prev_exp"] = [exp_done[126], exp_done[127]]
            state["prev_av"] = [av_done[126], state["prev_av"][1]]

            def _spill(p=p):
                E.w("gpsimd", "vV", v_norm_prev)
                tok_spill[p] = PG.dma("gpsimd", lambda e, p=p: e.dma_start(out=aoT_d[p], in_=aopair[:]))
            _spill.kind = "spill"
            pending.append(_spill)
        while pending:
            pop_pending()

        p2_v = E.cnt["vV"]; p2_t = E.cnt["tT"]; p2_s = E.cnt["sS"]

        E.wt("sync", tok_spill[NPAIR - 2])
        E.wt("sync", tok_spill[NPAIR - 1])
        E.opd("vector", lambda e: e.tensor_copy(out=rsv[:, 0:8], in_=ls16[:, 16:24]), inc="vV")
        v_rsv = E.cnt["vV"]
        E.w("tensor", "vV", max(v_rsv, state["mx_rd"]))
        E.wt("tensor", toks["identf"])
        E.op("tensor", lambda e: e.transpose(out=MX[0:8, 0:128], in_=rsv[:], identity=identf_sb[:]), inc="tT")
        t_rs = E.cnt["tT"]
        E.w("vector", "tT", t_rs)
        E.op("vector", lambda e: e.tensor_copy(out=rsT_sb[:], in_=MX[0:8, 0:128]), inc="vV")
        v_rsT = E.cnt["vV"]
        E.w("gpsimd", "vV", v_rsT)
        tok_rr = PG.dma("gpsimd", lambda e: e.dma_start(out=rsrow, in_=rsT_sb[:]))
        E.wt("tensor", tok_rr)
        E.w("tensor", "sS", p2_s)
        E.op("tensor", lambda e: e.matmul(out=SD0[:, 0:512], lhsT=onescol_sb[:], rhs=rsrow[0:1, 0:512], start=True, stop=True))
        E.op("tensor", lambda e: e.matmul(out=SD0[:, 512:1024], lhsT=onescol_sb[:], rhs=rsrow[0:1, 512:1024], start=True, stop=True), inc="tT")
        t_rsb = E.cnt["tT"]
        E.w("vector", "tT", t_rsb)
        E.op("vector", lambda e: e.tensor_copy(out=rsb[:, 0:512], in_=SD0[:, 0:512]))
        E.opd("vector", lambda e: e.tensor_copy(out=rsb[:, 512:1024], in_=SD0[:, 512:1024]), inc="vV")
        v_rsb = E.cnt["vV"]
        tok_bo = PL.dma("sync", lambda e: e.dma_start(out=bouts[:], in_=bout_d[:]))
        aoT_src = aoT_d.rearrange("p q c -> q p c")
        aoT_dst = aoTs.rearrange("q (p c) -> q p c", p=NPAIR)
        tok_ao = PL.dma("sync", lambda e, s=aoT_src, d=aoT_dst: e.dma_start(out=d, in_=s))

        st_tok = [None, None]
        tok_xs = [None, None]
        v_b = [0] * 16
        v_ql = [0] * 16
        v_g1l = [0] * 16
        v_s1l = [0] * 16
        for it in range(17):
            t = it
            if t < 16:
                if t >= 2:
                    E.w("sync", "vV", v_b[t - 2])
                else:
                    E.w("sync", "vV", state["v_a_all"])
                tok_xs[t % 2] = PL.dma("sync", lambda e, t=t: e.dma_start(out=xt[t % 2][:], in_=xin[t * 128:(t + 1) * 128, :]))
                if t == 0:
                    E.w("tensor", "vV", state["v_a_all"])
                    E.wt("tensor", tok_ao)
                else:
                    E.w("tensor", "vV", v_ql[t - 1])
                for c in range(8):
                    E.op("tensor", lambda e, c=c, t=t: e.transpose(out=SDB[:, c * 128:(c + 1) * 128], in_=ql16(t)[:, c * 128:(c + 1) * 128], identity=ident_sb[:]), inc="tT" if c == 7 else None)
                t_ql = E.cnt["tT"]
                E.w("vector", "tT", t_ql)
                E.op("vector", lambda e, t=t: e.tensor_copy(out=qlT2[t % 2][:], in_=SDB[:, 0:1024]), inc="vV")
                v_ql[t] = E.cnt["vV"]
            if it >= 1:
                u = it - 1
                E.w("tensor", "vV", v_ql[u])
                if u >= 1:
                    E.w("tensor", "vV", v_g1l[u - 1])
                for c in range(8):
                    for hh in range(2):
                        E.op("tensor", lambda e, c=c, hh=hh, u=u: e.matmul(
                            out=SD0[:, hh * 512:(hh + 1) * 512], lhsT=qlT2[u % 2][:, c * 128:(c + 1) * 128],
                            rhs=qrhs[:, c * 1024 + hh * 512: c * 1024 + (hh + 1) * 512],
                            start=(c == 0), stop=(c == 7)), inc="tT" if (c == 7 and hh == 1) else None)
                t_ffn = E.cnt["tT"]
                if u >= 1:
                    E.w("tensor", "vV", v_s1l[u - 1])
                for pp in range(8):
                    for hh in range(2):
                        E.op("tensor", lambda e, pp=pp, hh=hh, u=u: e.matmul(
                            out=SD1[:, hh * 512:(hh + 1) * 512],
                            lhsT=aoTs[:, pp * 2048 + u * 128: pp * 2048 + (u + 1) * 128],
                            rhs=woutb[:, pp * 1024 + hh * 512: pp * 1024 + (hh + 1) * 512],
                            start=(pp == 0), stop=(pp == 7)), inc="tT" if (pp == 7 and hh == 1) else None)
                t_op = E.cnt["tT"]

                E.w("vector", "tT", t_ffn)
                E.opd("vector", lambda e: e.tensor_mul(out=p3t[:], in0=SD0[:, 0:1024], in1=rsb[:]), inc="vV")
                v_g1l[u] = E.cnt["vV"]
                E.w("scalar", "vV", v_g1l[u])
                E.op("scalar", lambda e, u=u: e.activation(out=gbuf[:], in_=p3t[:], func=AF.Silu, scale=ls16[:, u:u + 1]), inc="sS")
                s_g = E.cnt["sS"]
                E.w("vector", "tT", t_op)
                E.wt("vector", tok_xs[u % 2])
                E.opd("vector", lambda e: e.tensor_add(out=sbf[:], in0=SD1[:, 0:1024], in1=bouts[:]), inc="vV")
                v_s1l[u] = E.cnt["vV"]
                E.w("vector", "sS", s_g)
                if u >= 2:
                    E.wt("vector", st_tok[u % 2])
                E.opd("vector", lambda e: e.tensor_mul(out=sbf[:], in0=sbf[:], in1=gbuf[:]))
                E.op("vector", lambda e, u=u: e.tensor_add(out=outb[u % 2][:], in0=sbf[:], in1=xt[u % 2][:]), inc="vV")
                v_b[u] = E.cnt["vV"]
                E.w("sync", "vV", v_b[u])
                st_tok[u % 2] = PS.dma("sync", lambda e, u=u: e.dma_start(out=out_d[u * 128:(u + 1) * 128, :], in_=outb[u % 2][:]))

        E.w("sync", "st0", E.cnt["st0"])
        E.w("sync", "st1", E.cnt["st1"])

        @block.sync
        def _(e):
            for fn in E.ops["sync"]:
                fn(e)

        @block.tensor
        def _(e):
            for fn in E.ops["tensor"]:
                fn(e)

        @block.vector
        def _(e):
            for fn in E.ops["vector"]:
                fn(e)

        @block.scalar
        def _(e):
            for fn in E.ops["scalar"]:
                fn(e)

        @block.gpsimd
        def _(e):
            for fn in E.ops["gpsimd"]:
                fn(e)

    return nc



def _prep_shared(w_qkv, w_out, b_out, rhs):
    wp = np.empty((NPAIR, 128, 3072), dtype=np.float32)
    for p in range(NPAIR):
        wq = w_qkv[:, p * 128:(p + 1) * 128]
        wk = w_qkv[:, 1024 + p * 128:1024 + (p + 1) * 128]
        wv = w_qkv[:, 2048 + p * 128:2048 + (p + 1) * 128]
        cat = np.concatenate([wq, wk, wv], axis=1)
        wp[p] = np.ascontiguousarray(cat.reshape(8, 128, 384).transpose(1, 0, 2).reshape(128, 3072))
    o2 = np.zeros((128, 2), dtype=np.float32)
    o2[0:64, 0] = 1.0
    o2[64:128, 1] = 1.0
    rhsT = np.ascontiguousarray(rhs.T)
    consts = {
        "wout": np.ascontiguousarray(w_out.reshape(8, 128, 1024).transpose(1, 0, 2).reshape(128, 8192)).astype(np.float32),
        "rhsT": np.ascontiguousarray(rhsT.reshape(8, 128, 1024).transpose(1, 0, 2).reshape(128, 8192)).astype(np.float32),
        "bout": np.ascontiguousarray(np.broadcast_to(b_out[None, :], (128, 1024))).astype(np.float32),
        "ident": np.eye(128, dtype=np.float32).astype(bfloat16),
        "identf": np.eye(128, dtype=np.float32),
        "ident2": np.eye(2, dtype=np.float32).astype(bfloat16),
        "onescol": np.ones((1, 128), dtype=np.float32),
        "onescolb": np.ones((1, 128), dtype=np.float32).astype(bfloat16),
        "ones2": o2.astype(bfloat16),
    }
    return wp, consts


def make_in_maps(x, w_qkv, w_out, b_out, rhs):
    wp, consts = _prep_shared(w_qkv, w_out, b_out, rhs)
    tab64 = _host_tables()
    in_maps = []
    for core in range(8):
        b, half = core // 2, core % 2
        perm = np.concatenate([np.arange(half * NQ, (half + 1) * NQ),
                               np.arange((1 - half) * NQ, (2 - half) * NQ)])
        xp = np.ascontiguousarray(x[b][perm])
        tabs = np.ascontiguousarray(
            tab64.reshape(128, 2, N)[:, :, perm].reshape(128, 2 * N)).astype(bfloat16)
        m = {"xin": xp, "wpair": wp, "tabs": tabs}
        m.update(consts)
        in_maps.append(m)
    return in_maps


def kernel(L0_attn_x, L0_attn_w_qkv, L0_attn_w_out, L0_attn_b_out, L0_ffn_rhs):
    x = np.asarray(L0_attn_x, dtype=np.float32)
    in_maps = make_in_maps(x,
                           np.asarray(L0_attn_w_qkv, dtype=np.float32),
                           np.asarray(L0_attn_w_out, dtype=np.float32),
                           np.asarray(L0_attn_b_out, dtype=np.float32),
                           np.asarray(L0_ffn_rhs, dtype=np.float32))
    nc = build()
    res = run_bass_kernel_spmd(nc, in_maps, core_ids=list(range(8)))
    out = np.empty((B, N, D), dtype=np.float32)
    for core in range(8):
        b, half = core // 2, core % 2
        out[b, half * NQ:(half + 1) * NQ] = res.results[core]["out"]
    return out
